# version 24
# baseline (speedup 1.0000x reference)
import numpy as np

import concourse.bass as bass
import concourse.mybir as mybir
from concourse.bass_utils import run_bass_kernel_spmd

F32 = mybir.dt.float32
BF16 = mybir.dt.bfloat16
I32 = mybir.dt.int32
AF = mybir.ActivationFunctionType
OP = mybir.AluOpType
AX = mybir.AxisListType

ENGINES = ("sync", "scalar", "vector", "gpsimd", "tensor")



class Buf:

    __slots__ = ("name", "handle", "space", "whole", "writes", "reads", "dtype")

    def __init__(self, name, handle, space, dtype, whole=False):
        self.name = name
        self.handle = handle
        self.space = space
        self.dtype = dtype
        self.whole = whole
        self.writes = []
        self.reads = []

    def tv(self, ps=None, fs=None, elem_bytes=None):
        eb = elem_bytes or np.dtype(mybir.dt.np(self.dtype)).itemsize
        p0, p1 = ps if ps is not None else (0, self.handle.shape[0])
        if fs is None:
            ap = self.handle.ap()[p0:p1, :]
            b0, b1 = 0, self.handle.shape[1] * eb
        else:
            e0, e1 = fs
            ap = self.handle.ap()[p0:p1, e0:e1]
            b0, b1 = e0 * eb, e1 * eb
        return TV(self, ap, p0, p1, b0, b1)

    def whole_tv(self):
        return self.tv()


class TV:

    __slots__ = ("buf", "ap", "p0", "p1", "b0", "b1")

    def __init__(self, buf, ap, p0, p1, b0, b1):
        self.buf = buf
        self.ap = ap
        self.p0 = p0
        self.p1 = p1
        self.b0 = b0
        self.b1 = b1

    def with_ap(self, ap):
        return TV(self.buf, ap, self.p0, self.p1, self.b0, self.b1)

    def rearrange(self, s, **kw):
        return self.with_ap(self.ap.rearrange(s, **kw))

    def bitcast(self, dt):
        return self.with_ap(self.ap.bitcast(dt))


class Opr:
    __slots__ = ("engine", "emit", "deps", "kind", "eidx", "didx", "ccidx",
                 "needs_inc", "ms", "tag", "phase")

    def __init__(self, engine, emit, kind, eidx, didx, ccidx, tag):
        self.engine = engine
        self.emit = emit
        self.kind = kind
        self.eidx = eidx
        self.didx = didx
        self.ccidx = ccidx
        self.deps = {}
        self.needs_inc = False
        self.ms = None
        self.tag = tag


def _overlap(a, b):
    return not (a[1] <= b[0] or b[1] <= a[0])


class Prog:
    def __init__(self, nc):
        self.nc = nc
        self.phase = "init"
        self.ops = {e: [] for e in ENGINES}
        self.dma_count = {e: 0 for e in ENGINES}
        self.cc_count = 0
        self.bufs = []
        self.n_ops = 0
        self.boundaries = {e: [] for e in ENGINES}
        self.gates = {e: set() for e in ENGINES}
        self.cc_boundaries = []
        self.cc_gates = set()

    def _dep_value(self, d):
        if d.kind == "c":
            return None
        if d.kind == "d":
            import bisect
            bl = self.boundaries[d.engine]
            need = d.didx + 1
            i = bisect.bisect_left(bl, need)
            if i < len(bl):
                return ("d", d.engine, bl[i])
            b = self.dma_count[d.engine]
            assert b >= need
            bl.append(b)
            self.gates[d.engine].add(b)
            return ("d", d.engine, b)
        import bisect
        need = d.ccidx + 1
        i = bisect.bisect_left(self.cc_boundaries, need)
        if i < len(self.cc_boundaries):
            return ("cc", None, self.cc_boundaries[i])
        b = self.cc_count
        assert b >= need
        self.cc_boundaries.append(b)
        self.cc_gates.add(b)
        return ("cc", None, b)

    def sbuf(self, name, shape, dtype):
        h = self.nc.alloc_sbuf_tensor(name, list(shape), dtype)
        b = Buf(name, h, "sbuf", dtype)
        self.bufs.append(b)
        return b

    def psum(self, name, shape=(128, 512), dtype=F32):
        h = self.nc.alloc_psum_tensor(name, list(shape), dtype)
        b = Buf(name, h, "psum", dtype, whole=True)
        self.bufs.append(b)
        return b

    def dram(self, name, shape, dtype, kind="Internal", addr_space="Local"):
        h = self.nc.dram_tensor(name, list(shape), dtype, kind=kind,
                                addr_space=addr_space)
        b = Buf(name, h, "dram", dtype)
        self.bufs.append(b)
        return b

    def _regions(self, tvs):
        out = []
        for tv in tvs:
            if tv is None:
                continue
            if isinstance(tv, Buf):
                tv = tv.whole_tv()
            b = tv.buf
            if b.whole:
                out.append((b, 0, 128, 0, 1 << 30))
            else:
                out.append((b, tv.p0, tv.p1, tv.b0, tv.b1))
        return out

    def op(self, engine, emit, reads=(), writes=(), kind="c", tag=None):
        didx = ccidx = None
        if kind == "d":
            didx = self.dma_count[engine]
        elif kind == "cc":
            ccidx = self.cc_count
        o = Opr(engine, emit, kind, len(self.ops[engine]), didx, ccidx, tag)
        o.phase = self.phase
        rr = self._regions(reads)
        ww = self._regions(writes)
        dep_set = set()
        for (b, p0, p1, b0, b1) in rr:
            for (w, wp0, wp1, wb0, wb1) in b.writes:
                if w is not o and _overlap((p0, p1), (wp0, wp1)) and _overlap((b0, b1), (wb0, wb1)):
                    dep_set.add(w)
        for (b, p0, p1, b0, b1) in ww:
            for (w, wp0, wp1, wb0, wb1) in b.writes:
                if _overlap((p0, p1), (wp0, wp1)) and _overlap((b0, b1), (wb0, wb1)):
                    dep_set.add(w)
            for (r, rp0, rp1, rb0, rb1) in b.reads:
                if _overlap((p0, p1), (rp0, rp1)) and _overlap((b0, b1), (rb0, rb1)):
                    dep_set.add(r)
        for d in dep_set:
            if d.engine == "tensor" and engine == "tensor" and d.kind == "c" and kind == "c":
                continue
            o.deps[d] = self._dep_value(d)
        if kind == "d":
            self.dma_count[engine] += 1
        elif kind == "cc":
            self.cc_count += 1
        for (b, p0, p1, b0, b1) in ww:
            b.writes = [wacc for wacc in b.writes
                        if not (p0 <= wacc[1] and wacc[2] <= p1 and b0 <= wacc[3] and wacc[4] <= b1)]
            b.reads = [racc for racc in b.reads
                       if not (p0 <= racc[1] and racc[2] <= p1 and b0 <= racc[3] and racc[4] <= b1)]
            b.writes.append((o, p0, p1, b0, b1))
        for (b, p0, p1, b0, b1) in rr:
            b.reads.append((o, p0, p1, b0, b1))
        self.ops[engine].append(o)
        self.n_ops += 1
        return o

    def dma(self, engine, out_tv, in_tv, tag=None, **kw):
        oap = out_tv.ap if isinstance(out_tv, TV) else out_tv
        iap = in_tv.ap if isinstance(in_tv, TV) else in_tv

        def emit(eng, _o=oap, _i=iap, _k=dict(kw)):
            return eng.dma_start(out=_o, in_=_i, **_k)

        reads = [in_tv] if isinstance(in_tv, TV) else []
        writes = [out_tv] if isinstance(out_tv, TV) else []
        return self.op(engine, emit, reads=reads, writes=writes, kind="d", tag=tag)

    def cc(self, kind, aluop, replica_groups, in_tv, out_tv, tag=None):
        iap = in_tv.ap if isinstance(in_tv, TV) else in_tv
        oap = out_tv.ap if isinstance(out_tv, TV) else out_tv

        def emit(eng, _k=kind, _op=aluop, _rg=replica_groups, _i=iap, _o=oap):
            return eng.collective_compute(_k, _op, replica_groups=_rg,
                                          ins=[_i.opt()], outs=[_o.opt()])

        reads = [in_tv] if isinstance(in_tv, TV) else []
        writes = [out_tv] if isinstance(out_tv, TV) else []
        return self.op("gpsimd", emit, reads=reads, writes=writes, kind="cc", tag=tag)

    ROT_MS = 30000
    ROT_DMA = 1800
    ROT_CC = 30000

    def emit_all(self):
        nc = self.nc
        for e in ENGINES:
            for o in self.ops[e]:
                for d in o.deps:
                    d.needs_inc = True
        ms_count = {e: 0 for e in ENGINES}
        for e in ENGINES:
            for o in self.ops[e]:
                if o.kind == "c":
                    if o.needs_inc:
                        ms_count[e] += 1
                        o.ms = ms_count[e]
                elif o.kind == "d":
                    o.ms = 16 * (o.didx + 1)
                    o.needs_inc = True
                elif o.kind == "cc":
                    o.ms = o.ccidx + 1
                    o.needs_inc = True
        def n_sems(count, rot):
            return max(1, -(-count // rot)) if count else 1

        csems = {e: [nc.alloc_semaphore(f"cs_{e}_{i}")
                     for i in range(n_sems(ms_count[e], self.ROT_MS))]
                 for e in ENGINES}
        dsems = {e: [nc.alloc_semaphore(f"ds_{e}_{i}")
                     for i in range(n_sems(self.dma_count[e], self.ROT_DMA))]
                 for e in ENGINES}
        ccsems = [nc.alloc_semaphore(f"ccs_{i}")
                  for i in range(n_sems(self.cc_count, self.ROT_CC))]

        def sem_val(o):
            if o.kind == "c":
                i = (o.ms - 1) // self.ROT_MS
                return csems[o.engine][i], o.ms - i * self.ROT_MS
            if o.kind == "d":
                i = o.didx // self.ROT_DMA
                return dsems[o.engine][i], o.ms - i * self.ROT_DMA * 16
            i = o.ccidx // self.ROT_CC
            return ccsems[i], o.ms - i * self.ROT_CC

        def inc_val(o):
            if o.kind == "c":
                return 1
            if o.kind == "d":
                return 16
            return None

        prog = self
        prog.phase_marks = []
        _scope_ids = []

        with nc.Block() as block:
            for ename in ENGINES:
                def body(eng, _e=ename):
                    known = {}

                    def dsem_of(qeng, b):
                        i = (b - 1) // prog.ROT_DMA
                        return dsems[qeng][i], 16 * (b - i * prog.ROT_DMA)

                    def ccsem_of(b):
                        i = (b - 1) // prog.ROT_CC
                        return ccsems[i], b - i * prog.ROT_CC

                    cur_phase = None
                    for o in prog.ops[_e]:
                        if o.phase != cur_phase:
                            if cur_phase is not None:
                                nc.pop_named_scope(cur_phase)
                            nc.push_named_scope(o.phase)
                            prog.phase_marks.append((nc.next_id(), o.phase))
                            cur_phase = o.phase
                        if o.kind == "d" and o.didx in prog.gates[_e] and o.didx > 0:
                            s, v = dsem_of(_e, o.didx)
                            if known.get(s, 0) < v:
                                eng.wait_ge(s, v)
                                known[s] = v
                        if o.kind == "cc" and o.ccidx in prog.cc_gates and o.ccidx > 0:
                            s, v = ccsem_of(o.ccidx)
                            if known.get(s, 0) < v:
                                eng.wait_ge(s, v)
                                known[s] = v
                        need = {}
                        for d, ov in o.deps.items():
                            if ov is None:
                                s, v = sem_val(d)
                            elif ov[0] == "d":
                                s, v = dsem_of(ov[1], ov[2])
                            else:
                                s, v = ccsem_of(ov[2])
                            if need.get(s, 0) < v:
                                need[s] = v
                        for s, v in sorted(need.items(), key=lambda kv: kv[0].num):
                            if known.get(s, 0) < v:
                                eng.wait_ge(s, v)
                                known[s] = v
                        inst = o.emit(eng)
                        if o.needs_inc:
                            if o.kind == "cc":
                                inst.then_inc(sem_val(o)[0])
                            else:
                                inst.then_inc(sem_val(o)[0], inc_val(o))
                    if cur_phase is not None:
                        nc.pop_named_scope(cur_phase)
                    for i, s in enumerate(dsems[_e]):
                        tot = prog.dma_count[_e] - i * prog.ROT_DMA
                        if tot > 0:
                            eng.wait_ge(s, 16 * min(tot, prog.ROT_DMA))
                    if _e == "gpsimd":
                        for i, s in enumerate(ccsems):
                            tot = prog.cc_count - i * prog.ROT_CC
                            if tot > 0:
                                eng.wait_ge(s, min(tot, prog.ROT_CC))
                getattr(block, ename)(body)
        return csems, dsems, ccsems




def ceil_div(a, b):
    return -(-a // b)


NT = 2048
HALO_M = 3
NRES = NT + HALO_M
NEXT_A = 2239
RES_OFF = 188
D = 1024
KT = D // 128

EPS = 1e-5


class ArenaTile:

    def __init__(self, core, byte_off, nelems, dtype, eb, buf):
        self.core = core
        self.off = byte_off
        self.n = nelems
        self.dtype = dtype
        self.eb = eb
        self.buf = buf
        self.dead = False

    def tv(self, e0=0, e1=None, p0=0, p1=128):
        assert not self.dead, "use of freed arena tile"
        e1 = self.n if e1 is None else e1
        a = self.core.arena
        b0 = self.off + e0 * self.eb
        b1 = self.off + e1 * self.eb
        ap = a.handle.ap()[p0:p1, b0:b1].bitcast(self.dtype)
        return TV(self.buf, ap, p0, p1, b0, b1)

    def sub(self, base):
        return _SubTile(self, base)


class _SubTile:
    def __init__(self, parent, base):
        self.parent = parent
        self.base = base

    def tv(self, e0, e1, p0=0, p1=128):
        return self.parent.tv(self.base + e0, self.base + e1, p0, p1)


class Core:
    def __init__(self, nc, prog, arena_kb=None, nt=NT):
        self.nc = nc
        self.p = prog
        self.NT = nt
        self.NRES = nt + HALO_M
        self.NEXT = nt + 191
        self.psum_all = prog.psum("psum_all", (128, 4096), F32)
        self.psum_all.whole = False
        self.resid = [prog.sbuf(f"res{k}", [128, self.NRES], F32) for k in range(KT)]
        self.ident = prog.sbuf("ident", [128, 128], BF16)
        self.ones_f32 = prog.sbuf("ones_f32", [128, 1], F32)
        self.ones_bf = prog.sbuf("ones_bf", [128, 1], BF16)
        self.ones_row_bf = prog.sbuf("ones_row_bf", [1, 128], BF16)
        self.ln_sq = [prog.sbuf(f"ln_sq{i}", [128, 512], F32) for i in range(2)]
        self.ln_xb = [prog.sbuf(f"ln_xb{i}", [128, 512], BF16) for i in range(2)]
        self.ln_sqb = [prog.sbuf(f"ln_sqb{i}", [128, 512], BF16) for i in range(2)]
        self.ln_stats = prog.sbuf("ln_stats", [1, 2048], F32)
        self.ln_bc16 = prog.sbuf("ln_bc16", [1, 1024], BF16)
        avail = (nc.sbuf_bytes_remaining - 512) // 32 * 32
        self.ARENA_BYTES = min(arena_kb * 1024, avail) if arena_kb else avail
        self.arena = prog.sbuf("arena", [128, self.ARENA_BYTES], mybir.dt.uint8)
        self._arena_off = 0
        self._arena_marks = []
        self._atiles = []
        self._atile_id = 0

    def init_consts(self):
        c = self
        c.vop("memset", c.ones_f32.tv(), 1.0)
        c.vop("memset", c.ones_bf.tv(), 1.0)
        c.vop("memset", c.ones_row_bf.tv(ps=(0, 1)), 1.0)
        c.gop("memset", c.ident.tv(), 1.0)
        c.gop("affine_select", c.ident.tv(), c.ident.tv(), [[1, 128]], OP.is_ge,
              0.0, base=0, channel_multiplier=-1)
        c.gop("affine_select", c.ident.tv(), c.ident.tv(), [[-1, 128]], OP.is_ge,
              0.0, base=0, channel_multiplier=1)

    def amark(self):
        self._arena_marks.append(self._arena_off)

    def areset(self):
        base = self._arena_marks.pop()
        for t in self._atiles:
            if t.off >= base:
                t.dead = True
        self._arena_off = base

    def aalloc(self, nbytes, align=32):
        off = -(-self._arena_off // align) * align
        self._arena_off = off + nbytes
        assert self._arena_off <= self.ARENA_BYTES, (
            f"arena overflow: {self._arena_off} > {self.ARENA_BYTES}")
        return off

    def atile(self, nelems, dtype):
        eb = {F32: 4, BF16: 2, I32: 4, mybir.dt.int8: 1}[dtype]
        off = self.aalloc(nelems * eb)
        b0, b1 = off, off + nelems * eb
        buf = Buf(f"at{self._atile_id}", self.arena.handle, "sbuf", dtype)
        self._atile_id += 1
        for t in self._atiles:
            if t.off < b1 and b0 < (t.off + t.n * t.eb):
                latest = {}
                for acc in t.buf.writes + t.buf.reads:
                    o = acc[0]
                    key = o.engine
                    prev = latest.get(key)
                    if prev is None or o.eidx > prev.eidx:
                        latest[key] = o
                for o in latest.values():
                    buf.writes.append((o, 0, 128, b0, b1))
        tile = ArenaTile(self, off, nelems, dtype, eb, buf)
        self._atiles.append(tile)
        return tile

    def vop(self, fname, out, *ins, extra_reads=(), **kw):
        return self._eop("vector", fname, out, ins, extra_reads, kw)

    def sop(self, fname, out, *ins, extra_reads=(), **kw):
        return self._eop("scalar", fname, out, ins, extra_reads, kw)

    def gop(self, fname, out, *ins, extra_reads=(), **kw):
        return self._eop("gpsimd", fname, out, ins, extra_reads, kw)

    def _eop(self, engine, fname, out, ins, extra_reads, kw):
        in_aps = [(x.ap if isinstance(x, TV) else x) for x in ins]
        out_ap = out.ap if isinstance(out, TV) else out
        kw2 = {k: (v.ap if isinstance(v, TV) else v) for k, v in kw.items()}

        def emit(eng, _f=fname, _o=out_ap, _i=in_aps, _k=kw2):
            return getattr(eng, _f)(_o, *_i, **_k)

        reads = [x for x in ins if isinstance(x, TV)] + \
                [v for k2, v in kw.items()
                 if isinstance(v, TV) and k2 != "accum_out"] + list(extra_reads)
        writes = ([out] if isinstance(out, TV) else []) + \
                 [v for k2, v in kw.items()
                  if isinstance(v, TV) and k2 == "accum_out"]
        return self.p.op(engine, emit, reads=reads, writes=writes)

    def mm(self, out, lhsT, rhs, start, stop):
        oap = out.ap if isinstance(out, TV) else out
        lap = lhsT.ap if isinstance(lhsT, TV) else lhsT
        rap = rhs.ap if isinstance(rhs, TV) else rhs

        def emit(eng, _o=oap, _l=lap, _r=rap, _s=start, _t=stop):
            return eng.matmul(_o, _l, _r, start=_s, stop=_t, skip_group_check=True)

        reads = [x for x in (lhsT, rhs) if isinstance(x, TV)]
        writes = [out] if isinstance(out, TV) else []
        return self.p.op("tensor", emit, reads=reads, writes=writes)

    def transpose(self, out, in_):
        oap = out.ap if isinstance(out, TV) else out
        iap = in_.ap if isinstance(in_, TV) else in_
        np_ = (in_.p1 - in_.p0) if isinstance(in_, TV) else 128
        ident = self.ident.tv(ps=(0, np_), fs=(0, np_))

        def emit(eng, _o=oap, _i=iap, _id=ident.ap):
            return eng.transpose(_o, _i, _id)

        reads = [x for x in (in_,) if isinstance(x, TV)] + [ident]
        writes = [out] if isinstance(out, TV) else []
        return self.p.op("tensor", emit, reads=reads, writes=writes)

    def pbtv(self, i, p0=0, p1=128, e0=0, e1=512):
        b = self.psum_all
        ap = b.handle.ap()[p0:p1, i * 512 + e0:i * 512 + e1]
        lo = ((i * 512 + e0) * 4 // 2048) * 2048
        hi = -(-((i * 512 + e1) * 4) // 2048) * 2048
        return TV(b, ap, 0, 128, lo, hi)

    def pbt16(self, i, p0=0, p1=128, e0=0, e1=1024):
        b = self.psum_all
        ap = b.handle.ap().bitcast(BF16)[p0:p1, i * 1024 + e0:i * 1024 + e1]
        lo = ((i * 1024 + e0) * 2 // 2048) * 2048
        hi = -(-((i * 1024 + e1) * 2) // 2048) * 2048
        return TV(b, ap, 0, 128, lo, hi)

    def pbw(self, i, nelem, p0=0, p1=128):
        b = self.psum_all
        ap = b.handle.ap()[p0:p1, i * 512:i * 512 + nelem]
        lo = i * 2048
        hi = -(-((i * 512 + nelem) * 4) // 2048) * 2048
        return TV(b, ap, 0, 128, lo, hi)



def emit_ln(core, c0, c1, out_tiles, o_off=0, src_tiles=None, pb0=6, pb1=7):
    c = core
    n = c1 - c0
    src = src_tiles if src_tiles is not None else c.resid
    for st0 in range(0, n, 512):
        w = min(512, n - st0)
        pm = c.pbtv(pb0, 0, 1, 0, w)
        pv = c.pbtv(pb1, 0, 1, 0, w)
        for k in range(KT):
            x = src[k].tv(fs=(c0 + st0, c0 + st0 + w)) if isinstance(src[k], Buf) \
                else src[k].tv(c0 + st0, c0 + st0 + w)
            xb = c.ln_xb[k % 2].tv(fs=(0, w))
            sq = c.ln_sqb[k % 2].tv(fs=(0, w))
            c.sop("activation", xb, x, func=AF.Identity)
            c.sop("activation", sq, x, func=AF.Square)
            c.mm(pm, c.ones_bf.tv(), xb, start=(k == 0), stop=(k == KT - 1))
            c.mm(pv, c.ones_bf.tv(), sq, start=(k == 0), stop=(k == KT - 1))
        st = c.ln_stats
        m = st.tv(ps=(0, 1), fs=(0, w))
        v = st.tv(ps=(0, 1), fs=(512, 512 + w))
        r = st.tv(ps=(0, 1), fs=(1024, 1024 + w))
        t = st.tv(ps=(0, 1), fs=(1536, 1536 + w))
        mb = c.ln_bc16.tv(ps=(0, 1), fs=(0, w))
        rb = c.ln_bc16.tv(ps=(0, 1), fs=(512, 512 + w))
        c.vop("tensor_scalar", m, pm, 1.0 / D, None, op0=OP.mult)
        c.vop("tensor_scalar", v, pv, 1.0 / D, None, op0=OP.mult)
        c.vop("tensor_mul", t, m, m)
        c.vop("tensor_sub", v, v, t)
        c.vop("tensor_scalar", t, v, float(EPS), None, op0=OP.add)
        c.sop("activation", v, t, func=AF.Ln)
        c.sop("activation", r, v, func=AF.Exp, scale=-0.5)
        c.vop("tensor_copy", mb, m)
        c.vop("tensor_copy", rb, r)
        bm = c.pbtv(pb0, 0, 128, 0, w)
        bv = c.pbtv(pb1, 0, 128, 0, w)
        c.mm(bm, c.ones_row_bf.tv(ps=(0, 1)), mb, start=True, stop=True)
        c.mm(bv, c.ones_row_bf.tv(ps=(0, 1)), rb, start=True, stop=True)
        for k in range(KT):
            x = src[k].tv(fs=(c0 + st0, c0 + st0 + w)) if isinstance(src[k], Buf) \
                else src[k].tv(c0 + st0, c0 + st0 + w)
            tmp = c.ln_sq[k % 2].tv(fs=(0, w))
            c.vop("tensor_sub", tmp, x, bm)
            c.vop("tensor_mul", out_tiles[k].tv(o_off + st0, o_off + st0 + w), tmp, bv)



def _wslab_dma(core, wb, dramT, m0, nelem, parts=None):
    parts = parts if parts is not None else dramT.handle.shape[1]
    src = TV(dramT, dramT.handle.ap()[m0], 0, parts, 0, 1 << 30)
    core.p.dma("gpsimd", wb.tv(0, nelem, p0=0, p1=parts), src)


def tile_kxm(wT, kt):
    K, M = wT.shape
    mw = min(M, 128)
    nmo = M // mw
    return np.ascontiguousarray(
        wT.reshape(kt, 128, nmo, mw).transpose(2, 1, 0, 3).reshape(nmo, 128, kt * mw))


def _wslab_dma2(core, wb, dramT, m0, nelem, nblk=2):
    src_ap = dramT.handle.ap()[m0:m0 + nblk].transpose([1, 0, 2])
    src = TV(dramT, src_ap, 0, 128, 0, 1 << 30)
    dst = wb.tv(0, nblk * nelem).with_ap(
        wb.tv(0, nblk * nelem).ap.rearrange("p (b n) -> p b n", b=nblk))
    core.p.dma("gpsimd", dst, src)


def _stream_blocks(core, nblk, dma_fn, compute_fn):
    dma_fn(0, 0)
    for m in range(nblk):
        if m + 1 < nblk:
            dma_fn(m + 1, (m + 1) % 2)
        compute_fn(m, m % 2)


def emit_ffn0(core, W, c0, c1, src_tiles=None):
    c = core
    sc = W["sc"]

    def scl(col):
        return sc.tv(fs=(col, col + 1))

    t0 = c0
    while t0 < c1:
        w = min(512, c1 - t0)
        c.amark()
        lnt = c.atile(KT * w, BF16)
        ln = [lnt.sub(k * w) for k in range(KT)]
        hA = c.atile(32 * w, BF16)
        hB = c.atile(32 * w, BF16)
        wsm = [c.atile(1024, BF16) for _ in range(4)]
        wbg = [c.atile(8192, BF16) for _ in range(2)]
        wi8 = [c.atile(8192, mybir.dt.int8)] * 2

        emit_ln(c, t0, t0 + w, ln)

        def dma1(m, par):
            _wslab_dma(c, wsm[par], W["w1T"], m, 1024)
            _wslab_dma(c, wsm[2 + par], W["w2T"], m, 1024)

        def comp1(m, par):
            pa = c.pbtv(m % 4, 0, 128, 0, w)
            pb = c.pbtv(4 + m % 4, 0, 128, 0, w)
            for k in range(KT):
                lt = wsm[par].tv(k * 128, (k + 1) * 128)
                c.mm(pa, lt, ln[k].tv(0, w), start=(k == 0), stop=(k == KT - 1))
            for k in range(KT):
                lt = wsm[2 + par].tv(k * 128, (k + 1) * 128)
                c.mm(pb, lt, ln[k].tv(0, w), start=(k == 0), stop=(k == KT - 1))
            hm = hA.tv(m * w, (m + 1) * w)
            c.sop("activation", hm, pa, func=AF.Silu, scale=scl(m))
            c.vop("scalar_tensor_tensor", hm, pb, scl(32 + m), hm,
                  op0=OP.mult, op1=OP.mult)

        _stream_blocks(c, 32, dma1, comp1)

        def big_phase(dramT, src_h, dst_fn, pbank):
            def dmaf(p, par):
                if p % 2 == 0:
                    _wslab_dma2(c, wbg[par], dramT, 2 * p, 4096)
                else:
                    src_ap = dramT.handle.ap()[2 * p:2 * p + 2].transpose([1, 0, 2])
                    srctv = TV(dramT, src_ap, 0, 128, 0, 1 << 30)
                    dst = wi8[par].tv(0, 8192).with_ap(
                        wi8[par].tv(0, 8192).ap.rearrange("q (b n) -> q b n", b=2))
                    c.p.dma("sync", dst, srctv)
                    c.sop("activation", wbg[par].tv(0, 8192), wi8[par].tv(0, 8192),
                          func=AF.Identity)

            def compf(p, par):
                for i in range(2):
                    m = 2 * p + i
                    ps = c.pbtv(pbank + m % 4, 0, 128, 0, w)
                    for k in range(32):
                        lt = wbg[par].tv(i * 4096 + k * 128, i * 4096 + (k + 1) * 128)
                        c.mm(ps, lt, src_h.tv(k * w, (k + 1) * w),
                             start=(k == 0), stop=(k == 31))
                    dst_fn(m, ps)
            return dmaf, compf

        def dst2(m, ps):
            c.sop("activation", hB.tv(m * w, (m + 1) * w), ps, func=AF.Gelu,
                  scale=scl(64 + m))
        dmaf, compf = big_phase(W["h1T"], hA, dst2, 4)
        _stream_blocks(c, 16, dmaf, compf)

        def dst3(m, ps):
            c.sop("activation", hA.tv(m * w, (m + 1) * w), ps, func=AF.Gelu,
                  scale=scl(96 + m))
        dmaf, compf = big_phase(W["h2T"], hB, dst3, 0)
        _stream_blocks(c, 16, dmaf, compf)

        def dst4(m, ps):
            rs = c.resid[m].tv(fs=(t0, t0 + w))
            c.vop("scalar_tensor_tensor", rs, ps, scl(128 + m), rs,
                  op0=OP.mult, op1=OP.add, extra_reads=[rs])
        dmaf, compf = big_phase(W["w3T"], hA, dst4, 4)
        _stream_blocks(c, 4, dmaf, compf)

        c.areset()
        t0 += w


def prep_ffn0_host(inputs):
    w1 = inputs["L0_ffn_w1_w"]; w2 = inputs["L0_ffn_w2_w"]
    h1 = inputs["L0_ffn_h1_w"]; h2 = inputs["L0_ffn_h2_w"]
    w3 = inputs["L0_ffn_w3_w"]
    out = {
        "ffn0_w1T": tile_kxm(w1.T.astype(np.int8), 8),
        "ffn0_w2T": tile_kxm(w2.T.astype(np.int8), 8),
        "ffn0_h1T": tile_kxm(h1.T.astype(np.int8), 32),
        "ffn0_h2T": tile_kxm(h2.T.astype(np.int8), 32),
        "ffn0_w3T": tile_kxm(w3.T.astype(np.int8), 32),
    }
    assert np.all(w1.T.astype(np.int8).astype(np.float32) == w1.T)
    sc = np.zeros((128, 136), np.float32)
    sc[:, 0:32] = inputs["L0_ffn_w1_s"].reshape(32, 128).T
    sc[:, 32:64] = inputs["L0_ffn_w2_s"].reshape(32, 128).T
    sc[:, 64:96] = inputs["L0_ffn_h1_s"].reshape(32, 128).T
    sc[:, 96:128] = inputs["L0_ffn_h2_s"].reshape(32, 128).T
    sc[:, 128:136] = inputs["L0_ffn_w3_s"].reshape(8, 128).T
    out["ffn0_sc"] = sc
    return out


def declare_ffn0(prog):
    W = {}
    W["w1T"] = prog.dram("ffn0_w1T", [32, 128, 1024], mybir.dt.int8, kind="ExternalInput")
    W["w2T"] = prog.dram("ffn0_w2T", [32, 128, 1024], mybir.dt.int8, kind="ExternalInput")
    W["h1T"] = prog.dram("ffn0_h1T", [32, 128, 4096], mybir.dt.int8, kind="ExternalInput")
    W["h2T"] = prog.dram("ffn0_h2T", [32, 128, 4096], mybir.dt.int8, kind="ExternalInput")
    W["w3T"] = prog.dram("ffn0_w3T", [8, 128, 4096], mybir.dt.int8, kind="ExternalInput")
    W["sc_dram"] = prog.dram("ffn0_sc", [128, 136], F32, kind="ExternalInput")
    W["sc"] = prog.sbuf("ffn0_sc_sb", [128, 136], F32)
    return W


def load_ffn0_consts(core, W):
    core.p.dma("sync", W["sc"].tv(), W["sc_dram"].tv())



DEQ = np.float32(1.0) / np.float32(127.0)


def emit_ffn1(core, W, c0, c1):
    c = core
    S = float(DEQ)
    t0 = c0
    while t0 < c1:
        w = min(512, c1 - t0)
        c.amark()
        lnt = c.atile(KT * w, BF16)
        ln = [lnt.sub(k * w) for k in range(KT)]
        hA = c.atile(24 * w, BF16)
        wsm = [c.atile(2048, BF16) for _ in range(4)]
        wbg = [c.atile(6144, BF16) for _ in range(2)]

        emit_ln(c, t0, t0 + w, ln)

        def dma1(p, par):
            _wslab_dma2(c, wsm[par], W["wgT"], 2 * p, 1024)
            _wslab_dma2(c, wsm[2 + par], W["wuT"], 2 * p, 1024)

        def comp1(p, par):
            for i in range(2):
                m = 2 * p + i
                pa = c.pbtv(m % 4, 0, 128, 0, w)
                pb = c.pbtv(4 + m % 4, 0, 128, 0, w)
                for k in range(KT):
                    c.mm(pa, wsm[par].tv(i * 1024 + k * 128, i * 1024 + (k + 1) * 128),
                         ln[k].tv(0, w), start=(k == 0), stop=(k == KT - 1))
                for k in range(KT):
                    c.mm(pb, wsm[2 + par].tv(i * 1024 + k * 128, i * 1024 + (k + 1) * 128),
                         ln[k].tv(0, w), start=(k == 0), stop=(k == KT - 1))
                hm = hA.tv(m * w, (m + 1) * w)
                c.sop("activation", hm, pa, func=AF.Silu, scale=S)
                c.vop("scalar_tensor_tensor", hm, pb, S, hm, op0=OP.mult, op1=OP.mult)

        _stream_blocks(c, 12, dma1, comp1)

        def dma2(p, par):
            _wslab_dma2(c, wbg[par], W["wdT"], 2 * p, 3072)

        def comp2(p, par):
            for i in range(2):
                m = 2 * p + i
                ps = c.pbtv(4 + m % 4, 0, 128, 0, w)
                for k in range(24):
                    c.mm(ps, wbg[par].tv(i * 3072 + k * 128, i * 3072 + (k + 1) * 128),
                         hA.tv(k * w, (k + 1) * w), start=(k == 0), stop=(k == 23))
                rs = c.resid[m].tv(fs=(t0, t0 + w))
                c.vop("scalar_tensor_tensor", rs, ps, S, rs,
                      op0=OP.mult, op1=OP.add, extra_reads=[rs])

        _stream_blocks(c, 4, dma2, comp2)

        c.areset()
        t0 += w


def prep_ffn1_host(inputs):
    wg = inputs["L1_ffn_w_gate"]; wu = inputs["L1_ffn_w_up"]; wd = inputs["L1_ffn_w_down"]
    return {
        "ffn1_wgT": tile_kxm(wg.T.astype(np.int8), 8),
        "ffn1_wuT": tile_kxm(wu.T.astype(np.int8), 8),
        "ffn1_wdT": tile_kxm(wd.T.astype(np.int8), 24),
    }


def declare_ffn1(prog):
    I8 = mybir.dt.int8
    return {
        "wgT": prog.dram("ffn1_wgT", [24, 128, 1024], I8, kind="ExternalInput"),
        "wuT": prog.dram("ffn1_wuT", [24, 128, 1024], I8, kind="ExternalInput"),
        "wdT": prog.dram("ffn1_wdT", [8, 128, 3072], I8, kind="ExternalInput"),
    }



def emit_attn1a(core, W, c0, c1, rg):
    c = core
    T = c1 - c0
    nst = ceil_div(T, 512)
    ntt = ceil_div(T, 128)
    c.amark()
    lnt = c.atile(KT * T, BF16)
    ln = [lnt.sub(k * T) for k in range(KT)]
    emit_ln(c, c0, c1, ln)
    qfm = c.atile(8 * T, BF16)
    kvin = c.atile(1024, F32)
    bias = W["bias_sb"]

    def bcol(i):
        return bias.tv(fs=(i, i + 1))

    slabs = [c.atile(1024, BF16) for _ in range(6)]

    def dmah(h, par):
        _wslab_dma(c, slabs[3 * par + 0], W["qkv"], h, 1024)
        _wslab_dma(c, slabs[3 * par + 1], W["qkv"], 8 + h, 1024)
        _wslab_dma(c, slabs[3 * par + 2], W["qkv"], 16 + h, 1024)

    c.amark()
    kfm = c.atile(T, BF16)
    vfm = c.atile(T, BF16)
    ktok = c.atile(T, BF16)
    vtok = c.atile(T, BF16)

    def comph(h, par):
        qs, ks, vs = slabs[3 * par], slabs[3 * par + 1], slabs[3 * par + 2]
        for st in range(nst):
            e0, e1 = st * 512, min(T, (st + 1) * 512)
            w = e1 - e0
            ps = c.pbtv(st % 2, 0, 128, 0, w)
            for k in range(KT):
                c.mm(ps, qs.tv(k * 128, (k + 1) * 128), ln[k].tv(e0, e1),
                     start=(k == 0), stop=(k == KT - 1))
            c.sop("activation", qfm.tv(h * T + e0, h * T + e1), ps,
                  func=AF.Identity, bias=bcol(h))
            ps2 = c.pbtv(2 + st % 2, 0, 128, 0, w)
            for k in range(KT):
                c.mm(ps2, ks.tv(k * 128, (k + 1) * 128), ln[k].tv(e0, e1),
                     start=(k == 0), stop=(k == KT - 1))
            c.sop("activation", kfm.tv(e0, e1), ps2, func=AF.Identity, bias=bcol(8 + h))
            ps3 = c.pbtv(4 + st % 2, 0, 128, 0, w)
            for k in range(KT):
                c.mm(ps3, vs.tv(k * 128, (k + 1) * 128), ln[k].tv(e0, e1),
                     start=(k == 0), stop=(k == KT - 1))
            c.sop("activation", vfm.tv(e0, e1), ps3, func=AF.Identity, bias=bcol(16 + h))
        for tt in range(ntt):
            e0, e1 = tt * 128, min(T, (tt + 1) * 128)
            w = e1 - e0
            pt = c.pbt16(6, 0, w, 0, 128)
            c.transpose(pt, kfm.tv(e0, e1))
            c.vop("tensor_copy", ktok.tv(tt * 128, tt * 128 + 128, p0=0, p1=w), pt)
            pt2 = c.pbt16(7, 0, w, 0, 128)
            c.transpose(pt2, vfm.tv(e0, e1))
            c.vop("tensor_copy", vtok.tv(tt * 128, tt * 128 + 128, p0=0, p1=w), pt2)
        pkv = c.pbtv(0, 0, 128, 0, 128)
        for tt in range(ntt):
            e0, e1 = tt * 128, min(T, (tt + 1) * 128)
            w = e1 - e0
            c.mm(pkv, ktok.tv(tt * 128, tt * 128 + 128, p0=0, p1=w),
                 vtok.tv(tt * 128, tt * 128 + 128, p0=0, p1=w),
                 start=(tt == 0), stop=(tt == ntt - 1))
        c.vop("tensor_copy", kvin.tv(h * 128, (h + 1) * 128), pkv)

    _stream_blocks(c, 8, dmah, comph)
    c.areset()

    c.p.dma("sync", W["cc_in"].tv(), kvin.tv())
    c.p.cc("AllReduce", OP.add, rg, W["cc_in"].tv(), W["cc_out"].tv())
    kvb = c.atile(1024, BF16)
    c.p.dma("gpsimd", kvb.tv(), W["cc_out"].tv())

    afm = c.atile(8 * T, BF16)
    for h in range(8):
        for st in range(nst):
            e0, e1 = st * 512, min(T, (st + 1) * 512)
            w = e1 - e0
            ps = c.pbtv((h + st) % 4, 0, 128, 0, w)
            c.mm(ps, kvb.tv(h * 128, (h + 1) * 128), qfm.tv(h * T + e0, h * T + e1),
                 start=True, stop=True)
            c.vop("tensor_copy", afm.tv(h * T + e0, h * T + e1), ps)

    wo = [c.atile(1024, BF16) for _ in range(2)]
    tmp = c.atile(512, F32)

    def dmao(mo, par):
        _wslab_dma(c, wo[par], W["woT"], mo, 1024)

    def compo(mo, par):
        for st in range(nst):
            e0, e1 = st * 512, min(T, (st + 1) * 512)
            w = e1 - e0
            ps = c.pbtv(4 + (mo + st) % 2, 0, 128, 0, w)
            for j in range(8):
                c.mm(ps, wo[par].tv(j * 128, (j + 1) * 128), afm.tv(j * T + e0, j * T + e1),
                     start=(j == 0), stop=(j == 7))
            c.sop("activation", tmp.tv(0, w), ps, func=AF.Identity, bias=bcol(24 + mo))
            rs = c.resid[mo].tv(fs=(c0 + e0, c0 + e1))
            c.vop("tensor_add", rs, rs, tmp.tv(0, w), extra_reads=[rs])

    _stream_blocks(c, 8, dmao, compo)
    c.areset()


def prep_attn1a_host(inputs):
    wqkv = inputs["L1_attn_a_w_qkv"]
    bqkv = inputs["L1_attn_a_b_qkv"]
    wo = inputs["L1_attn_a_w_o"]
    bo = inputs["L1_attn_a_b_o"]
    idx = 8 * np.arange(128)
    slabs = []
    for part in range(3):
        for h in range(8):
            Wh = wqkv[part * 1024 + idx + h, :]
            slabs.append(tile_kxm(np.ascontiguousarray(Wh.T), 8)[0])
    qkv = np.ascontiguousarray(np.stack(slabs).astype(np.float32))
    woT = tile_kxm(np.ascontiguousarray(wo.T), 8).astype(np.float32)
    bias = np.zeros((128, 32), np.float32)
    for part in range(3):
        for h in range(8):
            bias[:, part * 8 + h] = bqkv[part * 1024 + idx + h]
    bias[:, 24:32] = bo.reshape(8, 128).T
    return {"a1_qkv": qkv, "a1_woT": woT, "a1_bias": bias}


def declare_attn1a(prog):
    W = {
        "qkv": prog.dram("a1_qkv", [24, 128, 1024], F32, kind="ExternalInput"),
        "woT": prog.dram("a1_woT", [8, 128, 1024], F32, kind="ExternalInput"),
        "bias_dram": prog.dram("a1_bias", [128, 32], F32, kind="ExternalInput"),
        "bias_sb": prog.sbuf("a1_bias_sb", [128, 32], F32),
        "cc_in": prog.dram("a1_cc_in", [128, 1024], F32),
        "cc_out": prog.dram("a1_cc_out", [128, 1024], F32),
    }
    return W


def load_attn1a_consts(core, W):
    core.p.dma("sync", W["bias_sb"].tv(), W["bias_dram"].tv())



def emit_attn1b(core, W, c0, c1, rg, ranks):
    c = core
    T = c1 - c0
    KW = ranks * T
    nqt = T // 128
    nstr = KW // 512
    c.amark()
    lnt = c.atile(KT * T, BF16)
    ln = [lnt.sub(k * T) for k in range(KT)]
    emit_ln(c, c0, c1, ln)
    bias = W["bias_sb"]

    def bcol(i):
        return bias.tv(fs=(i, i + 1))

    c.vop("tensor_scalar", bias.tv(fs=(3, 4)), bias.tv(fs=(0, 1)), 0.125, None,
          op0=OP.mult)

    qfm = c.atile(T, BF16)
    kfm = c.atile(T, BF16)
    vtok = c.atile((T // 128) * 64, BF16)
    qkv_sb = [c.atile(512, BF16) for _ in range(3)]
    for i in range(3):
        _wslab_dma(c, qkv_sb[i], W["qkvw"], i, 512)
    for st in range(T // 512):
        e0, e1 = st * 512, (st + 1) * 512
        ps = c.pbtv(st % 2, 0, 64, 0, 512)
        for k in range(KT):
            c.mm(ps, qkv_sb[0].tv(k * 64, (k + 1) * 64), ln[k].tv(e0, e1),
                 start=(k == 0), stop=(k == KT - 1))
        c.sop("activation", qfm.tv(e0, e1, p0=0, p1=64), ps, func=AF.Identity,
              bias=bias.tv(ps=(0, 64), fs=(3, 4)), scale=0.125)
        ps2 = c.pbtv(2 + st % 2, 0, 64, 0, 512)
        for k in range(KT):
            c.mm(ps2, qkv_sb[1].tv(k * 64, (k + 1) * 64), ln[k].tv(e0, e1),
                 start=(k == 0), stop=(k == KT - 1))
        c.sop("activation", kfm.tv(e0, e1, p0=0, p1=64), ps2, func=AF.Identity,
              bias=bias.tv(ps=(0, 64), fs=(1, 2)))
    for tt in range(T // 128):
        ps = c.pbtv(4 + tt % 2, 0, 128, 0, 64)
        for k in range(KT):
            c.mm(ps, ln[k].tv(tt * 128, (tt + 1) * 128), qkv_sb[2].tv(k * 64, (k + 1) * 64),
                 start=(k == 0), stop=False)
        c.mm(ps, c.ones_row_bf.tv(ps=(0, 1)).with_ap(
            c.ones_row_bf.handle.ap()[0:1, 0:128]), W["bvrow_sb"].tv(ps=(0, 1)),
            start=False, stop=True)
        c.vop("tensor_copy", vtok.tv(tt * 64, (tt + 1) * 64), ps)
    c.p.dma("sync", W["cck_in"].tv(), kfm.tv(p0=0, p1=64))
    c.p.dma("sync", W["ccv_in"].tv().with_ap(
        W["ccv_in"].handle.ap().rearrange("(tt p) c -> p tt c", p=128)),
        vtok.tv().with_ap(vtok.tv().ap.rearrange("p (tt c) -> p tt c", c=64)))
    c.p.cc("AllGather", OP.bypass, rg, W["cck_in"].tv(), W["cck_out"].tv())
    c.p.cc("AllGather", OP.bypass, rg, W["ccv_in"].tv(), W["ccv_out"].tv())
    kfull = c.atile(KW, BF16)
    vfull = c.atile((KW // 128) * 64, BF16)
    c.p.dma("sync", kfull.tv(p0=0, p1=64).with_ap(
        kfull.tv(p0=0, p1=64).ap.rearrange("c (r t) -> c r t", r=ranks)),
        W["cck_out"].tv().with_ap(
            W["cck_out"].handle.ap().rearrange("(r c) t -> c r t", c=64)))
    c.p.dma("sync", vfull.tv().with_ap(
        vfull.tv().ap.rearrange("p (tt c) -> p tt c", c=64)),
        W["ccv_out"].tv().with_ap(
            W["ccv_out"].handle.ap().rearrange("(tt p) c -> p tt c", p=128)))

    stats = c.atile(3 * nqt + 4, F32)
    attn_tok = c.atile(nqt * 64, BF16)
    maskb = [c.atile(KW, BF16) for _ in range(2)]
    psb = [c.atile(512, BF16) for _ in range(2)]
    ptb = [c.atile(128, BF16) for _ in range(2)]

    for qt in range(nqt):
        run = stats.tv(qt, qt + 1)
        c.vop("memset", run, -3.0e38)
        c.p.dma("sync", maskb[qt % 2].tv(),
                TV(W["mask"], W["mask"].handle.ap()[qt], 0, 128, 0, 1 << 30))
        for st in range(nstr):
            ps = c.pbtv(st % 2, 0, 128, 0, 512)
            c.mm(ps, qfm.tv(qt * 128, (qt + 1) * 128, p0=0, p1=64),
                 kfull.tv(st * 512, (st + 1) * 512, p0=0, p1=64),
                 start=True, stop=False)
            c.mm(ps, c.ident.tv(), maskb[qt % 2].tv(st * 512, (st + 1) * 512),
                 start=False, stop=True)
            sm = stats.tv(3 * nqt + st % 2, 3 * nqt + st % 2 + 1)
            c.vop("tensor_reduce", sm, ps, AX.X, OP.max)
            c.vop("tensor_max", run, run, sm)
        nm = stats.tv(2 * nqt + qt, 2 * nqt + qt + 1)
        c.vop("tensor_scalar", nm, run, -1.0, None, op0=OP.mult)
        ls = stats.tv(nqt + qt, nqt + qt + 1)
        c.vop("memset", ls, 0.0)
        po = c.pbtv(6 + qt % 2, 0, 128, 0, 64)
        first = True
        for st in range(nstr):
            ps = c.pbtv(2 + st % 2, 0, 128, 0, 512)
            c.mm(ps, qfm.tv(qt * 128, (qt + 1) * 128, p0=0, p1=64),
                 kfull.tv(st * 512, (st + 1) * 512, p0=0, p1=64),
                 start=True, stop=False)
            c.mm(ps, c.ident.tv(), maskb[qt % 2].tv(st * 512, (st + 1) * 512),
                 start=False, stop=True)
            tmpc = stats.tv(3 * nqt + 2 + st % 2, 3 * nqt + 2 + st % 2 + 1)
            c.sop("activation", psb[qt % 2].tv(), ps, func=AF.Exp, bias=nm,
                  accum_out=tmpc)
            c.vop("tensor_add", ls, ls, tmpc)
            for c2 in range(4):
                pt = c.pbt16(4 + c2 % 2, 0, 128, 0, 128)
                c.transpose(pt, psb[qt % 2].tv(c2 * 128, (c2 + 1) * 128))
                c.vop("tensor_copy", ptb[(st * 4 + c2) % 2].tv(), pt)
                kc = st * 4 + c2
                c.mm(po, ptb[(st * 4 + c2) % 2].tv(), vfull.tv(kc * 64, (kc + 1) * 64),
                     start=first, stop=(st == nstr - 1 and c2 == 3))
                first = False
        rl = stats.tv(3 * nqt + 2 + qt % 2, 3 * nqt + 2 + qt % 2 + 1)
        c.vop("reciprocal", rl, ls)
        c.vop("tensor_scalar", attn_tok.tv(qt * 64, (qt + 1) * 64), po, rl, None,
              op0=OP.mult)
    afm = c.atile(T, BF16)
    for qt in range(nqt):
        pt = c.pbt16(0, 0, 64, 0, 128)
        c.transpose(pt, attn_tok.tv(qt * 64, (qt + 1) * 64))
        c.vop("tensor_copy", afm.tv(qt * 128, (qt + 1) * 128, p0=0, p1=64), pt)
    wo_sb = [c.atile(128, BF16) for _ in range(2)]
    tmp = c.atile(512, F32)

    def dmao(mo, par):
        _wslab_dma(c, wo_sb[par], W["woT"], mo, 128)

    def compo(mo, par):
        for st in range(T // 512):
            e0, e1 = st * 512, (st + 1) * 512
            ps = c.pbtv(2 + (mo + st) % 2, 0, 128, 0, 512)
            c.mm(ps, wo_sb[par].tv(0, 128, p0=0, p1=64),
                 afm.tv(e0, e1, p0=0, p1=64), start=True, stop=True)
            c.sop("activation", tmp.tv(0, 512), ps, func=AF.Identity,
                  bias=bcol(8 + mo))
            rs = c.resid[mo].tv(fs=(c0 + e0, c0 + e1))
            c.vop("tensor_add", rs, rs, tmp.tv(0, 512), extra_reads=[rs])

    _stream_blocks(c, 8, dmao, compo)
    c.areset()


def prep_attn1b_host(inputs, ranks=4, T=NT):
    wq = inputs["L1_attn_b_wq"]; wk = inputs["L1_attn_b_wk"]; wv = inputs["L1_attn_b_wv"]
    wo = inputs["L1_attn_b_wo"]
    qkvw = np.stack([tile_kxm(np.ascontiguousarray(w), 8)[0]
                     for w in (wq, wk, wv)]).astype(np.float32)
    woT = np.stack([np.ascontiguousarray(wo[:, mo * 128:(mo + 1) * 128])
                    for mo in range(8)]).astype(np.float32)
    bias = np.zeros((128, 16), np.float32)
    bias[0:64, 0] = inputs["L1_attn_b_bq"]
    bias[0:64, 1] = inputs["L1_attn_b_bk"]
    bias[0:64, 2] = inputs["L1_attn_b_bv"]
    bias[:, 8:16] = inputs["L1_attn_b_bo"].reshape(8, 128).T
    bvrow = inputs["L1_attn_b_bv"].reshape(1, 64).astype(np.float32)
    out = {"a1b_qkvw": qkvw, "a1b_woT": woT, "a1b_bias": bias, "a1b_bvrow": bvrow}
    import ml_dtypes
    nqt = T // 128
    KW = ranks * T
    kidx = np.arange(KW)[None, None, :]
    for r in range(ranks):
        sbase = r * T
        qidx = sbase + (np.arange(nqt) * 128)[:, None, None] + np.arange(128)[None, :, None]
        m = np.where(kidx > qidx, np.float32(-1e30), np.float32(0.0))
        out[f"a1b_mask_r{r}"] = m.astype(ml_dtypes.bfloat16)
    return out


def declare_attn1b(prog, ranks=4, T=NT):
    nqt = T // 128
    KW = ranks * T
    return {
        "qkvw": prog.dram("a1b_qkvw", [3, 128, 512], F32, kind="ExternalInput"),
        "woT": prog.dram("a1b_woT", [8, 64, 128], F32, kind="ExternalInput"),
        "bias_dram": prog.dram("a1b_bias", [128, 16], F32, kind="ExternalInput"),
        "bias_sb": prog.sbuf("a1b_bias_sb", [128, 16], F32),
        "bvrow_dram": prog.dram("a1b_bvrow", [1, 64], F32, kind="ExternalInput"),
        "bvrow_sb": prog.sbuf("a1b_bvrow_sb", [1, 64], BF16),
        "mask": prog.dram("a1b_mask", [nqt, 128, KW], BF16, kind="ExternalInput"),
        "cck_in": prog.dram("a1b_cck_in", [64, T], BF16),
        "cck_out": prog.dram("a1b_cck_out", [ranks * 64, T], BF16),
        "ccv_in": prog.dram("a1b_ccv_in", [T, 64], BF16),
        "ccv_out": prog.dram("a1b_ccv_out", [ranks * T, 64], BF16),
    }


def load_attn1b_consts(core, W):
    core.p.dma("sync", W["bias_sb"].tv(), W["bias_dram"].tv())
    core.p.dma("gpsimd", W["bvrow_sb"].tv(ps=(0, 1)), W["bvrow_dram"].tv(ps=(0, 1)))



def emit_mamba(core, W, rg, nprev=3):
    c = core
    T = c.NT
    NR = c.NRES
    nst = T // 512
    sel = W["sel_sb"]
    c.amark()

    for k in range(KT):
        rs = c.resid[k].tv(fs=(0, HALO_M))
        c.vop("tensor_scalar", rs, rs, sel.tv(fs=(3, 4)), None, op0=OP.mult,
              extra_reads=[rs])

    c.sop("activation", W["A_sb"].tv(), W["A_dram_sb"].tv(), func=AF.Exp)
    c.vop("tensor_scalar", W["A_sb"].tv(), W["A_sb"].tv(), -1.0, None,
          op0=OP.mult, extra_reads=[W["A_sb"].tv()])

    c.amark()
    lnm = c.atile(KT * NR, BF16)
    ln = [lnm.sub(k * NR) for k in range(KT)]
    emit_ln(c, 0, NR, ln)
    c.amark()
    xcpre = c.atile(NR, F32)
    cacc = c.atile(T, F32)
    wxs = [c.atile(1024, BF16) for _ in range(2)]

    def dmx(m, par):
        _wslab_dma(c, wxs[par], W["wxT"], m, 1024)

    def cpx(m, par):
        for st in range(ceil_div(NR, 512)):
            e0, e1 = st * 512, min(NR, (st + 1) * 512)
            ps = c.pbtv(st % 4, 0, 128, 0, e1 - e0)
            for k in range(KT):
                c.mm(ps, wxs[par].tv(k * 128, (k + 1) * 128), ln[k].tv(e0, e1),
                     start=(k == 0), stop=(k == KT - 1))
            c.vop("tensor_copy", xcpre.tv(e0, e1), ps)
        cw = W["conv_sb"]
        c.vop("tensor_scalar", cacc.tv(), xcpre.tv(0, T),
              cw.tv(fs=(m * 4, m * 4 + 1)), None, op0=OP.mult)
        for kk in range(1, 4):
            c.vop("scalar_tensor_tensor", cacc.tv(), xcpre.tv(kk, T + kk),
                  cw.tv(fs=(m * 4 + kk, m * 4 + kk + 1)), cacc.tv(),
                  op0=OP.mult, op1=OP.add, extra_reads=[cacc.tv()])
        xcm = c.atile(T, BF16)
        c.sop("activation", xcm.tv(), cacc.tv(), func=AF.Silu,
              bias=W["convb_sb"].tv(fs=(m, m + 1)))
        c.p.dma("sync", TV(W["xc_dram"], W["xc_dram"].handle.ap()[m], 0, 128,
                           m * T * 2, (m + 1) * T * 2), xcm.tv())
        return xcm

    dmx(0, 0)
    for m in range(16):
        if m + 1 < 16:
            dmx(m + 1, (m + 1) % 2)
        c.amark()
        cpx(m, m % 2)
        c.areset()
    c.areset()
    c.areset()

    c.amark()
    dlr = c.atile(T, BF16)
    Bc = c.atile(T, BF16)
    Cc = c.atile(T, BF16)
    xps = [c.atile(16 * 64, BF16), c.atile(16 * 16, BF16), c.atile(16 * 16, BF16)]
    _wslab_dma(c, xps[0], W["xpdT"], 0, 16 * 64)
    _wslab_dma(c, xps[1], W["xpbT"], 0, 16 * 16)
    _wslab_dma(c, xps[2], W["xpcT"], 0, 16 * 16)
    xck = c.atile(T, BF16)
    for st in range(nst):
        e0, e1 = st * 512, (st + 1) * 512
        psd = c.pbtv(st % 2, 0, 64, 0, 512)
        psb = c.pbtv(2 + st % 2, 0, 16, 0, 512)
        psc = c.pbtv(4 + st % 2, 0, 16, 0, 512)
        for k in range(16):
            src = TV(W["xc_dram"], W["xc_dram"].handle.ap()[k][:, e0:e1], 0, 128,
                     k * T * 2 + e0 * 2, k * T * 2 + e1 * 2)
            c.p.dma("sync", xck.tv(e0, e1), src)
            c.mm(psd, xps[0].tv(k * 64, (k + 1) * 64), xck.tv(e0, e1),
                 start=(k == 0), stop=(k == 15))
            c.mm(psb, xps[1].tv(k * 16, (k + 1) * 16), xck.tv(e0, e1),
                 start=(k == 0), stop=(k == 15))
            c.mm(psc, xps[2].tv(k * 16, (k + 1) * 16), xck.tv(e0, e1),
                 start=(k == 0), stop=(k == 15))
        c.vop("tensor_copy", dlr.tv(e0, e1, p0=0, p1=64), psd)
        c.vop("tensor_copy", Bc.tv(e0, e1, p0=0, p1=16), psb)
        c.vop("tensor_copy", Cc.tv(e0, e1, p0=0, p1=16), psc)

    dts = c.atile(16 * 128, BF16)
    for mo in range(16):
        src = TV(W["dtT"], W["dtT"].handle.ap()[mo], 0, 64, 0, 1 << 30)
        c.p.dma("gpsimd", dts.tv(mo * 128, (mo + 1) * 128, p0=0, p1=64), src)
    selw = c.atile(16 * 128, BF16)
    c.p.dma("gpsimd", selw.tv(p0=0, p1=16), W["sel16_dram"].tv(ps=(0, 16)))

    hfin = c.atile(256, F32)
    atot = c.atile(256, F32)
    hin = c.atile(256, F32)

    def compute_delta(m, dbuf, scratch, xcbuf=None):
        if xcbuf is not None:
            srcx = TV(W["xc_dram"], W["xc_dram"].handle.ap()[m], 0, 128,
                      m * T * 2, (m + 1) * T * 2)
            c.p.dma("sync", xcbuf.tv(), srcx)
        for st in range(nst):
            e0, e1 = st * 512, (st + 1) * 512
            ps = c.pbtv(6 + st % 2, 0, 128, 0, 512)
            c.mm(ps, dts.tv(m * 128, (m + 1) * 128, p0=0, p1=64),
                 dlr.tv(e0, e1, p0=0, p1=64), start=True, stop=True)
            c.sop("activation", dbuf.tv(e0, e1), ps, func=AF.Identity,
                  bias=W["dtb_sb"].tv(fs=(m, m + 1)))
        dabs = scratch
        c.sop("activation", dabs.tv(), dbuf.tv(), func=AF.Abs)
        c.sop("activation", dabs.tv(), dabs.tv(), func=AF.Exp, scale=-1.0)
        c.sop("activation", dabs.tv(), dabs.tv(), func=AF.Ln, bias=1.0)
        c.vop("tensor_relu", dbuf.tv(), dbuf.tv(), extra_reads=[dbuf.tv()])
        c.vop("tensor_add", dbuf.tv(), dbuf.tv(), dabs.tv(), extra_reads=[dbuf.tv()])
        return dabs

    def bc_bcast(bank0, srcbc, n):
        for st in range(nst):
            ps = c.pbtv(bank0 + st, 0, 128, 0, 512)
            c.mm(ps, selw.tv(n * 128, (n + 1) * 128, p0=0, p1=16),
                 srcbc.tv(st * 512, (st + 1) * 512, p0=0, p1=16),
                 start=True, stop=True)
        return c.pbw(bank0, T)

    def acol(m, n):
        return W["A_sb"].tv(fs=(m * 16 + n, m * 16 + n + 1))

    hfin = c.atile(256, F32)
    atot = c.atile(256, F32)
    hin = c.atile(256, F32)

    for m in range(16):
        c.amark()
        dbuf = c.atile(T, F32)
        scratch = c.atile(T, F32)
        xcb = c.atile(T, BF16)
        dx = compute_delta(m, dbuf, scratch, xcb)
        c.vop("tensor_mul", dx.tv(), dbuf.tv(), xcb.tv())
        a = [c.atile(T, F32) for _ in range(2)]
        b = [c.atile(T, BF16) for _ in range(2)]
        H = [c.atile(T, BF16) for _ in range(2)]
        g2 = [c.atile(T, BF16) for _ in range(2)]
        Bb = [c.atile(T, BF16) for _ in range(2)]
        Cb = [c.atile(T, BF16) for _ in range(2)]
        dxb = c.atile(T, BF16)
        c.vop("tensor_copy", dxb.tv(), dx.tv())
        y = scratch
        cdT = c.atile(1, F32)
        c.vop("tensor_reduce", cdT.tv(), dbuf.tv(), AX.X, OP.add)
        for n in range(16):
            pp = n % 2
            pB = bc_bcast((n % 2) * 4, Bc, n)
            c.sop("activation", Bb[pp].tv(), pB, func=AF.Identity)
            pC = bc_bcast(((n + 1) % 2) * 4, Cc, n)
            c.sop("activation", Cb[pp].tv(), pC, func=AF.Identity)
            c.sop("activation", a[pp].tv(), dbuf.tv(), func=AF.Exp, scale=acol(m, n))
            c.sop("activation", atot.tv(m * 16 + n, m * 16 + n + 1), cdT.tv(),
                  func=AF.Exp, scale=acol(m, n))
            c.vop("tensor_mul", b[pp].tv(), dxb.tv(), Bb[pp].tv())
            c.vop("tensor_tensor_scan", H[pp].tv(), a[pp].tv(), b[pp].tv(), 0.0,
                  op0=OP.mult, op1=OP.add)
            c.vop("tensor_copy", hfin.tv(m * 16 + n, m * 16 + n + 1),
                  H[pp].tv(T - 1, T))
            if n == 0:
                c.vop("tensor_mul", y.tv(), H[pp].tv(), Cb[pp].tv())
            else:
                c.vop("tensor_mul", g2[pp].tv(), H[pp].tv(), Cb[pp].tv())
                c.vop("tensor_add", y.tv(), y.tv(), g2[pp].tv(),
                      extra_reads=[y.tv()])
        c.vop("scalar_tensor_tensor", y.tv(), xcb.tv(), W["D_sb"].tv(fs=(m, m + 1)),
              y.tv(), op0=OP.mult, op1=OP.add, extra_reads=[y.tv()])
        yb = c.atile(T, BF16)
        c.vop("tensor_copy", yb.tv(), y.tv())
        c.p.dma("sync", TV(W["y_dram"], W["y_dram"].handle.ap()[m], 0, 128,
                           m * T * 2, (m + 1) * T * 2), yb.tv())
        c.areset()

    c.p.dma("sync", TV(W["cc_in"], W["cc_in"].handle.ap()[:, 0:256], 0, 128, 0, 1024),
            hfin.tv())
    c.p.dma("sync", TV(W["cc_in"], W["cc_in"].handle.ap()[:, 256:512], 0, 128, 1024, 2048),
            atot.tv())
    c.p.cc("AllGather", OP.bypass, rg, W["cc_in"].tv(), W["cc_out"].tv())
    acc = c.atile(256, F32)
    tmpc = c.atile(512, F32)
    c.vop("memset", acc.tv(), 0.0)
    c.vop("memset", hin.tv(), 0.0)
    for pr in range(nprev):
        blk = TV(W["cc_out"], W["cc_out"].handle.ap()[pr * 128:(pr + 1) * 128, :],
                 0, 128, 0, 1 << 30)
        c.p.dma("sync", tmpc.tv(), blk)
        c.vop("tensor_mul", acc.tv(), acc.tv(), tmpc.tv(256, 512),
              extra_reads=[acc.tv()])
        c.vop("tensor_add", acc.tv(), acc.tv(), tmpc.tv(0, 256),
              extra_reads=[acc.tv()])
        c.vop("tensor_scalar", tmpc.tv(0, 256), acc.tv(), sel.tv(fs=(pr, pr + 1)),
              None, op0=OP.mult)
        c.vop("tensor_add", hin.tv(), hin.tv(), tmpc.tv(0, 256),
              extra_reads=[hin.tv()])

    for m in range(16):
        c.amark()
        dbuf = c.atile(T, F32)
        scratch = c.atile(T, F32)
        compute_delta(m, dbuf, scratch)
        cd = c.atile(T, F32)
        ones_bc = c.ones_f32.tv().with_ap(
            c.ones_f32.tv().ap.broadcast_to((128, T)))
        c.vop("tensor_tensor_scan", cd.tv(), ones_bc, dbuf.tv(), 0.0,
              op0=OP.mult, op1=OP.add)
        P = [c.atile(T, BF16) for _ in range(2)]
        Cb2 = [c.atile(T, BF16) for _ in range(2)]
        acc2 = c.atile(T, BF16)
        yb = c.atile(T, BF16)
        c.p.dma("scalar", yb.tv(), TV(W["y_dram"], W["y_dram"].handle.ap()[m],
                                      0, 128, m * T * 2, (m + 1) * T * 2))
        for n in range(16):
            pp = n % 2
            pC = bc_bcast((n % 2) * 4, Cc, n)
            c.sop("activation", Cb2[pp].tv(), pC, func=AF.Identity)
            c.sop("activation", P[pp].tv(), cd.tv(), func=AF.Exp, scale=acol(m, n))
            c.vop("tensor_mul", P[pp].tv(), P[pp].tv(), Cb2[pp].tv(),
                  extra_reads=[P[pp].tv()])
            if n == 0:
                c.vop("tensor_scalar", acc2.tv(), P[pp].tv(),
                      hin.tv(m * 16 + n, m * 16 + n + 1), None, op0=OP.mult)
            else:
                c.vop("scalar_tensor_tensor", acc2.tv(), P[pp].tv(),
                      hin.tv(m * 16 + n, m * 16 + n + 1), acc2.tv(),
                      op0=OP.mult, op1=OP.add, extra_reads=[acc2.tv()])
        c.vop("tensor_add", yb.tv(), acc2.tv(), yb.tv(), extra_reads=[yb.tv()])
        c.p.dma("sync", TV(W["y_dram"], W["y_dram"].handle.ap()[m], 0, 128,
                           m * T * 2, (m + 1) * T * 2), yb.tv())
        c.areset()

    c.amark()
    ln2 = c.atile(KT * 512, BF16)
    g = c.atile(16 * 512, BF16)
    sz = c.atile(512, BF16)
    yb2 = c.atile(512, BF16)
    wxs2 = [c.atile(1024, BF16) for _ in range(2)]
    wos = [c.atile(2048, BF16) for _ in range(2)]
    for st in range(nst):
        e0 = HALO_M + st * 512
        lns = [ln2.sub(k * 512) for k in range(KT)]
        emit_ln(c, e0, e0 + 512, lns)
        _wslab_dma(c, wxs2[0], W["wxT"], 16, 1024)
        for mz in range(16):
            if mz + 1 < 16:
                _wslab_dma(c, wxs2[(mz + 1) % 2], W["wxT"], 16 + mz + 1, 1024)
            ps = c.pbtv(mz % 4, 0, 128, 0, 512)
            for k in range(KT):
                c.mm(ps, wxs2[mz % 2].tv(k * 128, (k + 1) * 128), lns[k].tv(0, 512),
                     start=(k == 0), stop=(k == KT - 1))
            c.sop("activation", sz.tv(), ps, func=AF.Silu)
            src = TV(W["y_dram"], W["y_dram"].handle.ap()[mz][:, st * 512:(st + 1) * 512],
                     0, 128, st * 1024, (st + 1) * 1024)
            c.p.dma("scalar", yb2.tv(), src)
            c.vop("tensor_mul", g.tv(mz * 512, (mz + 1) * 512), yb2.tv(), sz.tv())
        _wslab_dma(c, wos[0], W["outT"], 0, 2048)
        for mo in range(8):
            if mo + 1 < 8:
                _wslab_dma(c, wos[(mo + 1) % 2], W["outT"], mo + 1, 2048)
            ps = c.pbtv(4 + mo % 2, 0, 128, 0, 512)
            for k in range(16):
                c.mm(ps, wos[mo % 2].tv(k * 128, (k + 1) * 128),
                     g.tv(k * 512, (k + 1) * 512), start=(k == 0), stop=(k == 15))
            rs = c.resid[mo].tv(fs=(e0, e0 + 512))
            c.vop("scalar_tensor_tensor", rs, ps, 1.0, rs, op0=OP.mult, op1=OP.add,
                  extra_reads=[rs])
    c.areset()
    c.areset()


def prep_mamba_host(inputs, rank_sel):
    wx = inputs["L0_attn_b_in_proj_w"]
    xp = inputs["L0_attn_b_x_proj_w"]
    out = {
        "mam_wxT": tile_kxm(np.ascontiguousarray(wx), 8).astype(np.float32),
        "mam_xpdT": tile_kxm(np.ascontiguousarray(xp[:, 0:64]), 16).astype(np.float32),
        "mam_xpbT": tile_kxm(np.ascontiguousarray(xp[:, 64:80]), 16).astype(np.float32),
        "mam_xpcT": tile_kxm(np.ascontiguousarray(xp[:, 80:96]), 16).astype(np.float32),
        "mam_dtT": np.ascontiguousarray(
            inputs["L0_attn_b_dt_proj_w"].reshape(64, 16, 128).transpose(1, 0, 2)
        ).astype(np.float32),
        "mam_conv": np.ascontiguousarray(
            inputs["L0_attn_b_conv_w"][:, 0, :].reshape(16, 128, 4).transpose(1, 0, 2)
            .reshape(128, 64)).astype(np.float32),
        "mam_convb": inputs["L0_attn_b_conv_b"].reshape(16, 128).T.astype(np.float32),
        "mam_dtb": inputs["L0_attn_b_dt_proj_b"].reshape(16, 128).T.astype(np.float32),
        "mam_Alog": np.ascontiguousarray(
            inputs["L0_attn_b_A_log"].reshape(16, 128, 16).transpose(1, 0, 2)
            .reshape(128, 256)).astype(np.float32),
        "mam_D": inputs["L0_attn_b_D"].reshape(16, 128).T.astype(np.float32),
        "mam_outT": tile_kxm(np.ascontiguousarray(inputs["L0_attn_b_out_proj_w"]), 16
                             ).astype(np.float32),
    }
    sel16 = np.zeros((16, 16 * 128), np.float32)
    for n in range(16):
        sel16[n, n * 128:(n + 1) * 128] = 1.0
    out["mam_sel16"] = sel16
    selm = np.zeros((128, 8), np.float32)
    for pr in range(3):
        selm[:, pr] = 1.0 if rank_sel == pr + 1 else 0.0
    selm[:, 3] = 0.0 if rank_sel == 0 else 1.0
    out["mam_sel"] = selm
    return out


def declare_mamba(prog, T=NT, ranks=4):
    W = {
        "wxT": prog.dram("mam_wxT", [32, 128, 1024], F32, kind="ExternalInput"),
        "xpdT": prog.dram("mam_xpdT", [1, 128, 1024], F32, kind="ExternalInput"),
        "xpbT": prog.dram("mam_xpbT", [1, 128, 256], F32, kind="ExternalInput"),
        "xpcT": prog.dram("mam_xpcT", [1, 128, 256], F32, kind="ExternalInput"),
        "dtT": prog.dram("mam_dtT", [16, 64, 128], F32, kind="ExternalInput"),
        "conv_dram": prog.dram("mam_conv", [128, 64], F32, kind="ExternalInput"),
        "convb_dram": prog.dram("mam_convb", [128, 16], F32, kind="ExternalInput"),
        "dtb_dram": prog.dram("mam_dtb", [128, 16], F32, kind="ExternalInput"),
        "Alog_dram": prog.dram("mam_Alog", [128, 256], F32, kind="ExternalInput"),
        "D_dram": prog.dram("mam_D", [128, 16], F32, kind="ExternalInput"),
        "outT": prog.dram("mam_outT", [8, 128, 2048], F32, kind="ExternalInput"),
        "sel16_dram": prog.dram("mam_sel16", [16, 2048], F32, kind="ExternalInput"),
        "sel_dram": prog.dram("mam_sel", [128, 8], F32, kind="ExternalInput"),
        "xc_dram": prog.dram("mam_xc", [16, 128, T], BF16),
        "y_dram": prog.dram("mam_y", [16, 128, T], BF16),
        "cc_in": prog.dram("mam_cc_in", [128, 512], F32),
        "cc_out": prog.dram("mam_cc_out", [ranks * 128, 512], F32),
        "conv_sb": prog.sbuf("mam_conv_sb", [128, 64], F32),
        "convb_sb": prog.sbuf("mam_convb_sb", [128, 16], F32),
        "dtb_sb": prog.sbuf("mam_dtb_sb", [128, 16], F32),
        "A_dram_sb": prog.sbuf("mam_Alog_sb", [128, 256], F32),
        "A_sb": prog.sbuf("mam_A_sb", [128, 256], F32),
        "D_sb": prog.sbuf("mam_D_sb", [128, 16], F32),
        "sel_sb": prog.sbuf("mam_sel_sb", [128, 8], F32),
    }
    return W


def load_mamba_consts(core, W):
    p = core.p
    p.dma("sync", W["conv_sb"].tv(), W["conv_dram"].tv())
    p.dma("sync", W["convb_sb"].tv(), W["convb_dram"].tv())
    p.dma("sync", W["dtb_sb"].tv(), W["dtb_dram"].tv())
    p.dma("sync", W["A_dram_sb"].tv(), W["Alog_dram"].tv())
    p.dma("sync", W["D_sb"].tv(), W["D_dram"].tv())
    p.dma("sync", W["sel_sb"].tv(), W["sel_dram"].tv())



def emit_attn0a(core, W, nquarters=None):
    c = core
    NEXT = c.NEXT
    NQ = c.NT + 128
    ns = NQ // 32
    if nquarters is None:
        nquarters = ns // 4
    nsq = ns // nquarters
    bias = W["bias_sb"]

    def bcol(i, p1=128):
        return bias.tv(ps=(0, p1), fs=(i, i + 1))

    for k in range(KT):
        src = TV(W["x0e"], W["x0e"].handle.ap()[k * 128:(k + 1) * 128,
                                                RES_OFF:RES_OFF + c.NRES],
                 k * 128, (k + 1) * 128, RES_OFF * 4, (RES_OFF + c.NRES) * 4)
        c.p.dma("sync", c.resid[k].tv(fs=(0, c.NRES)), src)

    c.amark()
    relsb = c.atile(64, BF16)
    wpossb = c.atile(16, BF16)
    c.p.dma("gpsimd", relsb.tv(p0=0, p1=16), W["relT"].tv(ps=(0, 16)))
    c.p.dma("gpsimd", wpossb.tv(p0=0, p1=16), W["wpos"].tv(ps=(0, 16)))
    posf = c.atile(16, F32)
    pps = c.pbtv(0, 0, 64, 0, 16)
    c.mm(pps, relsb.tv(p0=0, p1=16), wpossb.tv(p0=0, p1=16), start=True, stop=True)
    c.vop("tensor_copy", posf.tv(p0=0, p1=64), pps)
    c.p.dma("sync", W["stageP"].tv(ps=(0, 64)), posf.tv(p0=0, p1=64))
    posfT = c.atile(64, F32)
    c.p.dma("sync", posfT.tv(p0=0, p1=16),
            TV(W["stageP"], W["stageP"].handle.ap().transpose([1, 0]), 0, 64, 0, 1 << 30),
            allow_slow_non_contiguous=True)
    posD = c.atile(3040, BF16)
    maskD = c.atile(3040, BF16)
    c.amark()
    posP = c.atile(128, F32)
    c.vop("memset", posP.tv(), 0.0)
    for h in range(8):
        c.p.dma("sync", posP.tv(32, 96, p0=h * 16, p1=(h + 1) * 16),
                posfT.tv(0, 64, p0=0, p1=16))
    pp = posP.tv(32, 96)
    c.vop("tensor_scalar", pp, pp, bcol(W["BFB"]), None, op0=OP.add,
          extra_reads=[pp])
    mext = c.atile(128, BF16)
    c.vop("memset", mext.tv(), -1.0e30)
    c.vop("memset", mext.tv(32, 96), 0.0)
    for dt in range(32):
        c.vop("tensor_copy", posD.tv(dt * 95, dt * 95 + 95),
              posP.tv(32 - dt, 127 - dt))
        c.vop("tensor_copy", maskD.tv(dt * 95, dt * 95 + 95),
              mext.tv(32 - dt, 127 - dt))
    c.areset()
    mb = maskD.tv(p0=0, p1=8)
    c.vop("tensor_scalar", mb, mb, bcol(17, 8), None, op0=OP.add,
          extra_reads=[mb])
    wfbd = c.atile(256, BF16)
    c.p.dma("gpsimd", wfbd.tv(0, 128), TV(W["wfbd"], W["wfbd"].handle.ap()[0], 0, 128, 0, 4))
    c.p.dma("gpsimd", wfbd.tv(128, 256), TV(W["wfbd"], W["wfbd"].handle.ap()[1], 0, 128, 4, 8))
    wbd = c.atile(16, BF16)
    c.p.dma("gpsimd", wbd.tv(0, 16), W["wbonddmg_dram"].tv())
    bvrow = c.atile(1024, BF16)
    c.p.dma("gpsimd", bvrow.tv(p0=0, p1=1), W["bvalrow"].tv(ps=(0, 1)))

    em = [c.atile(3040, BF16)] * 2
    lg = [c.atile(1520, F32) for _ in range(2)]
    wdt = [c.atile(1520, F32) for _ in range(2)]
    wn = [c.atile(1520, BF16) for _ in range(2)]
    wT = [c.atile(512, BF16) for _ in range(2)]
    stag = [c.atile(1024, BF16)] * 2
    atok = [c.atile(1024, BF16) for _ in range(2)]
    z1 = [c.atile(3040, BF16) for _ in range(2)]
    zbd = [c.atile(3040, BF16) for _ in range(2)]
    smst = [c.atile(32, F32) for _ in range(2)]
    vsl8 = c.atile(8 * 1024, BF16)
    cslab = [c.atile(1024, BF16) for _ in range(2)]
    dsl = c.atile(1024, BF16)

    for q in range(nquarters):
        i0, i1 = q * nsq, (q + 1) * nsq
        e_lo = (32 * i0) // 128 * 128
        e_hi = min(NEXT, 63 + 32 * i1)
        WH = e_hi - e_lo
        c.amark()
        lne = [c.atile(WH, BF16) for _ in range(KT)]
        c.amark()
        x0t = [c.atile(WH, F32) for _ in range(KT)]
        for k in range(KT):
            src = TV(W["x0e"], W["x0e"].handle.ap()[k * 128:(k + 1) * 128, e_lo:e_hi],
                     k * 128, (k + 1) * 128, e_lo * 4, e_hi * 4)
            c.p.dma("sync", x0t[k].tv(), src)
        emit_ln(c, 0, WH, lne, src_tiles=x0t, pb0=0, pb1=1)
        c.areset()
        Pb = [c.atile(WH, F32) for _ in range(2)]
        Pd = [c.atile(WH, F32) for _ in range(2)]
        disp = c.atile(WH, BF16)
        for g in range(2):
            _wslab_dma(c, dsl, W["dispT"], g, 1024)
            for st in range(ceil_div(WH, 512)):
                e0, e1 = st * 512, min(WH, (st + 1) * 512)
                ps = c.pbtv(st % 2, 0, 128, 0, e1 - e0)
                for k in range(KT):
                    c.mm(ps, dsl.tv(k * 128, (k + 1) * 128), lne[k].tv(e0, e1),
                         start=(k == 0), stop=(k == KT - 1))
                c.sop("activation", disp.tv(e0, e1), ps, func=AF.Identity,
                      bias=bcol(W["BDISP"] + g))
            for st in range(ceil_div(WH, 512)):
                e0, e1 = st * 512, min(WH, (st + 1) * 512)
                psb = c.pbtv(2 + st % 2, 0, 128, 0, e1 - e0)
                c.mm(psb, wfbd.tv(0, 128), disp.tv(e0, e1), start=True, stop=True)
                c.vop("tensor_copy", Pb[g].tv(e0, e1), psb)
                psd = c.pbtv(4 + st % 2, 0, 128, 0, e1 - e0)
                c.mm(psd, wfbd.tv(128, 256), disp.tv(e0, e1), start=True, stop=True)
                c.vop("tensor_copy", Pd[g].tv(e0, e1), psd)
        tt0 = e_lo // 128
        ntt = ceil_div(e_hi, 128) - tt0
        vtok = c.atile(ntt * 1024, BF16)
        for k in range(KT):
            _wslab_dma(c, vsl8.sub(k * 1024), W["valT"], k, 1024)
        for tt in range(ntt):
            a0 = tt * 128
            a1 = min(WH, a0 + 128)
            wq = a1 - a0
            for nh in range(2):
                ps = c.pbtv(2 + nh, 0, wq, 0, 512)
                for k in range(KT):
                    c.mm(ps, lne[k].tv(a0, a1),
                         vsl8.tv(k * 1024 + nh * 512, k * 1024 + (nh + 1) * 512),
                         start=(k == 0), stop=False)
                c.mm(ps, c.ones_row_bf.tv(ps=(0, 1), fs=(0, wq)),
                     bvrow.tv(nh * 512, (nh + 1) * 512, p0=0, p1=1),
                     start=False, stop=True)
                c.vop("tensor_copy",
                      vtok.tv(tt * 1024 + nh * 512, tt * 1024 + (nh + 1) * 512,
                              p0=0, p1=wq), ps)
        afm_all = c.atile(KT * nsq * 32, BF16)
        def sub_head(li):
            gi = i0 + li
            pr = li % 2
            qe = 63 + 32 * gi - e_lo
            klo = 32 * gi - e_lo
            if gi < 6:
                c.p.dma("scalar", em[pr].tv(p0=0, p1=8),
                        TV(W["em"], W["em"].handle.ap()[gi], 0, 8, 0, 1 << 30))
            for g in range(2):
                for c2 in range(2):
                    cp = (li * 4 + g * 2 + c2) % 2
                    dt0 = c2 * 16
                    pbs = Pb[g].tv(klo, klo + 95).with_ap(
                        Pb[g].tv(klo, klo + 95).ap.unsqueeze(1)
                        .broadcast_to((128, 16, 95)))
                    pbt = Pb[g].tv(qe + dt0, qe + dt0 + 16).with_ap(
                        Pb[g].tv(qe + dt0, qe + dt0 + 16).ap.unsqueeze(2)
                        .broadcast_to((128, 16, 95)))
                    z1v = z1[cp].tv(0, 1520).with_ap(
                        z1[cp].tv(0, 1520).ap.rearrange("p (t u) -> p t u", u=95))
                    c.vop("tensor_sub", z1v, pbs, pbt)
                    c.vop("tensor_add", z1[cp].tv(0, 1520), z1[cp].tv(0, 1520),
                          posD.tv(dt0 * 95, (dt0 + 16) * 95),
                          extra_reads=[z1[cp].tv(0, 1520)])
                    pds = Pd[g].tv(klo, klo + 95).with_ap(
                        Pd[g].tv(klo, klo + 95).ap.unsqueeze(1)
                        .broadcast_to((128, 16, 95)))
                    pdt = Pd[g].tv(qe + dt0, qe + dt0 + 16).with_ap(
                        Pd[g].tv(qe + dt0, qe + dt0 + 16).ap.unsqueeze(2)
                        .broadcast_to((128, 16, 95)))
                    z2v = z1[cp].tv(1520, 3040).with_ap(
                        z1[cp].tv(1520, 3040).ap.rearrange("p (t u) -> p t u", u=95))
                    c.gop("tensor_sub", z2v, pds, pdt)
                    c.sop("activation", zbd[cp].tv(0, 1520), z1[cp].tv(0, 1520),
                          func=AF.Gelu)
                    c.sop("activation", zbd[cp].tv(1520, 3040), z1[cp].tv(1520, 3040),
                          func=AF.Gelu, bias=bcol(W["BFD"]))
                    pwb = c.pbw(0, 1520, 0, 8)
                    pwd = c.pbw(3, 1520, 0, 8)
                    for sb in range(3):
                        s0, s1 = sb * 512, min(1520, (sb + 1) * 512)
                        psl = c.pbtv(sb, 0, 8, 0, s1 - s0)
                        c.mm(psl, wbd.tv(0, 8), zbd[cp].tv(s0, s1), start=True, stop=False)
                        c.mm(psl, c.ident.tv(ps=(0, 8), fs=(0, 8)),
                            maskD.tv(dt0 * 95 + s0, dt0 * 95 + s1, p0=0, p1=8),
                            start=False, stop=(gi >= 6))
                        if gi < 6:
                            c.mm(psl, c.ident.tv(ps=(0, 8), fs=(0, 8)),
                                em[pr].tv(dt0 * 95 + s0, dt0 * 95 + s1, p0=0, p1=8),
                                start=False, stop=True)
                        psl2 = c.pbtv(3 + sb, 0, 8, 0, s1 - s0)
                        c.mm(psl2, wbd.tv(8, 16), zbd[cp].tv(1520 + s0, 1520 + s1), start=True, stop=True)
                    c.sop("activation", lg[cp].tv(p0=0, p1=8), pwd, func=AF.Tanh,
                          bias=bcol(16, 8), scale=0.5)
                    c.vop("scalar_tensor_tensor", lg[cp].tv(p0=0, p1=8), lg[cp].tv(p0=0, p1=8),
                          -5.0, pwb, op0=OP.mult, op1=OP.add,
                          extra_reads=[lg[cp].tv(p0=0, p1=8)])
                    dst = TV(W["stage"], W["stage"].handle.ap()[pr][g * 8:(g + 1) * 8,
                                                               dt0 * 95:(dt0 + 16) * 95],
                             g * 8, (g + 1) * 8, pr * 50000 + dt0 * 380,
                             pr * 50000 + (dt0 + 16) * 380)
                    c.p.dma("sync", dst, lg[cp].tv(p0=0, p1=8))

        def sub_tail(li):
            gi = i0 + li
            pr = li % 2
            qe = 63 + 32 * gi - e_lo
            klo = 32 * gi - e_lo
            src = TV(W["stage"], W["stage"].handle.ap()[pr]
                     .rearrange("h (t u) -> h t u", u=95).transpose([1, 0, 2]),
                     0, 16, pr * 50000, pr * 50000 + 49000)
            c.p.dma("sync", wdt[pr].tv(p0=0, p1=32).with_ap(
                wdt[pr].tv(p0=0, p1=32).ap.rearrange("p (h u) -> p h u", u=95)), src)
            w3 = wdt[pr].tv(p0=0, p1=32).with_ap(
                wdt[pr].tv(p0=0, p1=32).ap.rearrange("p (h u) -> p h u", u=95))
            nm = smst[pr].tv(0, 16, p0=0, p1=32)
            c.vop("tensor_reduce", nm, w3, AX.X, OP.max, negate=True)
            nmb = nm.with_ap(nm.ap.unsqueeze(2).broadcast_to((32, 16, 95)))
            c.vop("tensor_add", w3, w3, nmb, extra_reads=[wdt[pr].tv(p0=0, p1=32)])
            c.sop("activation", wn[pr].tv(p0=0, p1=32), wdt[pr].tv(p0=0, p1=32),
                  func=AF.Exp)
            wnv = wn[pr].tv(p0=0, p1=32).with_ap(
                wn[pr].tv(p0=0, p1=32).ap.rearrange("p (h u) -> p h u", u=95))
            sm = smst[pr].tv(16, 32, p0=0, p1=32)
            c.vop("tensor_reduce", sm, wnv, AX.X, OP.add)
            c.vop("reciprocal", sm, sm, extra_reads=[sm])
            for h in range(16):
                pt = c.pbt16(6, 0, 95, h * 32, (h + 1) * 32)
                c.transpose(pt, wn[pr].tv(h * 95, (h + 1) * 95, p0=0, p1=32))
            c.vop("tensor_copy", wT[pr].tv(p0=0, p1=95), c.pbt16(6, 0, 95, 0, 512))
            abs0 = e_lo + klo
            ta = abs0 // 128 - tt0
            r0 = abs0 % 128
            n1 = min(95, 128 - r0)
            c.p.dma("scalar", stag[pr].tv(p0=0, p1=n1),
                    vtok.tv(ta * 1024, (ta + 1) * 1024, p0=r0, p1=r0 + n1))
            if n1 < 95:
                c.p.dma("scalar", stag[pr].tv(p0=n1, p1=95),
                        vtok.tv((ta + 1) * 1024, (ta + 2) * 1024, p0=0, p1=95 - n1))
            po = c.pbw(6, 1024, 0, 32)
            for h in range(16):
                pslice = c.pbtv(6 + h // 8, 0, 32, (h % 8) * 64, (h % 8) * 64 + 64)
                c.mm(pslice, wT[pr].tv(h * 32, (h + 1) * 32, p0=0, p1=95),
                     stag[pr].tv(h * 64, (h + 1) * 64, p0=0, p1=95),
                     start=True, stop=True)
            po3 = po.with_ap(po.ap.rearrange("p (h f) -> p h f", f=64))
            smb = sm.with_ap(sm.ap.unsqueeze(2).broadcast_to((32, 16, 64)))
            at3 = atok[pr].tv(p0=0, p1=32).with_ap(
                atok[pr].tv(p0=0, p1=32).ap.rearrange("p (h f) -> p h f", f=64))
            c.vop("tensor_mul", at3, po3, smb)
            if "dbg" in W and gi == W.get("dbg_gi", -1):
                c.p.dma("gpsimd", W["dbg"].tv(ps=(0, 32)), atok[pr].tv(p0=0, p1=32))
                c.p.dma("gpsimd", W["dbg2"].tv(ps=(0, 32)), wn[pr].tv(p0=0, p1=32))
                c.p.dma("gpsimd", W["dbg3"].tv(ps=(0, 95)), stag[pr].tv(p0=0, p1=95))
                c.p.dma("gpsimd", W["dbg4"].tv(), vtok.tv(0, 1024))
            for ft in range(KT):
                pt = c.pbt16(6, 0, 128, ft * 32, (ft + 1) * 32)
                c.transpose(pt, atok[pr].tv(ft * 128, (ft + 1) * 128, p0=0, p1=32))
            afull = afm_all.tv(0, KT * nsq * 32)
            dst3 = afull.with_ap(afull.ap.rearrange("p (f x) -> p f x", f=KT)
                                 [:, :, li * 32:(li + 1) * 32])
            s2 = c.pbt16(6, 0, 128, 0, KT * 32)
            src3 = s2.with_ap(s2.ap.rearrange("p (f u) -> p f u", f=KT))
            c.vop("tensor_copy", dst3, src3)

        sub_head(0)
        for li in range(1, nsq):
            sub_head(li)
            sub_tail(li - 1)
        sub_tail(nsq - 1)
        r0 = max(0, 63 + 32 * i0 - RES_OFF)
        r1 = min(c.NRES, 63 + 32 * i1 - RES_OFF)
        _wslab_dma(c, cslab[0], W["cprojT"], 0, 1024)
        for mo in range(KT):
            if mo + 1 < KT:
                _wslab_dma(c, cslab[(mo + 1) % 2], W["cprojT"], mo + 1, 1024)
            for st in range(ceil_div(r1 - r0, 512)):
                e0 = r0 + st * 512
                e1 = min(r1, e0 + 512)
                a0 = e0 + RES_OFF - 63 - 32 * i0
                ps = c.pbtv(6 + st % 2, 0, 128, 0, e1 - e0)
                for k in range(KT):
                    c.mm(ps, cslab[mo % 2].tv(k * 128, (k + 1) * 128),
                         afm_all.tv(k * nsq * 32 + a0, k * nsq * 32 + a0 + e1 - e0),
                         start=(k == 0), stop=(k == KT - 1))
                tmp = c.ln_sq[st % 2].tv(fs=(0, e1 - e0))
                c.sop("activation", tmp, ps, func=AF.Identity, bias=bcol(W["BCPROJ"] + mo))
                rs = c.resid[mo].tv(fs=(e0, e1))
                c.vop("tensor_add", rs, rs, tmp, extra_reads=[rs])
        c.areset()
    c.areset()


A0_BDISP, A0_BFD, A0_BFB, A0_BBOND, A0_BDMG, A0_BVAL, A0_BCPROJ = 0, 2, 3, 4, 5, 6, 8


def prep_attn0a_host(inputs, seq_start, T=NT):
    import ml_dtypes
    Wd = inputs["L0_attn_a_W_disp"]; Wv = inputs["L0_attn_a_W_val"]
    Wc = inputs["L0_attn_a_W_cproj"]; Wf = inputs["L0_attn_a_W_fused"]
    out = {
        "a0_dispT": tile_kxm(np.ascontiguousarray(Wd), 8).astype(np.float32),
        "a0_valT": np.ascontiguousarray(Wv.reshape(8, 128, 1024)).astype(np.float32),
        "a0_cprojT": tile_kxm(np.ascontiguousarray(Wc), 8).astype(np.float32),
        "a0_relT": np.ascontiguousarray(inputs["L0_attn_a_rel_pos_emb"].T).astype(np.float32),
        "a0_wpos": inputs["L0_attn_a_W_pos"].astype(np.float32),
        "a0_bvalrow": inputs["L0_attn_a_b_val"].reshape(1, 1024).astype(np.float32),
    }
    wfbd = np.zeros((2, 128, 128), np.float32)
    for s in range(2):
        for h in range(8):
            wfbd[s, h * 16:(h + 1) * 16, h * 16:(h + 1) * 16] = Wf[:, s * 16:(s + 1) * 16]
    out["a0_wfbd"] = wfbd
    wbd = np.zeros((128, 16), np.float32)
    for h in range(8):
        wbd[h * 16:(h + 1) * 16, h] = inputs["L0_attn_a_W_bond"][:, 0]
        wbd[h * 16:(h + 1) * 16, 8 + h] = inputs["L0_attn_a_W_dmg"][:, 0]
    out["a0_wbonddmg"] = wbd
    bias = np.zeros((128, 16), np.float32)
    bias[:, A0_BDISP] = inputs["L0_attn_a_b_disp"][0:128]
    bias[:, A0_BDISP + 1] = inputs["L0_attn_a_b_disp"][128:256]
    bias[:, A0_BFD] = np.tile(inputs["L0_attn_a_b_fused"][16:32], 8)
    bias[:, A0_BFB] = np.tile(inputs["L0_attn_a_b_fused"][0:16], 8)
    bias[:, A0_BBOND] = inputs["L0_attn_a_b_bond"][0]
    bias[:, A0_BDMG] = inputs["L0_attn_a_b_dmg"][0]
    for mo in range(8):
        bias[:, A0_BCPROJ + mo] = inputs["L0_attn_a_b_cproj"][mo * 128:(mo + 1) * 128]
    out["a0_bias"] = bias
    em = np.zeros((6, 8, 3040), np.float32)
    for gi in range(6):
        for dt in range(32):
            u = np.arange(95)
            kk = seq_start - 191 + 32 * gi + u
            em[gi, :, dt * 95:(dt + 1) * 95] = np.where(kk < 0, -1.0e30, 0.0)
    out["a0_em"] = em.astype(ml_dtypes.bfloat16)
    return out


def declare_attn0a(prog, T=NT):
    NEXTl = T + 191
    W = {
        "x0e": prog.dram("a0_x0e", [1024, NEXTl], F32, kind="ExternalInput"),
        "dispT": prog.dram("a0_dispT", [2, 128, 1024], F32, kind="ExternalInput"),
        "valT": prog.dram("a0_valT", [8, 128, 1024], F32, kind="ExternalInput"),
        "cprojT": prog.dram("a0_cprojT", [8, 128, 1024], F32, kind="ExternalInput"),
        "relT": prog.dram("a0_relT", [16, 64], F32, kind="ExternalInput"),
        "wpos": prog.dram("a0_wpos", [16, 16], F32, kind="ExternalInput"),
        "wfbd": prog.dram("a0_wfbd", [2, 128, 128], F32, kind="ExternalInput"),
        "wbonddmg_dram": prog.dram("a0_wbonddmg", [128, 16], F32, kind="ExternalInput"),
        "bvalrow": prog.dram("a0_bvalrow", [1, 1024], F32, kind="ExternalInput"),
        "bias_dram": prog.dram("a0_bias", [128, 16], F32, kind="ExternalInput"),
        "em": prog.dram("a0_em", [6, 8, 3040], BF16, kind="ExternalInput"),
        "stage": prog.dram("a0_stage", [2, 16, 3040], F32),
        "stageP": prog.dram("a0_stageP", [64, 16], F32),
        "bias_sb": prog.sbuf("a0_bias_sb", [128, 18], F32),
        "BDISP": A0_BDISP, "BFD": A0_BFD, "BFB": A0_BFB, "BBOND": A0_BBOND,
        "BDMG": A0_BDMG, "BCPROJ": A0_BCPROJ,
    }
    return W


def load_attn0a_consts(core, W):
    core.p.dma("sync", W["bias_sb"].tv(fs=(0, 16)), W["bias_dram"].tv())
    core.vop("tensor_scalar", W["bias_sb"].tv(fs=(16, 17)),
             W["bias_sb"].tv(fs=(A0_BDMG, A0_BDMG + 1)), 0.5, None, op0=OP.mult)
    core.vop("tensor_scalar", W["bias_sb"].tv(fs=(17, 18)),
             W["bias_sb"].tv(fs=(A0_BBOND, A0_BBOND + 1)), -5.0, None, op0=OP.add)




SUBLAYERS = ["a0a", "ffn0a", "mam", "ffn0b", "a1a", "ffn1a", "a1b", "ffn1b"]

GROUPS = [SUBLAYERS]


def _declare_for(prog, names, nt, ranks):
    D_ = {}
    if "a0a" in names:
        D_["a0a"] = declare_attn0a(prog, T=nt)
    if "ffn0a" in names or "ffn0b" in names:
        D_["ffn0"] = declare_ffn0(prog)
    if "mam" in names:
        D_["mam"] = declare_mamba(prog, T=nt, ranks=ranks)
    if "a1a" in names:
        D_["a1a"] = declare_attn1a(prog)
    if "ffn1a" in names or "ffn1b" in names:
        D_["ffn1"] = declare_ffn1(prog)
    if "a1b" in names:
        D_["a1b"] = declare_attn1b(prog, ranks=ranks, T=nt)
    return D_


def _load_consts_for(core, D_):
    if "a0a" in D_:
        load_attn0a_consts(core, D_["a0a"])
    if "ffn0" in D_:
        load_ffn0_consts(core, D_["ffn0"])
    if "mam" in D_:
        load_mamba_consts(core, D_["mam"])
    if "a1a" in D_:
        load_attn1a_consts(core, D_["a1a"])
    if "a1b" in D_:
        load_attn1b_consts(core, D_["a1b"])


def build_group(names, nt, rg, ranks, first, last):
    nc = bass.Bass("TRN2", target_bir_lowering=False, debug=False)
    prog = Prog(nc)
    nres = nt + HALO_M
    if not first:
        rin = prog.dram("rin", [1024, nres], F32, kind="ExternalInput")
    rout = prog.dram("rout", [1024, nt if last else nres], F32, kind="ExternalOutput")
    D_ = _declare_for(prog, names, nt, ranks)
    core = Core(nc, prog, nt=nt)
    core.init_consts()
    _load_consts_for(core, D_)
    if not first:
        for k in range(KT):
            src = TV(rin, rin.handle.ap()[k * 128:(k + 1) * 128, :],
                     k * 128, (k + 1) * 128, 0, 1 << 30)
            prog.dma("sync", core.resid[k].tv(fs=(0, nres)), src)
    for nm in names:
        prog.phase = nm
        if nm == "a0a":
            emit_attn0a(core, D_["a0a"])
        elif nm == "ffn0a":
            emit_ffn0(core, D_["ffn0"], 0, nres)
        elif nm == "ffn0b":
            emit_ffn0(core, D_["ffn0"], HALO_M, nres)
        elif nm == "mam":
            emit_mamba(core, D_["mam"], rg)
        elif nm == "a1a":
            emit_attn1a(core, D_["a1a"], HALO_M, nres, rg)
        elif nm == "ffn1a" or nm == "ffn1b":
            emit_ffn1(core, D_["ffn1"], HALO_M, nres)
        elif nm == "a1b":
            emit_attn1b(core, D_["a1b"], HALO_M, nres, rg, ranks)
    prog.phase = "out"
    c0 = HALO_M if last else 0
    for k in range(KT):
        dst = TV(rout, rout.handle.ap()[k * 128:(k + 1) * 128, :],
                 k * 128, (k + 1) * 128, 0, 1 << 30)
        prog.dma("sync", dst, core.resid[k].tv(fs=(c0, nres)))
    prog.emit_all()
    return nc, prog


def prep_host_all(inputs, nt=NT, ncores=8, per_batch=4):
    shared = {}
    shared.update(prep_ffn0_host(inputs))
    shared.update(prep_ffn1_host(inputs))
    shared.update(prep_attn1a_host(inputs))
    ab = prep_attn1b_host(inputs, ranks=per_batch, T=nt)
    masks = {r: ab.pop(f"a1b_mask_r{r}") for r in range(per_batch)}
    shared.update(ab)
    mam_shared = None
    x = inputs["L0_attn_a_x"]
    B = x.shape[0]
    percore = []
    for c in range(ncores):
        b, r = c // per_batch, c % per_batch
        s = r * nt
        pc = {}
        mh = prep_mamba_host(inputs, rank_sel=r)
        a0 = prep_attn0a_host(inputs, seq_start=s, T=nt)
        if mam_shared is None:
            mam_shared = {k: v for k, v in mh.items() if k != "mam_sel"}
            shared.update(mam_shared)
            shared.update({k: v for k, v in a0.items() if k != "a0_em"})
        pc["mam_sel"] = mh["mam_sel"]
        pc["a0_em"] = a0["a0_em"]
        pc["a1b_mask"] = masks[r]
        x0e = np.zeros((nt + 191, 1024), np.float32)
        lo = s - 191
        src_lo = max(0, lo)
        x0e[src_lo - lo:] = x[b, src_lo:s + nt]
        pc["a0_x0e"] = np.ascontiguousarray(x0e.T)
        percore.append(pc)
    return shared, percore


def kernel(**inputs):
    nt = NT
    ncores = 8
    per_batch = 4
    rg = [[0, 1, 2, 3], [4, 5, 6, 7]]
    shared, percore = prep_host_all(inputs, nt, ncores, per_batch)
    x = inputs["L0_attn_a_x"]
    B, S, Dm = x.shape
    state = [None] * ncores
    n_groups = len(GROUPS)
    for gi, names in enumerate(GROUPS):
        first = gi == 0
        last = gi == n_groups - 1
        nc, prog = build_group(names, nt, rg, per_batch, first, last)
        needed = {b.name for b in prog.bufs
                  if b.space == "dram" and b.handle is not None}
        req = set()
        for alloc in nc.m.functions[0].allocations:
            try:
                if alloc.kind == "ExternalInput":
                    req.add(alloc.memorylocations[0].name)
            except AttributeError:
                pass
        in_maps = []
        for c in range(ncores):
            m = {}
            for k2 in req:
                if k2 == "partition_id":
                    continue
                if k2 == "rin":
                    m[k2] = state[c]
                elif k2 in percore[c]:
                    m[k2] = percore[c][k2]
                elif k2 in shared:
                    m[k2] = shared[k2]
                else:
                    raise KeyError(f"missing input {k2}")
            in_maps.append(m)
        res = run_bass_kernel_spmd(nc, in_maps, core_ids=list(range(ncores)))
        for c in range(ncores):
            state[c] = res.results[c]["rout"]
    out = np.empty((B, S, Dm), np.float32)
    for c in range(ncores):
        b, r = c // per_batch, c % per_batch
        out[b, r * nt:(r + 1) * nt] = state[c].T
    return out


# version 25
# speedup vs baseline: 1.0233x; 1.0233x over previous
import numpy as np

import concourse.bass as bass
import concourse.mybir as mybir
from concourse.bass_utils import run_bass_kernel_spmd

F32 = mybir.dt.float32
BF16 = mybir.dt.bfloat16
I32 = mybir.dt.int32
AF = mybir.ActivationFunctionType
OP = mybir.AluOpType
AX = mybir.AxisListType

ENGINES = ("sync", "scalar", "vector", "gpsimd", "tensor")



class Buf:

    __slots__ = ("name", "handle", "space", "whole", "writes", "reads", "dtype")

    def __init__(self, name, handle, space, dtype, whole=False):
        self.name = name
        self.handle = handle
        self.space = space
        self.dtype = dtype
        self.whole = whole
        self.writes = []
        self.reads = []

    def tv(self, ps=None, fs=None, elem_bytes=None):
        eb = elem_bytes or np.dtype(mybir.dt.np(self.dtype)).itemsize
        p0, p1 = ps if ps is not None else (0, self.handle.shape[0])
        if fs is None:
            ap = self.handle.ap()[p0:p1, :]
            b0, b1 = 0, self.handle.shape[1] * eb
        else:
            e0, e1 = fs
            ap = self.handle.ap()[p0:p1, e0:e1]
            b0, b1 = e0 * eb, e1 * eb
        return TV(self, ap, p0, p1, b0, b1)

    def whole_tv(self):
        return self.tv()


class TV:

    __slots__ = ("buf", "ap", "p0", "p1", "b0", "b1")

    def __init__(self, buf, ap, p0, p1, b0, b1):
        self.buf = buf
        self.ap = ap
        self.p0 = p0
        self.p1 = p1
        self.b0 = b0
        self.b1 = b1

    def with_ap(self, ap):
        return TV(self.buf, ap, self.p0, self.p1, self.b0, self.b1)

    def rearrange(self, s, **kw):
        return self.with_ap(self.ap.rearrange(s, **kw))

    def bitcast(self, dt):
        return self.with_ap(self.ap.bitcast(dt))


class Opr:
    __slots__ = ("engine", "emit", "deps", "kind", "eidx", "didx", "ccidx",
                 "needs_inc", "ms", "tag", "phase")

    def __init__(self, engine, emit, kind, eidx, didx, ccidx, tag):
        self.engine = engine
        self.emit = emit
        self.kind = kind
        self.eidx = eidx
        self.didx = didx
        self.ccidx = ccidx
        self.deps = {}
        self.needs_inc = False
        self.ms = None
        self.tag = tag


def _overlap(a, b):
    return not (a[1] <= b[0] or b[1] <= a[0])


class Prog:
    def __init__(self, nc):
        self.nc = nc
        self.phase = "init"
        self.ops = {e: [] for e in ENGINES}
        self.dma_count = {e: 0 for e in ENGINES}
        self.cc_count = 0
        self.bufs = []
        self.n_ops = 0
        self.boundaries = {e: [] for e in ENGINES}
        self.gates = {e: set() for e in ENGINES}
        self.cc_boundaries = []
        self.cc_gates = set()

    def _dep_value(self, d):
        if d.kind == "c":
            return None
        if d.kind == "d":
            import bisect
            bl = self.boundaries[d.engine]
            need = d.didx + 1
            i = bisect.bisect_left(bl, need)
            if i < len(bl):
                return ("d", d.engine, bl[i])
            b = self.dma_count[d.engine]
            assert b >= need
            bl.append(b)
            self.gates[d.engine].add(b)
            return ("d", d.engine, b)
        import bisect
        need = d.ccidx + 1
        i = bisect.bisect_left(self.cc_boundaries, need)
        if i < len(self.cc_boundaries):
            return ("cc", None, self.cc_boundaries[i])
        b = self.cc_count
        assert b >= need
        self.cc_boundaries.append(b)
        self.cc_gates.add(b)
        return ("cc", None, b)

    def sbuf(self, name, shape, dtype):
        h = self.nc.alloc_sbuf_tensor(name, list(shape), dtype)
        b = Buf(name, h, "sbuf", dtype)
        self.bufs.append(b)
        return b

    def psum(self, name, shape=(128, 512), dtype=F32):
        h = self.nc.alloc_psum_tensor(name, list(shape), dtype)
        b = Buf(name, h, "psum", dtype, whole=True)
        self.bufs.append(b)
        return b

    def dram(self, name, shape, dtype, kind="Internal", addr_space="Local"):
        h = self.nc.dram_tensor(name, list(shape), dtype, kind=kind,
                                addr_space=addr_space)
        b = Buf(name, h, "dram", dtype)
        self.bufs.append(b)
        return b

    def _regions(self, tvs):
        out = []
        for tv in tvs:
            if tv is None:
                continue
            if isinstance(tv, Buf):
                tv = tv.whole_tv()
            b = tv.buf
            if b.whole:
                out.append((b, 0, 128, 0, 1 << 30))
            else:
                out.append((b, tv.p0, tv.p1, tv.b0, tv.b1))
        return out

    def op(self, engine, emit, reads=(), writes=(), kind="c", tag=None):
        didx = ccidx = None
        if kind == "d":
            didx = self.dma_count[engine]
        elif kind == "cc":
            ccidx = self.cc_count
        o = Opr(engine, emit, kind, len(self.ops[engine]), didx, ccidx, tag)
        o.phase = self.phase
        rr = self._regions(reads)
        ww = self._regions(writes)
        dep_set = set()
        for (b, p0, p1, b0, b1) in rr:
            for (w, wp0, wp1, wb0, wb1) in b.writes:
                if w is not o and _overlap((p0, p1), (wp0, wp1)) and _overlap((b0, b1), (wb0, wb1)):
                    dep_set.add(w)
        for (b, p0, p1, b0, b1) in ww:
            for (w, wp0, wp1, wb0, wb1) in b.writes:
                if _overlap((p0, p1), (wp0, wp1)) and _overlap((b0, b1), (wb0, wb1)):
                    dep_set.add(w)
            for (r, rp0, rp1, rb0, rb1) in b.reads:
                if _overlap((p0, p1), (rp0, rp1)) and _overlap((b0, b1), (rb0, rb1)):
                    dep_set.add(r)
        for d in dep_set:
            if d.engine == "tensor" and engine == "tensor" and d.kind == "c" and kind == "c":
                continue
            o.deps[d] = self._dep_value(d)
        if kind == "d":
            self.dma_count[engine] += 1
        elif kind == "cc":
            self.cc_count += 1
        for (b, p0, p1, b0, b1) in ww:
            b.writes = [wacc for wacc in b.writes
                        if not (p0 <= wacc[1] and wacc[2] <= p1 and b0 <= wacc[3] and wacc[4] <= b1)]
            b.reads = [racc for racc in b.reads
                       if not (p0 <= racc[1] and racc[2] <= p1 and b0 <= racc[3] and racc[4] <= b1)]
            b.writes.append((o, p0, p1, b0, b1))
        for (b, p0, p1, b0, b1) in rr:
            b.reads.append((o, p0, p1, b0, b1))
        self.ops[engine].append(o)
        self.n_ops += 1
        return o

    def dma(self, engine, out_tv, in_tv, tag=None, **kw):
        oap = out_tv.ap if isinstance(out_tv, TV) else out_tv
        iap = in_tv.ap if isinstance(in_tv, TV) else in_tv

        def emit(eng, _o=oap, _i=iap, _k=dict(kw)):
            return eng.dma_start(out=_o, in_=_i, **_k)

        reads = [in_tv] if isinstance(in_tv, TV) else []
        writes = [out_tv] if isinstance(out_tv, TV) else []
        return self.op(engine, emit, reads=reads, writes=writes, kind="d", tag=tag)

    def cc(self, kind, aluop, replica_groups, in_tv, out_tv, tag=None):
        iap = in_tv.ap if isinstance(in_tv, TV) else in_tv
        oap = out_tv.ap if isinstance(out_tv, TV) else out_tv

        def emit(eng, _k=kind, _op=aluop, _rg=replica_groups, _i=iap, _o=oap):
            return eng.collective_compute(_k, _op, replica_groups=_rg,
                                          ins=[_i.opt()], outs=[_o.opt()])

        reads = [in_tv] if isinstance(in_tv, TV) else []
        writes = [out_tv] if isinstance(out_tv, TV) else []
        return self.op("gpsimd", emit, reads=reads, writes=writes, kind="cc", tag=tag)

    ROT_MS = 30000
    ROT_DMA = 1800
    ROT_CC = 30000

    def emit_all(self):
        nc = self.nc
        for e in ENGINES:
            for o in self.ops[e]:
                for d in o.deps:
                    d.needs_inc = True
        ms_count = {e: 0 for e in ENGINES}
        for e in ENGINES:
            for o in self.ops[e]:
                if o.kind == "c":
                    if o.needs_inc:
                        ms_count[e] += 1
                        o.ms = ms_count[e]
                elif o.kind == "d":
                    o.ms = 16 * (o.didx + 1)
                    o.needs_inc = True
                elif o.kind == "cc":
                    o.ms = o.ccidx + 1
                    o.needs_inc = True
        def n_sems(count, rot):
            return max(1, -(-count // rot)) if count else 1

        csems = {e: [nc.alloc_semaphore(f"cs_{e}_{i}")
                     for i in range(n_sems(ms_count[e], self.ROT_MS))]
                 for e in ENGINES}
        dsems = {e: [nc.alloc_semaphore(f"ds_{e}_{i}")
                     for i in range(n_sems(self.dma_count[e], self.ROT_DMA))]
                 for e in ENGINES}
        ccsems = [nc.alloc_semaphore(f"ccs_{i}")
                  for i in range(n_sems(self.cc_count, self.ROT_CC))]

        def sem_val(o):
            if o.kind == "c":
                i = (o.ms - 1) // self.ROT_MS
                return csems[o.engine][i], o.ms - i * self.ROT_MS
            if o.kind == "d":
                i = o.didx // self.ROT_DMA
                return dsems[o.engine][i], o.ms - i * self.ROT_DMA * 16
            i = o.ccidx // self.ROT_CC
            return ccsems[i], o.ms - i * self.ROT_CC

        def inc_val(o):
            if o.kind == "c":
                return 1
            if o.kind == "d":
                return 16
            return None

        prog = self
        prog.phase_marks = []
        _scope_ids = []

        with nc.Block() as block:
            for ename in ENGINES:
                def body(eng, _e=ename):
                    known = {}

                    def dsem_of(qeng, b):
                        i = (b - 1) // prog.ROT_DMA
                        return dsems[qeng][i], 16 * (b - i * prog.ROT_DMA)

                    def ccsem_of(b):
                        i = (b - 1) // prog.ROT_CC
                        return ccsems[i], b - i * prog.ROT_CC

                    cur_phase = None
                    for o in prog.ops[_e]:
                        if o.phase != cur_phase:
                            if cur_phase is not None:
                                nc.pop_named_scope(cur_phase)
                            nc.push_named_scope(o.phase)
                            prog.phase_marks.append((nc.next_id(), o.phase))
                            cur_phase = o.phase
                        if o.kind == "d" and o.didx in prog.gates[_e] and o.didx > 0:
                            s, v = dsem_of(_e, o.didx)
                            if known.get(s, 0) < v:
                                eng.wait_ge(s, v)
                                known[s] = v
                        if o.kind == "cc" and o.ccidx in prog.cc_gates and o.ccidx > 0:
                            s, v = ccsem_of(o.ccidx)
                            if known.get(s, 0) < v:
                                eng.wait_ge(s, v)
                                known[s] = v
                        need = {}
                        for d, ov in o.deps.items():
                            if ov is None:
                                s, v = sem_val(d)
                            elif ov[0] == "d":
                                s, v = dsem_of(ov[1], ov[2])
                            else:
                                s, v = ccsem_of(ov[2])
                            if need.get(s, 0) < v:
                                need[s] = v
                        for s, v in sorted(need.items(), key=lambda kv: kv[0].num):
                            if known.get(s, 0) < v:
                                eng.wait_ge(s, v)
                                known[s] = v
                        inst = o.emit(eng)
                        if o.needs_inc:
                            if o.kind == "cc":
                                inst.then_inc(sem_val(o)[0])
                            else:
                                inst.then_inc(sem_val(o)[0], inc_val(o))
                    if cur_phase is not None:
                        nc.pop_named_scope(cur_phase)
                    for i, s in enumerate(dsems[_e]):
                        tot = prog.dma_count[_e] - i * prog.ROT_DMA
                        if tot > 0:
                            eng.wait_ge(s, 16 * min(tot, prog.ROT_DMA))
                    if _e == "gpsimd":
                        for i, s in enumerate(ccsems):
                            tot = prog.cc_count - i * prog.ROT_CC
                            if tot > 0:
                                eng.wait_ge(s, min(tot, prog.ROT_CC))
                getattr(block, ename)(body)
        return csems, dsems, ccsems




def ceil_div(a, b):
    return -(-a // b)


NT = 2048
HALO_M = 3
NRES = NT + HALO_M
NEXT_A = 2239
RES_OFF = 188
D = 1024
KT = D // 128

EPS = 1e-5


class ArenaTile:

    def __init__(self, core, byte_off, nelems, dtype, eb, buf):
        self.core = core
        self.off = byte_off
        self.n = nelems
        self.dtype = dtype
        self.eb = eb
        self.buf = buf
        self.dead = False

    def tv(self, e0=0, e1=None, p0=0, p1=128):
        assert not self.dead, "use of freed arena tile"
        e1 = self.n if e1 is None else e1
        a = self.core.arena
        b0 = self.off + e0 * self.eb
        b1 = self.off + e1 * self.eb
        ap = a.handle.ap()[p0:p1, b0:b1].bitcast(self.dtype)
        return TV(self.buf, ap, p0, p1, b0, b1)

    def sub(self, base):
        return _SubTile(self, base)


class _SubTile:
    def __init__(self, parent, base):
        self.parent = parent
        self.base = base

    def tv(self, e0, e1, p0=0, p1=128):
        return self.parent.tv(self.base + e0, self.base + e1, p0, p1)


class Core:
    def __init__(self, nc, prog, arena_kb=None, nt=NT):
        self.nc = nc
        self.p = prog
        self.NT = nt
        self.NRES = nt + HALO_M
        self.NEXT = nt + 191
        self.psum_all = prog.psum("psum_all", (128, 4096), F32)
        self.psum_all.whole = False
        self.resid = [prog.sbuf(f"res{k}", [128, self.NRES], F32) for k in range(KT)]
        self.ident = prog.sbuf("ident", [128, 128], BF16)
        self.ones_f32 = prog.sbuf("ones_f32", [128, 1], F32)
        self.ones_bf = prog.sbuf("ones_bf", [128, 1], BF16)
        self.ones_row_bf = prog.sbuf("ones_row_bf", [1, 128], BF16)
        self.ln_sq = [prog.sbuf(f"ln_sq{i}", [128, 512], F32) for i in range(2)]
        self.ln_xb = [prog.sbuf(f"ln_xb{i}", [128, 512], BF16) for i in range(2)]
        self.ln_sqb = [prog.sbuf(f"ln_sqb{i}", [128, 512], BF16) for i in range(2)]
        self.ln_stats = prog.sbuf("ln_stats", [1, 2048], F32)
        self.ln_bc16 = prog.sbuf("ln_bc16", [1, 1024], BF16)
        avail = (nc.sbuf_bytes_remaining - 512) // 32 * 32
        self.ARENA_BYTES = min(arena_kb * 1024, avail) if arena_kb else avail
        self.arena = prog.sbuf("arena", [128, self.ARENA_BYTES], mybir.dt.uint8)
        self._arena_off = 0
        self._arena_marks = []
        self._atiles = []
        self._atile_id = 0

    def init_consts(self):
        c = self
        c.vop("memset", c.ones_f32.tv(), 1.0)
        c.vop("memset", c.ones_bf.tv(), 1.0)
        c.vop("memset", c.ones_row_bf.tv(ps=(0, 1)), 1.0)
        c.gop("memset", c.ident.tv(), 1.0)
        c.gop("affine_select", c.ident.tv(), c.ident.tv(), [[1, 128]], OP.is_ge,
              0.0, base=0, channel_multiplier=-1)
        c.gop("affine_select", c.ident.tv(), c.ident.tv(), [[-1, 128]], OP.is_ge,
              0.0, base=0, channel_multiplier=1)

    def amark(self):
        self._arena_marks.append(self._arena_off)

    def areset(self):
        base = self._arena_marks.pop()
        for t in self._atiles:
            if t.off >= base:
                t.dead = True
        self._arena_off = base

    def aalloc(self, nbytes, align=32):
        off = -(-self._arena_off // align) * align
        self._arena_off = off + nbytes
        assert self._arena_off <= self.ARENA_BYTES, (
            f"arena overflow: {self._arena_off} > {self.ARENA_BYTES}")
        return off

    def atile(self, nelems, dtype):
        eb = {F32: 4, BF16: 2, I32: 4, mybir.dt.int8: 1}[dtype]
        off = self.aalloc(nelems * eb)
        b0, b1 = off, off + nelems * eb
        buf = Buf(f"at{self._atile_id}", self.arena.handle, "sbuf", dtype)
        self._atile_id += 1
        for t in self._atiles:
            if t.off < b1 and b0 < (t.off + t.n * t.eb):
                latest = {}
                for acc in t.buf.writes + t.buf.reads:
                    o = acc[0]
                    key = o.engine
                    prev = latest.get(key)
                    if prev is None or o.eidx > prev.eidx:
                        latest[key] = o
                for o in latest.values():
                    buf.writes.append((o, 0, 128, b0, b1))
        tile = ArenaTile(self, off, nelems, dtype, eb, buf)
        self._atiles.append(tile)
        return tile

    def vop(self, fname, out, *ins, extra_reads=(), **kw):
        return self._eop("vector", fname, out, ins, extra_reads, kw)

    def sop(self, fname, out, *ins, extra_reads=(), **kw):
        return self._eop("scalar", fname, out, ins, extra_reads, kw)

    def gop(self, fname, out, *ins, extra_reads=(), **kw):
        return self._eop("gpsimd", fname, out, ins, extra_reads, kw)

    def _eop(self, engine, fname, out, ins, extra_reads, kw):
        in_aps = [(x.ap if isinstance(x, TV) else x) for x in ins]
        out_ap = out.ap if isinstance(out, TV) else out
        kw2 = {k: (v.ap if isinstance(v, TV) else v) for k, v in kw.items()}

        def emit(eng, _f=fname, _o=out_ap, _i=in_aps, _k=kw2):
            return getattr(eng, _f)(_o, *_i, **_k)

        reads = [x for x in ins if isinstance(x, TV)] + \
                [v for k2, v in kw.items()
                 if isinstance(v, TV) and k2 != "accum_out"] + list(extra_reads)
        writes = ([out] if isinstance(out, TV) else []) + \
                 [v for k2, v in kw.items()
                  if isinstance(v, TV) and k2 == "accum_out"]
        return self.p.op(engine, emit, reads=reads, writes=writes)

    def mm(self, out, lhsT, rhs, start, stop):
        oap = out.ap if isinstance(out, TV) else out
        lap = lhsT.ap if isinstance(lhsT, TV) else lhsT
        rap = rhs.ap if isinstance(rhs, TV) else rhs

        def emit(eng, _o=oap, _l=lap, _r=rap, _s=start, _t=stop):
            return eng.matmul(_o, _l, _r, start=_s, stop=_t, skip_group_check=True)

        reads = [x for x in (lhsT, rhs) if isinstance(x, TV)]
        writes = [out] if isinstance(out, TV) else []
        return self.p.op("tensor", emit, reads=reads, writes=writes)

    def transpose(self, out, in_):
        oap = out.ap if isinstance(out, TV) else out
        iap = in_.ap if isinstance(in_, TV) else in_
        np_ = (in_.p1 - in_.p0) if isinstance(in_, TV) else 128
        ident = self.ident.tv(ps=(0, np_), fs=(0, np_))

        def emit(eng, _o=oap, _i=iap, _id=ident.ap):
            return eng.transpose(_o, _i, _id)

        reads = [x for x in (in_,) if isinstance(x, TV)] + [ident]
        writes = [out] if isinstance(out, TV) else []
        return self.p.op("tensor", emit, reads=reads, writes=writes)

    def pbtv(self, i, p0=0, p1=128, e0=0, e1=512):
        b = self.psum_all
        ap = b.handle.ap()[p0:p1, i * 512 + e0:i * 512 + e1]
        lo = ((i * 512 + e0) * 4 // 2048) * 2048
        hi = -(-((i * 512 + e1) * 4) // 2048) * 2048
        return TV(b, ap, 0, 128, lo, hi)

    def pbt16(self, i, p0=0, p1=128, e0=0, e1=1024):
        b = self.psum_all
        ap = b.handle.ap().bitcast(BF16)[p0:p1, i * 1024 + e0:i * 1024 + e1]
        lo = ((i * 1024 + e0) * 2 // 2048) * 2048
        hi = -(-((i * 1024 + e1) * 2) // 2048) * 2048
        return TV(b, ap, 0, 128, lo, hi)

    def pbw(self, i, nelem, p0=0, p1=128):
        b = self.psum_all
        ap = b.handle.ap()[p0:p1, i * 512:i * 512 + nelem]
        lo = i * 2048
        hi = -(-((i * 512 + nelem) * 4) // 2048) * 2048
        return TV(b, ap, 0, 128, lo, hi)



def emit_ln(core, c0, c1, out_tiles, o_off=0, src_tiles=None, pb0=6, pb1=7):
    c = core
    n = c1 - c0
    src = src_tiles if src_tiles is not None else c.resid
    for st0 in range(0, n, 512):
        w = min(512, n - st0)
        pm = c.pbtv(pb0, 0, 1, 0, w)
        pv = c.pbtv(pb1, 0, 1, 0, w)
        for k in range(KT):
            x = src[k].tv(fs=(c0 + st0, c0 + st0 + w)) if isinstance(src[k], Buf) \
                else src[k].tv(c0 + st0, c0 + st0 + w)
            xb = c.ln_xb[k % 2].tv(fs=(0, w))
            sq = c.ln_sqb[k % 2].tv(fs=(0, w))
            c.sop("activation", xb, x, func=AF.Identity)
            c.sop("activation", sq, x, func=AF.Square)
            c.mm(pm, c.ones_bf.tv(), xb, start=(k == 0), stop=(k == KT - 1))
            c.mm(pv, c.ones_bf.tv(), sq, start=(k == 0), stop=(k == KT - 1))
        st = c.ln_stats
        m = st.tv(ps=(0, 1), fs=(0, w))
        v = st.tv(ps=(0, 1), fs=(512, 512 + w))
        r = st.tv(ps=(0, 1), fs=(1024, 1024 + w))
        t = st.tv(ps=(0, 1), fs=(1536, 1536 + w))
        mb = c.ln_bc16.tv(ps=(0, 1), fs=(0, w))
        rb = c.ln_bc16.tv(ps=(0, 1), fs=(512, 512 + w))
        c.vop("tensor_scalar", m, pm, 1.0 / D, None, op0=OP.mult)
        c.vop("tensor_scalar", v, pv, 1.0 / D, None, op0=OP.mult)
        c.vop("tensor_mul", t, m, m)
        c.vop("tensor_sub", v, v, t)
        c.vop("tensor_scalar", t, v, float(EPS), None, op0=OP.add)
        c.sop("activation", v, t, func=AF.Ln)
        c.sop("activation", r, v, func=AF.Exp, scale=-0.5)
        c.vop("tensor_copy", mb, m)
        c.vop("tensor_copy", rb, r)
        bm = c.pbtv(pb0, 0, 128, 0, w)
        bv = c.pbtv(pb1, 0, 128, 0, w)
        c.mm(bm, c.ones_row_bf.tv(ps=(0, 1)), mb, start=True, stop=True)
        c.mm(bv, c.ones_row_bf.tv(ps=(0, 1)), rb, start=True, stop=True)
        for k in range(KT):
            x = src[k].tv(fs=(c0 + st0, c0 + st0 + w)) if isinstance(src[k], Buf) \
                else src[k].tv(c0 + st0, c0 + st0 + w)
            tmp = c.ln_sq[k % 2].tv(fs=(0, w))
            c.vop("tensor_sub", tmp, x, bm)
            c.vop("tensor_mul", out_tiles[k].tv(o_off + st0, o_off + st0 + w), tmp, bv)



def _wslab_dma(core, wb, dramT, m0, nelem, parts=None):
    parts = parts if parts is not None else dramT.handle.shape[1]
    src = TV(dramT, dramT.handle.ap()[m0], 0, parts, 0, 1 << 30)
    core.p.dma("gpsimd", wb.tv(0, nelem, p0=0, p1=parts), src)


def tile_kxm(wT, kt):
    K, M = wT.shape
    mw = min(M, 128)
    nmo = M // mw
    return np.ascontiguousarray(
        wT.reshape(kt, 128, nmo, mw).transpose(2, 1, 0, 3).reshape(nmo, 128, kt * mw))


def _wslab_dma2(core, wb, dramT, m0, nelem, nblk=2):
    src_ap = dramT.handle.ap()[m0:m0 + nblk].transpose([1, 0, 2])
    src = TV(dramT, src_ap, 0, 128, 0, 1 << 30)
    dst = wb.tv(0, nblk * nelem).with_ap(
        wb.tv(0, nblk * nelem).ap.rearrange("p (b n) -> p b n", b=nblk))
    core.p.dma("gpsimd", dst, src)


def _stream_blocks(core, nblk, dma_fn, compute_fn):
    dma_fn(0, 0)
    for m in range(nblk):
        if m + 1 < nblk:
            dma_fn(m + 1, (m + 1) % 2)
        compute_fn(m, m % 2)


def emit_ffn0(core, W, c0, c1, src_tiles=None):
    c = core
    sc = W["sc"]

    def scl(col):
        return sc.tv(fs=(col, col + 1))

    t0 = c0
    while t0 < c1:
        w = min(512, c1 - t0)
        c.amark()
        lnt = c.atile(KT * w, BF16)
        ln = [lnt.sub(k * w) for k in range(KT)]
        hA = c.atile(32 * w, BF16)
        hB = c.atile(32 * w, BF16)
        wsm = [c.atile(1024, BF16) for _ in range(4)]
        wbg = [c.atile(8192, BF16) for _ in range(2)]
        wi8 = [c.atile(8192, mybir.dt.int8)] * 2

        emit_ln(c, t0, t0 + w, ln)

        def dma1(m, par):
            _wslab_dma(c, wsm[par], W["w1T"], m, 1024)
            _wslab_dma(c, wsm[2 + par], W["w2T"], m, 1024)

        def comp1(m, par):
            pa = c.pbtv(m % 4, 0, 128, 0, w)
            pb = c.pbtv(4 + m % 4, 0, 128, 0, w)
            for k in range(KT):
                lt = wsm[par].tv(k * 128, (k + 1) * 128)
                c.mm(pa, lt, ln[k].tv(0, w), start=(k == 0), stop=(k == KT - 1))
            for k in range(KT):
                lt = wsm[2 + par].tv(k * 128, (k + 1) * 128)
                c.mm(pb, lt, ln[k].tv(0, w), start=(k == 0), stop=(k == KT - 1))
            hm = hA.tv(m * w, (m + 1) * w)
            c.sop("activation", hm, pa, func=AF.Silu, scale=scl(m))
            c.vop("scalar_tensor_tensor", hm, pb, scl(32 + m), hm,
                  op0=OP.mult, op1=OP.mult)

        _stream_blocks(c, 32, dma1, comp1)

        def big_phase(dramT, src_h, dst_fn, pbank):
            def dmaf(p, par):
                if p % 2 == 0:
                    _wslab_dma2(c, wbg[par], dramT, 2 * p, 4096)
                else:
                    src_ap = dramT.handle.ap()[2 * p:2 * p + 2].transpose([1, 0, 2])
                    srctv = TV(dramT, src_ap, 0, 128, 0, 1 << 30)
                    dst = wi8[par].tv(0, 8192).with_ap(
                        wi8[par].tv(0, 8192).ap.rearrange("q (b n) -> q b n", b=2))
                    c.p.dma("sync", dst, srctv)
                    c.sop("activation", wbg[par].tv(0, 8192), wi8[par].tv(0, 8192),
                          func=AF.Identity)

            def compf(p, par):
                for i in range(2):
                    m = 2 * p + i
                    ps = c.pbtv(pbank + m % 4, 0, 128, 0, w)
                    for k in range(32):
                        lt = wbg[par].tv(i * 4096 + k * 128, i * 4096 + (k + 1) * 128)
                        c.mm(ps, lt, src_h.tv(k * w, (k + 1) * w),
                             start=(k == 0), stop=(k == 31))
                    dst_fn(m, ps)
            return dmaf, compf

        def dst2(m, ps):
            c.sop("activation", hB.tv(m * w, (m + 1) * w), ps, func=AF.Gelu,
                  scale=scl(64 + m))
        dmaf, compf = big_phase(W["h1T"], hA, dst2, 4)
        _stream_blocks(c, 16, dmaf, compf)

        def dst3(m, ps):
            c.sop("activation", hA.tv(m * w, (m + 1) * w), ps, func=AF.Gelu,
                  scale=scl(96 + m))
        dmaf, compf = big_phase(W["h2T"], hB, dst3, 0)
        _stream_blocks(c, 16, dmaf, compf)

        def dst4(m, ps):
            rs = c.resid[m].tv(fs=(t0, t0 + w))
            c.vop("scalar_tensor_tensor", rs, ps, scl(128 + m), rs,
                  op0=OP.mult, op1=OP.add, extra_reads=[rs])
        dmaf, compf = big_phase(W["w3T"], hA, dst4, 4)
        _stream_blocks(c, 4, dmaf, compf)

        c.areset()
        t0 += w


def prep_ffn0_host(inputs):
    w1 = inputs["L0_ffn_w1_w"]; w2 = inputs["L0_ffn_w2_w"]
    h1 = inputs["L0_ffn_h1_w"]; h2 = inputs["L0_ffn_h2_w"]
    w3 = inputs["L0_ffn_w3_w"]
    out = {
        "ffn0_w1T": tile_kxm(w1.T.astype(np.int8), 8),
        "ffn0_w2T": tile_kxm(w2.T.astype(np.int8), 8),
        "ffn0_h1T": tile_kxm(h1.T.astype(np.int8), 32),
        "ffn0_h2T": tile_kxm(h2.T.astype(np.int8), 32),
        "ffn0_w3T": tile_kxm(w3.T.astype(np.int8), 32),
    }
    assert np.all(w1.T.astype(np.int8).astype(np.float32) == w1.T)
    sc = np.zeros((128, 136), np.float32)
    sc[:, 0:32] = inputs["L0_ffn_w1_s"].reshape(32, 128).T
    sc[:, 32:64] = inputs["L0_ffn_w2_s"].reshape(32, 128).T
    sc[:, 64:96] = inputs["L0_ffn_h1_s"].reshape(32, 128).T
    sc[:, 96:128] = inputs["L0_ffn_h2_s"].reshape(32, 128).T
    sc[:, 128:136] = inputs["L0_ffn_w3_s"].reshape(8, 128).T
    out["ffn0_sc"] = sc
    return out


def declare_ffn0(prog):
    W = {}
    W["w1T"] = prog.dram("ffn0_w1T", [32, 128, 1024], mybir.dt.int8, kind="ExternalInput")
    W["w2T"] = prog.dram("ffn0_w2T", [32, 128, 1024], mybir.dt.int8, kind="ExternalInput")
    W["h1T"] = prog.dram("ffn0_h1T", [32, 128, 4096], mybir.dt.int8, kind="ExternalInput")
    W["h2T"] = prog.dram("ffn0_h2T", [32, 128, 4096], mybir.dt.int8, kind="ExternalInput")
    W["w3T"] = prog.dram("ffn0_w3T", [8, 128, 4096], mybir.dt.int8, kind="ExternalInput")
    W["sc_dram"] = prog.dram("ffn0_sc", [128, 136], F32, kind="ExternalInput")
    W["sc"] = prog.sbuf("ffn0_sc_sb", [128, 136], F32)
    return W


def load_ffn0_consts(core, W):
    core.p.dma("sync", W["sc"].tv(), W["sc_dram"].tv())



DEQ = np.float32(1.0) / np.float32(127.0)


def emit_ffn1(core, W, c0, c1):
    c = core
    S = float(DEQ)
    t0 = c0
    while t0 < c1:
        w = min(512, c1 - t0)
        c.amark()
        lnt = c.atile(KT * w, BF16)
        ln = [lnt.sub(k * w) for k in range(KT)]
        hA = c.atile(24 * w, BF16)
        wsm = [c.atile(2048, BF16) for _ in range(4)]
        wbg = [c.atile(6144, BF16) for _ in range(2)]

        emit_ln(c, t0, t0 + w, ln)

        def dma1(p, par):
            _wslab_dma2(c, wsm[par], W["wgT"], 2 * p, 1024)
            _wslab_dma2(c, wsm[2 + par], W["wuT"], 2 * p, 1024)

        def comp1(p, par):
            for i in range(2):
                m = 2 * p + i
                pa = c.pbtv(m % 4, 0, 128, 0, w)
                pb = c.pbtv(4 + m % 4, 0, 128, 0, w)
                for k in range(KT):
                    c.mm(pa, wsm[par].tv(i * 1024 + k * 128, i * 1024 + (k + 1) * 128),
                         ln[k].tv(0, w), start=(k == 0), stop=(k == KT - 1))
                for k in range(KT):
                    c.mm(pb, wsm[2 + par].tv(i * 1024 + k * 128, i * 1024 + (k + 1) * 128),
                         ln[k].tv(0, w), start=(k == 0), stop=(k == KT - 1))
                hm = hA.tv(m * w, (m + 1) * w)
                c.sop("activation", hm, pa, func=AF.Silu, scale=S)
                c.vop("scalar_tensor_tensor", hm, pb, S, hm, op0=OP.mult, op1=OP.mult)

        _stream_blocks(c, 12, dma1, comp1)

        def dma2(p, par):
            _wslab_dma2(c, wbg[par], W["wdT"], 2 * p, 3072)

        def comp2(p, par):
            for i in range(2):
                m = 2 * p + i
                ps = c.pbtv(4 + m % 4, 0, 128, 0, w)
                for k in range(24):
                    c.mm(ps, wbg[par].tv(i * 3072 + k * 128, i * 3072 + (k + 1) * 128),
                         hA.tv(k * w, (k + 1) * w), start=(k == 0), stop=(k == 23))
                rs = c.resid[m].tv(fs=(t0, t0 + w))
                c.vop("scalar_tensor_tensor", rs, ps, S, rs,
                      op0=OP.mult, op1=OP.add, extra_reads=[rs])

        _stream_blocks(c, 4, dma2, comp2)

        c.areset()
        t0 += w


def prep_ffn1_host(inputs):
    wg = inputs["L1_ffn_w_gate"]; wu = inputs["L1_ffn_w_up"]; wd = inputs["L1_ffn_w_down"]
    return {
        "ffn1_wgT": tile_kxm(wg.T.astype(np.int8), 8),
        "ffn1_wuT": tile_kxm(wu.T.astype(np.int8), 8),
        "ffn1_wdT": tile_kxm(wd.T.astype(np.int8), 24),
    }


def declare_ffn1(prog):
    I8 = mybir.dt.int8
    return {
        "wgT": prog.dram("ffn1_wgT", [24, 128, 1024], I8, kind="ExternalInput"),
        "wuT": prog.dram("ffn1_wuT", [24, 128, 1024], I8, kind="ExternalInput"),
        "wdT": prog.dram("ffn1_wdT", [8, 128, 3072], I8, kind="ExternalInput"),
    }



def emit_attn1a(core, W, c0, c1, rg):
    c = core
    T = c1 - c0
    nst = ceil_div(T, 512)
    ntt = ceil_div(T, 128)
    c.amark()
    lnt = c.atile(KT * T, BF16)
    ln = [lnt.sub(k * T) for k in range(KT)]
    emit_ln(c, c0, c1, ln)
    qfm = c.atile(8 * T, BF16)
    kvin = c.atile(1024, F32)
    bias = W["bias_sb"]

    def bcol(i):
        return bias.tv(fs=(i, i + 1))

    slabs = [c.atile(1024, BF16) for _ in range(6)]

    def dmah(h, par):
        _wslab_dma(c, slabs[3 * par + 0], W["qkv"], h, 1024)
        _wslab_dma(c, slabs[3 * par + 1], W["qkv"], 8 + h, 1024)
        _wslab_dma(c, slabs[3 * par + 2], W["qkv"], 16 + h, 1024)

    c.amark()
    kfm = c.atile(T, BF16)
    vfm = c.atile(T, BF16)
    ktok = c.atile(T, BF16)
    vtok = c.atile(T, BF16)

    def comph(h, par):
        qs, ks, vs = slabs[3 * par], slabs[3 * par + 1], slabs[3 * par + 2]
        for st in range(nst):
            e0, e1 = st * 512, min(T, (st + 1) * 512)
            w = e1 - e0
            ps = c.pbtv(st % 2, 0, 128, 0, w)
            for k in range(KT):
                c.mm(ps, qs.tv(k * 128, (k + 1) * 128), ln[k].tv(e0, e1),
                     start=(k == 0), stop=(k == KT - 1))
            c.sop("activation", qfm.tv(h * T + e0, h * T + e1), ps,
                  func=AF.Identity, bias=bcol(h))
            ps2 = c.pbtv(2 + st % 2, 0, 128, 0, w)
            for k in range(KT):
                c.mm(ps2, ks.tv(k * 128, (k + 1) * 128), ln[k].tv(e0, e1),
                     start=(k == 0), stop=(k == KT - 1))
            c.sop("activation", kfm.tv(e0, e1), ps2, func=AF.Identity, bias=bcol(8 + h))
            ps3 = c.pbtv(4 + st % 2, 0, 128, 0, w)
            for k in range(KT):
                c.mm(ps3, vs.tv(k * 128, (k + 1) * 128), ln[k].tv(e0, e1),
                     start=(k == 0), stop=(k == KT - 1))
            c.sop("activation", vfm.tv(e0, e1), ps3, func=AF.Identity, bias=bcol(16 + h))
        for tt in range(ntt):
            e0, e1 = tt * 128, min(T, (tt + 1) * 128)
            w = e1 - e0
            pt = c.pbt16(6, 0, w, 0, 128)
            c.transpose(pt, kfm.tv(e0, e1))
            c.vop("tensor_copy", ktok.tv(tt * 128, tt * 128 + 128, p0=0, p1=w), pt)
            pt2 = c.pbt16(7, 0, w, 0, 128)
            c.transpose(pt2, vfm.tv(e0, e1))
            c.vop("tensor_copy", vtok.tv(tt * 128, tt * 128 + 128, p0=0, p1=w), pt2)
        pkv = c.pbtv(0, 0, 128, 0, 128)
        for tt in range(ntt):
            e0, e1 = tt * 128, min(T, (tt + 1) * 128)
            w = e1 - e0
            c.mm(pkv, ktok.tv(tt * 128, tt * 128 + 128, p0=0, p1=w),
                 vtok.tv(tt * 128, tt * 128 + 128, p0=0, p1=w),
                 start=(tt == 0), stop=(tt == ntt - 1))
        c.vop("tensor_copy", kvin.tv(h * 128, (h + 1) * 128), pkv)

    _stream_blocks(c, 8, dmah, comph)
    c.areset()

    c.p.dma("sync", W["cc_in"].tv(), kvin.tv())
    c.p.cc("AllReduce", OP.add, rg, W["cc_in"].tv(), W["cc_out"].tv())
    kvb = c.atile(1024, BF16)
    c.p.dma("gpsimd", kvb.tv(), W["cc_out"].tv())

    afm = c.atile(8 * T, BF16)
    for h in range(8):
        for st in range(nst):
            e0, e1 = st * 512, min(T, (st + 1) * 512)
            w = e1 - e0
            ps = c.pbtv((h + st) % 4, 0, 128, 0, w)
            c.mm(ps, kvb.tv(h * 128, (h + 1) * 128), qfm.tv(h * T + e0, h * T + e1),
                 start=True, stop=True)
            c.vop("tensor_copy", afm.tv(h * T + e0, h * T + e1), ps)

    wo = [c.atile(1024, BF16) for _ in range(2)]
    tmp = c.atile(512, F32)

    def dmao(mo, par):
        _wslab_dma(c, wo[par], W["woT"], mo, 1024)

    def compo(mo, par):
        for st in range(nst):
            e0, e1 = st * 512, min(T, (st + 1) * 512)
            w = e1 - e0
            ps = c.pbtv(4 + (mo + st) % 2, 0, 128, 0, w)
            for j in range(8):
                c.mm(ps, wo[par].tv(j * 128, (j + 1) * 128), afm.tv(j * T + e0, j * T + e1),
                     start=(j == 0), stop=(j == 7))
            c.sop("activation", tmp.tv(0, w), ps, func=AF.Identity, bias=bcol(24 + mo))
            rs = c.resid[mo].tv(fs=(c0 + e0, c0 + e1))
            c.vop("tensor_add", rs, rs, tmp.tv(0, w), extra_reads=[rs])

    _stream_blocks(c, 8, dmao, compo)
    c.areset()


def prep_attn1a_host(inputs):
    wqkv = inputs["L1_attn_a_w_qkv"]
    bqkv = inputs["L1_attn_a_b_qkv"]
    wo = inputs["L1_attn_a_w_o"]
    bo = inputs["L1_attn_a_b_o"]
    idx = 8 * np.arange(128)
    slabs = []
    for part in range(3):
        for h in range(8):
            Wh = wqkv[part * 1024 + idx + h, :]
            slabs.append(tile_kxm(np.ascontiguousarray(Wh.T), 8)[0])
    qkv = np.ascontiguousarray(np.stack(slabs).astype(np.float32))
    woT = tile_kxm(np.ascontiguousarray(wo.T), 8).astype(np.float32)
    bias = np.zeros((128, 32), np.float32)
    for part in range(3):
        for h in range(8):
            bias[:, part * 8 + h] = bqkv[part * 1024 + idx + h]
    bias[:, 24:32] = bo.reshape(8, 128).T
    return {"a1_qkv": qkv, "a1_woT": woT, "a1_bias": bias}


def declare_attn1a(prog):
    W = {
        "qkv": prog.dram("a1_qkv", [24, 128, 1024], F32, kind="ExternalInput"),
        "woT": prog.dram("a1_woT", [8, 128, 1024], F32, kind="ExternalInput"),
        "bias_dram": prog.dram("a1_bias", [128, 32], F32, kind="ExternalInput"),
        "bias_sb": prog.sbuf("a1_bias_sb", [128, 32], F32),
        "cc_in": prog.dram("a1_cc_in", [128, 1024], F32),
        "cc_out": prog.dram("a1_cc_out", [128, 1024], F32),
    }
    return W


def load_attn1a_consts(core, W):
    core.p.dma("sync", W["bias_sb"].tv(), W["bias_dram"].tv())



def emit_attn1b(core, W, c0, c1, rg, ranks):
    c = core
    T = c1 - c0
    KW = ranks * T
    nqt = T // 128
    nstr = KW // 512
    c.amark()
    lnt = c.atile(KT * T, BF16)
    ln = [lnt.sub(k * T) for k in range(KT)]
    emit_ln(c, c0, c1, ln)
    bias = W["bias_sb"]

    def bcol(i):
        return bias.tv(fs=(i, i + 1))

    c.vop("tensor_scalar", bias.tv(fs=(3, 4)), bias.tv(fs=(0, 1)), 0.125, None,
          op0=OP.mult)

    qfm = c.atile(T, BF16)
    kfm = c.atile(T, BF16)
    vtok = c.atile((T // 128) * 64, BF16)
    qkv_sb = [c.atile(512, BF16) for _ in range(3)]
    for i in range(3):
        _wslab_dma(c, qkv_sb[i], W["qkvw"], i, 512)
    for st in range(T // 512):
        e0, e1 = st * 512, (st + 1) * 512
        ps = c.pbtv(st % 2, 0, 64, 0, 512)
        for k in range(KT):
            c.mm(ps, qkv_sb[0].tv(k * 64, (k + 1) * 64), ln[k].tv(e0, e1),
                 start=(k == 0), stop=(k == KT - 1))
        c.sop("activation", qfm.tv(e0, e1, p0=0, p1=64), ps, func=AF.Identity,
              bias=bias.tv(ps=(0, 64), fs=(3, 4)), scale=0.125)
        ps2 = c.pbtv(2 + st % 2, 0, 64, 0, 512)
        for k in range(KT):
            c.mm(ps2, qkv_sb[1].tv(k * 64, (k + 1) * 64), ln[k].tv(e0, e1),
                 start=(k == 0), stop=(k == KT - 1))
        c.sop("activation", kfm.tv(e0, e1, p0=0, p1=64), ps2, func=AF.Identity,
              bias=bias.tv(ps=(0, 64), fs=(1, 2)))
    for tt in range(T // 128):
        ps = c.pbtv(4 + tt % 2, 0, 128, 0, 64)
        for k in range(KT):
            c.mm(ps, ln[k].tv(tt * 128, (tt + 1) * 128), qkv_sb[2].tv(k * 64, (k + 1) * 64),
                 start=(k == 0), stop=False)
        c.mm(ps, c.ones_row_bf.tv(ps=(0, 1)).with_ap(
            c.ones_row_bf.handle.ap()[0:1, 0:128]), W["bvrow_sb"].tv(ps=(0, 1)),
            start=False, stop=True)
        c.vop("tensor_copy", vtok.tv(tt * 64, (tt + 1) * 64), ps)
    c.p.dma("sync", W["cck_in"].tv(), kfm.tv(p0=0, p1=64))
    c.p.dma("sync", W["ccv_in"].tv().with_ap(
        W["ccv_in"].handle.ap().rearrange("(tt p) c -> p tt c", p=128)),
        vtok.tv().with_ap(vtok.tv().ap.rearrange("p (tt c) -> p tt c", c=64)))
    c.p.cc("AllGather", OP.bypass, rg, W["cck_in"].tv(), W["cck_out"].tv())
    c.p.cc("AllGather", OP.bypass, rg, W["ccv_in"].tv(), W["ccv_out"].tv())
    kfull = c.atile(KW, BF16)
    vfull = c.atile((KW // 128) * 64, BF16)
    c.p.dma("sync", kfull.tv(p0=0, p1=64).with_ap(
        kfull.tv(p0=0, p1=64).ap.rearrange("c (r t) -> c r t", r=ranks)),
        W["cck_out"].tv().with_ap(
            W["cck_out"].handle.ap().rearrange("(r c) t -> c r t", c=64)))
    c.p.dma("sync", vfull.tv().with_ap(
        vfull.tv().ap.rearrange("p (tt c) -> p tt c", c=64)),
        W["ccv_out"].tv().with_ap(
            W["ccv_out"].handle.ap().rearrange("(tt p) c -> p tt c", p=128)))

    stats = c.atile(3 * nqt + 4, F32)
    attn_tok = c.atile(nqt * 64, BF16)
    maskb = [c.atile(KW, BF16) for _ in range(2)]
    psb = [c.atile(512, BF16) for _ in range(2)]
    ptb = [c.atile(128, BF16) for _ in range(2)]

    for qt in range(nqt):
        run = stats.tv(qt, qt + 1)
        c.vop("memset", run, -3.0e38)
        c.p.dma("sync", maskb[qt % 2].tv(),
                TV(W["mask"], W["mask"].handle.ap()[qt], 0, 128, 0, 1 << 30))
        for st in range(nstr):
            ps = c.pbtv(st % 2, 0, 128, 0, 512)
            c.mm(ps, qfm.tv(qt * 128, (qt + 1) * 128, p0=0, p1=64),
                 kfull.tv(st * 512, (st + 1) * 512, p0=0, p1=64),
                 start=True, stop=False)
            c.mm(ps, c.ident.tv(), maskb[qt % 2].tv(st * 512, (st + 1) * 512),
                 start=False, stop=True)
            sm = stats.tv(3 * nqt + st % 2, 3 * nqt + st % 2 + 1)
            c.vop("tensor_reduce", sm, ps, AX.X, OP.max)
            c.vop("tensor_max", run, run, sm)
        nm = stats.tv(2 * nqt + qt, 2 * nqt + qt + 1)
        c.vop("tensor_scalar", nm, run, -1.0, None, op0=OP.mult)
        ls = stats.tv(nqt + qt, nqt + qt + 1)
        c.vop("memset", ls, 0.0)
        po = c.pbtv(6 + qt % 2, 0, 128, 0, 64)
        first = True
        for st in range(nstr):
            ps = c.pbtv(2 + st % 2, 0, 128, 0, 512)
            c.mm(ps, qfm.tv(qt * 128, (qt + 1) * 128, p0=0, p1=64),
                 kfull.tv(st * 512, (st + 1) * 512, p0=0, p1=64),
                 start=True, stop=False)
            c.mm(ps, c.ident.tv(), maskb[qt % 2].tv(st * 512, (st + 1) * 512),
                 start=False, stop=True)
            tmpc = stats.tv(3 * nqt + 2 + st % 2, 3 * nqt + 2 + st % 2 + 1)
            c.sop("activation", psb[qt % 2].tv(), ps, func=AF.Exp, bias=nm,
                  accum_out=tmpc)
            c.vop("tensor_add", ls, ls, tmpc)
            for c2 in range(4):
                pt = c.pbt16(4 + c2 % 2, 0, 128, 0, 128)
                c.transpose(pt, psb[qt % 2].tv(c2 * 128, (c2 + 1) * 128))
                c.vop("tensor_copy", ptb[(st * 4 + c2) % 2].tv(), pt)
                kc = st * 4 + c2
                c.mm(po, ptb[(st * 4 + c2) % 2].tv(), vfull.tv(kc * 64, (kc + 1) * 64),
                     start=first, stop=(st == nstr - 1 and c2 == 3))
                first = False
        rl = stats.tv(3 * nqt + 2 + qt % 2, 3 * nqt + 2 + qt % 2 + 1)
        c.vop("reciprocal", rl, ls)
        c.vop("tensor_scalar", attn_tok.tv(qt * 64, (qt + 1) * 64), po, rl, None,
              op0=OP.mult)
    afm = c.atile(T, BF16)
    for qt in range(nqt):
        pt = c.pbt16(0, 0, 64, 0, 128)
        c.transpose(pt, attn_tok.tv(qt * 64, (qt + 1) * 64))
        c.vop("tensor_copy", afm.tv(qt * 128, (qt + 1) * 128, p0=0, p1=64), pt)
    wo_sb = [c.atile(128, BF16) for _ in range(2)]
    tmp = c.atile(512, F32)

    def dmao(mo, par):
        _wslab_dma(c, wo_sb[par], W["woT"], mo, 128)

    def compo(mo, par):
        for st in range(T // 512):
            e0, e1 = st * 512, (st + 1) * 512
            ps = c.pbtv(2 + (mo + st) % 2, 0, 128, 0, 512)
            c.mm(ps, wo_sb[par].tv(0, 128, p0=0, p1=64),
                 afm.tv(e0, e1, p0=0, p1=64), start=True, stop=True)
            c.sop("activation", tmp.tv(0, 512), ps, func=AF.Identity,
                  bias=bcol(8 + mo))
            rs = c.resid[mo].tv(fs=(c0 + e0, c0 + e1))
            c.vop("tensor_add", rs, rs, tmp.tv(0, 512), extra_reads=[rs])

    _stream_blocks(c, 8, dmao, compo)
    c.areset()


def prep_attn1b_host(inputs, ranks=4, T=NT):
    wq = inputs["L1_attn_b_wq"]; wk = inputs["L1_attn_b_wk"]; wv = inputs["L1_attn_b_wv"]
    wo = inputs["L1_attn_b_wo"]
    qkvw = np.stack([tile_kxm(np.ascontiguousarray(w), 8)[0]
                     for w in (wq, wk, wv)]).astype(np.float32)
    woT = np.stack([np.ascontiguousarray(wo[:, mo * 128:(mo + 1) * 128])
                    for mo in range(8)]).astype(np.float32)
    bias = np.zeros((128, 16), np.float32)
    bias[0:64, 0] = inputs["L1_attn_b_bq"]
    bias[0:64, 1] = inputs["L1_attn_b_bk"]
    bias[0:64, 2] = inputs["L1_attn_b_bv"]
    bias[:, 8:16] = inputs["L1_attn_b_bo"].reshape(8, 128).T
    bvrow = inputs["L1_attn_b_bv"].reshape(1, 64).astype(np.float32)
    out = {"a1b_qkvw": qkvw, "a1b_woT": woT, "a1b_bias": bias, "a1b_bvrow": bvrow}
    import ml_dtypes
    nqt = T // 128
    KW = ranks * T
    kidx = np.arange(KW)[None, None, :]
    for r in range(ranks):
        sbase = r * T
        qidx = sbase + (np.arange(nqt) * 128)[:, None, None] + np.arange(128)[None, :, None]
        m = np.where(kidx > qidx, np.float32(-1e30), np.float32(0.0))
        out[f"a1b_mask_r{r}"] = m.astype(ml_dtypes.bfloat16)
    return out


def declare_attn1b(prog, ranks=4, T=NT):
    nqt = T // 128
    KW = ranks * T
    return {
        "qkvw": prog.dram("a1b_qkvw", [3, 128, 512], F32, kind="ExternalInput"),
        "woT": prog.dram("a1b_woT", [8, 64, 128], F32, kind="ExternalInput"),
        "bias_dram": prog.dram("a1b_bias", [128, 16], F32, kind="ExternalInput"),
        "bias_sb": prog.sbuf("a1b_bias_sb", [128, 16], F32),
        "bvrow_dram": prog.dram("a1b_bvrow", [1, 64], F32, kind="ExternalInput"),
        "bvrow_sb": prog.sbuf("a1b_bvrow_sb", [1, 64], BF16),
        "mask": prog.dram("a1b_mask", [nqt, 128, KW], BF16, kind="ExternalInput"),
        "cck_in": prog.dram("a1b_cck_in", [64, T], BF16),
        "cck_out": prog.dram("a1b_cck_out", [ranks * 64, T], BF16),
        "ccv_in": prog.dram("a1b_ccv_in", [T, 64], BF16),
        "ccv_out": prog.dram("a1b_ccv_out", [ranks * T, 64], BF16),
    }


def load_attn1b_consts(core, W):
    core.p.dma("sync", W["bias_sb"].tv(), W["bias_dram"].tv())
    core.p.dma("gpsimd", W["bvrow_sb"].tv(ps=(0, 1)), W["bvrow_dram"].tv(ps=(0, 1)))



def emit_mamba(core, W, rg, nprev=3):
    c = core
    T = c.NT
    NR = c.NRES
    nst = T // 512
    sel = W["sel_sb"]
    c.amark()

    for k in range(KT):
        rs = c.resid[k].tv(fs=(0, HALO_M))
        c.vop("tensor_scalar", rs, rs, sel.tv(fs=(3, 4)), None, op0=OP.mult,
              extra_reads=[rs])

    c.sop("activation", W["A_sb"].tv(), W["A_dram_sb"].tv(), func=AF.Exp)
    c.vop("tensor_scalar", W["A_sb"].tv(), W["A_sb"].tv(), -1.0, None,
          op0=OP.mult, extra_reads=[W["A_sb"].tv()])

    c.amark()
    lnm = c.atile(KT * NR, BF16)
    ln = [lnm.sub(k * NR) for k in range(KT)]
    emit_ln(c, 0, NR, ln)
    c.amark()
    xcpre = c.atile(NR, F32)
    cacc = c.atile(T, F32)
    wxs = [c.atile(1024, BF16) for _ in range(2)]

    def dmx(m, par):
        _wslab_dma(c, wxs[par], W["wxT"], m, 1024)

    def cpx(m, par):
        for st in range(ceil_div(NR, 512)):
            e0, e1 = st * 512, min(NR, (st + 1) * 512)
            ps = c.pbtv(st % 4, 0, 128, 0, e1 - e0)
            for k in range(KT):
                c.mm(ps, wxs[par].tv(k * 128, (k + 1) * 128), ln[k].tv(e0, e1),
                     start=(k == 0), stop=(k == KT - 1))
            c.vop("tensor_copy", xcpre.tv(e0, e1), ps)
        cw = W["conv_sb"]
        c.vop("tensor_scalar", cacc.tv(), xcpre.tv(0, T),
              cw.tv(fs=(m * 4, m * 4 + 1)), None, op0=OP.mult)
        for kk in range(1, 4):
            c.vop("scalar_tensor_tensor", cacc.tv(), xcpre.tv(kk, T + kk),
                  cw.tv(fs=(m * 4 + kk, m * 4 + kk + 1)), cacc.tv(),
                  op0=OP.mult, op1=OP.add, extra_reads=[cacc.tv()])
        xcm = c.atile(T, BF16)
        c.sop("activation", xcm.tv(), cacc.tv(), func=AF.Silu,
              bias=W["convb_sb"].tv(fs=(m, m + 1)))
        c.p.dma("sync", TV(W["xc_dram"], W["xc_dram"].handle.ap()[m], 0, 128,
                           m * T * 2, (m + 1) * T * 2), xcm.tv())
        return xcm

    dmx(0, 0)
    for m in range(16):
        if m + 1 < 16:
            dmx(m + 1, (m + 1) % 2)
        c.amark()
        cpx(m, m % 2)
        c.areset()
    c.areset()
    c.areset()

    c.amark()
    dlr = c.atile(T, BF16)
    Bc = c.atile(T, BF16)
    Cc = c.atile(T, BF16)
    xps = [c.atile(16 * 64, BF16), c.atile(16 * 16, BF16), c.atile(16 * 16, BF16)]
    _wslab_dma(c, xps[0], W["xpdT"], 0, 16 * 64)
    _wslab_dma(c, xps[1], W["xpbT"], 0, 16 * 16)
    _wslab_dma(c, xps[2], W["xpcT"], 0, 16 * 16)
    xck = c.atile(T, BF16)
    for st in range(nst):
        e0, e1 = st * 512, (st + 1) * 512
        psd = c.pbtv(st % 2, 0, 64, 0, 512)
        psb = c.pbtv(2 + st % 2, 0, 16, 0, 512)
        psc = c.pbtv(4 + st % 2, 0, 16, 0, 512)
        for k in range(16):
            src = TV(W["xc_dram"], W["xc_dram"].handle.ap()[k][:, e0:e1], 0, 128,
                     k * T * 2 + e0 * 2, k * T * 2 + e1 * 2)
            c.p.dma("sync", xck.tv(e0, e1), src)
            c.mm(psd, xps[0].tv(k * 64, (k + 1) * 64), xck.tv(e0, e1),
                 start=(k == 0), stop=(k == 15))
            c.mm(psb, xps[1].tv(k * 16, (k + 1) * 16), xck.tv(e0, e1),
                 start=(k == 0), stop=(k == 15))
            c.mm(psc, xps[2].tv(k * 16, (k + 1) * 16), xck.tv(e0, e1),
                 start=(k == 0), stop=(k == 15))
        c.vop("tensor_copy", dlr.tv(e0, e1, p0=0, p1=64), psd)
        c.vop("tensor_copy", Bc.tv(e0, e1, p0=0, p1=16), psb)
        c.vop("tensor_copy", Cc.tv(e0, e1, p0=0, p1=16), psc)

    dts = c.atile(16 * 128, BF16)
    for mo in range(16):
        src = TV(W["dtT"], W["dtT"].handle.ap()[mo], 0, 64, 0, 1 << 30)
        c.p.dma("gpsimd", dts.tv(mo * 128, (mo + 1) * 128, p0=0, p1=64), src)
    selw = c.atile(16 * 128, BF16)
    c.p.dma("gpsimd", selw.tv(p0=0, p1=16), W["sel16_dram"].tv(ps=(0, 16)))

    hfin = c.atile(256, F32)
    atot = c.atile(256, F32)
    hin = c.atile(256, F32)

    def compute_delta(m, dbuf, scratch, xcbuf=None):
        if xcbuf is not None:
            srcx = TV(W["xc_dram"], W["xc_dram"].handle.ap()[m], 0, 128,
                      m * T * 2, (m + 1) * T * 2)
            c.p.dma("sync", xcbuf.tv(), srcx)
        for st in range(nst):
            e0, e1 = st * 512, (st + 1) * 512
            ps = c.pbtv(6 + st % 2, 0, 128, 0, 512)
            c.mm(ps, dts.tv(m * 128, (m + 1) * 128, p0=0, p1=64),
                 dlr.tv(e0, e1, p0=0, p1=64), start=True, stop=True)
            c.sop("activation", dbuf.tv(e0, e1), ps, func=AF.Identity,
                  bias=W["dtb_sb"].tv(fs=(m, m + 1)))
        dabs = scratch
        c.sop("activation", dabs.tv(), dbuf.tv(), func=AF.Abs)
        c.sop("activation", dabs.tv(), dabs.tv(), func=AF.Exp, scale=-1.0)
        c.sop("activation", dabs.tv(), dabs.tv(), func=AF.Ln, bias=1.0)
        c.vop("tensor_relu", dbuf.tv(), dbuf.tv(), extra_reads=[dbuf.tv()])
        c.vop("tensor_add", dbuf.tv(), dbuf.tv(), dabs.tv(), extra_reads=[dbuf.tv()])
        return dabs

    def bc_bcast(bank0, srcbc, n):
        for st in range(nst):
            ps = c.pbtv(bank0 + st, 0, 128, 0, 512)
            c.mm(ps, selw.tv(n * 128, (n + 1) * 128, p0=0, p1=16),
                 srcbc.tv(st * 512, (st + 1) * 512, p0=0, p1=16),
                 start=True, stop=True)
        return c.pbw(bank0, T)

    def acol(m, n):
        return W["A_sb"].tv(fs=(m * 16 + n, m * 16 + n + 1))

    hfin = c.atile(256, F32)
    atot = c.atile(256, F32)
    hin = c.atile(256, F32)

    for m in range(16):
        c.amark()
        dbuf = c.atile(T, F32)
        scratch = c.atile(T, F32)
        xcb = c.atile(T, BF16)
        dx = compute_delta(m, dbuf, scratch, xcb)
        c.vop("tensor_mul", dx.tv(), dbuf.tv(), xcb.tv())
        a = [c.atile(T, F32) for _ in range(2)]
        b = [c.atile(T, BF16) for _ in range(2)]
        H = [c.atile(T, BF16) for _ in range(2)]
        g2 = [c.atile(T, BF16) for _ in range(2)]
        Bb = [c.atile(T, BF16) for _ in range(2)]
        Cb = [c.atile(T, BF16) for _ in range(2)]
        dxb = c.atile(T, BF16)
        c.vop("tensor_copy", dxb.tv(), dx.tv())
        y = scratch
        cdT = c.atile(1, F32)
        c.vop("tensor_reduce", cdT.tv(), dbuf.tv(), AX.X, OP.add)
        for n in range(16):
            pp = n % 2
            pB = bc_bcast((n % 2) * 4, Bc, n)
            c.sop("activation", Bb[pp].tv(), pB, func=AF.Identity)
            pC = bc_bcast(((n + 1) % 2) * 4, Cc, n)
            c.sop("activation", Cb[pp].tv(), pC, func=AF.Identity)
            c.sop("activation", a[pp].tv(), dbuf.tv(), func=AF.Exp, scale=acol(m, n))
            c.sop("activation", atot.tv(m * 16 + n, m * 16 + n + 1), cdT.tv(),
                  func=AF.Exp, scale=acol(m, n))
            c.vop("tensor_mul", b[pp].tv(), dxb.tv(), Bb[pp].tv())
            c.vop("tensor_tensor_scan", H[pp].tv(), a[pp].tv(), b[pp].tv(), 0.0,
                  op0=OP.mult, op1=OP.add)
            c.vop("tensor_copy", hfin.tv(m * 16 + n, m * 16 + n + 1),
                  H[pp].tv(T - 1, T))
            if n == 0:
                c.vop("tensor_mul", y.tv(), H[pp].tv(), Cb[pp].tv())
            else:
                c.vop("tensor_mul", g2[pp].tv(), H[pp].tv(), Cb[pp].tv())
                c.vop("tensor_add", y.tv(), y.tv(), g2[pp].tv(),
                      extra_reads=[y.tv()])
        c.vop("scalar_tensor_tensor", y.tv(), xcb.tv(), W["D_sb"].tv(fs=(m, m + 1)),
              y.tv(), op0=OP.mult, op1=OP.add, extra_reads=[y.tv()])
        yb = c.atile(T, BF16)
        c.vop("tensor_copy", yb.tv(), y.tv())
        c.p.dma("sync", TV(W["y_dram"], W["y_dram"].handle.ap()[m], 0, 128,
                           m * T * 2, (m + 1) * T * 2), yb.tv())
        c.areset()

    c.p.dma("sync", TV(W["cc_in"], W["cc_in"].handle.ap()[:, 0:256], 0, 128, 0, 1024),
            hfin.tv())
    c.p.dma("sync", TV(W["cc_in"], W["cc_in"].handle.ap()[:, 256:512], 0, 128, 1024, 2048),
            atot.tv())
    c.p.cc("AllGather", OP.bypass, rg, W["cc_in"].tv(), W["cc_out"].tv())
    acc = c.atile(256, F32)
    tmpc = c.atile(512, F32)
    c.vop("memset", acc.tv(), 0.0)
    c.vop("memset", hin.tv(), 0.0)
    for pr in range(nprev):
        blk = TV(W["cc_out"], W["cc_out"].handle.ap()[pr * 128:(pr + 1) * 128, :],
                 0, 128, 0, 1 << 30)
        c.p.dma("sync", tmpc.tv(), blk)
        c.vop("tensor_mul", acc.tv(), acc.tv(), tmpc.tv(256, 512),
              extra_reads=[acc.tv()])
        c.vop("tensor_add", acc.tv(), acc.tv(), tmpc.tv(0, 256),
              extra_reads=[acc.tv()])
        c.vop("tensor_scalar", tmpc.tv(0, 256), acc.tv(), sel.tv(fs=(pr, pr + 1)),
              None, op0=OP.mult)
        c.vop("tensor_add", hin.tv(), hin.tv(), tmpc.tv(0, 256),
              extra_reads=[hin.tv()])

    for m in range(16):
        c.amark()
        dbuf = c.atile(T, F32)
        scratch = c.atile(T, F32)
        compute_delta(m, dbuf, scratch)
        cd = c.atile(T, F32)
        ones_bc = c.ones_f32.tv().with_ap(
            c.ones_f32.tv().ap.broadcast_to((128, T)))
        c.vop("tensor_tensor_scan", cd.tv(), ones_bc, dbuf.tv(), 0.0,
              op0=OP.mult, op1=OP.add)
        P = [c.atile(T, BF16) for _ in range(2)]
        Cb2 = [c.atile(T, BF16) for _ in range(2)]
        acc2 = c.atile(T, BF16)
        yb = c.atile(T, BF16)
        c.p.dma("scalar", yb.tv(), TV(W["y_dram"], W["y_dram"].handle.ap()[m],
                                      0, 128, m * T * 2, (m + 1) * T * 2))
        for n in range(16):
            pp = n % 2
            pC = bc_bcast((n % 2) * 4, Cc, n)
            c.sop("activation", Cb2[pp].tv(), pC, func=AF.Identity)
            c.sop("activation", P[pp].tv(), cd.tv(), func=AF.Exp, scale=acol(m, n))
            c.vop("tensor_mul", P[pp].tv(), P[pp].tv(), Cb2[pp].tv(),
                  extra_reads=[P[pp].tv()])
            if n == 0:
                c.vop("tensor_scalar", acc2.tv(), P[pp].tv(),
                      hin.tv(m * 16 + n, m * 16 + n + 1), None, op0=OP.mult)
            else:
                c.vop("scalar_tensor_tensor", acc2.tv(), P[pp].tv(),
                      hin.tv(m * 16 + n, m * 16 + n + 1), acc2.tv(),
                      op0=OP.mult, op1=OP.add, extra_reads=[acc2.tv()])
        c.vop("tensor_add", yb.tv(), acc2.tv(), yb.tv(), extra_reads=[yb.tv()])
        c.p.dma("sync", TV(W["y_dram"], W["y_dram"].handle.ap()[m], 0, 128,
                           m * T * 2, (m + 1) * T * 2), yb.tv())
        c.areset()

    c.amark()
    ln2 = c.atile(KT * 512, BF16)
    g = c.atile(16 * 512, BF16)
    sz = c.atile(512, BF16)
    yb2 = c.atile(512, BF16)
    wxs2 = [c.atile(1024, BF16) for _ in range(2)]
    wos = [c.atile(2048, BF16) for _ in range(2)]
    for st in range(nst):
        e0 = HALO_M + st * 512
        lns = [ln2.sub(k * 512) for k in range(KT)]
        emit_ln(c, e0, e0 + 512, lns)
        _wslab_dma(c, wxs2[0], W["wxT"], 16, 1024)
        for mz in range(16):
            if mz + 1 < 16:
                _wslab_dma(c, wxs2[(mz + 1) % 2], W["wxT"], 16 + mz + 1, 1024)
            ps = c.pbtv(mz % 4, 0, 128, 0, 512)
            for k in range(KT):
                c.mm(ps, wxs2[mz % 2].tv(k * 128, (k + 1) * 128), lns[k].tv(0, 512),
                     start=(k == 0), stop=(k == KT - 1))
            c.sop("activation", sz.tv(), ps, func=AF.Silu)
            src = TV(W["y_dram"], W["y_dram"].handle.ap()[mz][:, st * 512:(st + 1) * 512],
                     0, 128, st * 1024, (st + 1) * 1024)
            c.p.dma("scalar", yb2.tv(), src)
            c.vop("tensor_mul", g.tv(mz * 512, (mz + 1) * 512), yb2.tv(), sz.tv())
        _wslab_dma(c, wos[0], W["outT"], 0, 2048)
        for mo in range(8):
            if mo + 1 < 8:
                _wslab_dma(c, wos[(mo + 1) % 2], W["outT"], mo + 1, 2048)
            ps = c.pbtv(4 + mo % 2, 0, 128, 0, 512)
            for k in range(16):
                c.mm(ps, wos[mo % 2].tv(k * 128, (k + 1) * 128),
                     g.tv(k * 512, (k + 1) * 512), start=(k == 0), stop=(k == 15))
            rs = c.resid[mo].tv(fs=(e0, e0 + 512))
            c.vop("scalar_tensor_tensor", rs, ps, 1.0, rs, op0=OP.mult, op1=OP.add,
                  extra_reads=[rs])
    c.areset()
    c.areset()


def prep_mamba_host(inputs, rank_sel):
    wx = inputs["L0_attn_b_in_proj_w"]
    xp = inputs["L0_attn_b_x_proj_w"]
    out = {
        "mam_wxT": tile_kxm(np.ascontiguousarray(wx), 8).astype(np.float32),
        "mam_xpdT": tile_kxm(np.ascontiguousarray(xp[:, 0:64]), 16).astype(np.float32),
        "mam_xpbT": tile_kxm(np.ascontiguousarray(xp[:, 64:80]), 16).astype(np.float32),
        "mam_xpcT": tile_kxm(np.ascontiguousarray(xp[:, 80:96]), 16).astype(np.float32),
        "mam_dtT": np.ascontiguousarray(
            inputs["L0_attn_b_dt_proj_w"].reshape(64, 16, 128).transpose(1, 0, 2)
        ).astype(np.float32),
        "mam_conv": np.ascontiguousarray(
            inputs["L0_attn_b_conv_w"][:, 0, :].reshape(16, 128, 4).transpose(1, 0, 2)
            .reshape(128, 64)).astype(np.float32),
        "mam_convb": inputs["L0_attn_b_conv_b"].reshape(16, 128).T.astype(np.float32),
        "mam_dtb": inputs["L0_attn_b_dt_proj_b"].reshape(16, 128).T.astype(np.float32),
        "mam_Alog": np.ascontiguousarray(
            inputs["L0_attn_b_A_log"].reshape(16, 128, 16).transpose(1, 0, 2)
            .reshape(128, 256)).astype(np.float32),
        "mam_D": inputs["L0_attn_b_D"].reshape(16, 128).T.astype(np.float32),
        "mam_outT": tile_kxm(np.ascontiguousarray(inputs["L0_attn_b_out_proj_w"]), 16
                             ).astype(np.float32),
    }
    sel16 = np.zeros((16, 16 * 128), np.float32)
    for n in range(16):
        sel16[n, n * 128:(n + 1) * 128] = 1.0
    out["mam_sel16"] = sel16
    selm = np.zeros((128, 8), np.float32)
    for pr in range(3):
        selm[:, pr] = 1.0 if rank_sel == pr + 1 else 0.0
    selm[:, 3] = 0.0 if rank_sel == 0 else 1.0
    out["mam_sel"] = selm
    return out


def declare_mamba(prog, T=NT, ranks=4):
    W = {
        "wxT": prog.dram("mam_wxT", [32, 128, 1024], F32, kind="ExternalInput"),
        "xpdT": prog.dram("mam_xpdT", [1, 128, 1024], F32, kind="ExternalInput"),
        "xpbT": prog.dram("mam_xpbT", [1, 128, 256], F32, kind="ExternalInput"),
        "xpcT": prog.dram("mam_xpcT", [1, 128, 256], F32, kind="ExternalInput"),
        "dtT": prog.dram("mam_dtT", [16, 64, 128], F32, kind="ExternalInput"),
        "conv_dram": prog.dram("mam_conv", [128, 64], F32, kind="ExternalInput"),
        "convb_dram": prog.dram("mam_convb", [128, 16], F32, kind="ExternalInput"),
        "dtb_dram": prog.dram("mam_dtb", [128, 16], F32, kind="ExternalInput"),
        "Alog_dram": prog.dram("mam_Alog", [128, 256], F32, kind="ExternalInput"),
        "D_dram": prog.dram("mam_D", [128, 16], F32, kind="ExternalInput"),
        "outT": prog.dram("mam_outT", [8, 128, 2048], F32, kind="ExternalInput"),
        "sel16_dram": prog.dram("mam_sel16", [16, 2048], F32, kind="ExternalInput"),
        "sel_dram": prog.dram("mam_sel", [128, 8], F32, kind="ExternalInput"),
        "xc_dram": prog.dram("mam_xc", [16, 128, T], BF16),
        "y_dram": prog.dram("mam_y", [16, 128, T], BF16),
        "cc_in": prog.dram("mam_cc_in", [128, 512], F32),
        "cc_out": prog.dram("mam_cc_out", [ranks * 128, 512], F32),
        "conv_sb": prog.sbuf("mam_conv_sb", [128, 64], F32),
        "convb_sb": prog.sbuf("mam_convb_sb", [128, 16], F32),
        "dtb_sb": prog.sbuf("mam_dtb_sb", [128, 16], F32),
        "A_dram_sb": prog.sbuf("mam_Alog_sb", [128, 256], F32),
        "A_sb": prog.sbuf("mam_A_sb", [128, 256], F32),
        "D_sb": prog.sbuf("mam_D_sb", [128, 16], F32),
        "sel_sb": prog.sbuf("mam_sel_sb", [128, 8], F32),
    }
    return W


def load_mamba_consts(core, W):
    p = core.p
    p.dma("sync", W["conv_sb"].tv(), W["conv_dram"].tv())
    p.dma("sync", W["convb_sb"].tv(), W["convb_dram"].tv())
    p.dma("sync", W["dtb_sb"].tv(), W["dtb_dram"].tv())
    p.dma("sync", W["A_dram_sb"].tv(), W["Alog_dram"].tv())
    p.dma("sync", W["D_sb"].tv(), W["D_dram"].tv())
    p.dma("sync", W["sel_sb"].tv(), W["sel_dram"].tv())



def emit_attn0a(core, W, nquarters=None):
    c = core
    NEXT = c.NEXT
    NQ = c.NT + 128
    ns = NQ // 32
    if nquarters is None:
        nquarters = ns // 4
    nsq = ns // nquarters
    bias = W["bias_sb"]

    def bcol(i, p1=128):
        return bias.tv(ps=(0, p1), fs=(i, i + 1))

    for k in range(KT):
        src = TV(W["x0e"], W["x0e"].handle.ap()[k * 128:(k + 1) * 128,
                                                RES_OFF:RES_OFF + c.NRES],
                 k * 128, (k + 1) * 128, RES_OFF * 4, (RES_OFF + c.NRES) * 4)
        c.p.dma("sync", c.resid[k].tv(fs=(0, c.NRES)), src)

    c.amark()
    relsb = c.atile(64, BF16)
    wpossb = c.atile(16, BF16)
    c.p.dma("gpsimd", relsb.tv(p0=0, p1=16), W["relT"].tv(ps=(0, 16)))
    c.p.dma("gpsimd", wpossb.tv(p0=0, p1=16), W["wpos"].tv(ps=(0, 16)))
    posf = c.atile(16, F32)
    pps = c.pbtv(0, 0, 64, 0, 16)
    c.mm(pps, relsb.tv(p0=0, p1=16), wpossb.tv(p0=0, p1=16), start=True, stop=True)
    c.vop("tensor_copy", posf.tv(p0=0, p1=64), pps)
    c.p.dma("sync", W["stageP"].tv(ps=(0, 64)), posf.tv(p0=0, p1=64))
    posfT = c.atile(64, F32)
    c.p.dma("sync", posfT.tv(p0=0, p1=16),
            TV(W["stageP"], W["stageP"].handle.ap().transpose([1, 0]), 0, 64, 0, 1 << 30),
            allow_slow_non_contiguous=True)
    posD = c.atile(3040, BF16)
    maskD = c.atile(3040, BF16)
    c.amark()
    posP = c.atile(128, F32)
    c.vop("memset", posP.tv(), 0.0)
    for h in range(8):
        c.p.dma("sync", posP.tv(32, 96, p0=h * 16, p1=(h + 1) * 16),
                posfT.tv(0, 64, p0=0, p1=16))
    pp = posP.tv(32, 96)
    c.vop("tensor_scalar", pp, pp, bcol(W["BFB"]), None, op0=OP.add,
          extra_reads=[pp])
    mext = c.atile(128, BF16)
    c.vop("memset", mext.tv(), -1.0e30)
    c.vop("memset", mext.tv(32, 96), 0.0)
    for dt in range(32):
        c.vop("tensor_copy", posD.tv(dt * 95, dt * 95 + 95),
              posP.tv(32 - dt, 127 - dt))
        c.vop("tensor_copy", maskD.tv(dt * 95, dt * 95 + 95),
              mext.tv(32 - dt, 127 - dt))
    c.areset()
    mb = maskD.tv(p0=0, p1=8)
    c.vop("tensor_scalar", mb, mb, bcol(17, 8), None, op0=OP.add,
          extra_reads=[mb])
    wfbd = c.atile(256, BF16)
    c.p.dma("gpsimd", wfbd.tv(0, 128), TV(W["wfbd"], W["wfbd"].handle.ap()[0], 0, 128, 0, 4))
    c.p.dma("gpsimd", wfbd.tv(128, 256), TV(W["wfbd"], W["wfbd"].handle.ap()[1], 0, 128, 4, 8))
    wbd = c.atile(16, BF16)
    c.p.dma("gpsimd", wbd.tv(0, 16), W["wbonddmg_dram"].tv())
    bvrow = c.atile(1024, BF16)
    c.p.dma("gpsimd", bvrow.tv(p0=0, p1=1), W["bvalrow"].tv(ps=(0, 1)))

    em = [c.atile(3040, BF16)] * 2
    lg = [c.atile(1520, F32) for _ in range(2)]
    wdt = [c.atile(1520, F32) for _ in range(2)]
    wn = [c.atile(1520, BF16) for _ in range(2)]
    wT = [c.atile(512, BF16) for _ in range(2)]
    stag = [c.atile(1024, BF16)] * 2
    atok = [c.atile(1024, BF16) for _ in range(2)]
    z1 = [c.atile(3040, BF16) for _ in range(2)]
    zbd = z1
    smst = [c.atile(32, F32) for _ in range(2)]
    ctmp = [c.atile(128, F32)] * 2
    vsl8 = c.atile(8 * 1024, BF16)
    dsl = c.atile(2 * 1024, BF16)
    cslab = [c.atile(1024, BF16) for _ in range(2)]
    for k in range(KT):
        _wslab_dma(c, cslab[k % 2], W["cprojT"], k, 1024)
        c.p.dma("sync",
                TV(W["cprojT16"], W["cprojT16"].handle.ap()[k], 0, 128,
                   k * 262144, (k + 1) * 262144),
                cslab[k % 2].tv(0, 1024))
    for g in range(2):
        _wslab_dma(c, dsl.sub(g * 1024), W["dispT"], g, 1024)
    for k in range(KT):
        _wslab_dma(c, vsl8.sub(k * 1024), W["valT"], k, 1024)

    WH = 191
    assert nsq == 4 and NEXT - 128 * (nquarters - 1) == WH
    lne_s = [c.atile(WH, BF16) for _ in range(KT)]
    disp_s = c.atile(WH, BF16)
    qt = []
    for par in range(2):
        qt.append(dict(
            lne=lne_s,
            Pb=[c.atile(WH, F32) for _ in range(2)],
            Pd=[c.atile(WH, F32) for _ in range(2)],
            disp=disp_s,
            vtok=c.atile(2 * 1024, BF16),
        ))
    afm_all = c.atile(KT * nsq * 32, BF16)

    def preamble(q):
        t = qt[q % 2]
        e_lo = 128 * q
        e_hi = e_lo + WH
        lne, Pb, Pd, disp, vtok = (t["lne"], t["Pb"], t["Pd"], t["disp"],
                                   t["vtok"])
        c.amark()
        x0t = [c.atile(WH, F32) for _ in range(KT)]
        for k in range(KT):
            src = TV(W["x0e"], W["x0e"].handle.ap()[k * 128:(k + 1) * 128,
                                                    e_lo:e_hi],
                     k * 128, (k + 1) * 128, e_lo * 4, e_hi * 4)
            c.p.dma("sync", x0t[k].tv(), src)
        emit_ln(c, 0, WH, lne, src_tiles=x0t, pb0=0, pb1=1)
        c.areset()
        for g in range(2):
            ps = c.pbtv(g, 0, 128, 0, WH)
            for k in range(KT):
                c.mm(ps, dsl.tv(g * 1024 + k * 128, g * 1024 + (k + 1) * 128),
                     lne[k].tv(0, WH), start=(k == 0), stop=(k == KT - 1))
            c.sop("activation", disp.tv(0, WH), ps, func=AF.Identity,
                  bias=bcol(W["BDISP"] + g))
            psb = c.pbtv(2 + g, 0, 128, 0, WH)
            c.mm(psb, wfbd.tv(0, 128), disp.tv(0, WH), start=True, stop=True)
            c.vop("tensor_copy", Pb[g].tv(0, WH), psb)
            psd = c.pbtv(4 + g, 0, 128, 0, WH)
            c.mm(psd, wfbd.tv(128, 256), disp.tv(0, WH), start=True, stop=True)
            c.vop("tensor_copy", Pd[g].tv(0, WH), psd)
        for tt in range(2):
            a0 = tt * 128
            a1 = min(WH, a0 + 128)
            wq = a1 - a0
            for nh in range(2):
                ps = c.pbtv(nh, 0, wq, 0, 512)
                for k in range(KT):
                    c.mm(ps, lne[k].tv(a0, a1),
                         vsl8.tv(k * 1024 + nh * 512, k * 1024 + (nh + 1) * 512),
                         start=(k == 0), stop=False)
                c.mm(ps, c.ones_row_bf.tv(ps=(0, 1), fs=(0, wq)),
                     bvrow.tv(nh * 512, (nh + 1) * 512, p0=0, p1=1),
                     start=False, stop=True)
                c.vop("tensor_copy",
                      vtok.tv(tt * 1024 + nh * 512, tt * 1024 + (nh + 1) * 512,
                              p0=0, p1=wq), ps)

    def sub_head(q, li):
        Pb, Pd = qt[q % 2]["Pb"], qt[q % 2]["Pd"]
        i0 = q * nsq
        e_lo = 128 * q
        gi = i0 + li
        pr = li % 2
        qe = 63 + 32 * gi - e_lo
        klo = 32 * gi - e_lo
        if gi < 6:
            c.p.dma("scalar", em[pr].tv(p0=0, p1=8),
                    TV(W["em"], W["em"].handle.ap()[gi], 0, 8, 0, 1 << 30))
        for g in range(2):
            for c2 in range(2):
                cp = (li * 4 + g * 2 + c2) % 2
                dt0 = c2 * 16
                pbs = Pb[g].tv(klo, klo + 95).with_ap(
                    Pb[g].tv(klo, klo + 95).ap.unsqueeze(1)
                    .broadcast_to((128, 16, 95)))
                pbt = Pb[g].tv(qe + dt0, qe + dt0 + 16).with_ap(
                    Pb[g].tv(qe + dt0, qe + dt0 + 16).ap.unsqueeze(2)
                    .broadcast_to((128, 16, 95)))
                z1v = z1[cp].tv(0, 1520).with_ap(
                    z1[cp].tv(0, 1520).ap.rearrange("p (t u) -> p t u", u=95))
                c.gop("tensor_sub", z1v, pbs, pbt)
                c.vop("tensor_add", z1[cp].tv(0, 1520), z1[cp].tv(0, 1520),
                      posD.tv(dt0 * 95, (dt0 + 16) * 95),
                      extra_reads=[z1[cp].tv(0, 1520)])
                pds = Pd[g].tv(klo, klo + 95).with_ap(
                    Pd[g].tv(klo, klo + 95).ap.unsqueeze(1)
                    .broadcast_to((128, 16, 95)))
                pdt = Pd[g].tv(qe + dt0, qe + dt0 + 16).with_ap(
                    Pd[g].tv(qe + dt0, qe + dt0 + 16).ap.unsqueeze(2)
                    .broadcast_to((128, 16, 95)))
                z2v = z1[cp].tv(1520, 3040).with_ap(
                    z1[cp].tv(1520, 3040).ap.rearrange("p (t u) -> p t u", u=95))
                c.gop("tensor_sub", z2v, pds, pdt)
                c.sop("activation", zbd[cp].tv(0, 1520), z1[cp].tv(0, 1520),
                      func=AF.Gelu)
                c.sop("activation", zbd[cp].tv(1520, 3040), z1[cp].tv(1520, 3040),
                      func=AF.Gelu, bias=bcol(W["BFD"]))
                pwb = c.pbw(0, 1520, 0, 8)
                pwd = c.pbw(3, 1520, 0, 8)
                for sb in range(3):
                    s0, s1 = sb * 512, min(1520, (sb + 1) * 512)
                    psl = c.pbtv(sb, 0, 8, 0, s1 - s0)
                    c.mm(psl, wbd.tv(0, 8), zbd[cp].tv(s0, s1), start=True, stop=False)
                    c.mm(psl, c.ident.tv(ps=(0, 8), fs=(0, 8)),
                        maskD.tv(dt0 * 95 + s0, dt0 * 95 + s1, p0=0, p1=8),
                        start=False, stop=(gi >= 6))
                    if gi < 6:
                        c.mm(psl, c.ident.tv(ps=(0, 8), fs=(0, 8)),
                            em[pr].tv(dt0 * 95 + s0, dt0 * 95 + s1, p0=0, p1=8),
                            start=False, stop=True)
                    psl2 = c.pbtv(3 + sb, 0, 8, 0, s1 - s0)
                    c.mm(psl2, wbd.tv(8, 16), zbd[cp].tv(1520 + s0, 1520 + s1), start=True, stop=True)
                c.sop("activation", lg[cp].tv(p0=0, p1=8), pwd, func=AF.Tanh,
                      bias=bcol(16, 8), scale=0.5)
                c.vop("scalar_tensor_tensor", lg[cp].tv(p0=0, p1=8), lg[cp].tv(p0=0, p1=8),
                      -5.0, pwb, op0=OP.mult, op1=OP.add,
                      extra_reads=[lg[cp].tv(p0=0, p1=8)])
                dst = TV(W["stage"], W["stage"].handle.ap()[pr][g * 8:(g + 1) * 8,
                                                           dt0 * 95:(dt0 + 16) * 95],
                         g * 8, (g + 1) * 8, pr * 50000 + dt0 * 380,
                         pr * 50000 + (dt0 + 16) * 380)
                c.p.dma("sync", dst, lg[cp].tv(p0=0, p1=8))

    def sub_tail(q, li):
        vtok = qt[q % 2]["vtok"]
        i0 = q * nsq
        e_lo = 128 * q
        tt0 = q
        gi = i0 + li
        pr = li % 2
        klo = 32 * gi - e_lo
        src = TV(W["stage"], W["stage"].handle.ap()[pr]
                 .rearrange("h (t u) -> h t u", u=95).transpose([1, 0, 2]),
                 0, 16, pr * 50000, pr * 50000 + 49000)
        c.p.dma("sync", wdt[pr].tv(p0=0, p1=32).with_ap(
            wdt[pr].tv(p0=0, p1=32).ap.rearrange("p (h u) -> p h u", u=95)), src)
        w3 = wdt[pr].tv(p0=0, p1=32).with_ap(
            wdt[pr].tv(p0=0, p1=32).ap.rearrange("p (h u) -> p h u", u=95))
        nm = smst[pr].tv(0, 16, p0=0, p1=32)
        c.vop("tensor_reduce", nm, w3, AX.X, OP.max, negate=True)
        nmb = nm.with_ap(nm.ap.unsqueeze(2).broadcast_to((32, 16, 95)))
        c.vop("tensor_add", w3, w3, nmb, extra_reads=[wdt[pr].tv(p0=0, p1=32)])
        c.sop("activation", wn[pr].tv(p0=0, p1=32), wdt[pr].tv(p0=0, p1=32),
              func=AF.Exp)
        wnv = wn[pr].tv(p0=0, p1=32).with_ap(
            wn[pr].tv(p0=0, p1=32).ap.rearrange("p (h u) -> p h u", u=95))
        sm = smst[pr].tv(16, 32, p0=0, p1=32)
        c.vop("tensor_reduce", sm, wnv, AX.X, OP.add)
        c.vop("reciprocal", sm, sm, extra_reads=[sm])
        for h in range(16):
            pt = c.pbt16(6, 0, 95, h * 32, (h + 1) * 32)
            c.transpose(pt, wn[pr].tv(h * 95, (h + 1) * 95, p0=0, p1=32))
        c.vop("tensor_copy", wT[pr].tv(p0=0, p1=95), c.pbt16(6, 0, 95, 0, 512))
        abs0 = e_lo + klo
        ta = abs0 // 128 - tt0
        r0 = abs0 % 128
        n1 = min(95, 128 - r0)
        c.p.dma("scalar", stag[pr].tv(p0=0, p1=n1),
                vtok.tv(ta * 1024, (ta + 1) * 1024, p0=r0, p1=r0 + n1))
        if n1 < 95:
            c.p.dma("scalar", stag[pr].tv(p0=n1, p1=95),
                    vtok.tv((ta + 1) * 1024, (ta + 2) * 1024, p0=0, p1=95 - n1))
        po = c.pbw(6, 1024, 0, 32)
        for h in range(16):
            pslice = c.pbtv(6 + h // 8, 0, 32, (h % 8) * 64, (h % 8) * 64 + 64)
            c.mm(pslice, wT[pr].tv(h * 32, (h + 1) * 32, p0=0, p1=95),
                 stag[pr].tv(h * 64, (h + 1) * 64, p0=0, p1=95),
                 start=True, stop=True)
        po3 = po.with_ap(po.ap.rearrange("p (h f) -> p h f", f=64))
        smb = sm.with_ap(sm.ap.unsqueeze(2).broadcast_to((32, 16, 64)))
        at3 = atok[pr].tv(p0=0, p1=32).with_ap(
            atok[pr].tv(p0=0, p1=32).ap.rearrange("p (h f) -> p h f", f=64))
        c.vop("tensor_mul", at3, po3, smb)
        if "dbg" in W and gi == W.get("dbg_gi", -1):
            c.p.dma("gpsimd", W["dbg"].tv(ps=(0, 32)), atok[pr].tv(p0=0, p1=32))
            c.p.dma("gpsimd", W["dbg2"].tv(ps=(0, 32)), wn[pr].tv(p0=0, p1=32))
            c.p.dma("gpsimd", W["dbg3"].tv(ps=(0, 95)), stag[pr].tv(p0=0, p1=95))
            c.p.dma("gpsimd", W["dbg4"].tv(), vtok.tv(0, 1024))
        for ft in range(KT):
            pt = c.pbt16(6, 0, 128, ft * 32, (ft + 1) * 32)
            c.transpose(pt, atok[pr].tv(ft * 128, (ft + 1) * 128, p0=0, p1=32))
        afull = afm_all.tv(0, KT * nsq * 32)
        dst3 = afull.with_ap(afull.ap.rearrange("p (f x) -> p f x", f=KT)
                             [:, :, li * 32:(li + 1) * 32])
        s2 = c.pbt16(6, 0, 128, 0, KT * 32)
        src3 = s2.with_ap(s2.ap.rearrange("p (f u) -> p f u", f=KT))
        c.vop("tensor_copy", dst3, src3)

    def _cpl(mo):
        c.p.dma("sync", cslab[mo % 2].tv(0, 1024),
                TV(W["cprojT16"], W["cprojT16"].handle.ap()[mo], 0, 128,
                   mo * 262144, (mo + 1) * 262144))

    def cproj(q):
        i0, i1 = q * nsq, (q + 1) * nsq
        r0 = max(0, 63 + 32 * i0 - RES_OFF)
        r1 = min(c.NRES, 63 + 32 * i1 - RES_OFF)
        a0 = r0 + RES_OFF - 63 - 32 * i0
        _cpl(0)
        for mo in range(KT):
            if mo + 1 < KT:
                _cpl(mo + 1)
            ps = c.pbtv(6 + mo % 2, 0, 128, 0, r1 - r0)
            for k in range(KT):
                c.mm(ps, cslab[mo % 2].tv(k * 128, (k + 1) * 128),
                     afm_all.tv(k * nsq * 32 + a0, k * nsq * 32 + a0 + r1 - r0),
                     start=(k == 0), stop=(k == KT - 1))
            tmp = ctmp[mo % 2].tv(0, r1 - r0)
            c.sop("activation", tmp, ps, func=AF.Identity,
                  bias=bcol(W["BCPROJ"] + mo))
            rs = c.resid[mo].tv(fs=(r0, r1))
            c.vop("tensor_add", rs, rs, tmp, extra_reads=[rs])

    preamble(0)
    for q in range(nquarters):
        sub_head(q, 0)
        sub_head(q, 1)
        sub_tail(q, 0)
        sub_head(q, 2)
        sub_tail(q, 1)
        sub_head(q, 3)
        if q + 1 < nquarters:
            preamble(q + 1)
        sub_tail(q, 2)
        sub_tail(q, 3)
        cproj(q)
    c.areset()


A0_BDISP, A0_BFD, A0_BFB, A0_BBOND, A0_BDMG, A0_BVAL, A0_BCPROJ = 0, 2, 3, 4, 5, 6, 8


def prep_attn0a_host(inputs, seq_start, T=NT):
    import ml_dtypes
    Wd = inputs["L0_attn_a_W_disp"]; Wv = inputs["L0_attn_a_W_val"]
    Wc = inputs["L0_attn_a_W_cproj"]; Wf = inputs["L0_attn_a_W_fused"]
    out = {
        "a0_dispT": tile_kxm(np.ascontiguousarray(Wd), 8).astype(np.float32),
        "a0_valT": np.ascontiguousarray(Wv.reshape(8, 128, 1024)).astype(np.float32),
        "a0_cprojT": tile_kxm(np.ascontiguousarray(Wc), 8).astype(np.float32),
        "a0_relT": np.ascontiguousarray(inputs["L0_attn_a_rel_pos_emb"].T).astype(np.float32),
        "a0_wpos": inputs["L0_attn_a_W_pos"].astype(np.float32),
        "a0_bvalrow": inputs["L0_attn_a_b_val"].reshape(1, 1024).astype(np.float32),
    }
    wfbd = np.zeros((2, 128, 128), np.float32)
    for s in range(2):
        for h in range(8):
            wfbd[s, h * 16:(h + 1) * 16, h * 16:(h + 1) * 16] = Wf[:, s * 16:(s + 1) * 16]
    out["a0_wfbd"] = wfbd
    wbd = np.zeros((128, 16), np.float32)
    for h in range(8):
        wbd[h * 16:(h + 1) * 16, h] = inputs["L0_attn_a_W_bond"][:, 0]
        wbd[h * 16:(h + 1) * 16, 8 + h] = inputs["L0_attn_a_W_dmg"][:, 0]
    out["a0_wbonddmg"] = wbd
    bias = np.zeros((128, 16), np.float32)
    bias[:, A0_BDISP] = inputs["L0_attn_a_b_disp"][0:128]
    bias[:, A0_BDISP + 1] = inputs["L0_attn_a_b_disp"][128:256]
    bias[:, A0_BFD] = np.tile(inputs["L0_attn_a_b_fused"][16:32], 8)
    bias[:, A0_BFB] = np.tile(inputs["L0_attn_a_b_fused"][0:16], 8)
    bias[:, A0_BBOND] = inputs["L0_attn_a_b_bond"][0]
    bias[:, A0_BDMG] = inputs["L0_attn_a_b_dmg"][0]
    for mo in range(8):
        bias[:, A0_BCPROJ + mo] = inputs["L0_attn_a_b_cproj"][mo * 128:(mo + 1) * 128]
    out["a0_bias"] = bias
    em = np.zeros((6, 8, 3040), np.float32)
    for gi in range(6):
        for dt in range(32):
            u = np.arange(95)
            kk = seq_start - 191 + 32 * gi + u
            em[gi, :, dt * 95:(dt + 1) * 95] = np.where(kk < 0, -1.0e30, 0.0)
    out["a0_em"] = em.astype(ml_dtypes.bfloat16)
    return out


def declare_attn0a(prog, T=NT):
    NEXTl = T + 191
    W = {
        "x0e": prog.dram("a0_x0e", [1024, NEXTl], F32, kind="ExternalInput"),
        "dispT": prog.dram("a0_dispT", [2, 128, 1024], F32, kind="ExternalInput"),
        "valT": prog.dram("a0_valT", [8, 128, 1024], F32, kind="ExternalInput"),
        "cprojT": prog.dram("a0_cprojT", [8, 128, 1024], F32, kind="ExternalInput"),
        "relT": prog.dram("a0_relT", [16, 64], F32, kind="ExternalInput"),
        "wpos": prog.dram("a0_wpos", [16, 16], F32, kind="ExternalInput"),
        "wfbd": prog.dram("a0_wfbd", [2, 128, 128], F32, kind="ExternalInput"),
        "wbonddmg_dram": prog.dram("a0_wbonddmg", [128, 16], F32, kind="ExternalInput"),
        "bvalrow": prog.dram("a0_bvalrow", [1, 1024], F32, kind="ExternalInput"),
        "bias_dram": prog.dram("a0_bias", [128, 16], F32, kind="ExternalInput"),
        "em": prog.dram("a0_em", [6, 8, 3040], BF16, kind="ExternalInput"),
        "stage": prog.dram("a0_stage", [2, 16, 3040], F32),
        "cprojT16": prog.dram("a0_cprojT16", [8, 128, 1024], BF16),
        "stageP": prog.dram("a0_stageP", [64, 16], F32),
        "bias_sb": prog.sbuf("a0_bias_sb", [128, 18], F32),
        "BDISP": A0_BDISP, "BFD": A0_BFD, "BFB": A0_BFB, "BBOND": A0_BBOND,
        "BDMG": A0_BDMG, "BCPROJ": A0_BCPROJ,
    }
    return W


def load_attn0a_consts(core, W):
    core.p.dma("sync", W["bias_sb"].tv(fs=(0, 16)), W["bias_dram"].tv())
    core.vop("tensor_scalar", W["bias_sb"].tv(fs=(16, 17)),
             W["bias_sb"].tv(fs=(A0_BDMG, A0_BDMG + 1)), 0.5, None, op0=OP.mult)
    core.vop("tensor_scalar", W["bias_sb"].tv(fs=(17, 18)),
             W["bias_sb"].tv(fs=(A0_BBOND, A0_BBOND + 1)), -5.0, None, op0=OP.add)




SUBLAYERS = ["a0a", "ffn0a", "mam", "ffn0b", "a1a", "ffn1a", "a1b", "ffn1b"]

GROUPS = [SUBLAYERS]


def _declare_for(prog, names, nt, ranks):
    D_ = {}
    if "a0a" in names:
        D_["a0a"] = declare_attn0a(prog, T=nt)
    if "ffn0a" in names or "ffn0b" in names:
        D_["ffn0"] = declare_ffn0(prog)
    if "mam" in names:
        D_["mam"] = declare_mamba(prog, T=nt, ranks=ranks)
    if "a1a" in names:
        D_["a1a"] = declare_attn1a(prog)
    if "ffn1a" in names or "ffn1b" in names:
        D_["ffn1"] = declare_ffn1(prog)
    if "a1b" in names:
        D_["a1b"] = declare_attn1b(prog, ranks=ranks, T=nt)
    return D_


def _load_consts_for(core, D_):
    if "a0a" in D_:
        load_attn0a_consts(core, D_["a0a"])
    if "ffn0" in D_:
        load_ffn0_consts(core, D_["ffn0"])
    if "mam" in D_:
        load_mamba_consts(core, D_["mam"])
    if "a1a" in D_:
        load_attn1a_consts(core, D_["a1a"])
    if "a1b" in D_:
        load_attn1b_consts(core, D_["a1b"])


def build_group(names, nt, rg, ranks, first, last):
    nc = bass.Bass("TRN2", target_bir_lowering=False, debug=False)
    prog = Prog(nc)
    nres = nt + HALO_M
    if not first:
        rin = prog.dram("rin", [1024, nres], F32, kind="ExternalInput")
    rout = prog.dram("rout", [1024, nt if last else nres], F32, kind="ExternalOutput")
    D_ = _declare_for(prog, names, nt, ranks)
    core = Core(nc, prog, nt=nt)
    core.init_consts()
    _load_consts_for(core, D_)
    if not first:
        for k in range(KT):
            src = TV(rin, rin.handle.ap()[k * 128:(k + 1) * 128, :],
                     k * 128, (k + 1) * 128, 0, 1 << 30)
            prog.dma("sync", core.resid[k].tv(fs=(0, nres)), src)
    for nm in names:
        prog.phase = nm
        if nm == "a0a":
            emit_attn0a(core, D_["a0a"])
        elif nm == "ffn0a":
            emit_ffn0(core, D_["ffn0"], 0, nres)
        elif nm == "ffn0b":
            emit_ffn0(core, D_["ffn0"], HALO_M, nres)
        elif nm == "mam":
            emit_mamba(core, D_["mam"], rg)
        elif nm == "a1a":
            emit_attn1a(core, D_["a1a"], HALO_M, nres, rg)
        elif nm == "ffn1a" or nm == "ffn1b":
            emit_ffn1(core, D_["ffn1"], HALO_M, nres)
        elif nm == "a1b":
            emit_attn1b(core, D_["a1b"], HALO_M, nres, rg, ranks)
    prog.phase = "out"
    c0 = HALO_M if last else 0
    for k in range(KT):
        dst = TV(rout, rout.handle.ap()[k * 128:(k + 1) * 128, :],
                 k * 128, (k + 1) * 128, 0, 1 << 30)
        prog.dma("sync", dst, core.resid[k].tv(fs=(c0, nres)))
    prog.emit_all()
    return nc, prog


def prep_host_all(inputs, nt=NT, ncores=8, per_batch=4):
    shared = {}
    shared.update(prep_ffn0_host(inputs))
    shared.update(prep_ffn1_host(inputs))
    shared.update(prep_attn1a_host(inputs))
    ab = prep_attn1b_host(inputs, ranks=per_batch, T=nt)
    masks = {r: ab.pop(f"a1b_mask_r{r}") for r in range(per_batch)}
    shared.update(ab)
    mam_shared = None
    x = inputs["L0_attn_a_x"]
    B = x.shape[0]
    percore = []
    for c in range(ncores):
        b, r = c // per_batch, c % per_batch
        s = r * nt
        pc = {}
        mh = prep_mamba_host(inputs, rank_sel=r)
        a0 = prep_attn0a_host(inputs, seq_start=s, T=nt)
        if mam_shared is None:
            mam_shared = {k: v for k, v in mh.items() if k != "mam_sel"}
            shared.update(mam_shared)
            shared.update({k: v for k, v in a0.items() if k != "a0_em"})
        pc["mam_sel"] = mh["mam_sel"]
        pc["a0_em"] = a0["a0_em"]
        pc["a1b_mask"] = masks[r]
        x0e = np.zeros((nt + 191, 1024), np.float32)
        lo = s - 191
        src_lo = max(0, lo)
        x0e[src_lo - lo:] = x[b, src_lo:s + nt]
        pc["a0_x0e"] = np.ascontiguousarray(x0e.T)
        percore.append(pc)
    return shared, percore


def kernel(**inputs):
    nt = NT
    ncores = 8
    per_batch = 4
    rg = [[0, 1, 2, 3], [4, 5, 6, 7]]
    shared, percore = prep_host_all(inputs, nt, ncores, per_batch)
    x = inputs["L0_attn_a_x"]
    B, S, Dm = x.shape
    state = [None] * ncores
    n_groups = len(GROUPS)
    for gi, names in enumerate(GROUPS):
        first = gi == 0
        last = gi == n_groups - 1
        nc, prog = build_group(names, nt, rg, per_batch, first, last)
        needed = {b.name for b in prog.bufs
                  if b.space == "dram" and b.handle is not None}
        req = set()
        for alloc in nc.m.functions[0].allocations:
            try:
                if alloc.kind == "ExternalInput":
                    req.add(alloc.memorylocations[0].name)
            except AttributeError:
                pass
        in_maps = []
        for c in range(ncores):
            m = {}
            for k2 in req:
                if k2 == "partition_id":
                    continue
                if k2 == "rin":
                    m[k2] = state[c]
                elif k2 in percore[c]:
                    m[k2] = percore[c][k2]
                elif k2 in shared:
                    m[k2] = shared[k2]
                else:
                    raise KeyError(f"missing input {k2}")
            in_maps.append(m)
        res = run_bass_kernel_spmd(nc, in_maps, core_ids=list(range(ncores)))
        for c in range(ncores):
            state[c] = res.results[c]["rout"]
    out = np.empty((B, S, Dm), np.float32)
    for c in range(ncores):
        b, r = c // per_batch, c % per_batch
        out[b, r * nt:(r + 1) * nt] = state[c].T
    return out


# version 26
# speedup vs baseline: 1.0316x; 1.0081x over previous
import numpy as np

import concourse.bass as bass
import concourse.mybir as mybir
from concourse.bass_utils import run_bass_kernel_spmd

F32 = mybir.dt.float32
BF16 = mybir.dt.bfloat16
I32 = mybir.dt.int32
AF = mybir.ActivationFunctionType
OP = mybir.AluOpType
AX = mybir.AxisListType

ENGINES = ("sync", "scalar", "vector", "gpsimd", "tensor")



class Buf:

    __slots__ = ("name", "handle", "space", "whole", "writes", "reads", "dtype")

    def __init__(self, name, handle, space, dtype, whole=False):
        self.name = name
        self.handle = handle
        self.space = space
        self.dtype = dtype
        self.whole = whole
        self.writes = []
        self.reads = []

    def tv(self, ps=None, fs=None, elem_bytes=None):
        eb = elem_bytes or np.dtype(mybir.dt.np(self.dtype)).itemsize
        p0, p1 = ps if ps is not None else (0, self.handle.shape[0])
        if fs is None:
            ap = self.handle.ap()[p0:p1, :]
            b0, b1 = 0, self.handle.shape[1] * eb
        else:
            e0, e1 = fs
            ap = self.handle.ap()[p0:p1, e0:e1]
            b0, b1 = e0 * eb, e1 * eb
        return TV(self, ap, p0, p1, b0, b1)

    def whole_tv(self):
        return self.tv()


class TV:

    __slots__ = ("buf", "ap", "p0", "p1", "b0", "b1")

    def __init__(self, buf, ap, p0, p1, b0, b1):
        self.buf = buf
        self.ap = ap
        self.p0 = p0
        self.p1 = p1
        self.b0 = b0
        self.b1 = b1

    def with_ap(self, ap):
        return TV(self.buf, ap, self.p0, self.p1, self.b0, self.b1)

    def rearrange(self, s, **kw):
        return self.with_ap(self.ap.rearrange(s, **kw))

    def bitcast(self, dt):
        return self.with_ap(self.ap.bitcast(dt))


class Opr:
    __slots__ = ("engine", "emit", "deps", "kind", "eidx", "didx", "ccidx",
                 "needs_inc", "ms", "tag", "phase")

    def __init__(self, engine, emit, kind, eidx, didx, ccidx, tag):
        self.engine = engine
        self.emit = emit
        self.kind = kind
        self.eidx = eidx
        self.didx = didx
        self.ccidx = ccidx
        self.deps = {}
        self.needs_inc = False
        self.ms = None
        self.tag = tag


def _overlap(a, b):
    return not (a[1] <= b[0] or b[1] <= a[0])


class Prog:
    def __init__(self, nc):
        self.nc = nc
        self.phase = "init"
        self.ops = {e: [] for e in ENGINES}
        self.dma_count = {e: 0 for e in ENGINES}
        self.cc_count = 0
        self.bufs = []
        self.n_ops = 0
        self.boundaries = {e: [] for e in ENGINES}
        self.gates = {e: set() for e in ENGINES}
        self.cc_boundaries = []
        self.cc_gates = set()

    def _dep_value(self, d):
        if d.kind == "c":
            return None
        if d.kind == "d":
            import bisect
            bl = self.boundaries[d.engine]
            need = d.didx + 1
            i = bisect.bisect_left(bl, need)
            if i < len(bl):
                return ("d", d.engine, bl[i])
            b = self.dma_count[d.engine]
            assert b >= need
            bl.append(b)
            self.gates[d.engine].add(b)
            return ("d", d.engine, b)
        import bisect
        need = d.ccidx + 1
        i = bisect.bisect_left(self.cc_boundaries, need)
        if i < len(self.cc_boundaries):
            return ("cc", None, self.cc_boundaries[i])
        b = self.cc_count
        assert b >= need
        self.cc_boundaries.append(b)
        self.cc_gates.add(b)
        return ("cc", None, b)

    def sbuf(self, name, shape, dtype):
        h = self.nc.alloc_sbuf_tensor(name, list(shape), dtype)
        b = Buf(name, h, "sbuf", dtype)
        self.bufs.append(b)
        return b

    def psum(self, name, shape=(128, 512), dtype=F32):
        h = self.nc.alloc_psum_tensor(name, list(shape), dtype)
        b = Buf(name, h, "psum", dtype, whole=True)
        self.bufs.append(b)
        return b

    def dram(self, name, shape, dtype, kind="Internal", addr_space="Local"):
        h = self.nc.dram_tensor(name, list(shape), dtype, kind=kind,
                                addr_space=addr_space)
        b = Buf(name, h, "dram", dtype)
        self.bufs.append(b)
        return b

    def _regions(self, tvs):
        out = []
        for tv in tvs:
            if tv is None:
                continue
            if isinstance(tv, Buf):
                tv = tv.whole_tv()
            b = tv.buf
            if b.whole:
                out.append((b, 0, 128, 0, 1 << 30))
            else:
                out.append((b, tv.p0, tv.p1, tv.b0, tv.b1))
        return out

    def op(self, engine, emit, reads=(), writes=(), kind="c", tag=None):
        didx = ccidx = None
        if kind == "d":
            didx = self.dma_count[engine]
        elif kind == "cc":
            ccidx = self.cc_count
        o = Opr(engine, emit, kind, len(self.ops[engine]), didx, ccidx, tag)
        o.phase = self.phase
        rr = self._regions(reads)
        ww = self._regions(writes)
        dep_set = set()
        for (b, p0, p1, b0, b1) in rr:
            for (w, wp0, wp1, wb0, wb1) in b.writes:
                if w is not o and _overlap((p0, p1), (wp0, wp1)) and _overlap((b0, b1), (wb0, wb1)):
                    dep_set.add(w)
        for (b, p0, p1, b0, b1) in ww:
            for (w, wp0, wp1, wb0, wb1) in b.writes:
                if _overlap((p0, p1), (wp0, wp1)) and _overlap((b0, b1), (wb0, wb1)):
                    dep_set.add(w)
            for (r, rp0, rp1, rb0, rb1) in b.reads:
                if _overlap((p0, p1), (rp0, rp1)) and _overlap((b0, b1), (rb0, rb1)):
                    dep_set.add(r)
        for d in dep_set:
            if d.engine == "tensor" and engine == "tensor" and d.kind == "c" and kind == "c":
                continue
            o.deps[d] = self._dep_value(d)
        if kind == "d":
            self.dma_count[engine] += 1
        elif kind == "cc":
            self.cc_count += 1
        for (b, p0, p1, b0, b1) in ww:
            b.writes = [wacc for wacc in b.writes
                        if not (p0 <= wacc[1] and wacc[2] <= p1 and b0 <= wacc[3] and wacc[4] <= b1)]
            b.reads = [racc for racc in b.reads
                       if not (p0 <= racc[1] and racc[2] <= p1 and b0 <= racc[3] and racc[4] <= b1)]
            b.writes.append((o, p0, p1, b0, b1))
        for (b, p0, p1, b0, b1) in rr:
            b.reads.append((o, p0, p1, b0, b1))
        self.ops[engine].append(o)
        self.n_ops += 1
        return o

    def dma(self, engine, out_tv, in_tv, tag=None, **kw):
        oap = out_tv.ap if isinstance(out_tv, TV) else out_tv
        iap = in_tv.ap if isinstance(in_tv, TV) else in_tv

        def emit(eng, _o=oap, _i=iap, _k=dict(kw)):
            return eng.dma_start(out=_o, in_=_i, **_k)

        reads = [in_tv] if isinstance(in_tv, TV) else []
        writes = [out_tv] if isinstance(out_tv, TV) else []
        return self.op(engine, emit, reads=reads, writes=writes, kind="d", tag=tag)

    def cc(self, kind, aluop, replica_groups, in_tv, out_tv, tag=None):
        iap = in_tv.ap if isinstance(in_tv, TV) else in_tv
        oap = out_tv.ap if isinstance(out_tv, TV) else out_tv

        def emit(eng, _k=kind, _op=aluop, _rg=replica_groups, _i=iap, _o=oap):
            return eng.collective_compute(_k, _op, replica_groups=_rg,
                                          ins=[_i.opt()], outs=[_o.opt()])

        reads = [in_tv] if isinstance(in_tv, TV) else []
        writes = [out_tv] if isinstance(out_tv, TV) else []
        return self.op("gpsimd", emit, reads=reads, writes=writes, kind="cc", tag=tag)

    ROT_MS = 30000
    ROT_DMA = 1800
    ROT_CC = 30000

    def emit_all(self):
        nc = self.nc
        for e in ENGINES:
            for o in self.ops[e]:
                for d in o.deps:
                    d.needs_inc = True
        ms_count = {e: 0 for e in ENGINES}
        for e in ENGINES:
            for o in self.ops[e]:
                if o.kind == "c":
                    if o.needs_inc:
                        ms_count[e] += 1
                        o.ms = ms_count[e]
                elif o.kind == "d":
                    o.ms = 16 * (o.didx + 1)
                    o.needs_inc = True
                elif o.kind == "cc":
                    o.ms = o.ccidx + 1
                    o.needs_inc = True
        def n_sems(count, rot):
            return max(1, -(-count // rot)) if count else 1

        csems = {e: [nc.alloc_semaphore(f"cs_{e}_{i}")
                     for i in range(n_sems(ms_count[e], self.ROT_MS))]
                 for e in ENGINES}
        dsems = {e: [nc.alloc_semaphore(f"ds_{e}_{i}")
                     for i in range(n_sems(self.dma_count[e], self.ROT_DMA))]
                 for e in ENGINES}
        ccsems = [nc.alloc_semaphore(f"ccs_{i}")
                  for i in range(n_sems(self.cc_count, self.ROT_CC))]

        def sem_val(o):
            if o.kind == "c":
                i = (o.ms - 1) // self.ROT_MS
                return csems[o.engine][i], o.ms - i * self.ROT_MS
            if o.kind == "d":
                i = o.didx // self.ROT_DMA
                return dsems[o.engine][i], o.ms - i * self.ROT_DMA * 16
            i = o.ccidx // self.ROT_CC
            return ccsems[i], o.ms - i * self.ROT_CC

        def inc_val(o):
            if o.kind == "c":
                return 1
            if o.kind == "d":
                return 16
            return None

        prog = self
        prog.phase_marks = []
        _scope_ids = []

        with nc.Block() as block:
            for ename in ENGINES:
                def body(eng, _e=ename):
                    known = {}

                    def dsem_of(qeng, b):
                        i = (b - 1) // prog.ROT_DMA
                        return dsems[qeng][i], 16 * (b - i * prog.ROT_DMA)

                    def ccsem_of(b):
                        i = (b - 1) // prog.ROT_CC
                        return ccsems[i], b - i * prog.ROT_CC

                    cur_phase = None
                    for o in prog.ops[_e]:
                        if o.phase != cur_phase:
                            if cur_phase is not None:
                                nc.pop_named_scope(cur_phase)
                            nc.push_named_scope(o.phase)
                            prog.phase_marks.append((nc.next_id(), o.phase))
                            cur_phase = o.phase
                        if o.kind == "d" and o.didx in prog.gates[_e] and o.didx > 0:
                            s, v = dsem_of(_e, o.didx)
                            if known.get(s, 0) < v:
                                eng.wait_ge(s, v)
                                known[s] = v
                        if o.kind == "cc" and o.ccidx in prog.cc_gates and o.ccidx > 0:
                            s, v = ccsem_of(o.ccidx)
                            if known.get(s, 0) < v:
                                eng.wait_ge(s, v)
                                known[s] = v
                        need = {}
                        for d, ov in o.deps.items():
                            if ov is None:
                                s, v = sem_val(d)
                            elif ov[0] == "d":
                                s, v = dsem_of(ov[1], ov[2])
                            else:
                                s, v = ccsem_of(ov[2])
                            if need.get(s, 0) < v:
                                need[s] = v
                        for s, v in sorted(need.items(), key=lambda kv: kv[0].num):
                            if known.get(s, 0) < v:
                                eng.wait_ge(s, v)
                                known[s] = v
                        inst = o.emit(eng)
                        if o.needs_inc:
                            if o.kind == "cc":
                                inst.then_inc(sem_val(o)[0])
                            else:
                                inst.then_inc(sem_val(o)[0], inc_val(o))
                    if cur_phase is not None:
                        nc.pop_named_scope(cur_phase)
                    for i, s in enumerate(dsems[_e]):
                        tot = prog.dma_count[_e] - i * prog.ROT_DMA
                        if tot > 0:
                            eng.wait_ge(s, 16 * min(tot, prog.ROT_DMA))
                    if _e == "gpsimd":
                        for i, s in enumerate(ccsems):
                            tot = prog.cc_count - i * prog.ROT_CC
                            if tot > 0:
                                eng.wait_ge(s, min(tot, prog.ROT_CC))
                getattr(block, ename)(body)
        return csems, dsems, ccsems




def ceil_div(a, b):
    return -(-a // b)


NT = 2048
HALO_M = 3
NRES = NT + HALO_M
NEXT_A = 2239
RES_OFF = 188
D = 1024
KT = D // 128

EPS = 1e-5


class ArenaTile:

    def __init__(self, core, byte_off, nelems, dtype, eb, buf):
        self.core = core
        self.off = byte_off
        self.n = nelems
        self.dtype = dtype
        self.eb = eb
        self.buf = buf
        self.dead = False

    def tv(self, e0=0, e1=None, p0=0, p1=128):
        assert not self.dead, "use of freed arena tile"
        e1 = self.n if e1 is None else e1
        a = self.core.arena
        b0 = self.off + e0 * self.eb
        b1 = self.off + e1 * self.eb
        ap = a.handle.ap()[p0:p1, b0:b1].bitcast(self.dtype)
        return TV(self.buf, ap, p0, p1, b0, b1)

    def sub(self, base):
        return _SubTile(self, base)


class _SubTile:
    def __init__(self, parent, base):
        self.parent = parent
        self.base = base

    def tv(self, e0, e1, p0=0, p1=128):
        return self.parent.tv(self.base + e0, self.base + e1, p0, p1)


class Core:
    def __init__(self, nc, prog, arena_kb=None, nt=NT):
        self.nc = nc
        self.p = prog
        self.NT = nt
        self.NRES = nt + HALO_M
        self.NEXT = nt + 191
        self.psum_all = prog.psum("psum_all", (128, 4096), F32)
        self.psum_all.whole = False
        self.resid = [prog.sbuf(f"res{k}", [128, self.NRES], F32) for k in range(KT)]
        self.ident = prog.sbuf("ident", [128, 128], BF16)
        self.ones_f32 = prog.sbuf("ones_f32", [128, 1], F32)
        self.ones_bf = prog.sbuf("ones_bf", [128, 1], BF16)
        self.ones_row_bf = prog.sbuf("ones_row_bf", [1, 128], BF16)
        self.ln_sq = [prog.sbuf(f"ln_sq{i}", [128, 512], F32) for i in range(2)]
        self.ln_xb = [prog.sbuf(f"ln_xb{i}", [128, 512], BF16) for i in range(2)]
        self.ln_sqb = [prog.sbuf(f"ln_sqb{i}", [128, 512], BF16) for i in range(2)]
        self.ln_stats = prog.sbuf("ln_stats", [1, 2048], F32)
        self.ln_bc16 = prog.sbuf("ln_bc16", [1, 1024], BF16)
        avail = (nc.sbuf_bytes_remaining - 512) // 32 * 32
        self.ARENA_BYTES = min(arena_kb * 1024, avail) if arena_kb else avail
        self.arena = prog.sbuf("arena", [128, self.ARENA_BYTES], mybir.dt.uint8)
        self._arena_off = 0
        self._arena_marks = []
        self._atiles = []
        self._atile_id = 0

    def init_consts(self):
        c = self
        c.vop("memset", c.ones_f32.tv(), 1.0)
        c.vop("memset", c.ones_bf.tv(), 1.0)
        c.vop("memset", c.ones_row_bf.tv(ps=(0, 1)), 1.0)
        c.gop("memset", c.ident.tv(), 1.0)
        c.gop("affine_select", c.ident.tv(), c.ident.tv(), [[1, 128]], OP.is_ge,
              0.0, base=0, channel_multiplier=-1)
        c.gop("affine_select", c.ident.tv(), c.ident.tv(), [[-1, 128]], OP.is_ge,
              0.0, base=0, channel_multiplier=1)

    def amark(self):
        self._arena_marks.append(self._arena_off)

    def areset(self):
        base = self._arena_marks.pop()
        for t in self._atiles:
            if t.off >= base:
                t.dead = True
        self._arena_off = base

    def aalloc(self, nbytes, align=32):
        off = -(-self._arena_off // align) * align
        self._arena_off = off + nbytes
        assert self._arena_off <= self.ARENA_BYTES, (
            f"arena overflow: {self._arena_off} > {self.ARENA_BYTES}")
        return off

    def atile(self, nelems, dtype):
        eb = {F32: 4, BF16: 2, I32: 4, mybir.dt.int8: 1}[dtype]
        off = self.aalloc(nelems * eb)
        b0, b1 = off, off + nelems * eb
        buf = Buf(f"at{self._atile_id}", self.arena.handle, "sbuf", dtype)
        self._atile_id += 1
        for t in self._atiles:
            if t.off < b1 and b0 < (t.off + t.n * t.eb):
                latest = {}
                for acc in t.buf.writes + t.buf.reads:
                    o = acc[0]
                    key = o.engine
                    prev = latest.get(key)
                    if prev is None or o.eidx > prev.eidx:
                        latest[key] = o
                for o in latest.values():
                    buf.writes.append((o, 0, 128, b0, b1))
        tile = ArenaTile(self, off, nelems, dtype, eb, buf)
        self._atiles.append(tile)
        return tile

    def vop(self, fname, out, *ins, extra_reads=(), **kw):
        return self._eop("vector", fname, out, ins, extra_reads, kw)

    def sop(self, fname, out, *ins, extra_reads=(), **kw):
        return self._eop("scalar", fname, out, ins, extra_reads, kw)

    def gop(self, fname, out, *ins, extra_reads=(), **kw):
        return self._eop("gpsimd", fname, out, ins, extra_reads, kw)

    def _eop(self, engine, fname, out, ins, extra_reads, kw):
        in_aps = [(x.ap if isinstance(x, TV) else x) for x in ins]
        out_ap = out.ap if isinstance(out, TV) else out
        kw2 = {k: (v.ap if isinstance(v, TV) else v) for k, v in kw.items()}

        def emit(eng, _f=fname, _o=out_ap, _i=in_aps, _k=kw2):
            return getattr(eng, _f)(_o, *_i, **_k)

        reads = [x for x in ins if isinstance(x, TV)] + \
                [v for k2, v in kw.items()
                 if isinstance(v, TV) and k2 != "accum_out"] + list(extra_reads)
        writes = ([out] if isinstance(out, TV) else []) + \
                 [v for k2, v in kw.items()
                  if isinstance(v, TV) and k2 == "accum_out"]
        return self.p.op(engine, emit, reads=reads, writes=writes)

    def mm(self, out, lhsT, rhs, start, stop):
        oap = out.ap if isinstance(out, TV) else out
        lap = lhsT.ap if isinstance(lhsT, TV) else lhsT
        rap = rhs.ap if isinstance(rhs, TV) else rhs

        def emit(eng, _o=oap, _l=lap, _r=rap, _s=start, _t=stop):
            return eng.matmul(_o, _l, _r, start=_s, stop=_t, skip_group_check=True)

        reads = [x for x in (lhsT, rhs) if isinstance(x, TV)]
        writes = [out] if isinstance(out, TV) else []
        return self.p.op("tensor", emit, reads=reads, writes=writes)

    def transpose(self, out, in_):
        oap = out.ap if isinstance(out, TV) else out
        iap = in_.ap if isinstance(in_, TV) else in_
        np_ = (in_.p1 - in_.p0) if isinstance(in_, TV) else 128
        ident = self.ident.tv(ps=(0, np_), fs=(0, np_))

        def emit(eng, _o=oap, _i=iap, _id=ident.ap):
            return eng.transpose(_o, _i, _id)

        reads = [x for x in (in_,) if isinstance(x, TV)] + [ident]
        writes = [out] if isinstance(out, TV) else []
        return self.p.op("tensor", emit, reads=reads, writes=writes)

    def pbtv(self, i, p0=0, p1=128, e0=0, e1=512):
        b = self.psum_all
        ap = b.handle.ap()[p0:p1, i * 512 + e0:i * 512 + e1]
        lo = ((i * 512 + e0) * 4 // 2048) * 2048
        hi = -(-((i * 512 + e1) * 4) // 2048) * 2048
        return TV(b, ap, 0, 128, lo, hi)

    def pbt16(self, i, p0=0, p1=128, e0=0, e1=1024):
        b = self.psum_all
        ap = b.handle.ap().bitcast(BF16)[p0:p1, i * 1024 + e0:i * 1024 + e1]
        lo = ((i * 1024 + e0) * 2 // 2048) * 2048
        hi = -(-((i * 1024 + e1) * 2) // 2048) * 2048
        return TV(b, ap, 0, 128, lo, hi)

    def pbw(self, i, nelem, p0=0, p1=128):
        b = self.psum_all
        ap = b.handle.ap()[p0:p1, i * 512:i * 512 + nelem]
        lo = i * 2048
        hi = -(-((i * 512 + nelem) * 4) // 2048) * 2048
        return TV(b, ap, 0, 128, lo, hi)



def emit_ln(core, c0, c1, out_tiles, o_off=0, src_tiles=None, pb0=6, pb1=7):
    c = core
    n = c1 - c0
    src = src_tiles if src_tiles is not None else c.resid
    for st0 in range(0, n, 512):
        w = min(512, n - st0)
        pm = c.pbtv(pb0, 0, 1, 0, w)
        pv = c.pbtv(pb1, 0, 1, 0, w)
        for k in range(KT):
            x = src[k].tv(fs=(c0 + st0, c0 + st0 + w)) if isinstance(src[k], Buf) \
                else src[k].tv(c0 + st0, c0 + st0 + w)
            xb = c.ln_xb[k % 2].tv(fs=(0, w))
            sq = c.ln_sqb[k % 2].tv(fs=(0, w))
            c.sop("activation", xb, x, func=AF.Identity)
            c.sop("activation", sq, x, func=AF.Square)
            c.mm(pm, c.ones_bf.tv(), xb, start=(k == 0), stop=(k == KT - 1))
            c.mm(pv, c.ones_bf.tv(), sq, start=(k == 0), stop=(k == KT - 1))
        st = c.ln_stats
        m = st.tv(ps=(0, 1), fs=(0, w))
        v = st.tv(ps=(0, 1), fs=(512, 512 + w))
        r = st.tv(ps=(0, 1), fs=(1024, 1024 + w))
        t = st.tv(ps=(0, 1), fs=(1536, 1536 + w))
        mb = c.ln_bc16.tv(ps=(0, 1), fs=(0, w))
        rb = c.ln_bc16.tv(ps=(0, 1), fs=(512, 512 + w))
        c.vop("tensor_scalar", m, pm, 1.0 / D, None, op0=OP.mult)
        c.vop("tensor_scalar", v, pv, 1.0 / D, None, op0=OP.mult)
        c.vop("tensor_mul", t, m, m)
        c.vop("tensor_sub", v, v, t)
        c.vop("tensor_scalar", t, v, float(EPS), None, op0=OP.add)
        c.sop("activation", v, t, func=AF.Ln)
        c.sop("activation", r, v, func=AF.Exp, scale=-0.5)
        c.vop("tensor_copy", mb, m)
        c.vop("tensor_copy", rb, r)
        bm = c.pbtv(pb0, 0, 128, 0, w)
        bv = c.pbtv(pb1, 0, 128, 0, w)
        c.mm(bm, c.ones_row_bf.tv(ps=(0, 1)), mb, start=True, stop=True)
        c.mm(bv, c.ones_row_bf.tv(ps=(0, 1)), rb, start=True, stop=True)
        for k in range(KT):
            x = src[k].tv(fs=(c0 + st0, c0 + st0 + w)) if isinstance(src[k], Buf) \
                else src[k].tv(c0 + st0, c0 + st0 + w)
            tmp = c.ln_sq[k % 2].tv(fs=(0, w))
            c.vop("tensor_sub", tmp, x, bm)
            c.vop("tensor_mul", out_tiles[k].tv(o_off + st0, o_off + st0 + w), tmp, bv)



def _wslab_dma(core, wb, dramT, m0, nelem, parts=None):
    parts = parts if parts is not None else dramT.handle.shape[1]
    src = TV(dramT, dramT.handle.ap()[m0], 0, parts, 0, 1 << 30)
    core.p.dma("gpsimd", wb.tv(0, nelem, p0=0, p1=parts), src)


def tile_kxm(wT, kt):
    K, M = wT.shape
    mw = min(M, 128)
    nmo = M // mw
    return np.ascontiguousarray(
        wT.reshape(kt, 128, nmo, mw).transpose(2, 1, 0, 3).reshape(nmo, 128, kt * mw))


def _wslab_dma2(core, wb, dramT, m0, nelem, nblk=2):
    src_ap = dramT.handle.ap()[m0:m0 + nblk].transpose([1, 0, 2])
    src = TV(dramT, src_ap, 0, 128, 0, 1 << 30)
    dst = wb.tv(0, nblk * nelem).with_ap(
        wb.tv(0, nblk * nelem).ap.rearrange("p (b n) -> p b n", b=nblk))
    core.p.dma("gpsimd", dst, src)


def _stream_blocks(core, nblk, dma_fn, compute_fn):
    dma_fn(0, 0)
    for m in range(nblk):
        if m + 1 < nblk:
            dma_fn(m + 1, (m + 1) % 2)
        compute_fn(m, m % 2)


def emit_ffn0(core, W, c0, c1, src_tiles=None):
    c = core
    sc = W["sc"]

    def scl(col):
        return sc.tv(fs=(col, col + 1))

    t0 = c0
    while t0 < c1:
        w = min(512, c1 - t0)
        c.amark()
        lnt = c.atile(KT * w, BF16)
        ln = [lnt.sub(k * w) for k in range(KT)]
        hA = c.atile(32 * w, BF16)
        hB = c.atile(32 * w, BF16)
        wsm = [c.atile(1024, BF16) for _ in range(4)]
        wbg = [c.atile(8192, BF16) for _ in range(2)]
        wi8 = [c.atile(8192, mybir.dt.int8)] * 2

        emit_ln(c, t0, t0 + w, ln)

        def dma1(m, par):
            _wslab_dma(c, wsm[par], W["w1T"], m, 1024)
            _wslab_dma(c, wsm[2 + par], W["w2T"], m, 1024)

        def comp1(m, par):
            pa = c.pbtv(m % 4, 0, 128, 0, w)
            pb = c.pbtv(4 + m % 4, 0, 128, 0, w)
            for k in range(KT):
                lt = wsm[par].tv(k * 128, (k + 1) * 128)
                c.mm(pa, lt, ln[k].tv(0, w), start=(k == 0), stop=(k == KT - 1))
            for k in range(KT):
                lt = wsm[2 + par].tv(k * 128, (k + 1) * 128)
                c.mm(pb, lt, ln[k].tv(0, w), start=(k == 0), stop=(k == KT - 1))
            hm = hA.tv(m * w, (m + 1) * w)
            c.sop("activation", hm, pa, func=AF.Silu, scale=scl(m))
            c.vop("scalar_tensor_tensor", hm, pb, scl(32 + m), hm,
                  op0=OP.mult, op1=OP.mult)

        _stream_blocks(c, 32, dma1, comp1)

        def big_phase(dramT, src_h, dst_fn, pbank):
            def dmaf(p, par):
                if p % 2 == 0:
                    _wslab_dma2(c, wbg[par], dramT, 2 * p, 4096)
                else:
                    src_ap = dramT.handle.ap()[2 * p:2 * p + 2].transpose([1, 0, 2])
                    srctv = TV(dramT, src_ap, 0, 128, 0, 1 << 30)
                    dst = wi8[par].tv(0, 8192).with_ap(
                        wi8[par].tv(0, 8192).ap.rearrange("q (b n) -> q b n", b=2))
                    c.p.dma("sync", dst, srctv)
                    c.sop("activation", wbg[par].tv(0, 8192), wi8[par].tv(0, 8192),
                          func=AF.Identity)

            def compf(p, par):
                for i in range(2):
                    m = 2 * p + i
                    ps = c.pbtv(pbank + m % 4, 0, 128, 0, w)
                    for k in range(32):
                        lt = wbg[par].tv(i * 4096 + k * 128, i * 4096 + (k + 1) * 128)
                        c.mm(ps, lt, src_h.tv(k * w, (k + 1) * w),
                             start=(k == 0), stop=(k == 31))
                    dst_fn(m, ps)
            return dmaf, compf

        def dst2(m, ps):
            c.sop("activation", hB.tv(m * w, (m + 1) * w), ps, func=AF.Gelu,
                  scale=scl(64 + m))
        dmaf, compf = big_phase(W["h1T"], hA, dst2, 4)
        _stream_blocks(c, 16, dmaf, compf)

        def dst3(m, ps):
            c.sop("activation", hA.tv(m * w, (m + 1) * w), ps, func=AF.Gelu,
                  scale=scl(96 + m))
        dmaf, compf = big_phase(W["h2T"], hB, dst3, 0)
        _stream_blocks(c, 16, dmaf, compf)

        def dst4(m, ps):
            rs = c.resid[m].tv(fs=(t0, t0 + w))
            c.vop("scalar_tensor_tensor", rs, ps, scl(128 + m), rs,
                  op0=OP.mult, op1=OP.add, extra_reads=[rs])
        dmaf, compf = big_phase(W["w3T"], hA, dst4, 4)
        _stream_blocks(c, 4, dmaf, compf)

        c.areset()
        t0 += w


def prep_ffn0_host(inputs):
    w1 = inputs["L0_ffn_w1_w"]; w2 = inputs["L0_ffn_w2_w"]
    h1 = inputs["L0_ffn_h1_w"]; h2 = inputs["L0_ffn_h2_w"]
    w3 = inputs["L0_ffn_w3_w"]
    out = {
        "ffn0_w1T": tile_kxm(w1.T.astype(np.int8), 8),
        "ffn0_w2T": tile_kxm(w2.T.astype(np.int8), 8),
        "ffn0_h1T": tile_kxm(h1.T.astype(np.int8), 32),
        "ffn0_h2T": tile_kxm(h2.T.astype(np.int8), 32),
        "ffn0_w3T": tile_kxm(w3.T.astype(np.int8), 32),
    }
    assert np.all(w1.T.astype(np.int8).astype(np.float32) == w1.T)
    sc = np.zeros((128, 136), np.float32)
    sc[:, 0:32] = inputs["L0_ffn_w1_s"].reshape(32, 128).T
    sc[:, 32:64] = inputs["L0_ffn_w2_s"].reshape(32, 128).T
    sc[:, 64:96] = inputs["L0_ffn_h1_s"].reshape(32, 128).T
    sc[:, 96:128] = inputs["L0_ffn_h2_s"].reshape(32, 128).T
    sc[:, 128:136] = inputs["L0_ffn_w3_s"].reshape(8, 128).T
    out["ffn0_sc"] = sc
    return out


def declare_ffn0(prog):
    W = {}
    W["w1T"] = prog.dram("ffn0_w1T", [32, 128, 1024], mybir.dt.int8, kind="ExternalInput")
    W["w2T"] = prog.dram("ffn0_w2T", [32, 128, 1024], mybir.dt.int8, kind="ExternalInput")
    W["h1T"] = prog.dram("ffn0_h1T", [32, 128, 4096], mybir.dt.int8, kind="ExternalInput")
    W["h2T"] = prog.dram("ffn0_h2T", [32, 128, 4096], mybir.dt.int8, kind="ExternalInput")
    W["w3T"] = prog.dram("ffn0_w3T", [8, 128, 4096], mybir.dt.int8, kind="ExternalInput")
    W["sc_dram"] = prog.dram("ffn0_sc", [128, 136], F32, kind="ExternalInput")
    W["sc"] = prog.sbuf("ffn0_sc_sb", [128, 136], F32)
    return W


def load_ffn0_consts(core, W):
    core.p.dma("sync", W["sc"].tv(), W["sc_dram"].tv())



DEQ = np.float32(1.0) / np.float32(127.0)


def emit_ffn1(core, W, c0, c1):
    c = core
    S = float(DEQ)
    t0 = c0
    while t0 < c1:
        w = min(512, c1 - t0)
        c.amark()
        lnt = c.atile(KT * w, BF16)
        ln = [lnt.sub(k * w) for k in range(KT)]
        hA = c.atile(24 * w, BF16)
        wsm = [c.atile(2048, BF16) for _ in range(4)]
        wbg = [c.atile(6144, BF16) for _ in range(2)]

        emit_ln(c, t0, t0 + w, ln)

        def dma1(p, par):
            _wslab_dma2(c, wsm[par], W["wgT"], 2 * p, 1024)
            _wslab_dma2(c, wsm[2 + par], W["wuT"], 2 * p, 1024)

        def comp1(p, par):
            for i in range(2):
                m = 2 * p + i
                pa = c.pbtv(m % 4, 0, 128, 0, w)
                pb = c.pbtv(4 + m % 4, 0, 128, 0, w)
                for k in range(KT):
                    c.mm(pa, wsm[par].tv(i * 1024 + k * 128, i * 1024 + (k + 1) * 128),
                         ln[k].tv(0, w), start=(k == 0), stop=(k == KT - 1))
                for k in range(KT):
                    c.mm(pb, wsm[2 + par].tv(i * 1024 + k * 128, i * 1024 + (k + 1) * 128),
                         ln[k].tv(0, w), start=(k == 0), stop=(k == KT - 1))
                hm = hA.tv(m * w, (m + 1) * w)
                c.sop("activation", hm, pa, func=AF.Silu, scale=S)
                c.vop("scalar_tensor_tensor", hm, pb, S, hm, op0=OP.mult, op1=OP.mult)

        _stream_blocks(c, 12, dma1, comp1)

        def dma2(p, par):
            _wslab_dma2(c, wbg[par], W["wdT"], 2 * p, 3072)

        def comp2(p, par):
            for i in range(2):
                m = 2 * p + i
                ps = c.pbtv(4 + m % 4, 0, 128, 0, w)
                for k in range(24):
                    c.mm(ps, wbg[par].tv(i * 3072 + k * 128, i * 3072 + (k + 1) * 128),
                         hA.tv(k * w, (k + 1) * w), start=(k == 0), stop=(k == 23))
                rs = c.resid[m].tv(fs=(t0, t0 + w))
                c.vop("scalar_tensor_tensor", rs, ps, S, rs,
                      op0=OP.mult, op1=OP.add, extra_reads=[rs])

        _stream_blocks(c, 4, dma2, comp2)

        c.areset()
        t0 += w


def prep_ffn1_host(inputs):
    wg = inputs["L1_ffn_w_gate"]; wu = inputs["L1_ffn_w_up"]; wd = inputs["L1_ffn_w_down"]
    return {
        "ffn1_wgT": tile_kxm(wg.T.astype(np.int8), 8),
        "ffn1_wuT": tile_kxm(wu.T.astype(np.int8), 8),
        "ffn1_wdT": tile_kxm(wd.T.astype(np.int8), 24),
    }


def declare_ffn1(prog):
    I8 = mybir.dt.int8
    return {
        "wgT": prog.dram("ffn1_wgT", [24, 128, 1024], I8, kind="ExternalInput"),
        "wuT": prog.dram("ffn1_wuT", [24, 128, 1024], I8, kind="ExternalInput"),
        "wdT": prog.dram("ffn1_wdT", [8, 128, 3072], I8, kind="ExternalInput"),
    }



def emit_attn1a(core, W, c0, c1, rg):
    c = core
    T = c1 - c0
    nst = ceil_div(T, 512)
    ntt = ceil_div(T, 128)
    c.amark()
    lnt = c.atile(KT * T, BF16)
    ln = [lnt.sub(k * T) for k in range(KT)]
    emit_ln(c, c0, c1, ln)
    qfm = c.atile(8 * T, BF16)
    kvin = c.atile(1024, F32)
    bias = W["bias_sb"]

    def bcol(i):
        return bias.tv(fs=(i, i + 1))

    slabs = [c.atile(1024, BF16) for _ in range(6)]

    def dmah(h, par):
        _wslab_dma(c, slabs[3 * par + 0], W["qkv"], h, 1024)
        _wslab_dma(c, slabs[3 * par + 1], W["qkv"], 8 + h, 1024)
        _wslab_dma(c, slabs[3 * par + 2], W["qkv"], 16 + h, 1024)

    c.amark()
    kfm = c.atile(T, BF16)
    vfm = c.atile(T, BF16)
    ktok = c.atile(T, BF16)
    vtok = c.atile(T, BF16)

    def comph(h, par):
        qs, ks, vs = slabs[3 * par], slabs[3 * par + 1], slabs[3 * par + 2]
        for st in range(nst):
            e0, e1 = st * 512, min(T, (st + 1) * 512)
            w = e1 - e0
            ps = c.pbtv(st % 2, 0, 128, 0, w)
            for k in range(KT):
                c.mm(ps, qs.tv(k * 128, (k + 1) * 128), ln[k].tv(e0, e1),
                     start=(k == 0), stop=(k == KT - 1))
            c.sop("activation", qfm.tv(h * T + e0, h * T + e1), ps,
                  func=AF.Identity, bias=bcol(h))
            ps2 = c.pbtv(2 + st % 2, 0, 128, 0, w)
            for k in range(KT):
                c.mm(ps2, ks.tv(k * 128, (k + 1) * 128), ln[k].tv(e0, e1),
                     start=(k == 0), stop=(k == KT - 1))
            c.sop("activation", kfm.tv(e0, e1), ps2, func=AF.Identity, bias=bcol(8 + h))
            ps3 = c.pbtv(4 + st % 2, 0, 128, 0, w)
            for k in range(KT):
                c.mm(ps3, vs.tv(k * 128, (k + 1) * 128), ln[k].tv(e0, e1),
                     start=(k == 0), stop=(k == KT - 1))
            c.sop("activation", vfm.tv(e0, e1), ps3, func=AF.Identity, bias=bcol(16 + h))
        for tt in range(ntt):
            e0, e1 = tt * 128, min(T, (tt + 1) * 128)
            w = e1 - e0
            pt = c.pbt16(6, 0, w, 0, 128)
            c.transpose(pt, kfm.tv(e0, e1))
            c.vop("tensor_copy", ktok.tv(tt * 128, tt * 128 + 128, p0=0, p1=w), pt)
            pt2 = c.pbt16(7, 0, w, 0, 128)
            c.transpose(pt2, vfm.tv(e0, e1))
            c.vop("tensor_copy", vtok.tv(tt * 128, tt * 128 + 128, p0=0, p1=w), pt2)
        pkv = c.pbtv(0, 0, 128, 0, 128)
        for tt in range(ntt):
            e0, e1 = tt * 128, min(T, (tt + 1) * 128)
            w = e1 - e0
            c.mm(pkv, ktok.tv(tt * 128, tt * 128 + 128, p0=0, p1=w),
                 vtok.tv(tt * 128, tt * 128 + 128, p0=0, p1=w),
                 start=(tt == 0), stop=(tt == ntt - 1))
        c.vop("tensor_copy", kvin.tv(h * 128, (h + 1) * 128), pkv)

    _stream_blocks(c, 8, dmah, comph)
    c.areset()

    c.p.dma("sync", W["cc_in"].tv(), kvin.tv())
    c.p.cc("AllReduce", OP.add, rg, W["cc_in"].tv(), W["cc_out"].tv())
    kvb = c.atile(1024, BF16)
    c.p.dma("gpsimd", kvb.tv(), W["cc_out"].tv())

    afm = c.atile(8 * T, BF16)
    for h in range(8):
        for st in range(nst):
            e0, e1 = st * 512, min(T, (st + 1) * 512)
            w = e1 - e0
            ps = c.pbtv((h + st) % 4, 0, 128, 0, w)
            c.mm(ps, kvb.tv(h * 128, (h + 1) * 128), qfm.tv(h * T + e0, h * T + e1),
                 start=True, stop=True)
            c.vop("tensor_copy", afm.tv(h * T + e0, h * T + e1), ps)

    wo = [c.atile(1024, BF16) for _ in range(2)]
    tmp = c.atile(512, F32)

    def dmao(mo, par):
        _wslab_dma(c, wo[par], W["woT"], mo, 1024)

    def compo(mo, par):
        for st in range(nst):
            e0, e1 = st * 512, min(T, (st + 1) * 512)
            w = e1 - e0
            ps = c.pbtv(4 + (mo + st) % 2, 0, 128, 0, w)
            for j in range(8):
                c.mm(ps, wo[par].tv(j * 128, (j + 1) * 128), afm.tv(j * T + e0, j * T + e1),
                     start=(j == 0), stop=(j == 7))
            c.sop("activation", tmp.tv(0, w), ps, func=AF.Identity, bias=bcol(24 + mo))
            rs = c.resid[mo].tv(fs=(c0 + e0, c0 + e1))
            c.vop("tensor_add", rs, rs, tmp.tv(0, w), extra_reads=[rs])

    _stream_blocks(c, 8, dmao, compo)
    c.areset()


def prep_attn1a_host(inputs):
    wqkv = inputs["L1_attn_a_w_qkv"]
    bqkv = inputs["L1_attn_a_b_qkv"]
    wo = inputs["L1_attn_a_w_o"]
    bo = inputs["L1_attn_a_b_o"]
    idx = 8 * np.arange(128)
    slabs = []
    for part in range(3):
        for h in range(8):
            Wh = wqkv[part * 1024 + idx + h, :]
            slabs.append(tile_kxm(np.ascontiguousarray(Wh.T), 8)[0])
    qkv = np.ascontiguousarray(np.stack(slabs).astype(np.float32))
    woT = tile_kxm(np.ascontiguousarray(wo.T), 8).astype(np.float32)
    bias = np.zeros((128, 32), np.float32)
    for part in range(3):
        for h in range(8):
            bias[:, part * 8 + h] = bqkv[part * 1024 + idx + h]
    bias[:, 24:32] = bo.reshape(8, 128).T
    return {"a1_qkv": qkv, "a1_woT": woT, "a1_bias": bias}


def declare_attn1a(prog):
    W = {
        "qkv": prog.dram("a1_qkv", [24, 128, 1024], F32, kind="ExternalInput"),
        "woT": prog.dram("a1_woT", [8, 128, 1024], F32, kind="ExternalInput"),
        "bias_dram": prog.dram("a1_bias", [128, 32], F32, kind="ExternalInput"),
        "bias_sb": prog.sbuf("a1_bias_sb", [128, 32], F32),
        "cc_in": prog.dram("a1_cc_in", [128, 1024], F32),
        "cc_out": prog.dram("a1_cc_out", [128, 1024], F32),
    }
    return W


def load_attn1a_consts(core, W):
    core.p.dma("sync", W["bias_sb"].tv(), W["bias_dram"].tv())



def emit_attn1b(core, W, c0, c1, rg, ranks):
    c = core
    T = c1 - c0
    KW = ranks * T
    nqt = T // 128
    nstr = KW // 512
    c.amark()
    lnt = c.atile(KT * T, BF16)
    ln = [lnt.sub(k * T) for k in range(KT)]
    emit_ln(c, c0, c1, ln)
    bias = W["bias_sb"]

    def bcol(i):
        return bias.tv(fs=(i, i + 1))

    c.vop("tensor_scalar", bias.tv(fs=(3, 4)), bias.tv(fs=(0, 1)), 0.125, None,
          op0=OP.mult)

    qfm = c.atile(T, BF16)
    kfm = c.atile(T, BF16)
    vtok = c.atile((T // 128) * 64, BF16)
    qkv_sb = [c.atile(512, BF16) for _ in range(3)]
    for i in range(3):
        _wslab_dma(c, qkv_sb[i], W["qkvw"], i, 512)
    for st in range(T // 512):
        e0, e1 = st * 512, (st + 1) * 512
        ps = c.pbtv(st % 2, 0, 64, 0, 512)
        for k in range(KT):
            c.mm(ps, qkv_sb[0].tv(k * 64, (k + 1) * 64), ln[k].tv(e0, e1),
                 start=(k == 0), stop=(k == KT - 1))
        c.sop("activation", qfm.tv(e0, e1, p0=0, p1=64), ps, func=AF.Identity,
              bias=bias.tv(ps=(0, 64), fs=(3, 4)), scale=0.125)
        ps2 = c.pbtv(2 + st % 2, 0, 64, 0, 512)
        for k in range(KT):
            c.mm(ps2, qkv_sb[1].tv(k * 64, (k + 1) * 64), ln[k].tv(e0, e1),
                 start=(k == 0), stop=(k == KT - 1))
        c.sop("activation", kfm.tv(e0, e1, p0=0, p1=64), ps2, func=AF.Identity,
              bias=bias.tv(ps=(0, 64), fs=(1, 2)))
    for tt in range(T // 128):
        ps = c.pbtv(4 + tt % 2, 0, 128, 0, 64)
        for k in range(KT):
            c.mm(ps, ln[k].tv(tt * 128, (tt + 1) * 128), qkv_sb[2].tv(k * 64, (k + 1) * 64),
                 start=(k == 0), stop=False)
        c.mm(ps, c.ones_row_bf.tv(ps=(0, 1)).with_ap(
            c.ones_row_bf.handle.ap()[0:1, 0:128]), W["bvrow_sb"].tv(ps=(0, 1)),
            start=False, stop=True)
        c.vop("tensor_copy", vtok.tv(tt * 64, (tt + 1) * 64), ps)
    c.p.dma("sync", W["cck_in"].tv(), kfm.tv(p0=0, p1=64))
    c.p.dma("sync", W["ccv_in"].tv().with_ap(
        W["ccv_in"].handle.ap().rearrange("(tt p) c -> p tt c", p=128)),
        vtok.tv().with_ap(vtok.tv().ap.rearrange("p (tt c) -> p tt c", c=64)))
    c.p.cc("AllGather", OP.bypass, rg, W["cck_in"].tv(), W["cck_out"].tv())
    c.p.cc("AllGather", OP.bypass, rg, W["ccv_in"].tv(), W["ccv_out"].tv())
    kfull = c.atile(KW, BF16)
    vfull = c.atile((KW // 128) * 64, BF16)
    c.p.dma("sync", kfull.tv(p0=0, p1=64).with_ap(
        kfull.tv(p0=0, p1=64).ap.rearrange("c (r t) -> c r t", r=ranks)),
        W["cck_out"].tv().with_ap(
            W["cck_out"].handle.ap().rearrange("(r c) t -> c r t", c=64)))
    c.p.dma("sync", vfull.tv().with_ap(
        vfull.tv().ap.rearrange("p (tt c) -> p tt c", c=64)),
        W["ccv_out"].tv().with_ap(
            W["ccv_out"].handle.ap().rearrange("(tt p) c -> p tt c", p=128)))

    stats = c.atile(3 * nqt + 4, F32)
    attn_tok = c.atile(nqt * 64, BF16)
    maskb = [c.atile(KW, BF16) for _ in range(2)]
    psb = [c.atile(512, BF16) for _ in range(2)]
    ptb = [c.atile(128, BF16) for _ in range(2)]

    for qt in range(nqt):
        run = stats.tv(qt, qt + 1)
        c.vop("memset", run, -3.0e38)
        c.p.dma("sync", maskb[qt % 2].tv(),
                TV(W["mask"], W["mask"].handle.ap()[qt], 0, 128, 0, 1 << 30))
        for st in range(nstr):
            ps = c.pbtv(st % 2, 0, 128, 0, 512)
            c.mm(ps, qfm.tv(qt * 128, (qt + 1) * 128, p0=0, p1=64),
                 kfull.tv(st * 512, (st + 1) * 512, p0=0, p1=64),
                 start=True, stop=False)
            c.mm(ps, c.ident.tv(), maskb[qt % 2].tv(st * 512, (st + 1) * 512),
                 start=False, stop=True)
            sm = stats.tv(3 * nqt + st % 2, 3 * nqt + st % 2 + 1)
            c.vop("tensor_reduce", sm, ps, AX.X, OP.max)
            c.vop("tensor_max", run, run, sm)
        nm = stats.tv(2 * nqt + qt, 2 * nqt + qt + 1)
        c.vop("tensor_scalar", nm, run, -1.0, None, op0=OP.mult)
        ls = stats.tv(nqt + qt, nqt + qt + 1)
        c.vop("memset", ls, 0.0)
        po = c.pbtv(6 + qt % 2, 0, 128, 0, 64)
        first = True
        for st in range(nstr):
            ps = c.pbtv(2 + st % 2, 0, 128, 0, 512)
            c.mm(ps, qfm.tv(qt * 128, (qt + 1) * 128, p0=0, p1=64),
                 kfull.tv(st * 512, (st + 1) * 512, p0=0, p1=64),
                 start=True, stop=False)
            c.mm(ps, c.ident.tv(), maskb[qt % 2].tv(st * 512, (st + 1) * 512),
                 start=False, stop=True)
            tmpc = stats.tv(3 * nqt + 2 + st % 2, 3 * nqt + 2 + st % 2 + 1)
            c.sop("activation", psb[qt % 2].tv(), ps, func=AF.Exp, bias=nm,
                  accum_out=tmpc)
            c.vop("tensor_add", ls, ls, tmpc)
            for c2 in range(4):
                pt = c.pbt16(4 + c2 % 2, 0, 128, 0, 128)
                c.transpose(pt, psb[qt % 2].tv(c2 * 128, (c2 + 1) * 128))
                c.vop("tensor_copy", ptb[(st * 4 + c2) % 2].tv(), pt)
                kc = st * 4 + c2
                c.mm(po, ptb[(st * 4 + c2) % 2].tv(), vfull.tv(kc * 64, (kc + 1) * 64),
                     start=first, stop=(st == nstr - 1 and c2 == 3))
                first = False
        rl = stats.tv(3 * nqt + 2 + qt % 2, 3 * nqt + 2 + qt % 2 + 1)
        c.vop("reciprocal", rl, ls)
        c.vop("tensor_scalar", attn_tok.tv(qt * 64, (qt + 1) * 64), po, rl, None,
              op0=OP.mult)
    afm = c.atile(T, BF16)
    for qt in range(nqt):
        pt = c.pbt16(0, 0, 64, 0, 128)
        c.transpose(pt, attn_tok.tv(qt * 64, (qt + 1) * 64))
        c.vop("tensor_copy", afm.tv(qt * 128, (qt + 1) * 128, p0=0, p1=64), pt)
    wo_sb = [c.atile(128, BF16) for _ in range(2)]
    tmp = c.atile(512, F32)

    def dmao(mo, par):
        _wslab_dma(c, wo_sb[par], W["woT"], mo, 128)

    def compo(mo, par):
        for st in range(T // 512):
            e0, e1 = st * 512, (st + 1) * 512
            ps = c.pbtv(2 + (mo + st) % 2, 0, 128, 0, 512)
            c.mm(ps, wo_sb[par].tv(0, 128, p0=0, p1=64),
                 afm.tv(e0, e1, p0=0, p1=64), start=True, stop=True)
            c.sop("activation", tmp.tv(0, 512), ps, func=AF.Identity,
                  bias=bcol(8 + mo))
            rs = c.resid[mo].tv(fs=(c0 + e0, c0 + e1))
            c.vop("tensor_add", rs, rs, tmp.tv(0, 512), extra_reads=[rs])

    _stream_blocks(c, 8, dmao, compo)
    c.areset()


def prep_attn1b_host(inputs, ranks=4, T=NT):
    wq = inputs["L1_attn_b_wq"]; wk = inputs["L1_attn_b_wk"]; wv = inputs["L1_attn_b_wv"]
    wo = inputs["L1_attn_b_wo"]
    qkvw = np.stack([tile_kxm(np.ascontiguousarray(w), 8)[0]
                     for w in (wq, wk, wv)]).astype(np.float32)
    woT = np.stack([np.ascontiguousarray(wo[:, mo * 128:(mo + 1) * 128])
                    for mo in range(8)]).astype(np.float32)
    bias = np.zeros((128, 16), np.float32)
    bias[0:64, 0] = inputs["L1_attn_b_bq"]
    bias[0:64, 1] = inputs["L1_attn_b_bk"]
    bias[0:64, 2] = inputs["L1_attn_b_bv"]
    bias[:, 8:16] = inputs["L1_attn_b_bo"].reshape(8, 128).T
    bvrow = inputs["L1_attn_b_bv"].reshape(1, 64).astype(np.float32)
    out = {"a1b_qkvw": qkvw, "a1b_woT": woT, "a1b_bias": bias, "a1b_bvrow": bvrow}
    import ml_dtypes
    nqt = T // 128
    KW = ranks * T
    kidx = np.arange(KW)[None, None, :]
    for r in range(ranks):
        sbase = r * T
        qidx = sbase + (np.arange(nqt) * 128)[:, None, None] + np.arange(128)[None, :, None]
        m = np.where(kidx > qidx, np.float32(-1e30), np.float32(0.0))
        out[f"a1b_mask_r{r}"] = m.astype(ml_dtypes.bfloat16)
    return out


def declare_attn1b(prog, ranks=4, T=NT):
    nqt = T // 128
    KW = ranks * T
    return {
        "qkvw": prog.dram("a1b_qkvw", [3, 128, 512], F32, kind="ExternalInput"),
        "woT": prog.dram("a1b_woT", [8, 64, 128], F32, kind="ExternalInput"),
        "bias_dram": prog.dram("a1b_bias", [128, 16], F32, kind="ExternalInput"),
        "bias_sb": prog.sbuf("a1b_bias_sb", [128, 16], F32),
        "bvrow_dram": prog.dram("a1b_bvrow", [1, 64], F32, kind="ExternalInput"),
        "bvrow_sb": prog.sbuf("a1b_bvrow_sb", [1, 64], BF16),
        "mask": prog.dram("a1b_mask", [nqt, 128, KW], BF16, kind="ExternalInput"),
        "cck_in": prog.dram("a1b_cck_in", [64, T], BF16),
        "cck_out": prog.dram("a1b_cck_out", [ranks * 64, T], BF16),
        "ccv_in": prog.dram("a1b_ccv_in", [T, 64], BF16),
        "ccv_out": prog.dram("a1b_ccv_out", [ranks * T, 64], BF16),
    }


def load_attn1b_consts(core, W):
    core.p.dma("sync", W["bias_sb"].tv(), W["bias_dram"].tv())
    core.p.dma("gpsimd", W["bvrow_sb"].tv(ps=(0, 1)), W["bvrow_dram"].tv(ps=(0, 1)))



def emit_mamba(core, W, rg, nprev=3):
    c = core
    T = c.NT
    NR = c.NRES
    nst = T // 512
    sel = W["sel_sb"]
    c.amark()

    for k in range(KT):
        rs = c.resid[k].tv(fs=(0, HALO_M))
        c.vop("tensor_scalar", rs, rs, sel.tv(fs=(3, 4)), None, op0=OP.mult,
              extra_reads=[rs])

    c.sop("activation", W["A_sb"].tv(), W["A_dram_sb"].tv(), func=AF.Exp)
    c.vop("tensor_scalar", W["A_sb"].tv(), W["A_sb"].tv(), -1.0, None,
          op0=OP.mult, extra_reads=[W["A_sb"].tv()])

    c.amark()
    lnm = c.atile(KT * NR, BF16)
    ln = [lnm.sub(k * NR) for k in range(KT)]
    emit_ln(c, 0, NR, ln)
    c.amark()
    xcpre = c.atile(NR, F32)
    cacc = c.atile(T, F32)
    wxs = [c.atile(1024, BF16) for _ in range(2)]

    def dmx(m, par):
        _wslab_dma(c, wxs[par], W["wxT"], m, 1024)

    def cpx(m, par):
        for st in range(ceil_div(NR, 512)):
            e0, e1 = st * 512, min(NR, (st + 1) * 512)
            ps = c.pbtv(st % 4, 0, 128, 0, e1 - e0)
            for k in range(KT):
                c.mm(ps, wxs[par].tv(k * 128, (k + 1) * 128), ln[k].tv(e0, e1),
                     start=(k == 0), stop=(k == KT - 1))
            c.vop("tensor_copy", xcpre.tv(e0, e1), ps)
        cw = W["conv_sb"]
        c.vop("tensor_scalar", cacc.tv(), xcpre.tv(0, T),
              cw.tv(fs=(m * 4, m * 4 + 1)), None, op0=OP.mult)
        for kk in range(1, 4):
            c.vop("scalar_tensor_tensor", cacc.tv(), xcpre.tv(kk, T + kk),
                  cw.tv(fs=(m * 4 + kk, m * 4 + kk + 1)), cacc.tv(),
                  op0=OP.mult, op1=OP.add, extra_reads=[cacc.tv()])
        xcm = c.atile(T, BF16)
        c.sop("activation", xcm.tv(), cacc.tv(), func=AF.Silu,
              bias=W["convb_sb"].tv(fs=(m, m + 1)))
        c.p.dma("sync", TV(W["xc_dram"], W["xc_dram"].handle.ap()[m], 0, 128,
                           m * T * 2, (m + 1) * T * 2), xcm.tv())
        return xcm

    dmx(0, 0)
    for m in range(16):
        if m + 1 < 16:
            dmx(m + 1, (m + 1) % 2)
        c.amark()
        cpx(m, m % 2)
        c.areset()
    c.areset()
    c.areset()

    c.amark()
    dlr = c.atile(T, BF16)
    Bc = c.atile(T, BF16)
    Cc = c.atile(T, BF16)
    xps = [c.atile(16 * 64, BF16), c.atile(16 * 16, BF16), c.atile(16 * 16, BF16)]
    _wslab_dma(c, xps[0], W["xpdT"], 0, 16 * 64)
    _wslab_dma(c, xps[1], W["xpbT"], 0, 16 * 16)
    _wslab_dma(c, xps[2], W["xpcT"], 0, 16 * 16)
    xck = c.atile(T, BF16)
    for st in range(nst):
        e0, e1 = st * 512, (st + 1) * 512
        psd = c.pbtv(st % 2, 0, 64, 0, 512)
        psb = c.pbtv(2 + st % 2, 0, 16, 0, 512)
        psc = c.pbtv(4 + st % 2, 0, 16, 0, 512)
        for k in range(16):
            src = TV(W["xc_dram"], W["xc_dram"].handle.ap()[k][:, e0:e1], 0, 128,
                     k * T * 2 + e0 * 2, k * T * 2 + e1 * 2)
            c.p.dma("sync", xck.tv(e0, e1), src)
            c.mm(psd, xps[0].tv(k * 64, (k + 1) * 64), xck.tv(e0, e1),
                 start=(k == 0), stop=(k == 15))
            c.mm(psb, xps[1].tv(k * 16, (k + 1) * 16), xck.tv(e0, e1),
                 start=(k == 0), stop=(k == 15))
            c.mm(psc, xps[2].tv(k * 16, (k + 1) * 16), xck.tv(e0, e1),
                 start=(k == 0), stop=(k == 15))
        c.vop("tensor_copy", dlr.tv(e0, e1, p0=0, p1=64), psd)
        c.vop("tensor_copy", Bc.tv(e0, e1, p0=0, p1=16), psb)
        c.vop("tensor_copy", Cc.tv(e0, e1, p0=0, p1=16), psc)

    dts = c.atile(16 * 128, BF16)
    for mo in range(16):
        src = TV(W["dtT"], W["dtT"].handle.ap()[mo], 0, 64, 0, 1 << 30)
        c.p.dma("gpsimd", dts.tv(mo * 128, (mo + 1) * 128, p0=0, p1=64), src)
    selw = c.atile(16 * 128, BF16)
    c.p.dma("gpsimd", selw.tv(p0=0, p1=16), W["sel16_dram"].tv(ps=(0, 16)))

    hfin = c.atile(256, F32)
    atot = c.atile(256, F32)
    hin = c.atile(256, F32)

    def compute_delta(m, dbuf, scratch, xcbuf=None):
        if xcbuf is not None:
            srcx = TV(W["xc_dram"], W["xc_dram"].handle.ap()[m], 0, 128,
                      m * T * 2, (m + 1) * T * 2)
            c.p.dma("sync", xcbuf.tv(), srcx)
        for st in range(nst):
            e0, e1 = st * 512, (st + 1) * 512
            ps = c.pbtv(6 + st % 2, 0, 128, 0, 512)
            c.mm(ps, dts.tv(m * 128, (m + 1) * 128, p0=0, p1=64),
                 dlr.tv(e0, e1, p0=0, p1=64), start=True, stop=True)
            c.sop("activation", dbuf.tv(e0, e1), ps, func=AF.Identity,
                  bias=W["dtb_sb"].tv(fs=(m, m + 1)))
        dabs = scratch
        c.sop("activation", dabs.tv(), dbuf.tv(), func=AF.Abs)
        c.sop("activation", dabs.tv(), dabs.tv(), func=AF.Exp, scale=-1.0)
        c.sop("activation", dabs.tv(), dabs.tv(), func=AF.Ln, bias=1.0)
        c.vop("tensor_relu", dbuf.tv(), dbuf.tv(), extra_reads=[dbuf.tv()])
        c.vop("tensor_add", dbuf.tv(), dbuf.tv(), dabs.tv(), extra_reads=[dbuf.tv()])
        return dabs

    def bc_bcast(bank0, srcbc, n):
        for st in range(nst):
            ps = c.pbtv(bank0 + st, 0, 128, 0, 512)
            c.mm(ps, selw.tv(n * 128, (n + 1) * 128, p0=0, p1=16),
                 srcbc.tv(st * 512, (st + 1) * 512, p0=0, p1=16),
                 start=True, stop=True)
        return c.pbw(bank0, T)

    def acol(m, n):
        return W["A_sb"].tv(fs=(m * 16 + n, m * 16 + n + 1))

    hfin = c.atile(256, F32)
    atot = c.atile(256, F32)
    hin = c.atile(256, F32)

    for m in range(16):
        c.amark()
        dbuf = c.atile(T, F32)
        scratch = c.atile(T, F32)
        xcb = c.atile(T, BF16)
        dx = compute_delta(m, dbuf, scratch, xcb)
        c.vop("tensor_mul", dx.tv(), dbuf.tv(), xcb.tv())
        a = [c.atile(T, F32) for _ in range(2)]
        b = [c.atile(T, BF16) for _ in range(2)]
        H = [c.atile(T, BF16) for _ in range(2)]
        g2 = [c.atile(T, BF16) for _ in range(2)]
        Bb = [c.atile(T, BF16) for _ in range(2)]
        Cb = [c.atile(T, BF16) for _ in range(2)]
        dxb = c.atile(T, BF16)
        c.vop("tensor_copy", dxb.tv(), dx.tv())
        y = scratch
        cdT = c.atile(1, F32)
        c.vop("tensor_reduce", cdT.tv(), dbuf.tv(), AX.X, OP.add)
        for n in range(16):
            pp = n % 2
            pB = bc_bcast((n % 2) * 4, Bc, n)
            c.sop("activation", Bb[pp].tv(), pB, func=AF.Identity)
            pC = bc_bcast(((n + 1) % 2) * 4, Cc, n)
            c.sop("activation", Cb[pp].tv(), pC, func=AF.Identity)
            c.sop("activation", a[pp].tv(), dbuf.tv(), func=AF.Exp, scale=acol(m, n))
            c.sop("activation", atot.tv(m * 16 + n, m * 16 + n + 1), cdT.tv(),
                  func=AF.Exp, scale=acol(m, n))
            c.vop("tensor_mul", b[pp].tv(), dxb.tv(), Bb[pp].tv())
            c.vop("tensor_tensor_scan", H[pp].tv(), a[pp].tv(), b[pp].tv(), 0.0,
                  op0=OP.mult, op1=OP.add)
            c.vop("tensor_copy", hfin.tv(m * 16 + n, m * 16 + n + 1),
                  H[pp].tv(T - 1, T))
            if n == 0:
                c.vop("tensor_mul", y.tv(), H[pp].tv(), Cb[pp].tv())
            else:
                c.vop("tensor_mul", g2[pp].tv(), H[pp].tv(), Cb[pp].tv())
                c.vop("tensor_add", y.tv(), y.tv(), g2[pp].tv(),
                      extra_reads=[y.tv()])
        c.vop("scalar_tensor_tensor", y.tv(), xcb.tv(), W["D_sb"].tv(fs=(m, m + 1)),
              y.tv(), op0=OP.mult, op1=OP.add, extra_reads=[y.tv()])
        yb = c.atile(T, BF16)
        c.vop("tensor_copy", yb.tv(), y.tv())
        c.p.dma("sync", TV(W["y_dram"], W["y_dram"].handle.ap()[m], 0, 128,
                           m * T * 2, (m + 1) * T * 2), yb.tv())
        c.areset()

    c.p.dma("sync", TV(W["cc_in"], W["cc_in"].handle.ap()[:, 0:256], 0, 128, 0, 1024),
            hfin.tv())
    c.p.dma("sync", TV(W["cc_in"], W["cc_in"].handle.ap()[:, 256:512], 0, 128, 1024, 2048),
            atot.tv())
    c.p.cc("AllGather", OP.bypass, rg, W["cc_in"].tv(), W["cc_out"].tv())
    acc = c.atile(256, F32)
    tmpc = c.atile(512, F32)
    c.vop("memset", acc.tv(), 0.0)
    c.vop("memset", hin.tv(), 0.0)
    for pr in range(nprev):
        blk = TV(W["cc_out"], W["cc_out"].handle.ap()[pr * 128:(pr + 1) * 128, :],
                 0, 128, 0, 1 << 30)
        c.p.dma("sync", tmpc.tv(), blk)
        c.vop("tensor_mul", acc.tv(), acc.tv(), tmpc.tv(256, 512),
              extra_reads=[acc.tv()])
        c.vop("tensor_add", acc.tv(), acc.tv(), tmpc.tv(0, 256),
              extra_reads=[acc.tv()])
        c.vop("tensor_scalar", tmpc.tv(0, 256), acc.tv(), sel.tv(fs=(pr, pr + 1)),
              None, op0=OP.mult)
        c.vop("tensor_add", hin.tv(), hin.tv(), tmpc.tv(0, 256),
              extra_reads=[hin.tv()])

    for m in range(16):
        c.amark()
        dbuf = c.atile(T, F32)
        scratch = c.atile(T, F32)
        compute_delta(m, dbuf, scratch)
        cd = c.atile(T, F32)
        ones_bc = c.ones_f32.tv().with_ap(
            c.ones_f32.tv().ap.broadcast_to((128, T)))
        c.vop("tensor_tensor_scan", cd.tv(), ones_bc, dbuf.tv(), 0.0,
              op0=OP.mult, op1=OP.add)
        P = [c.atile(T, BF16) for _ in range(2)]
        Cb2 = [c.atile(T, BF16) for _ in range(2)]
        acc2 = c.atile(T, BF16)
        yb = c.atile(T, BF16)
        c.p.dma("scalar", yb.tv(), TV(W["y_dram"], W["y_dram"].handle.ap()[m],
                                      0, 128, m * T * 2, (m + 1) * T * 2))
        for n in range(16):
            pp = n % 2
            pC = bc_bcast((n % 2) * 4, Cc, n)
            c.sop("activation", Cb2[pp].tv(), pC, func=AF.Identity)
            c.sop("activation", P[pp].tv(), cd.tv(), func=AF.Exp, scale=acol(m, n))
            c.vop("tensor_mul", P[pp].tv(), P[pp].tv(), Cb2[pp].tv(),
                  extra_reads=[P[pp].tv()])
            if n == 0:
                c.vop("tensor_scalar", acc2.tv(), P[pp].tv(),
                      hin.tv(m * 16 + n, m * 16 + n + 1), None, op0=OP.mult)
            else:
                c.vop("scalar_tensor_tensor", acc2.tv(), P[pp].tv(),
                      hin.tv(m * 16 + n, m * 16 + n + 1), acc2.tv(),
                      op0=OP.mult, op1=OP.add, extra_reads=[acc2.tv()])
        c.vop("tensor_add", yb.tv(), acc2.tv(), yb.tv(), extra_reads=[yb.tv()])
        c.p.dma("sync", TV(W["y_dram"], W["y_dram"].handle.ap()[m], 0, 128,
                           m * T * 2, (m + 1) * T * 2), yb.tv())
        c.areset()

    c.amark()
    ln2 = c.atile(KT * 512, BF16)
    g = c.atile(16 * 512, BF16)
    sz = c.atile(512, BF16)
    yb2 = c.atile(512, BF16)
    wxs2 = [c.atile(1024, BF16) for _ in range(2)]
    wos = [c.atile(2048, BF16) for _ in range(2)]
    for st in range(nst):
        e0 = HALO_M + st * 512
        lns = [ln2.sub(k * 512) for k in range(KT)]
        emit_ln(c, e0, e0 + 512, lns)
        _wslab_dma(c, wxs2[0], W["wxT"], 16, 1024)
        for mz in range(16):
            if mz + 1 < 16:
                _wslab_dma(c, wxs2[(mz + 1) % 2], W["wxT"], 16 + mz + 1, 1024)
            ps = c.pbtv(mz % 4, 0, 128, 0, 512)
            for k in range(KT):
                c.mm(ps, wxs2[mz % 2].tv(k * 128, (k + 1) * 128), lns[k].tv(0, 512),
                     start=(k == 0), stop=(k == KT - 1))
            c.sop("activation", sz.tv(), ps, func=AF.Silu)
            src = TV(W["y_dram"], W["y_dram"].handle.ap()[mz][:, st * 512:(st + 1) * 512],
                     0, 128, st * 1024, (st + 1) * 1024)
            c.p.dma("scalar", yb2.tv(), src)
            c.vop("tensor_mul", g.tv(mz * 512, (mz + 1) * 512), yb2.tv(), sz.tv())
        _wslab_dma(c, wos[0], W["outT"], 0, 2048)
        for mo in range(8):
            if mo + 1 < 8:
                _wslab_dma(c, wos[(mo + 1) % 2], W["outT"], mo + 1, 2048)
            ps = c.pbtv(4 + mo % 2, 0, 128, 0, 512)
            for k in range(16):
                c.mm(ps, wos[mo % 2].tv(k * 128, (k + 1) * 128),
                     g.tv(k * 512, (k + 1) * 512), start=(k == 0), stop=(k == 15))
            rs = c.resid[mo].tv(fs=(e0, e0 + 512))
            c.vop("scalar_tensor_tensor", rs, ps, 1.0, rs, op0=OP.mult, op1=OP.add,
                  extra_reads=[rs])
    c.areset()
    c.areset()


def prep_mamba_host(inputs, rank_sel):
    wx = inputs["L0_attn_b_in_proj_w"]
    xp = inputs["L0_attn_b_x_proj_w"]
    out = {
        "mam_wxT": tile_kxm(np.ascontiguousarray(wx), 8).astype(np.float32),
        "mam_xpdT": tile_kxm(np.ascontiguousarray(xp[:, 0:64]), 16).astype(np.float32),
        "mam_xpbT": tile_kxm(np.ascontiguousarray(xp[:, 64:80]), 16).astype(np.float32),
        "mam_xpcT": tile_kxm(np.ascontiguousarray(xp[:, 80:96]), 16).astype(np.float32),
        "mam_dtT": np.ascontiguousarray(
            inputs["L0_attn_b_dt_proj_w"].reshape(64, 16, 128).transpose(1, 0, 2)
        ).astype(np.float32),
        "mam_conv": np.ascontiguousarray(
            inputs["L0_attn_b_conv_w"][:, 0, :].reshape(16, 128, 4).transpose(1, 0, 2)
            .reshape(128, 64)).astype(np.float32),
        "mam_convb": inputs["L0_attn_b_conv_b"].reshape(16, 128).T.astype(np.float32),
        "mam_dtb": inputs["L0_attn_b_dt_proj_b"].reshape(16, 128).T.astype(np.float32),
        "mam_Alog": np.ascontiguousarray(
            inputs["L0_attn_b_A_log"].reshape(16, 128, 16).transpose(1, 0, 2)
            .reshape(128, 256)).astype(np.float32),
        "mam_D": inputs["L0_attn_b_D"].reshape(16, 128).T.astype(np.float32),
        "mam_outT": tile_kxm(np.ascontiguousarray(inputs["L0_attn_b_out_proj_w"]), 16
                             ).astype(np.float32),
    }
    sel16 = np.zeros((16, 16 * 128), np.float32)
    for n in range(16):
        sel16[n, n * 128:(n + 1) * 128] = 1.0
    out["mam_sel16"] = sel16
    selm = np.zeros((128, 8), np.float32)
    for pr in range(3):
        selm[:, pr] = 1.0 if rank_sel == pr + 1 else 0.0
    selm[:, 3] = 0.0 if rank_sel == 0 else 1.0
    out["mam_sel"] = selm
    return out


def declare_mamba(prog, T=NT, ranks=4):
    W = {
        "wxT": prog.dram("mam_wxT", [32, 128, 1024], F32, kind="ExternalInput"),
        "xpdT": prog.dram("mam_xpdT", [1, 128, 1024], F32, kind="ExternalInput"),
        "xpbT": prog.dram("mam_xpbT", [1, 128, 256], F32, kind="ExternalInput"),
        "xpcT": prog.dram("mam_xpcT", [1, 128, 256], F32, kind="ExternalInput"),
        "dtT": prog.dram("mam_dtT", [16, 64, 128], F32, kind="ExternalInput"),
        "conv_dram": prog.dram("mam_conv", [128, 64], F32, kind="ExternalInput"),
        "convb_dram": prog.dram("mam_convb", [128, 16], F32, kind="ExternalInput"),
        "dtb_dram": prog.dram("mam_dtb", [128, 16], F32, kind="ExternalInput"),
        "Alog_dram": prog.dram("mam_Alog", [128, 256], F32, kind="ExternalInput"),
        "D_dram": prog.dram("mam_D", [128, 16], F32, kind="ExternalInput"),
        "outT": prog.dram("mam_outT", [8, 128, 2048], F32, kind="ExternalInput"),
        "sel16_dram": prog.dram("mam_sel16", [16, 2048], F32, kind="ExternalInput"),
        "sel_dram": prog.dram("mam_sel", [128, 8], F32, kind="ExternalInput"),
        "xc_dram": prog.dram("mam_xc", [16, 128, T], BF16),
        "y_dram": prog.dram("mam_y", [16, 128, T], BF16),
        "cc_in": prog.dram("mam_cc_in", [128, 512], F32),
        "cc_out": prog.dram("mam_cc_out", [ranks * 128, 512], F32),
        "conv_sb": prog.sbuf("mam_conv_sb", [128, 64], F32),
        "convb_sb": prog.sbuf("mam_convb_sb", [128, 16], F32),
        "dtb_sb": prog.sbuf("mam_dtb_sb", [128, 16], F32),
        "A_dram_sb": prog.sbuf("mam_Alog_sb", [128, 256], F32),
        "A_sb": prog.sbuf("mam_A_sb", [128, 256], F32),
        "D_sb": prog.sbuf("mam_D_sb", [128, 16], F32),
        "sel_sb": prog.sbuf("mam_sel_sb", [128, 8], F32),
    }
    return W


def load_mamba_consts(core, W):
    p = core.p
    p.dma("sync", W["conv_sb"].tv(), W["conv_dram"].tv())
    p.dma("sync", W["convb_sb"].tv(), W["convb_dram"].tv())
    p.dma("sync", W["dtb_sb"].tv(), W["dtb_dram"].tv())
    p.dma("sync", W["A_dram_sb"].tv(), W["Alog_dram"].tv())
    p.dma("sync", W["D_sb"].tv(), W["D_dram"].tv())
    p.dma("sync", W["sel_sb"].tv(), W["sel_dram"].tv())



def emit_attn0a(core, W, nquarters=None):
    c = core
    NEXT = c.NEXT
    NQ = c.NT + 128
    ns = NQ // 32
    if nquarters is None:
        nquarters = ns // 4
    nsq = ns // nquarters
    bias = W["bias_sb"]

    def bcol(i, p1=128):
        return bias.tv(ps=(0, p1), fs=(i, i + 1))

    for k in range(KT):
        src = TV(W["x0e"], W["x0e"].handle.ap()[k * 128:(k + 1) * 128,
                                                RES_OFF:RES_OFF + c.NRES],
                 k * 128, (k + 1) * 128, RES_OFF * 4, (RES_OFF + c.NRES) * 4)
        c.p.dma("sync", c.resid[k].tv(fs=(0, c.NRES)), src)

    c.amark()
    posD = c.atile(3040, BF16)
    maskD = c.atile(3040, BF16)
    c.amark()
    relsb = c.atile(64, BF16)
    wpossb = c.atile(16, BF16)
    c.p.dma("gpsimd", relsb.tv(p0=0, p1=16), W["relT"].tv(ps=(0, 16)))
    c.p.dma("gpsimd", wpossb.tv(p0=0, p1=16), W["wpos"].tv(ps=(0, 16)))
    posf = c.atile(16, F32)
    pps = c.pbtv(0, 0, 64, 0, 16)
    c.mm(pps, relsb.tv(p0=0, p1=16), wpossb.tv(p0=0, p1=16), start=True, stop=True)
    c.vop("tensor_copy", posf.tv(p0=0, p1=64), pps)
    c.p.dma("sync", W["stageP"].tv(ps=(0, 64)), posf.tv(p0=0, p1=64))
    posfT = c.atile(64, F32)
    c.p.dma("sync", posfT.tv(p0=0, p1=16),
            TV(W["stageP"], W["stageP"].handle.ap().transpose([1, 0]), 0, 64, 0, 1 << 30),
            allow_slow_non_contiguous=True)
    posP = c.atile(128, F32)
    c.vop("memset", posP.tv(), 0.0)
    for h in range(8):
        c.p.dma("sync", posP.tv(32, 96, p0=h * 16, p1=(h + 1) * 16),
                posfT.tv(0, 64, p0=0, p1=16))
    pp = posP.tv(32, 96)
    c.vop("tensor_scalar", pp, pp, bcol(W["BFB"]), None, op0=OP.add,
          extra_reads=[pp])
    mext = c.atile(128, BF16)
    c.vop("memset", mext.tv(), -1.0e30)
    c.vop("memset", mext.tv(32, 96), 0.0)
    for dt in range(32):
        c.vop("tensor_copy", posD.tv(dt * 95, dt * 95 + 95),
              posP.tv(32 - dt, 127 - dt))
        c.vop("tensor_copy", maskD.tv(dt * 95, dt * 95 + 95),
              mext.tv(32 - dt, 127 - dt))
    c.areset()
    mb = maskD.tv(p0=0, p1=8)
    c.vop("tensor_scalar", mb, mb, bcol(17, 8), None, op0=OP.add,
          extra_reads=[mb])
    wfbd = c.atile(256, BF16)
    c.p.dma("gpsimd", wfbd.tv(0, 128), TV(W["wfbd"], W["wfbd"].handle.ap()[0], 0, 128, 0, 4))
    c.p.dma("gpsimd", wfbd.tv(128, 256), TV(W["wfbd"], W["wfbd"].handle.ap()[1], 0, 128, 4, 8))
    wbd = c.atile(16, BF16)
    c.p.dma("gpsimd", wbd.tv(0, 16), W["wbonddmg_dram"].tv())
    bvrow = c.atile(1024, BF16)
    c.p.dma("gpsimd", bvrow.tv(p0=0, p1=1), W["bvalrow"].tv(ps=(0, 1)))

    em = [c.atile(3040, BF16)] * 2
    lg = [c.atile(1520, F32) for _ in range(2)]
    wdt = [c.atile(1520, F32)] * 2
    wn = [c.atile(1520, BF16)] * 2
    wT = [c.atile(512, BF16)] * 2
    stag = [c.atile(1024, BF16)] * 2
    atok = [c.atile(1024, BF16)] * 2
    z1 = [c.atile(3040, BF16) for _ in range(2)]
    zbd = z1
    smst = [c.atile(32, F32)] * 2
    ctmp = [c.atile(128, F32)] * 2
    vsl8 = c.atile(8 * 1024, BF16)
    cs8 = c.atile(8 * 1024, BF16)
    dsl = c.atile(2 * 1024, BF16)
    for g in range(2):
        _wslab_dma(c, dsl.sub(g * 1024), W["dispT"], g, 1024)
    for k in range(KT):
        _wslab_dma(c, vsl8.sub(k * 1024), W["valT"], k, 1024)
        _wslab_dma(c, cs8.sub(k * 1024), W["cprojT"], k, 1024)

    WH = 191
    assert nsq == 4 and NEXT - 128 * (nquarters - 1) == WH
    lne_s = [c.atile(WH, BF16) for _ in range(KT)]
    disp_s = c.atile(WH, BF16)
    qt = []
    for par in range(2):
        qt.append(dict(
            lne=lne_s,
            Pb=[c.atile(WH, F32) for _ in range(2)],
            Pd=[c.atile(WH, F32) for _ in range(2)],
            disp=disp_s,
            vtok=c.atile(2 * 1024, BF16),
        ))
    afm_all = c.atile(KT * nsq * 32, BF16)

    def preamble(q):
        t = qt[q % 2]
        e_lo = 128 * q
        e_hi = e_lo + WH
        lne, Pb, Pd, disp, vtok = (t["lne"], t["Pb"], t["Pd"], t["disp"],
                                   t["vtok"])
        c.amark()
        x0t = [c.atile(WH, F32) for _ in range(KT)]
        for k in range(KT):
            src = TV(W["x0e"], W["x0e"].handle.ap()[k * 128:(k + 1) * 128,
                                                    e_lo:e_hi],
                     k * 128, (k + 1) * 128, e_lo * 4, e_hi * 4)
            c.p.dma("sync", x0t[k].tv(), src)
        emit_ln(c, 0, WH, lne, src_tiles=x0t, pb0=0, pb1=1)
        c.areset()
        for g in range(2):
            ps = c.pbtv(g, 0, 128, 0, WH)
            for k in range(KT):
                c.mm(ps, dsl.tv(g * 1024 + k * 128, g * 1024 + (k + 1) * 128),
                     lne[k].tv(0, WH), start=(k == 0), stop=(k == KT - 1))
            c.sop("activation", disp.tv(0, WH), ps, func=AF.Identity,
                  bias=bcol(W["BDISP"] + g))
            psb = c.pbtv(2 + g, 0, 128, 0, WH)
            c.mm(psb, wfbd.tv(0, 128), disp.tv(0, WH), start=True, stop=True)
            c.vop("tensor_copy", Pb[g].tv(0, WH), psb)
            psd = c.pbtv(4 + g, 0, 128, 0, WH)
            c.mm(psd, wfbd.tv(128, 256), disp.tv(0, WH), start=True, stop=True)
            c.vop("tensor_copy", Pd[g].tv(0, WH), psd)
        for tt in range(2):
            a0 = tt * 128
            a1 = min(WH, a0 + 128)
            wq = a1 - a0
            for nh in range(2):
                ps = c.pbtv(nh, 0, wq, 0, 512)
                for k in range(KT):
                    c.mm(ps, lne[k].tv(a0, a1),
                         vsl8.tv(k * 1024 + nh * 512, k * 1024 + (nh + 1) * 512),
                         start=(k == 0), stop=False)
                c.mm(ps, c.ones_row_bf.tv(ps=(0, 1), fs=(0, wq)),
                     bvrow.tv(nh * 512, (nh + 1) * 512, p0=0, p1=1),
                     start=False, stop=True)
                c.vop("tensor_copy",
                      vtok.tv(tt * 1024 + nh * 512, tt * 1024 + (nh + 1) * 512,
                              p0=0, p1=wq), ps)

    def sub_head(q, li):
        Pb, Pd = qt[q % 2]["Pb"], qt[q % 2]["Pd"]
        i0 = q * nsq
        e_lo = 128 * q
        gi = i0 + li
        pr = li % 2
        qe = 63 + 32 * gi - e_lo
        klo = 32 * gi - e_lo
        if gi < 6:
            c.p.dma("scalar", em[pr].tv(p0=0, p1=8),
                    TV(W["em"], W["em"].handle.ap()[gi], 0, 8, 0, 1 << 30))
        for g in range(2):
            for c2 in range(2):
                cp = (li * 4 + g * 2 + c2) % 2
                dt0 = c2 * 16
                pbs = Pb[g].tv(klo, klo + 95).with_ap(
                    Pb[g].tv(klo, klo + 95).ap.unsqueeze(1)
                    .broadcast_to((128, 16, 95)))
                pbt = Pb[g].tv(qe + dt0, qe + dt0 + 16).with_ap(
                    Pb[g].tv(qe + dt0, qe + dt0 + 16).ap.unsqueeze(2)
                    .broadcast_to((128, 16, 95)))
                z1v = z1[cp].tv(0, 1520).with_ap(
                    z1[cp].tv(0, 1520).ap.rearrange("p (t u) -> p t u", u=95))
                c.gop("tensor_sub", z1v, pbs, pbt)
                c.vop("tensor_add", z1[cp].tv(0, 1520), z1[cp].tv(0, 1520),
                      posD.tv(dt0 * 95, (dt0 + 16) * 95),
                      extra_reads=[z1[cp].tv(0, 1520)])
                pds = Pd[g].tv(klo, klo + 95).with_ap(
                    Pd[g].tv(klo, klo + 95).ap.unsqueeze(1)
                    .broadcast_to((128, 16, 95)))
                pdt = Pd[g].tv(qe + dt0, qe + dt0 + 16).with_ap(
                    Pd[g].tv(qe + dt0, qe + dt0 + 16).ap.unsqueeze(2)
                    .broadcast_to((128, 16, 95)))
                z2v = z1[cp].tv(1520, 3040).with_ap(
                    z1[cp].tv(1520, 3040).ap.rearrange("p (t u) -> p t u", u=95))
                c.gop("tensor_sub", z2v, pds, pdt)
                c.sop("activation", zbd[cp].tv(0, 1520), z1[cp].tv(0, 1520),
                      func=AF.Gelu)
                c.sop("activation", zbd[cp].tv(1520, 3040), z1[cp].tv(1520, 3040),
                      func=AF.Gelu, bias=bcol(W["BFD"]))
                pwb = c.pbw(0, 1520, 0, 8)
                pwd = c.pbw(3, 1520, 0, 8)
                for sb in range(3):
                    s0, s1 = sb * 512, min(1520, (sb + 1) * 512)
                    psl = c.pbtv(sb, 0, 8, 0, s1 - s0)
                    c.mm(psl, wbd.tv(0, 8), zbd[cp].tv(s0, s1), start=True, stop=False)
                    c.mm(psl, c.ident.tv(ps=(0, 8), fs=(0, 8)),
                        maskD.tv(dt0 * 95 + s0, dt0 * 95 + s1, p0=0, p1=8),
                        start=False, stop=(gi >= 6))
                    if gi < 6:
                        c.mm(psl, c.ident.tv(ps=(0, 8), fs=(0, 8)),
                            em[pr].tv(dt0 * 95 + s0, dt0 * 95 + s1, p0=0, p1=8),
                            start=False, stop=True)
                    psl2 = c.pbtv(3 + sb, 0, 8, 0, s1 - s0)
                    c.mm(psl2, wbd.tv(8, 16), zbd[cp].tv(1520 + s0, 1520 + s1), start=True, stop=True)
                c.sop("activation", lg[cp].tv(p0=0, p1=8), pwd, func=AF.Tanh,
                      bias=bcol(16, 8), scale=0.5)
                c.vop("scalar_tensor_tensor", lg[cp].tv(p0=0, p1=8), lg[cp].tv(p0=0, p1=8),
                      -5.0, pwb, op0=OP.mult, op1=OP.add,
                      extra_reads=[lg[cp].tv(p0=0, p1=8)])
                dst = TV(W["stage"], W["stage"].handle.ap()[pr][g * 8:(g + 1) * 8,
                                                           dt0 * 95:(dt0 + 16) * 95],
                         g * 8, (g + 1) * 8, pr * 50000 + dt0 * 380,
                         pr * 50000 + (dt0 + 16) * 380)
                c.p.dma("sync", dst, lg[cp].tv(p0=0, p1=8))

    def sub_tail(q, li):
        vtok = qt[q % 2]["vtok"]
        i0 = q * nsq
        e_lo = 128 * q
        tt0 = q
        gi = i0 + li
        pr = li % 2
        klo = 32 * gi - e_lo
        src = TV(W["stage"], W["stage"].handle.ap()[pr]
                 .rearrange("h (t u) -> h t u", u=95).transpose([1, 0, 2]),
                 0, 16, pr * 50000, pr * 50000 + 49000)
        c.p.dma("sync", wdt[pr].tv(p0=0, p1=32).with_ap(
            wdt[pr].tv(p0=0, p1=32).ap.rearrange("p (h u) -> p h u", u=95)), src)
        w3 = wdt[pr].tv(p0=0, p1=32).with_ap(
            wdt[pr].tv(p0=0, p1=32).ap.rearrange("p (h u) -> p h u", u=95))
        nm = smst[pr].tv(0, 16, p0=0, p1=32)
        c.vop("tensor_reduce", nm, w3, AX.X, OP.max, negate=True)
        nmb = nm.with_ap(nm.ap.unsqueeze(2).broadcast_to((32, 16, 95)))
        c.vop("tensor_add", w3, w3, nmb, extra_reads=[wdt[pr].tv(p0=0, p1=32)])
        c.sop("activation", wn[pr].tv(p0=0, p1=32), wdt[pr].tv(p0=0, p1=32),
              func=AF.Exp)
        wnv = wn[pr].tv(p0=0, p1=32).with_ap(
            wn[pr].tv(p0=0, p1=32).ap.rearrange("p (h u) -> p h u", u=95))
        sm = smst[pr].tv(16, 32, p0=0, p1=32)
        c.vop("tensor_reduce", sm, wnv, AX.X, OP.add)
        c.vop("reciprocal", sm, sm, extra_reads=[sm])
        for h in range(16):
            pt = c.pbt16(6, 0, 95, h * 32, (h + 1) * 32)
            c.transpose(pt, wn[pr].tv(h * 95, (h + 1) * 95, p0=0, p1=32))
        c.vop("tensor_copy", wT[pr].tv(p0=0, p1=95), c.pbt16(6, 0, 95, 0, 512))
        abs0 = e_lo + klo
        ta = abs0 // 128 - tt0
        r0 = abs0 % 128
        n1 = min(95, 128 - r0)
        c.p.dma("scalar", stag[pr].tv(p0=0, p1=n1),
                vtok.tv(ta * 1024, (ta + 1) * 1024, p0=r0, p1=r0 + n1))
        if n1 < 95:
            c.p.dma("scalar", stag[pr].tv(p0=n1, p1=95),
                    vtok.tv((ta + 1) * 1024, (ta + 2) * 1024, p0=0, p1=95 - n1))
        po = c.pbw(6, 1024, 0, 32)
        for h in range(16):
            pslice = c.pbtv(6 + h // 8, 0, 32, (h % 8) * 64, (h % 8) * 64 + 64)
            c.mm(pslice, wT[pr].tv(h * 32, (h + 1) * 32, p0=0, p1=95),
                 stag[pr].tv(h * 64, (h + 1) * 64, p0=0, p1=95),
                 start=True, stop=True)
        po3 = po.with_ap(po.ap.rearrange("p (h f) -> p h f", f=64))
        smb = sm.with_ap(sm.ap.unsqueeze(2).broadcast_to((32, 16, 64)))
        at3 = atok[pr].tv(p0=0, p1=32).with_ap(
            atok[pr].tv(p0=0, p1=32).ap.rearrange("p (h f) -> p h f", f=64))
        c.vop("tensor_mul", at3, po3, smb)
        if "dbg" in W and gi == W.get("dbg_gi", -1):
            c.p.dma("gpsimd", W["dbg"].tv(ps=(0, 32)), atok[pr].tv(p0=0, p1=32))
            c.p.dma("gpsimd", W["dbg2"].tv(ps=(0, 32)), wn[pr].tv(p0=0, p1=32))
            c.p.dma("gpsimd", W["dbg3"].tv(ps=(0, 95)), stag[pr].tv(p0=0, p1=95))
            c.p.dma("gpsimd", W["dbg4"].tv(), vtok.tv(0, 1024))
        for ft in range(KT):
            pt = c.pbt16(6, 0, 128, ft * 32, (ft + 1) * 32)
            c.transpose(pt, atok[pr].tv(ft * 128, (ft + 1) * 128, p0=0, p1=32))
        afull = afm_all.tv(0, KT * nsq * 32)
        dst3 = afull.with_ap(afull.ap.rearrange("p (f x) -> p f x", f=KT)
                             [:, :, li * 32:(li + 1) * 32])
        s2 = c.pbt16(6, 0, 128, 0, KT * 32)
        src3 = s2.with_ap(s2.ap.rearrange("p (f u) -> p f u", f=KT))
        c.vop("tensor_copy", dst3, src3)

    def cproj(q):
        i0, i1 = q * nsq, (q + 1) * nsq
        r0 = max(0, 63 + 32 * i0 - RES_OFF)
        r1 = min(c.NRES, 63 + 32 * i1 - RES_OFF)
        a0 = r0 + RES_OFF - 63 - 32 * i0
        for mo in range(KT):
            ps = c.pbtv(6 + mo % 2, 0, 128, 0, r1 - r0)
            for k in range(KT):
                c.mm(ps, cs8.tv(mo * 1024 + k * 128, mo * 1024 + (k + 1) * 128),
                     afm_all.tv(k * nsq * 32 + a0, k * nsq * 32 + a0 + r1 - r0),
                     start=(k == 0), stop=(k == KT - 1))
            tmp = ctmp[mo % 2].tv(0, r1 - r0)
            c.sop("activation", tmp, ps, func=AF.Identity,
                  bias=bcol(W["BCPROJ"] + mo))
            rs = c.resid[mo].tv(fs=(r0, r1))
            c.vop("tensor_add", rs, rs, tmp, extra_reads=[rs])

    preamble(0)
    for q in range(nquarters):
        sub_head(q, 0)
        sub_head(q, 1)
        sub_tail(q, 0)
        sub_head(q, 2)
        sub_tail(q, 1)
        sub_head(q, 3)
        if q + 1 < nquarters:
            preamble(q + 1)
        sub_tail(q, 2)
        sub_tail(q, 3)
        cproj(q)
    c.areset()


A0_BDISP, A0_BFD, A0_BFB, A0_BBOND, A0_BDMG, A0_BVAL, A0_BCPROJ = 0, 2, 3, 4, 5, 6, 8


def prep_attn0a_host(inputs, seq_start, T=NT):
    import ml_dtypes
    Wd = inputs["L0_attn_a_W_disp"]; Wv = inputs["L0_attn_a_W_val"]
    Wc = inputs["L0_attn_a_W_cproj"]; Wf = inputs["L0_attn_a_W_fused"]
    out = {
        "a0_dispT": tile_kxm(np.ascontiguousarray(Wd), 8).astype(np.float32),
        "a0_valT": np.ascontiguousarray(Wv.reshape(8, 128, 1024)).astype(np.float32),
        "a0_cprojT": tile_kxm(np.ascontiguousarray(Wc), 8).astype(np.float32),
        "a0_relT": np.ascontiguousarray(inputs["L0_attn_a_rel_pos_emb"].T).astype(np.float32),
        "a0_wpos": inputs["L0_attn_a_W_pos"].astype(np.float32),
        "a0_bvalrow": inputs["L0_attn_a_b_val"].reshape(1, 1024).astype(np.float32),
    }
    wfbd = np.zeros((2, 128, 128), np.float32)
    for s in range(2):
        for h in range(8):
            wfbd[s, h * 16:(h + 1) * 16, h * 16:(h + 1) * 16] = Wf[:, s * 16:(s + 1) * 16]
    out["a0_wfbd"] = wfbd
    wbd = np.zeros((128, 16), np.float32)
    for h in range(8):
        wbd[h * 16:(h + 1) * 16, h] = inputs["L0_attn_a_W_bond"][:, 0]
        wbd[h * 16:(h + 1) * 16, 8 + h] = inputs["L0_attn_a_W_dmg"][:, 0]
    out["a0_wbonddmg"] = wbd
    bias = np.zeros((128, 16), np.float32)
    bias[:, A0_BDISP] = inputs["L0_attn_a_b_disp"][0:128]
    bias[:, A0_BDISP + 1] = inputs["L0_attn_a_b_disp"][128:256]
    bias[:, A0_BFD] = np.tile(inputs["L0_attn_a_b_fused"][16:32], 8)
    bias[:, A0_BFB] = np.tile(inputs["L0_attn_a_b_fused"][0:16], 8)
    bias[:, A0_BBOND] = inputs["L0_attn_a_b_bond"][0]
    bias[:, A0_BDMG] = inputs["L0_attn_a_b_dmg"][0]
    for mo in range(8):
        bias[:, A0_BCPROJ + mo] = inputs["L0_attn_a_b_cproj"][mo * 128:(mo + 1) * 128]
    out["a0_bias"] = bias
    em = np.zeros((6, 8, 3040), np.float32)
    for gi in range(6):
        for dt in range(32):
            u = np.arange(95)
            kk = seq_start - 191 + 32 * gi + u
            em[gi, :, dt * 95:(dt + 1) * 95] = np.where(kk < 0, -1.0e30, 0.0)
    out["a0_em"] = em.astype(ml_dtypes.bfloat16)
    return out


def declare_attn0a(prog, T=NT):
    NEXTl = T + 191
    W = {
        "x0e": prog.dram("a0_x0e", [1024, NEXTl], F32, kind="ExternalInput"),
        "dispT": prog.dram("a0_dispT", [2, 128, 1024], F32, kind="ExternalInput"),
        "valT": prog.dram("a0_valT", [8, 128, 1024], F32, kind="ExternalInput"),
        "cprojT": prog.dram("a0_cprojT", [8, 128, 1024], F32, kind="ExternalInput"),
        "relT": prog.dram("a0_relT", [16, 64], F32, kind="ExternalInput"),
        "wpos": prog.dram("a0_wpos", [16, 16], F32, kind="ExternalInput"),
        "wfbd": prog.dram("a0_wfbd", [2, 128, 128], F32, kind="ExternalInput"),
        "wbonddmg_dram": prog.dram("a0_wbonddmg", [128, 16], F32, kind="ExternalInput"),
        "bvalrow": prog.dram("a0_bvalrow", [1, 1024], F32, kind="ExternalInput"),
        "bias_dram": prog.dram("a0_bias", [128, 16], F32, kind="ExternalInput"),
        "em": prog.dram("a0_em", [6, 8, 3040], BF16, kind="ExternalInput"),
        "stage": prog.dram("a0_stage", [2, 16, 3040], F32),
        "cprojT16": prog.dram("a0_cprojT16", [8, 128, 1024], BF16),
        "stageP": prog.dram("a0_stageP", [64, 16], F32),
        "bias_sb": prog.sbuf("a0_bias_sb", [128, 18], F32),
        "BDISP": A0_BDISP, "BFD": A0_BFD, "BFB": A0_BFB, "BBOND": A0_BBOND,
        "BDMG": A0_BDMG, "BCPROJ": A0_BCPROJ,
    }
    return W


def load_attn0a_consts(core, W):
    core.p.dma("sync", W["bias_sb"].tv(fs=(0, 16)), W["bias_dram"].tv())
    core.vop("tensor_scalar", W["bias_sb"].tv(fs=(16, 17)),
             W["bias_sb"].tv(fs=(A0_BDMG, A0_BDMG + 1)), 0.5, None, op0=OP.mult)
    core.vop("tensor_scalar", W["bias_sb"].tv(fs=(17, 18)),
             W["bias_sb"].tv(fs=(A0_BBOND, A0_BBOND + 1)), -5.0, None, op0=OP.add)




SUBLAYERS = ["a0a", "ffn0a", "mam", "ffn0b", "a1a", "ffn1a", "a1b", "ffn1b"]

GROUPS = [SUBLAYERS]


def _declare_for(prog, names, nt, ranks):
    D_ = {}
    if "a0a" in names:
        D_["a0a"] = declare_attn0a(prog, T=nt)
    if "ffn0a" in names or "ffn0b" in names:
        D_["ffn0"] = declare_ffn0(prog)
    if "mam" in names:
        D_["mam"] = declare_mamba(prog, T=nt, ranks=ranks)
    if "a1a" in names:
        D_["a1a"] = declare_attn1a(prog)
    if "ffn1a" in names or "ffn1b" in names:
        D_["ffn1"] = declare_ffn1(prog)
    if "a1b" in names:
        D_["a1b"] = declare_attn1b(prog, ranks=ranks, T=nt)
    return D_


def _load_consts_for(core, D_):
    if "a0a" in D_:
        load_attn0a_consts(core, D_["a0a"])
    if "ffn0" in D_:
        load_ffn0_consts(core, D_["ffn0"])
    if "mam" in D_:
        load_mamba_consts(core, D_["mam"])
    if "a1a" in D_:
        load_attn1a_consts(core, D_["a1a"])
    if "a1b" in D_:
        load_attn1b_consts(core, D_["a1b"])


def build_group(names, nt, rg, ranks, first, last):
    nc = bass.Bass("TRN2", target_bir_lowering=False, debug=False)
    prog = Prog(nc)
    nres = nt + HALO_M
    if not first:
        rin = prog.dram("rin", [1024, nres], F32, kind="ExternalInput")
    rout = prog.dram("rout", [1024, nt if last else nres], F32, kind="ExternalOutput")
    D_ = _declare_for(prog, names, nt, ranks)
    core = Core(nc, prog, nt=nt)
    core.init_consts()
    _load_consts_for(core, D_)
    if not first:
        for k in range(KT):
            src = TV(rin, rin.handle.ap()[k * 128:(k + 1) * 128, :],
                     k * 128, (k + 1) * 128, 0, 1 << 30)
            prog.dma("sync", core.resid[k].tv(fs=(0, nres)), src)
    for nm in names:
        prog.phase = nm
        if nm == "a0a":
            emit_attn0a(core, D_["a0a"])
        elif nm == "ffn0a":
            emit_ffn0(core, D_["ffn0"], 0, nres)
        elif nm == "ffn0b":
            emit_ffn0(core, D_["ffn0"], HALO_M, nres)
        elif nm == "mam":
            emit_mamba(core, D_["mam"], rg)
        elif nm == "a1a":
            emit_attn1a(core, D_["a1a"], HALO_M, nres, rg)
        elif nm == "ffn1a" or nm == "ffn1b":
            emit_ffn1(core, D_["ffn1"], HALO_M, nres)
        elif nm == "a1b":
            emit_attn1b(core, D_["a1b"], HALO_M, nres, rg, ranks)
    prog.phase = "out"
    c0 = HALO_M if last else 0
    for k in range(KT):
        dst = TV(rout, rout.handle.ap()[k * 128:(k + 1) * 128, :],
                 k * 128, (k + 1) * 128, 0, 1 << 30)
        prog.dma("sync", dst, core.resid[k].tv(fs=(c0, nres)))
    prog.emit_all()
    return nc, prog


def prep_host_all(inputs, nt=NT, ncores=8, per_batch=4):
    shared = {}
    shared.update(prep_ffn0_host(inputs))
    shared.update(prep_ffn1_host(inputs))
    shared.update(prep_attn1a_host(inputs))
    ab = prep_attn1b_host(inputs, ranks=per_batch, T=nt)
    masks = {r: ab.pop(f"a1b_mask_r{r}") for r in range(per_batch)}
    shared.update(ab)
    mam_shared = None
    x = inputs["L0_attn_a_x"]
    B = x.shape[0]
    percore = []
    for c in range(ncores):
        b, r = c // per_batch, c % per_batch
        s = r * nt
        pc = {}
        mh = prep_mamba_host(inputs, rank_sel=r)
        a0 = prep_attn0a_host(inputs, seq_start=s, T=nt)
        if mam_shared is None:
            mam_shared = {k: v for k, v in mh.items() if k != "mam_sel"}
            shared.update(mam_shared)
            shared.update({k: v for k, v in a0.items() if k != "a0_em"})
        pc["mam_sel"] = mh["mam_sel"]
        pc["a0_em"] = a0["a0_em"]
        pc["a1b_mask"] = masks[r]
        x0e = np.zeros((nt + 191, 1024), np.float32)
        lo = s - 191
        src_lo = max(0, lo)
        x0e[src_lo - lo:] = x[b, src_lo:s + nt]
        pc["a0_x0e"] = np.ascontiguousarray(x0e.T)
        percore.append(pc)
    return shared, percore


def kernel(**inputs):
    nt = NT
    ncores = 8
    per_batch = 4
    rg = [[0, 1, 2, 3], [4, 5, 6, 7]]
    shared, percore = prep_host_all(inputs, nt, ncores, per_batch)
    x = inputs["L0_attn_a_x"]
    B, S, Dm = x.shape
    state = [None] * ncores
    n_groups = len(GROUPS)
    for gi, names in enumerate(GROUPS):
        first = gi == 0
        last = gi == n_groups - 1
        nc, prog = build_group(names, nt, rg, per_batch, first, last)
        needed = {b.name for b in prog.bufs
                  if b.space == "dram" and b.handle is not None}
        req = set()
        for alloc in nc.m.functions[0].allocations:
            try:
                if alloc.kind == "ExternalInput":
                    req.add(alloc.memorylocations[0].name)
            except AttributeError:
                pass
        in_maps = []
        for c in range(ncores):
            m = {}
            for k2 in req:
                if k2 == "partition_id":
                    continue
                if k2 == "rin":
                    m[k2] = state[c]
                elif k2 in percore[c]:
                    m[k2] = percore[c][k2]
                elif k2 in shared:
                    m[k2] = shared[k2]
                else:
                    raise KeyError(f"missing input {k2}")
            in_maps.append(m)
        res = run_bass_kernel_spmd(nc, in_maps, core_ids=list(range(ncores)))
        for c in range(ncores):
            state[c] = res.results[c]["rout"]
    out = np.empty((B, S, Dm), np.float32)
    for c in range(ncores):
        b, r = c // per_batch, c % per_batch
        out[b, r * nt:(r + 1) * nt] = state[c].T
    return out


# version 27
# speedup vs baseline: 1.0461x; 1.0140x over previous
import numpy as np

import concourse.bass as bass
import concourse.mybir as mybir
from concourse.bass_utils import run_bass_kernel_spmd

F32 = mybir.dt.float32
BF16 = mybir.dt.bfloat16
I32 = mybir.dt.int32
AF = mybir.ActivationFunctionType
OP = mybir.AluOpType
AX = mybir.AxisListType

ENGINES = ("sync", "scalar", "vector", "gpsimd", "tensor")



class Buf:

    __slots__ = ("name", "handle", "space", "whole", "writes", "reads", "dtype")

    def __init__(self, name, handle, space, dtype, whole=False):
        self.name = name
        self.handle = handle
        self.space = space
        self.dtype = dtype
        self.whole = whole
        self.writes = []
        self.reads = []

    def tv(self, ps=None, fs=None, elem_bytes=None):
        eb = elem_bytes or np.dtype(mybir.dt.np(self.dtype)).itemsize
        p0, p1 = ps if ps is not None else (0, self.handle.shape[0])
        if fs is None:
            ap = self.handle.ap()[p0:p1, :]
            b0, b1 = 0, self.handle.shape[1] * eb
        else:
            e0, e1 = fs
            ap = self.handle.ap()[p0:p1, e0:e1]
            b0, b1 = e0 * eb, e1 * eb
        return TV(self, ap, p0, p1, b0, b1)

    def whole_tv(self):
        return self.tv()


class TV:

    __slots__ = ("buf", "ap", "p0", "p1", "b0", "b1")

    def __init__(self, buf, ap, p0, p1, b0, b1):
        self.buf = buf
        self.ap = ap
        self.p0 = p0
        self.p1 = p1
        self.b0 = b0
        self.b1 = b1

    def with_ap(self, ap):
        return TV(self.buf, ap, self.p0, self.p1, self.b0, self.b1)

    def rearrange(self, s, **kw):
        return self.with_ap(self.ap.rearrange(s, **kw))

    def bitcast(self, dt):
        return self.with_ap(self.ap.bitcast(dt))


class Opr:
    __slots__ = ("engine", "emit", "deps", "kind", "eidx", "didx", "ccidx",
                 "needs_inc", "ms", "tag", "phase")

    def __init__(self, engine, emit, kind, eidx, didx, ccidx, tag):
        self.engine = engine
        self.emit = emit
        self.kind = kind
        self.eidx = eidx
        self.didx = didx
        self.ccidx = ccidx
        self.deps = {}
        self.needs_inc = False
        self.ms = None
        self.tag = tag


def _overlap(a, b):
    return not (a[1] <= b[0] or b[1] <= a[0])


class Prog:
    def __init__(self, nc):
        self.nc = nc
        self.phase = "init"
        self.ops = {e: [] for e in ENGINES}
        self.dma_count = {e: 0 for e in ENGINES}
        self.cc_count = 0
        self.bufs = []
        self.n_ops = 0
        self.boundaries = {e: [] for e in ENGINES}
        self.gates = {e: set() for e in ENGINES}
        self.cc_boundaries = []
        self.cc_gates = set()

    def _dep_value(self, d):
        if d.kind == "c":
            return None
        if d.kind == "d":
            import bisect
            bl = self.boundaries[d.engine]
            need = d.didx + 1
            i = bisect.bisect_left(bl, need)
            if i < len(bl):
                return ("d", d.engine, bl[i])
            b = self.dma_count[d.engine]
            assert b >= need
            bl.append(b)
            self.gates[d.engine].add(b)
            return ("d", d.engine, b)
        import bisect
        need = d.ccidx + 1
        i = bisect.bisect_left(self.cc_boundaries, need)
        if i < len(self.cc_boundaries):
            return ("cc", None, self.cc_boundaries[i])
        b = self.cc_count
        assert b >= need
        self.cc_boundaries.append(b)
        self.cc_gates.add(b)
        return ("cc", None, b)

    def sbuf(self, name, shape, dtype):
        h = self.nc.alloc_sbuf_tensor(name, list(shape), dtype)
        b = Buf(name, h, "sbuf", dtype)
        self.bufs.append(b)
        return b

    def psum(self, name, shape=(128, 512), dtype=F32):
        h = self.nc.alloc_psum_tensor(name, list(shape), dtype)
        b = Buf(name, h, "psum", dtype, whole=True)
        self.bufs.append(b)
        return b

    def dram(self, name, shape, dtype, kind="Internal", addr_space="Local"):
        h = self.nc.dram_tensor(name, list(shape), dtype, kind=kind,
                                addr_space=addr_space)
        b = Buf(name, h, "dram", dtype)
        self.bufs.append(b)
        return b

    def _regions(self, tvs):
        out = []
        for tv in tvs:
            if tv is None:
                continue
            if isinstance(tv, Buf):
                tv = tv.whole_tv()
            b = tv.buf
            if b.whole:
                out.append((b, 0, 128, 0, 1 << 30))
            else:
                out.append((b, tv.p0, tv.p1, tv.b0, tv.b1))
        return out

    def op(self, engine, emit, reads=(), writes=(), kind="c", tag=None):
        didx = ccidx = None
        if kind == "d":
            didx = self.dma_count[engine]
        elif kind == "cc":
            ccidx = self.cc_count
        o = Opr(engine, emit, kind, len(self.ops[engine]), didx, ccidx, tag)
        o.phase = self.phase
        rr = self._regions(reads)
        ww = self._regions(writes)
        dep_set = set()
        for (b, p0, p1, b0, b1) in rr:
            for (w, wp0, wp1, wb0, wb1) in b.writes:
                if w is not o and _overlap((p0, p1), (wp0, wp1)) and _overlap((b0, b1), (wb0, wb1)):
                    dep_set.add(w)
        for (b, p0, p1, b0, b1) in ww:
            for (w, wp0, wp1, wb0, wb1) in b.writes:
                if _overlap((p0, p1), (wp0, wp1)) and _overlap((b0, b1), (wb0, wb1)):
                    dep_set.add(w)
            for (r, rp0, rp1, rb0, rb1) in b.reads:
                if _overlap((p0, p1), (rp0, rp1)) and _overlap((b0, b1), (rb0, rb1)):
                    dep_set.add(r)
        for d in dep_set:
            if d.engine == "tensor" and engine == "tensor" and d.kind == "c" and kind == "c":
                continue
            o.deps[d] = self._dep_value(d)
        if kind == "d":
            self.dma_count[engine] += 1
        elif kind == "cc":
            self.cc_count += 1
        for (b, p0, p1, b0, b1) in ww:
            b.writes = [wacc for wacc in b.writes
                        if not (p0 <= wacc[1] and wacc[2] <= p1 and b0 <= wacc[3] and wacc[4] <= b1)]
            b.reads = [racc for racc in b.reads
                       if not (p0 <= racc[1] and racc[2] <= p1 and b0 <= racc[3] and racc[4] <= b1)]
            b.writes.append((o, p0, p1, b0, b1))
        for (b, p0, p1, b0, b1) in rr:
            b.reads.append((o, p0, p1, b0, b1))
        self.ops[engine].append(o)
        self.n_ops += 1
        return o

    def dma(self, engine, out_tv, in_tv, tag=None, **kw):
        oap = out_tv.ap if isinstance(out_tv, TV) else out_tv
        iap = in_tv.ap if isinstance(in_tv, TV) else in_tv

        def emit(eng, _o=oap, _i=iap, _k=dict(kw)):
            return eng.dma_start(out=_o, in_=_i, **_k)

        reads = [in_tv] if isinstance(in_tv, TV) else []
        writes = [out_tv] if isinstance(out_tv, TV) else []
        return self.op(engine, emit, reads=reads, writes=writes, kind="d", tag=tag)

    def cc(self, kind, aluop, replica_groups, in_tv, out_tv, tag=None):
        iap = in_tv.ap if isinstance(in_tv, TV) else in_tv
        oap = out_tv.ap if isinstance(out_tv, TV) else out_tv

        def emit(eng, _k=kind, _op=aluop, _rg=replica_groups, _i=iap, _o=oap):
            return eng.collective_compute(_k, _op, replica_groups=_rg,
                                          ins=[_i.opt()], outs=[_o.opt()])

        reads = [in_tv] if isinstance(in_tv, TV) else []
        writes = [out_tv] if isinstance(out_tv, TV) else []
        return self.op("gpsimd", emit, reads=reads, writes=writes, kind="cc", tag=tag)

    ROT_MS = 30000
    ROT_DMA = 1800
    ROT_CC = 30000

    def emit_all(self):
        nc = self.nc
        for e in ENGINES:
            for o in self.ops[e]:
                for d in o.deps:
                    d.needs_inc = True
        ms_count = {e: 0 for e in ENGINES}
        for e in ENGINES:
            for o in self.ops[e]:
                if o.kind == "c":
                    if o.needs_inc:
                        ms_count[e] += 1
                        o.ms = ms_count[e]
                elif o.kind == "d":
                    o.ms = 16 * (o.didx + 1)
                    o.needs_inc = True
                elif o.kind == "cc":
                    o.ms = o.ccidx + 1
                    o.needs_inc = True
        def n_sems(count, rot):
            return max(1, -(-count // rot)) if count else 1

        csems = {e: [nc.alloc_semaphore(f"cs_{e}_{i}")
                     for i in range(n_sems(ms_count[e], self.ROT_MS))]
                 for e in ENGINES}
        dsems = {e: [nc.alloc_semaphore(f"ds_{e}_{i}")
                     for i in range(n_sems(self.dma_count[e], self.ROT_DMA))]
                 for e in ENGINES}
        ccsems = [nc.alloc_semaphore(f"ccs_{i}")
                  for i in range(n_sems(self.cc_count, self.ROT_CC))]

        def sem_val(o):
            if o.kind == "c":
                i = (o.ms - 1) // self.ROT_MS
                return csems[o.engine][i], o.ms - i * self.ROT_MS
            if o.kind == "d":
                i = o.didx // self.ROT_DMA
                return dsems[o.engine][i], o.ms - i * self.ROT_DMA * 16
            i = o.ccidx // self.ROT_CC
            return ccsems[i], o.ms - i * self.ROT_CC

        def inc_val(o):
            if o.kind == "c":
                return 1
            if o.kind == "d":
                return 16
            return None

        prog = self
        prog.phase_marks = []
        _scope_ids = []

        with nc.Block() as block:
            for ename in ENGINES:
                def body(eng, _e=ename):
                    known = {}

                    def dsem_of(qeng, b):
                        i = (b - 1) // prog.ROT_DMA
                        return dsems[qeng][i], 16 * (b - i * prog.ROT_DMA)

                    def ccsem_of(b):
                        i = (b - 1) // prog.ROT_CC
                        return ccsems[i], b - i * prog.ROT_CC

                    cur_phase = None
                    for o in prog.ops[_e]:
                        if o.phase != cur_phase:
                            if cur_phase is not None:
                                nc.pop_named_scope(cur_phase)
                            nc.push_named_scope(o.phase)
                            prog.phase_marks.append((nc.next_id(), o.phase))
                            cur_phase = o.phase
                        if o.kind == "d" and o.didx in prog.gates[_e] and o.didx > 0:
                            s, v = dsem_of(_e, o.didx)
                            if known.get(s, 0) < v:
                                eng.wait_ge(s, v)
                                known[s] = v
                        if o.kind == "cc" and o.ccidx in prog.cc_gates and o.ccidx > 0:
                            s, v = ccsem_of(o.ccidx)
                            if known.get(s, 0) < v:
                                eng.wait_ge(s, v)
                                known[s] = v
                        need = {}
                        for d, ov in o.deps.items():
                            if ov is None:
                                s, v = sem_val(d)
                            elif ov[0] == "d":
                                s, v = dsem_of(ov[1], ov[2])
                            else:
                                s, v = ccsem_of(ov[2])
                            if need.get(s, 0) < v:
                                need[s] = v
                        for s, v in sorted(need.items(), key=lambda kv: kv[0].num):
                            if known.get(s, 0) < v:
                                eng.wait_ge(s, v)
                                known[s] = v
                        inst = o.emit(eng)
                        if o.needs_inc:
                            if o.kind == "cc":
                                inst.then_inc(sem_val(o)[0])
                            else:
                                inst.then_inc(sem_val(o)[0], inc_val(o))
                    if cur_phase is not None:
                        nc.pop_named_scope(cur_phase)
                    for i, s in enumerate(dsems[_e]):
                        tot = prog.dma_count[_e] - i * prog.ROT_DMA
                        if tot > 0:
                            eng.wait_ge(s, 16 * min(tot, prog.ROT_DMA))
                    if _e == "gpsimd":
                        for i, s in enumerate(ccsems):
                            tot = prog.cc_count - i * prog.ROT_CC
                            if tot > 0:
                                eng.wait_ge(s, min(tot, prog.ROT_CC))
                getattr(block, ename)(body)
        return csems, dsems, ccsems




def ceil_div(a, b):
    return -(-a // b)


NT = 2048
HALO_M = 3
NRES = NT + HALO_M
NEXT_A = 2239
RES_OFF = 188
D = 1024
KT = D // 128

EPS = 1e-5


class ArenaTile:

    def __init__(self, core, byte_off, nelems, dtype, eb, buf):
        self.core = core
        self.off = byte_off
        self.n = nelems
        self.dtype = dtype
        self.eb = eb
        self.buf = buf
        self.dead = False

    def tv(self, e0=0, e1=None, p0=0, p1=128):
        assert not self.dead, "use of freed arena tile"
        e1 = self.n if e1 is None else e1
        a = self.core.arena
        b0 = self.off + e0 * self.eb
        b1 = self.off + e1 * self.eb
        ap = a.handle.ap()[p0:p1, b0:b1].bitcast(self.dtype)
        return TV(self.buf, ap, p0, p1, b0, b1)

    def sub(self, base):
        return _SubTile(self, base)


class _SubTile:
    def __init__(self, parent, base):
        self.parent = parent
        self.base = base

    def tv(self, e0, e1, p0=0, p1=128):
        return self.parent.tv(self.base + e0, self.base + e1, p0, p1)


class Core:
    def __init__(self, nc, prog, arena_kb=None, nt=NT):
        self.nc = nc
        self.p = prog
        self.NT = nt
        self.NRES = nt + HALO_M
        self.NEXT = nt + 191
        self.psum_all = prog.psum("psum_all", (128, 4096), F32)
        self.psum_all.whole = False
        self.resid = [prog.sbuf(f"res{k}", [128, self.NRES], F32) for k in range(KT)]
        self.ident = prog.sbuf("ident", [128, 128], BF16)
        self.ones_f32 = prog.sbuf("ones_f32", [128, 1], F32)
        self.ones_bf = prog.sbuf("ones_bf", [128, 1], BF16)
        self.ones_row_bf = prog.sbuf("ones_row_bf", [1, 128], BF16)
        self.ln_sq = [prog.sbuf(f"ln_sq{i}", [128, 512], F32) for i in range(2)]
        self.ln_xb = [prog.sbuf(f"ln_xb{i}", [128, 512], BF16) for i in range(2)]
        self.ln_sqb = [prog.sbuf(f"ln_sqb{i}", [128, 512], BF16) for i in range(2)]
        self.ln_stats = prog.sbuf("ln_stats", [1, 2048], F32)
        self.ln_bc16 = prog.sbuf("ln_bc16", [1, 1024], BF16)
        avail = (nc.sbuf_bytes_remaining - 512) // 32 * 32
        self.ARENA_BYTES = min(arena_kb * 1024, avail) if arena_kb else avail
        self.arena = prog.sbuf("arena", [128, self.ARENA_BYTES], mybir.dt.uint8)
        self._arena_off = 0
        self._arena_marks = []
        self._atiles = []
        self._atile_id = 0

    def init_consts(self):
        c = self
        c.vop("memset", c.ones_f32.tv(), 1.0)
        c.vop("memset", c.ones_bf.tv(), 1.0)
        c.vop("memset", c.ones_row_bf.tv(ps=(0, 1)), 1.0)
        c.gop("memset", c.ident.tv(), 1.0)
        c.gop("affine_select", c.ident.tv(), c.ident.tv(), [[1, 128]], OP.is_ge,
              0.0, base=0, channel_multiplier=-1)
        c.gop("affine_select", c.ident.tv(), c.ident.tv(), [[-1, 128]], OP.is_ge,
              0.0, base=0, channel_multiplier=1)

    def amark(self):
        self._arena_marks.append(self._arena_off)

    def areset(self):
        base = self._arena_marks.pop()
        for t in self._atiles:
            if t.off >= base:
                t.dead = True
        self._arena_off = base

    def aalloc(self, nbytes, align=32):
        off = -(-self._arena_off // align) * align
        self._arena_off = off + nbytes
        assert self._arena_off <= self.ARENA_BYTES, (
            f"arena overflow: {self._arena_off} > {self.ARENA_BYTES}")
        return off

    def atile(self, nelems, dtype):
        eb = {F32: 4, BF16: 2, I32: 4, mybir.dt.int8: 1}[dtype]
        off = self.aalloc(nelems * eb)
        b0, b1 = off, off + nelems * eb
        buf = Buf(f"at{self._atile_id}", self.arena.handle, "sbuf", dtype)
        self._atile_id += 1
        for t in self._atiles:
            if t.off < b1 and b0 < (t.off + t.n * t.eb):
                latest = {}
                for acc in t.buf.writes + t.buf.reads:
                    o = acc[0]
                    key = o.engine
                    prev = latest.get(key)
                    if prev is None or o.eidx > prev.eidx:
                        latest[key] = o
                for o in latest.values():
                    buf.writes.append((o, 0, 128, b0, b1))
        tile = ArenaTile(self, off, nelems, dtype, eb, buf)
        self._atiles.append(tile)
        return tile

    def vop(self, fname, out, *ins, extra_reads=(), **kw):
        return self._eop("vector", fname, out, ins, extra_reads, kw)

    def sop(self, fname, out, *ins, extra_reads=(), **kw):
        return self._eop("scalar", fname, out, ins, extra_reads, kw)

    def gop(self, fname, out, *ins, extra_reads=(), **kw):
        return self._eop("gpsimd", fname, out, ins, extra_reads, kw)

    def _eop(self, engine, fname, out, ins, extra_reads, kw):
        in_aps = [(x.ap if isinstance(x, TV) else x) for x in ins]
        out_ap = out.ap if isinstance(out, TV) else out
        kw2 = {k: (v.ap if isinstance(v, TV) else v) for k, v in kw.items()}

        def emit(eng, _f=fname, _o=out_ap, _i=in_aps, _k=kw2):
            return getattr(eng, _f)(_o, *_i, **_k)

        reads = [x for x in ins if isinstance(x, TV)] + \
                [v for k2, v in kw.items()
                 if isinstance(v, TV) and k2 != "accum_out"] + list(extra_reads)
        writes = ([out] if isinstance(out, TV) else []) + \
                 [v for k2, v in kw.items()
                  if isinstance(v, TV) and k2 == "accum_out"]
        return self.p.op(engine, emit, reads=reads, writes=writes)

    def mm(self, out, lhsT, rhs, start, stop):
        oap = out.ap if isinstance(out, TV) else out
        lap = lhsT.ap if isinstance(lhsT, TV) else lhsT
        rap = rhs.ap if isinstance(rhs, TV) else rhs

        def emit(eng, _o=oap, _l=lap, _r=rap, _s=start, _t=stop):
            return eng.matmul(_o, _l, _r, start=_s, stop=_t, skip_group_check=True)

        reads = [x for x in (lhsT, rhs) if isinstance(x, TV)]
        writes = [out] if isinstance(out, TV) else []
        return self.p.op("tensor", emit, reads=reads, writes=writes)

    def transpose(self, out, in_):
        oap = out.ap if isinstance(out, TV) else out
        iap = in_.ap if isinstance(in_, TV) else in_
        np_ = (in_.p1 - in_.p0) if isinstance(in_, TV) else 128
        ident = self.ident.tv(ps=(0, np_), fs=(0, np_))

        def emit(eng, _o=oap, _i=iap, _id=ident.ap):
            return eng.transpose(_o, _i, _id)

        reads = [x for x in (in_,) if isinstance(x, TV)] + [ident]
        writes = [out] if isinstance(out, TV) else []
        return self.p.op("tensor", emit, reads=reads, writes=writes)

    def pbtv(self, i, p0=0, p1=128, e0=0, e1=512):
        b = self.psum_all
        ap = b.handle.ap()[p0:p1, i * 512 + e0:i * 512 + e1]
        lo = ((i * 512 + e0) * 4 // 2048) * 2048
        hi = -(-((i * 512 + e1) * 4) // 2048) * 2048
        return TV(b, ap, 0, 128, lo, hi)

    def pbt16(self, i, p0=0, p1=128, e0=0, e1=1024):
        b = self.psum_all
        ap = b.handle.ap().bitcast(BF16)[p0:p1, i * 1024 + e0:i * 1024 + e1]
        lo = ((i * 1024 + e0) * 2 // 2048) * 2048
        hi = -(-((i * 1024 + e1) * 2) // 2048) * 2048
        return TV(b, ap, 0, 128, lo, hi)

    def pbw(self, i, nelem, p0=0, p1=128):
        b = self.psum_all
        ap = b.handle.ap()[p0:p1, i * 512:i * 512 + nelem]
        lo = i * 2048
        hi = -(-((i * 512 + nelem) * 4) // 2048) * 2048
        return TV(b, ap, 0, 128, lo, hi)



def emit_ln(core, c0, c1, out_tiles, o_off=0, src_tiles=None, pb0=6, pb1=7):
    c = core
    n = c1 - c0
    src = src_tiles if src_tiles is not None else c.resid
    for st0 in range(0, n, 512):
        w = min(512, n - st0)
        pm = c.pbtv(pb0, 0, 1, 0, w)
        pv = c.pbtv(pb1, 0, 1, 0, w)
        for k in range(KT):
            x = src[k].tv(fs=(c0 + st0, c0 + st0 + w)) if isinstance(src[k], Buf) \
                else src[k].tv(c0 + st0, c0 + st0 + w)
            xb = c.ln_xb[k % 2].tv(fs=(0, w))
            sq = c.ln_sqb[k % 2].tv(fs=(0, w))
            c.sop("activation", xb, x, func=AF.Identity)
            c.sop("activation", sq, x, func=AF.Square)
            c.mm(pm, c.ones_bf.tv(), xb, start=(k == 0), stop=(k == KT - 1))
            c.mm(pv, c.ones_bf.tv(), sq, start=(k == 0), stop=(k == KT - 1))
        st = c.ln_stats
        m = st.tv(ps=(0, 1), fs=(0, w))
        v = st.tv(ps=(0, 1), fs=(512, 512 + w))
        r = st.tv(ps=(0, 1), fs=(1024, 1024 + w))
        t = st.tv(ps=(0, 1), fs=(1536, 1536 + w))
        mb = c.ln_bc16.tv(ps=(0, 1), fs=(0, w))
        rb = c.ln_bc16.tv(ps=(0, 1), fs=(512, 512 + w))
        c.vop("tensor_scalar", m, pm, 1.0 / D, None, op0=OP.mult)
        c.vop("tensor_scalar", v, pv, 1.0 / D, None, op0=OP.mult)
        c.vop("tensor_mul", t, m, m)
        c.vop("tensor_sub", v, v, t)
        c.vop("tensor_scalar", t, v, float(EPS), None, op0=OP.add)
        c.sop("activation", v, t, func=AF.Ln)
        c.sop("activation", r, v, func=AF.Exp, scale=-0.5)
        c.vop("tensor_copy", mb, m)
        c.vop("tensor_copy", rb, r)
        bm = c.pbtv(pb0, 0, 128, 0, w)
        bv = c.pbtv(pb1, 0, 128, 0, w)
        c.mm(bm, c.ones_row_bf.tv(ps=(0, 1)), mb, start=True, stop=True)
        c.mm(bv, c.ones_row_bf.tv(ps=(0, 1)), rb, start=True, stop=True)
        for k in range(KT):
            x = src[k].tv(fs=(c0 + st0, c0 + st0 + w)) if isinstance(src[k], Buf) \
                else src[k].tv(c0 + st0, c0 + st0 + w)
            tmp = c.ln_sq[k % 2].tv(fs=(0, w))
            c.vop("tensor_sub", tmp, x, bm)
            c.vop("tensor_mul", out_tiles[k].tv(o_off + st0, o_off + st0 + w), tmp, bv)



def _wslab_dma(core, wb, dramT, m0, nelem, parts=None):
    parts = parts if parts is not None else dramT.handle.shape[1]
    src = TV(dramT, dramT.handle.ap()[m0], 0, parts, 0, 1 << 30)
    core.p.dma("gpsimd", wb.tv(0, nelem, p0=0, p1=parts), src)


def tile_kxm(wT, kt):
    K, M = wT.shape
    mw = min(M, 128)
    nmo = M // mw
    return np.ascontiguousarray(
        wT.reshape(kt, 128, nmo, mw).transpose(2, 1, 0, 3).reshape(nmo, 128, kt * mw))


def _wslab_dma2(core, wb, dramT, m0, nelem, nblk=2):
    src_ap = dramT.handle.ap()[m0:m0 + nblk].transpose([1, 0, 2])
    src = TV(dramT, src_ap, 0, 128, 0, 1 << 30)
    dst = wb.tv(0, nblk * nelem).with_ap(
        wb.tv(0, nblk * nelem).ap.rearrange("p (b n) -> p b n", b=nblk))
    core.p.dma("gpsimd", dst, src)


def _stream_blocks(core, nblk, dma_fn, compute_fn):
    dma_fn(0, 0)
    for m in range(nblk):
        if m + 1 < nblk:
            dma_fn(m + 1, (m + 1) % 2)
        compute_fn(m, m % 2)


def emit_ffn0(core, W, c0, c1, src_tiles=None):
    c = core
    sc = W["sc"]

    def scl(col):
        return sc.tv(fs=(col, col + 1))

    t0 = c0
    while t0 < c1:
        w = min(512, c1 - t0)
        c.amark()
        lnt = c.atile(KT * w, BF16)
        ln = [lnt.sub(k * w) for k in range(KT)]
        hA = c.atile(32 * w, BF16)
        hB = c.atile(32 * w, BF16)
        wsm = [c.atile(1024, BF16) for _ in range(4)]
        wbg = [c.atile(8192, BF16) for _ in range(2)]
        wi8 = [c.atile(8192, mybir.dt.int8)] * 2

        emit_ln(c, t0, t0 + w, ln)

        def dma1(m, par):
            _wslab_dma(c, wsm[par], W["w1T"], m, 1024)
            _wslab_dma(c, wsm[2 + par], W["w2T"], m, 1024)

        def comp1(m, par):
            pa = c.pbtv(m % 4, 0, 128, 0, w)
            pb = c.pbtv(4 + m % 4, 0, 128, 0, w)
            for k in range(KT):
                lt = wsm[par].tv(k * 128, (k + 1) * 128)
                c.mm(pa, lt, ln[k].tv(0, w), start=(k == 0), stop=(k == KT - 1))
            for k in range(KT):
                lt = wsm[2 + par].tv(k * 128, (k + 1) * 128)
                c.mm(pb, lt, ln[k].tv(0, w), start=(k == 0), stop=(k == KT - 1))
            hm = hA.tv(m * w, (m + 1) * w)
            c.sop("activation", hm, pa, func=AF.Silu, scale=scl(m))
            c.vop("scalar_tensor_tensor", hm, pb, scl(32 + m), hm,
                  op0=OP.mult, op1=OP.mult)

        _stream_blocks(c, 32, dma1, comp1)

        def big_phase(dramT, src_h, dst_fn, pbank):
            def dmaf(p, par):
                if p % 2 == 0:
                    _wslab_dma2(c, wbg[par], dramT, 2 * p, 4096)
                else:
                    src_ap = dramT.handle.ap()[2 * p:2 * p + 2].transpose([1, 0, 2])
                    srctv = TV(dramT, src_ap, 0, 128, 0, 1 << 30)
                    dst = wi8[par].tv(0, 8192).with_ap(
                        wi8[par].tv(0, 8192).ap.rearrange("q (b n) -> q b n", b=2))
                    c.p.dma("sync", dst, srctv)
                    c.sop("activation", wbg[par].tv(0, 8192), wi8[par].tv(0, 8192),
                          func=AF.Identity)

            def compf(p, par):
                for i in range(2):
                    m = 2 * p + i
                    ps = c.pbtv(pbank + m % 4, 0, 128, 0, w)
                    for k in range(32):
                        lt = wbg[par].tv(i * 4096 + k * 128, i * 4096 + (k + 1) * 128)
                        c.mm(ps, lt, src_h.tv(k * w, (k + 1) * w),
                             start=(k == 0), stop=(k == 31))
                    dst_fn(m, ps)
            return dmaf, compf

        def dst2(m, ps):
            c.sop("activation", hB.tv(m * w, (m + 1) * w), ps, func=AF.Gelu,
                  scale=scl(64 + m))
        dmaf, compf = big_phase(W["h1T"], hA, dst2, 4)
        _stream_blocks(c, 16, dmaf, compf)

        def dst3(m, ps):
            c.sop("activation", hA.tv(m * w, (m + 1) * w), ps, func=AF.Gelu,
                  scale=scl(96 + m))
        dmaf, compf = big_phase(W["h2T"], hB, dst3, 0)
        _stream_blocks(c, 16, dmaf, compf)

        def dst4(m, ps):
            rs = c.resid[m].tv(fs=(t0, t0 + w))
            c.vop("scalar_tensor_tensor", rs, ps, scl(128 + m), rs,
                  op0=OP.mult, op1=OP.add, extra_reads=[rs])
        dmaf, compf = big_phase(W["w3T"], hA, dst4, 4)
        _stream_blocks(c, 4, dmaf, compf)

        c.areset()
        t0 += w


def prep_ffn0_host(inputs):
    w1 = inputs["L0_ffn_w1_w"]; w2 = inputs["L0_ffn_w2_w"]
    h1 = inputs["L0_ffn_h1_w"]; h2 = inputs["L0_ffn_h2_w"]
    w3 = inputs["L0_ffn_w3_w"]
    out = {
        "ffn0_w1T": tile_kxm(w1.T.astype(np.int8), 8),
        "ffn0_w2T": tile_kxm(w2.T.astype(np.int8), 8),
        "ffn0_h1T": tile_kxm(h1.T.astype(np.int8), 32),
        "ffn0_h2T": tile_kxm(h2.T.astype(np.int8), 32),
        "ffn0_w3T": tile_kxm(w3.T.astype(np.int8), 32),
    }
    assert np.all(w1.T.astype(np.int8).astype(np.float32) == w1.T)
    sc = np.zeros((128, 136), np.float32)
    sc[:, 0:32] = inputs["L0_ffn_w1_s"].reshape(32, 128).T
    sc[:, 32:64] = inputs["L0_ffn_w2_s"].reshape(32, 128).T
    sc[:, 64:96] = inputs["L0_ffn_h1_s"].reshape(32, 128).T
    sc[:, 96:128] = inputs["L0_ffn_h2_s"].reshape(32, 128).T
    sc[:, 128:136] = inputs["L0_ffn_w3_s"].reshape(8, 128).T
    out["ffn0_sc"] = sc
    return out


def declare_ffn0(prog):
    W = {}
    W["w1T"] = prog.dram("ffn0_w1T", [32, 128, 1024], mybir.dt.int8, kind="ExternalInput")
    W["w2T"] = prog.dram("ffn0_w2T", [32, 128, 1024], mybir.dt.int8, kind="ExternalInput")
    W["h1T"] = prog.dram("ffn0_h1T", [32, 128, 4096], mybir.dt.int8, kind="ExternalInput")
    W["h2T"] = prog.dram("ffn0_h2T", [32, 128, 4096], mybir.dt.int8, kind="ExternalInput")
    W["w3T"] = prog.dram("ffn0_w3T", [8, 128, 4096], mybir.dt.int8, kind="ExternalInput")
    W["sc_dram"] = prog.dram("ffn0_sc", [128, 136], F32, kind="ExternalInput")
    W["sc"] = prog.sbuf("ffn0_sc_sb", [128, 136], F32)
    return W


def load_ffn0_consts(core, W):
    core.p.dma("sync", W["sc"].tv(), W["sc_dram"].tv())



DEQ = np.float32(1.0) / np.float32(127.0)


def emit_ffn1(core, W, c0, c1):
    c = core
    S = float(DEQ)
    t0 = c0
    while t0 < c1:
        w = min(512, c1 - t0)
        c.amark()
        lnt = c.atile(KT * w, BF16)
        ln = [lnt.sub(k * w) for k in range(KT)]
        hA = c.atile(24 * w, BF16)
        wsm = [c.atile(2048, BF16) for _ in range(4)]
        wbg = [c.atile(6144, BF16) for _ in range(2)]

        emit_ln(c, t0, t0 + w, ln)

        def dma1(p, par):
            _wslab_dma2(c, wsm[par], W["wgT"], 2 * p, 1024)
            _wslab_dma2(c, wsm[2 + par], W["wuT"], 2 * p, 1024)

        def comp1(p, par):
            for i in range(2):
                m = 2 * p + i
                pa = c.pbtv(m % 4, 0, 128, 0, w)
                pb = c.pbtv(4 + m % 4, 0, 128, 0, w)
                for k in range(KT):
                    c.mm(pa, wsm[par].tv(i * 1024 + k * 128, i * 1024 + (k + 1) * 128),
                         ln[k].tv(0, w), start=(k == 0), stop=(k == KT - 1))
                for k in range(KT):
                    c.mm(pb, wsm[2 + par].tv(i * 1024 + k * 128, i * 1024 + (k + 1) * 128),
                         ln[k].tv(0, w), start=(k == 0), stop=(k == KT - 1))
                hm = hA.tv(m * w, (m + 1) * w)
                c.sop("activation", hm, pa, func=AF.Silu, scale=S)
                c.vop("scalar_tensor_tensor", hm, pb, S, hm, op0=OP.mult, op1=OP.mult)

        _stream_blocks(c, 12, dma1, comp1)

        def dma2(p, par):
            _wslab_dma2(c, wbg[par], W["wdT"], 2 * p, 3072)

        def comp2(p, par):
            for i in range(2):
                m = 2 * p + i
                ps = c.pbtv(4 + m % 4, 0, 128, 0, w)
                for k in range(24):
                    c.mm(ps, wbg[par].tv(i * 3072 + k * 128, i * 3072 + (k + 1) * 128),
                         hA.tv(k * w, (k + 1) * w), start=(k == 0), stop=(k == 23))
                rs = c.resid[m].tv(fs=(t0, t0 + w))
                c.vop("scalar_tensor_tensor", rs, ps, S, rs,
                      op0=OP.mult, op1=OP.add, extra_reads=[rs])

        _stream_blocks(c, 4, dma2, comp2)

        c.areset()
        t0 += w


def prep_ffn1_host(inputs):
    wg = inputs["L1_ffn_w_gate"]; wu = inputs["L1_ffn_w_up"]; wd = inputs["L1_ffn_w_down"]
    return {
        "ffn1_wgT": tile_kxm(wg.T.astype(np.int8), 8),
        "ffn1_wuT": tile_kxm(wu.T.astype(np.int8), 8),
        "ffn1_wdT": tile_kxm(wd.T.astype(np.int8), 24),
    }


def declare_ffn1(prog):
    I8 = mybir.dt.int8
    return {
        "wgT": prog.dram("ffn1_wgT", [24, 128, 1024], I8, kind="ExternalInput"),
        "wuT": prog.dram("ffn1_wuT", [24, 128, 1024], I8, kind="ExternalInput"),
        "wdT": prog.dram("ffn1_wdT", [8, 128, 3072], I8, kind="ExternalInput"),
    }



def emit_attn1a(core, W, c0, c1, rg):
    c = core
    T = c1 - c0
    nst = ceil_div(T, 512)
    ntt = ceil_div(T, 128)
    c.amark()
    lnt = c.atile(KT * T, BF16)
    ln = [lnt.sub(k * T) for k in range(KT)]
    emit_ln(c, c0, c1, ln)
    qfm = c.atile(8 * T, BF16)
    kvin = c.atile(1024, F32)
    bias = W["bias_sb"]

    def bcol(i):
        return bias.tv(fs=(i, i + 1))

    slabs = [c.atile(1024, BF16) for _ in range(6)]

    def dmah(h, par):
        _wslab_dma(c, slabs[3 * par + 0], W["qkv"], h, 1024)
        _wslab_dma(c, slabs[3 * par + 1], W["qkv"], 8 + h, 1024)
        _wslab_dma(c, slabs[3 * par + 2], W["qkv"], 16 + h, 1024)

    c.amark()
    kfm = c.atile(T, BF16)
    vfm = c.atile(T, BF16)
    ktok = c.atile(T, BF16)
    vtok = c.atile(T, BF16)

    def comph(h, par):
        qs, ks, vs = slabs[3 * par], slabs[3 * par + 1], slabs[3 * par + 2]
        for st in range(nst):
            e0, e1 = st * 512, min(T, (st + 1) * 512)
            w = e1 - e0
            ps = c.pbtv(st % 2, 0, 128, 0, w)
            for k in range(KT):
                c.mm(ps, qs.tv(k * 128, (k + 1) * 128), ln[k].tv(e0, e1),
                     start=(k == 0), stop=(k == KT - 1))
            c.sop("activation", qfm.tv(h * T + e0, h * T + e1), ps,
                  func=AF.Identity, bias=bcol(h))
            ps2 = c.pbtv(2 + st % 2, 0, 128, 0, w)
            for k in range(KT):
                c.mm(ps2, ks.tv(k * 128, (k + 1) * 128), ln[k].tv(e0, e1),
                     start=(k == 0), stop=(k == KT - 1))
            c.sop("activation", kfm.tv(e0, e1), ps2, func=AF.Identity, bias=bcol(8 + h))
            ps3 = c.pbtv(4 + st % 2, 0, 128, 0, w)
            for k in range(KT):
                c.mm(ps3, vs.tv(k * 128, (k + 1) * 128), ln[k].tv(e0, e1),
                     start=(k == 0), stop=(k == KT - 1))
            c.sop("activation", vfm.tv(e0, e1), ps3, func=AF.Identity, bias=bcol(16 + h))
        for tt in range(ntt):
            e0, e1 = tt * 128, min(T, (tt + 1) * 128)
            w = e1 - e0
            pt = c.pbt16(6, 0, w, 0, 128)
            c.transpose(pt, kfm.tv(e0, e1))
            c.vop("tensor_copy", ktok.tv(tt * 128, tt * 128 + 128, p0=0, p1=w), pt)
            pt2 = c.pbt16(7, 0, w, 0, 128)
            c.transpose(pt2, vfm.tv(e0, e1))
            c.vop("tensor_copy", vtok.tv(tt * 128, tt * 128 + 128, p0=0, p1=w), pt2)
        pkv = c.pbtv(0, 0, 128, 0, 128)
        for tt in range(ntt):
            e0, e1 = tt * 128, min(T, (tt + 1) * 128)
            w = e1 - e0
            c.mm(pkv, ktok.tv(tt * 128, tt * 128 + 128, p0=0, p1=w),
                 vtok.tv(tt * 128, tt * 128 + 128, p0=0, p1=w),
                 start=(tt == 0), stop=(tt == ntt - 1))
        c.vop("tensor_copy", kvin.tv(h * 128, (h + 1) * 128), pkv)

    _stream_blocks(c, 8, dmah, comph)
    c.areset()

    c.p.dma("sync", W["cc_in"].tv(), kvin.tv())
    c.p.cc("AllReduce", OP.add, rg, W["cc_in"].tv(), W["cc_out"].tv())
    kvb = c.atile(1024, BF16)
    c.p.dma("gpsimd", kvb.tv(), W["cc_out"].tv())

    afm = c.atile(8 * T, BF16)
    for h in range(8):
        for st in range(nst):
            e0, e1 = st * 512, min(T, (st + 1) * 512)
            w = e1 - e0
            ps = c.pbtv((h + st) % 4, 0, 128, 0, w)
            c.mm(ps, kvb.tv(h * 128, (h + 1) * 128), qfm.tv(h * T + e0, h * T + e1),
                 start=True, stop=True)
            c.vop("tensor_copy", afm.tv(h * T + e0, h * T + e1), ps)

    wo = [c.atile(1024, BF16) for _ in range(2)]
    tmp = c.atile(512, F32)

    def dmao(mo, par):
        _wslab_dma(c, wo[par], W["woT"], mo, 1024)

    def compo(mo, par):
        for st in range(nst):
            e0, e1 = st * 512, min(T, (st + 1) * 512)
            w = e1 - e0
            ps = c.pbtv(4 + (mo + st) % 2, 0, 128, 0, w)
            for j in range(8):
                c.mm(ps, wo[par].tv(j * 128, (j + 1) * 128), afm.tv(j * T + e0, j * T + e1),
                     start=(j == 0), stop=(j == 7))
            c.sop("activation", tmp.tv(0, w), ps, func=AF.Identity, bias=bcol(24 + mo))
            rs = c.resid[mo].tv(fs=(c0 + e0, c0 + e1))
            c.vop("tensor_add", rs, rs, tmp.tv(0, w), extra_reads=[rs])

    _stream_blocks(c, 8, dmao, compo)
    c.areset()


def prep_attn1a_host(inputs):
    wqkv = inputs["L1_attn_a_w_qkv"]
    bqkv = inputs["L1_attn_a_b_qkv"]
    wo = inputs["L1_attn_a_w_o"]
    bo = inputs["L1_attn_a_b_o"]
    idx = 8 * np.arange(128)
    slabs = []
    for part in range(3):
        for h in range(8):
            Wh = wqkv[part * 1024 + idx + h, :]
            slabs.append(tile_kxm(np.ascontiguousarray(Wh.T), 8)[0])
    qkv = np.ascontiguousarray(np.stack(slabs).astype(np.float32))
    woT = tile_kxm(np.ascontiguousarray(wo.T), 8).astype(np.float32)
    bias = np.zeros((128, 32), np.float32)
    for part in range(3):
        for h in range(8):
            bias[:, part * 8 + h] = bqkv[part * 1024 + idx + h]
    bias[:, 24:32] = bo.reshape(8, 128).T
    return {"a1_qkv": qkv, "a1_woT": woT, "a1_bias": bias}


def declare_attn1a(prog):
    W = {
        "qkv": prog.dram("a1_qkv", [24, 128, 1024], F32, kind="ExternalInput"),
        "woT": prog.dram("a1_woT", [8, 128, 1024], F32, kind="ExternalInput"),
        "bias_dram": prog.dram("a1_bias", [128, 32], F32, kind="ExternalInput"),
        "bias_sb": prog.sbuf("a1_bias_sb", [128, 32], F32),
        "cc_in": prog.dram("a1_cc_in", [128, 1024], F32),
        "cc_out": prog.dram("a1_cc_out", [128, 1024], F32),
    }
    return W


def load_attn1a_consts(core, W):
    core.p.dma("sync", W["bias_sb"].tv(), W["bias_dram"].tv())



def emit_attn1b(core, W, c0, c1, rg, ranks):
    c = core
    T = c1 - c0
    KW = ranks * T
    nqt = T // 128
    nstr = KW // 512
    c.amark()
    lnt = c.atile(KT * T, BF16)
    ln = [lnt.sub(k * T) for k in range(KT)]
    emit_ln(c, c0, c1, ln)
    bias = W["bias_sb"]

    def bcol(i):
        return bias.tv(fs=(i, i + 1))

    c.vop("tensor_scalar", bias.tv(fs=(3, 4)), bias.tv(fs=(0, 1)), 0.125, None,
          op0=OP.mult)

    qfm = c.atile(T, BF16)
    kfm = c.atile(T, BF16)
    vtok = c.atile((T // 128) * 64, BF16)
    qkv_sb = [c.atile(512, BF16) for _ in range(3)]
    for i in range(3):
        _wslab_dma(c, qkv_sb[i], W["qkvw"], i, 512)
    for st in range(T // 512):
        e0, e1 = st * 512, (st + 1) * 512
        ps = c.pbtv(st % 2, 0, 64, 0, 512)
        for k in range(KT):
            c.mm(ps, qkv_sb[0].tv(k * 64, (k + 1) * 64), ln[k].tv(e0, e1),
                 start=(k == 0), stop=(k == KT - 1))
        c.sop("activation", qfm.tv(e0, e1, p0=0, p1=64), ps, func=AF.Identity,
              bias=bias.tv(ps=(0, 64), fs=(3, 4)), scale=0.125)
        ps2 = c.pbtv(2 + st % 2, 0, 64, 0, 512)
        for k in range(KT):
            c.mm(ps2, qkv_sb[1].tv(k * 64, (k + 1) * 64), ln[k].tv(e0, e1),
                 start=(k == 0), stop=(k == KT - 1))
        c.sop("activation", kfm.tv(e0, e1, p0=0, p1=64), ps2, func=AF.Identity,
              bias=bias.tv(ps=(0, 64), fs=(1, 2)))
    for tt in range(T // 128):
        ps = c.pbtv(4 + tt % 2, 0, 128, 0, 64)
        for k in range(KT):
            c.mm(ps, ln[k].tv(tt * 128, (tt + 1) * 128), qkv_sb[2].tv(k * 64, (k + 1) * 64),
                 start=(k == 0), stop=False)
        c.mm(ps, c.ones_row_bf.tv(ps=(0, 1)).with_ap(
            c.ones_row_bf.handle.ap()[0:1, 0:128]), W["bvrow_sb"].tv(ps=(0, 1)),
            start=False, stop=True)
        c.vop("tensor_copy", vtok.tv(tt * 64, (tt + 1) * 64), ps)
    c.p.dma("sync", W["cck_in"].tv(), kfm.tv(p0=0, p1=64))
    c.p.dma("sync", W["ccv_in"].tv().with_ap(
        W["ccv_in"].handle.ap().rearrange("(tt p) c -> p tt c", p=128)),
        vtok.tv().with_ap(vtok.tv().ap.rearrange("p (tt c) -> p tt c", c=64)))
    c.p.cc("AllGather", OP.bypass, rg, W["cck_in"].tv(), W["cck_out"].tv())
    c.p.cc("AllGather", OP.bypass, rg, W["ccv_in"].tv(), W["ccv_out"].tv())
    kfull = c.atile(KW, BF16)
    vfull = c.atile((KW // 128) * 64, BF16)
    c.p.dma("sync", kfull.tv(p0=0, p1=64).with_ap(
        kfull.tv(p0=0, p1=64).ap.rearrange("c (r t) -> c r t", r=ranks)),
        W["cck_out"].tv().with_ap(
            W["cck_out"].handle.ap().rearrange("(r c) t -> c r t", c=64)))
    c.p.dma("sync", vfull.tv().with_ap(
        vfull.tv().ap.rearrange("p (tt c) -> p tt c", c=64)),
        W["ccv_out"].tv().with_ap(
            W["ccv_out"].handle.ap().rearrange("(tt p) c -> p tt c", p=128)))

    stats = c.atile(3 * nqt + 4, F32)
    attn_tok = c.atile(nqt * 64, BF16)
    maskb = [c.atile(KW, BF16) for _ in range(2)]
    psb = [c.atile(512, BF16) for _ in range(2)]
    ptb = [c.atile(128, BF16) for _ in range(2)]

    for qt in range(nqt):
        run = stats.tv(qt, qt + 1)
        c.vop("memset", run, -3.0e38)
        c.p.dma("sync", maskb[qt % 2].tv(),
                TV(W["mask"], W["mask"].handle.ap()[qt], 0, 128, 0, 1 << 30))
        for st in range(nstr):
            ps = c.pbtv(st % 2, 0, 128, 0, 512)
            c.mm(ps, qfm.tv(qt * 128, (qt + 1) * 128, p0=0, p1=64),
                 kfull.tv(st * 512, (st + 1) * 512, p0=0, p1=64),
                 start=True, stop=False)
            c.mm(ps, c.ident.tv(), maskb[qt % 2].tv(st * 512, (st + 1) * 512),
                 start=False, stop=True)
            sm = stats.tv(3 * nqt + st % 2, 3 * nqt + st % 2 + 1)
            c.vop("tensor_reduce", sm, ps, AX.X, OP.max)
            c.vop("tensor_max", run, run, sm)
        nm = stats.tv(2 * nqt + qt, 2 * nqt + qt + 1)
        c.vop("tensor_scalar", nm, run, -1.0, None, op0=OP.mult)
        ls = stats.tv(nqt + qt, nqt + qt + 1)
        c.vop("memset", ls, 0.0)
        po = c.pbtv(6 + qt % 2, 0, 128, 0, 64)
        first = True
        for st in range(nstr):
            ps = c.pbtv(2 + st % 2, 0, 128, 0, 512)
            c.mm(ps, qfm.tv(qt * 128, (qt + 1) * 128, p0=0, p1=64),
                 kfull.tv(st * 512, (st + 1) * 512, p0=0, p1=64),
                 start=True, stop=False)
            c.mm(ps, c.ident.tv(), maskb[qt % 2].tv(st * 512, (st + 1) * 512),
                 start=False, stop=True)
            tmpc = stats.tv(3 * nqt + 2 + st % 2, 3 * nqt + 2 + st % 2 + 1)
            c.sop("activation", psb[qt % 2].tv(), ps, func=AF.Exp, bias=nm,
                  accum_out=tmpc)
            c.vop("tensor_add", ls, ls, tmpc)
            for c2 in range(4):
                pt = c.pbt16(4 + c2 % 2, 0, 128, 0, 128)
                c.transpose(pt, psb[qt % 2].tv(c2 * 128, (c2 + 1) * 128))
                c.vop("tensor_copy", ptb[(st * 4 + c2) % 2].tv(), pt)
                kc = st * 4 + c2
                c.mm(po, ptb[(st * 4 + c2) % 2].tv(), vfull.tv(kc * 64, (kc + 1) * 64),
                     start=first, stop=(st == nstr - 1 and c2 == 3))
                first = False
        rl = stats.tv(3 * nqt + 2 + qt % 2, 3 * nqt + 2 + qt % 2 + 1)
        c.vop("reciprocal", rl, ls)
        c.vop("tensor_scalar", attn_tok.tv(qt * 64, (qt + 1) * 64), po, rl, None,
              op0=OP.mult)
    afm = c.atile(T, BF16)
    for qt in range(nqt):
        pt = c.pbt16(0, 0, 64, 0, 128)
        c.transpose(pt, attn_tok.tv(qt * 64, (qt + 1) * 64))
        c.vop("tensor_copy", afm.tv(qt * 128, (qt + 1) * 128, p0=0, p1=64), pt)
    wo_sb = [c.atile(128, BF16) for _ in range(2)]
    tmp = c.atile(512, F32)

    def dmao(mo, par):
        _wslab_dma(c, wo_sb[par], W["woT"], mo, 128)

    def compo(mo, par):
        for st in range(T // 512):
            e0, e1 = st * 512, (st + 1) * 512
            ps = c.pbtv(2 + (mo + st) % 2, 0, 128, 0, 512)
            c.mm(ps, wo_sb[par].tv(0, 128, p0=0, p1=64),
                 afm.tv(e0, e1, p0=0, p1=64), start=True, stop=True)
            c.sop("activation", tmp.tv(0, 512), ps, func=AF.Identity,
                  bias=bcol(8 + mo))
            rs = c.resid[mo].tv(fs=(c0 + e0, c0 + e1))
            c.vop("tensor_add", rs, rs, tmp.tv(0, 512), extra_reads=[rs])

    _stream_blocks(c, 8, dmao, compo)
    c.areset()


def prep_attn1b_host(inputs, ranks=4, T=NT):
    wq = inputs["L1_attn_b_wq"]; wk = inputs["L1_attn_b_wk"]; wv = inputs["L1_attn_b_wv"]
    wo = inputs["L1_attn_b_wo"]
    qkvw = np.stack([tile_kxm(np.ascontiguousarray(w), 8)[0]
                     for w in (wq, wk, wv)]).astype(np.float32)
    woT = np.stack([np.ascontiguousarray(wo[:, mo * 128:(mo + 1) * 128])
                    for mo in range(8)]).astype(np.float32)
    bias = np.zeros((128, 16), np.float32)
    bias[0:64, 0] = inputs["L1_attn_b_bq"]
    bias[0:64, 1] = inputs["L1_attn_b_bk"]
    bias[0:64, 2] = inputs["L1_attn_b_bv"]
    bias[:, 8:16] = inputs["L1_attn_b_bo"].reshape(8, 128).T
    bvrow = inputs["L1_attn_b_bv"].reshape(1, 64).astype(np.float32)
    out = {"a1b_qkvw": qkvw, "a1b_woT": woT, "a1b_bias": bias, "a1b_bvrow": bvrow}
    import ml_dtypes
    nqt = T // 128
    KW = ranks * T
    kidx = np.arange(KW)[None, None, :]
    for r in range(ranks):
        sbase = r * T
        qidx = sbase + (np.arange(nqt) * 128)[:, None, None] + np.arange(128)[None, :, None]
        m = np.where(kidx > qidx, np.float32(-1e30), np.float32(0.0))
        out[f"a1b_mask_r{r}"] = m.astype(ml_dtypes.bfloat16)
    return out


def declare_attn1b(prog, ranks=4, T=NT):
    nqt = T // 128
    KW = ranks * T
    return {
        "qkvw": prog.dram("a1b_qkvw", [3, 128, 512], F32, kind="ExternalInput"),
        "woT": prog.dram("a1b_woT", [8, 64, 128], F32, kind="ExternalInput"),
        "bias_dram": prog.dram("a1b_bias", [128, 16], F32, kind="ExternalInput"),
        "bias_sb": prog.sbuf("a1b_bias_sb", [128, 16], F32),
        "bvrow_dram": prog.dram("a1b_bvrow", [1, 64], F32, kind="ExternalInput"),
        "bvrow_sb": prog.sbuf("a1b_bvrow_sb", [1, 64], BF16),
        "mask": prog.dram("a1b_mask", [nqt, 128, KW], BF16, kind="ExternalInput"),
        "cck_in": prog.dram("a1b_cck_in", [64, T], BF16),
        "cck_out": prog.dram("a1b_cck_out", [ranks * 64, T], BF16),
        "ccv_in": prog.dram("a1b_ccv_in", [T, 64], BF16),
        "ccv_out": prog.dram("a1b_ccv_out", [ranks * T, 64], BF16),
    }


def load_attn1b_consts(core, W):
    core.p.dma("sync", W["bias_sb"].tv(), W["bias_dram"].tv())
    core.p.dma("gpsimd", W["bvrow_sb"].tv(ps=(0, 1)), W["bvrow_dram"].tv(ps=(0, 1)))



def emit_mamba(core, W, rg, nprev=3):
    c = core
    T = c.NT
    NR = c.NRES
    nst = T // 512
    sel = W["sel_sb"]
    c.amark()

    for k in range(KT):
        rs = c.resid[k].tv(fs=(0, HALO_M))
        c.vop("tensor_scalar", rs, rs, sel.tv(fs=(3, 4)), None, op0=OP.mult,
              extra_reads=[rs])

    c.sop("activation", W["A_sb"].tv(), W["A_dram_sb"].tv(), func=AF.Exp)
    c.vop("tensor_scalar", W["A_sb"].tv(), W["A_sb"].tv(), -1.0, None,
          op0=OP.mult, extra_reads=[W["A_sb"].tv()])

    c.amark()
    lnm = c.atile(KT * NR, BF16)
    ln = [lnm.sub(k * NR) for k in range(KT)]
    emit_ln(c, 0, NR, ln)
    c.amark()
    xcpre = c.atile(NR, F32)
    cacc = c.atile(T, F32)
    wxs = [c.atile(1024, BF16) for _ in range(2)]

    def dmx(m, par):
        _wslab_dma(c, wxs[par], W["wxT"], m, 1024)

    def cpx(m, par):
        for st in range(ceil_div(NR, 512)):
            e0, e1 = st * 512, min(NR, (st + 1) * 512)
            ps = c.pbtv(st % 4, 0, 128, 0, e1 - e0)
            for k in range(KT):
                c.mm(ps, wxs[par].tv(k * 128, (k + 1) * 128), ln[k].tv(e0, e1),
                     start=(k == 0), stop=(k == KT - 1))
            c.vop("tensor_copy", xcpre.tv(e0, e1), ps)
        cw = W["conv_sb"]
        c.vop("tensor_scalar", cacc.tv(), xcpre.tv(0, T),
              cw.tv(fs=(m * 4, m * 4 + 1)), None, op0=OP.mult)
        for kk in range(1, 4):
            c.vop("scalar_tensor_tensor", cacc.tv(), xcpre.tv(kk, T + kk),
                  cw.tv(fs=(m * 4 + kk, m * 4 + kk + 1)), cacc.tv(),
                  op0=OP.mult, op1=OP.add, extra_reads=[cacc.tv()])
        xcm = c.atile(T, BF16)
        c.sop("activation", xcm.tv(), cacc.tv(), func=AF.Silu,
              bias=W["convb_sb"].tv(fs=(m, m + 1)))
        c.p.dma("sync", TV(W["xc_dram"], W["xc_dram"].handle.ap()[m], 0, 128,
                           m * T * 2, (m + 1) * T * 2), xcm.tv())
        return xcm

    dmx(0, 0)
    for m in range(16):
        if m + 1 < 16:
            dmx(m + 1, (m + 1) % 2)
        c.amark()
        cpx(m, m % 2)
        c.areset()
    c.areset()
    c.areset()

    c.amark()
    dlr = c.atile(T, BF16)
    Bc = c.atile(T, BF16)
    Cc = c.atile(T, BF16)
    xps = [c.atile(16 * 64, BF16), c.atile(16 * 16, BF16), c.atile(16 * 16, BF16)]
    _wslab_dma(c, xps[0], W["xpdT"], 0, 16 * 64)
    _wslab_dma(c, xps[1], W["xpbT"], 0, 16 * 16)
    _wslab_dma(c, xps[2], W["xpcT"], 0, 16 * 16)
    xck = c.atile(T, BF16)
    for st in range(nst):
        e0, e1 = st * 512, (st + 1) * 512
        psd = c.pbtv(st % 2, 0, 64, 0, 512)
        psb = c.pbtv(2 + st % 2, 0, 16, 0, 512)
        psc = c.pbtv(4 + st % 2, 0, 16, 0, 512)
        for k in range(16):
            src = TV(W["xc_dram"], W["xc_dram"].handle.ap()[k][:, e0:e1], 0, 128,
                     k * T * 2 + e0 * 2, k * T * 2 + e1 * 2)
            c.p.dma("sync", xck.tv(e0, e1), src)
            c.mm(psd, xps[0].tv(k * 64, (k + 1) * 64), xck.tv(e0, e1),
                 start=(k == 0), stop=(k == 15))
            c.mm(psb, xps[1].tv(k * 16, (k + 1) * 16), xck.tv(e0, e1),
                 start=(k == 0), stop=(k == 15))
            c.mm(psc, xps[2].tv(k * 16, (k + 1) * 16), xck.tv(e0, e1),
                 start=(k == 0), stop=(k == 15))
        c.vop("tensor_copy", dlr.tv(e0, e1, p0=0, p1=64), psd)
        c.vop("tensor_copy", Bc.tv(e0, e1, p0=0, p1=16), psb)
        c.vop("tensor_copy", Cc.tv(e0, e1, p0=0, p1=16), psc)

    dts = c.atile(16 * 128, BF16)
    for mo in range(16):
        src = TV(W["dtT"], W["dtT"].handle.ap()[mo], 0, 64, 0, 1 << 30)
        c.p.dma("gpsimd", dts.tv(mo * 128, (mo + 1) * 128, p0=0, p1=64), src)
    selw = c.atile(16 * 128, BF16)
    c.p.dma("gpsimd", selw.tv(p0=0, p1=16), W["sel16_dram"].tv(ps=(0, 16)))

    hfin = c.atile(256, F32)
    atot = c.atile(256, F32)
    hin = c.atile(256, F32)

    def compute_delta(m, dbuf, scratch, xcbuf=None):
        if xcbuf is not None:
            srcx = TV(W["xc_dram"], W["xc_dram"].handle.ap()[m], 0, 128,
                      m * T * 2, (m + 1) * T * 2)
            c.p.dma("sync", xcbuf.tv(), srcx)
        for st in range(nst):
            e0, e1 = st * 512, (st + 1) * 512
            ps = c.pbtv(6 + st % 2, 0, 128, 0, 512)
            c.mm(ps, dts.tv(m * 128, (m + 1) * 128, p0=0, p1=64),
                 dlr.tv(e0, e1, p0=0, p1=64), start=True, stop=True)
            c.sop("activation", dbuf.tv(e0, e1), ps, func=AF.Identity,
                  bias=W["dtb_sb"].tv(fs=(m, m + 1)))
        dabs = scratch
        c.sop("activation", dabs.tv(), dbuf.tv(), func=AF.Abs)
        c.sop("activation", dabs.tv(), dabs.tv(), func=AF.Exp, scale=-1.0)
        c.sop("activation", dabs.tv(), dabs.tv(), func=AF.Ln, bias=1.0)
        c.vop("tensor_relu", dbuf.tv(), dbuf.tv(), extra_reads=[dbuf.tv()])
        c.vop("tensor_add", dbuf.tv(), dbuf.tv(), dabs.tv(), extra_reads=[dbuf.tv()])
        return dabs

    def bc_bcast(bank0, srcbc, n):
        for st in range(nst):
            ps = c.pbtv(bank0 + st, 0, 128, 0, 512)
            c.mm(ps, selw.tv(n * 128, (n + 1) * 128, p0=0, p1=16),
                 srcbc.tv(st * 512, (st + 1) * 512, p0=0, p1=16),
                 start=True, stop=True)
        return c.pbw(bank0, T)

    def acol(m, n):
        return W["A_sb"].tv(fs=(m * 16 + n, m * 16 + n + 1))

    hfin = c.atile(256, F32)
    atot = c.atile(256, F32)
    hin = c.atile(256, F32)

    for m in range(16):
        c.amark()
        dbuf = c.atile(T, F32)
        scratch = c.atile(T, F32)
        xcb = c.atile(T, BF16)
        dx = compute_delta(m, dbuf, scratch, xcb)
        c.vop("tensor_mul", dx.tv(), dbuf.tv(), xcb.tv())
        a = [c.atile(T, F32) for _ in range(2)]
        b = [c.atile(T, BF16) for _ in range(2)]
        H = [c.atile(T, BF16) for _ in range(2)]
        g2 = [c.atile(T, BF16) for _ in range(2)]
        Bb = [c.atile(T, BF16) for _ in range(2)]
        Cb = [c.atile(T, BF16) for _ in range(2)]
        dxb = c.atile(T, BF16)
        c.vop("tensor_copy", dxb.tv(), dx.tv())
        y = c.atile(T, BF16)
        cdT = c.atile(1, F32)
        c.vop("tensor_reduce", cdT.tv(), dbuf.tv(), AX.X, OP.add)
        for n in range(16):
            pp = n % 2
            pB = bc_bcast((n % 2) * 4, Bc, n)
            c.sop("activation", Bb[pp].tv(), pB, func=AF.Identity)
            pC = bc_bcast(((n + 1) % 2) * 4, Cc, n)
            c.sop("activation", Cb[pp].tv(), pC, func=AF.Identity)
            c.sop("activation", a[pp].tv(), dbuf.tv(), func=AF.Exp, scale=acol(m, n))
            c.sop("activation", atot.tv(m * 16 + n, m * 16 + n + 1), cdT.tv(),
                  func=AF.Exp, scale=acol(m, n))
            c.vop("tensor_mul", b[pp].tv(), dxb.tv(), Bb[pp].tv())
            c.vop("tensor_tensor_scan", H[pp].tv(), a[pp].tv(), b[pp].tv(), 0.0,
                  op0=OP.mult, op1=OP.add)
            c.vop("tensor_copy", hfin.tv(m * 16 + n, m * 16 + n + 1),
                  H[pp].tv(T - 1, T))
            if n == 0:
                c.vop("tensor_mul", y.tv(), H[pp].tv(), Cb[pp].tv())
            else:
                c.vop("tensor_mul", g2[pp].tv(), H[pp].tv(), Cb[pp].tv())
                c.vop("tensor_add", y.tv(), y.tv(), g2[pp].tv(),
                      extra_reads=[y.tv()])
        c.vop("scalar_tensor_tensor", y.tv(), xcb.tv(), W["D_sb"].tv(fs=(m, m + 1)),
              y.tv(), op0=OP.mult, op1=OP.add, extra_reads=[y.tv()])
        c.p.dma("sync", TV(W["y_dram"], W["y_dram"].handle.ap()[m], 0, 128,
                           m * T * 2, (m + 1) * T * 2), y.tv())
        c.areset()

    c.p.dma("sync", TV(W["cc_in"], W["cc_in"].handle.ap()[:, 0:256], 0, 128, 0, 1024),
            hfin.tv())
    c.p.dma("sync", TV(W["cc_in"], W["cc_in"].handle.ap()[:, 256:512], 0, 128, 1024, 2048),
            atot.tv())
    c.p.cc("AllGather", OP.bypass, rg, W["cc_in"].tv(), W["cc_out"].tv())
    acc = c.atile(256, F32)
    tmpc = c.atile(512, F32)
    c.vop("memset", acc.tv(), 0.0)
    c.vop("memset", hin.tv(), 0.0)
    for pr in range(nprev):
        blk = TV(W["cc_out"], W["cc_out"].handle.ap()[pr * 128:(pr + 1) * 128, :],
                 0, 128, 0, 1 << 30)
        c.p.dma("sync", tmpc.tv(), blk)
        c.vop("tensor_mul", acc.tv(), acc.tv(), tmpc.tv(256, 512),
              extra_reads=[acc.tv()])
        c.vop("tensor_add", acc.tv(), acc.tv(), tmpc.tv(0, 256),
              extra_reads=[acc.tv()])
        c.vop("tensor_scalar", tmpc.tv(0, 256), acc.tv(), sel.tv(fs=(pr, pr + 1)),
              None, op0=OP.mult)
        c.vop("tensor_add", hin.tv(), hin.tv(), tmpc.tv(0, 256),
              extra_reads=[hin.tv()])

    for m in range(16):
        c.amark()
        dbuf = c.atile(T, F32)
        scratch = c.atile(T, F32)
        compute_delta(m, dbuf, scratch)
        cd = c.atile(T, F32)
        ones_bc = c.ones_f32.tv().with_ap(
            c.ones_f32.tv().ap.broadcast_to((128, T)))
        c.vop("tensor_tensor_scan", cd.tv(), ones_bc, dbuf.tv(), 0.0,
              op0=OP.mult, op1=OP.add)
        P = [c.atile(T, BF16) for _ in range(2)]
        Cb2 = [c.atile(T, BF16) for _ in range(2)]
        acc2 = c.atile(T, BF16)
        yb = c.atile(T, BF16)
        c.p.dma("scalar", yb.tv(), TV(W["y_dram"], W["y_dram"].handle.ap()[m],
                                      0, 128, m * T * 2, (m + 1) * T * 2))
        for n in range(16):
            pp = n % 2
            pC = bc_bcast((n % 2) * 4, Cc, n)
            c.sop("activation", Cb2[pp].tv(), pC, func=AF.Identity)
            c.sop("activation", P[pp].tv(), cd.tv(), func=AF.Exp, scale=acol(m, n))
            c.vop("tensor_mul", P[pp].tv(), P[pp].tv(), Cb2[pp].tv(),
                  extra_reads=[P[pp].tv()])
            if n == 0:
                c.vop("tensor_scalar", acc2.tv(), P[pp].tv(),
                      hin.tv(m * 16 + n, m * 16 + n + 1), None, op0=OP.mult)
            else:
                c.vop("scalar_tensor_tensor", acc2.tv(), P[pp].tv(),
                      hin.tv(m * 16 + n, m * 16 + n + 1), acc2.tv(),
                      op0=OP.mult, op1=OP.add, extra_reads=[acc2.tv()])
        c.vop("tensor_add", yb.tv(), acc2.tv(), yb.tv(), extra_reads=[yb.tv()])
        c.p.dma("sync", TV(W["y_dram"], W["y_dram"].handle.ap()[m], 0, 128,
                           m * T * 2, (m + 1) * T * 2), yb.tv())
        c.areset()

    c.amark()
    ln2 = c.atile(KT * 512, BF16)
    g = c.atile(16 * 512, BF16)
    sz = c.atile(512, BF16)
    yb2 = c.atile(512, BF16)
    wxs2 = [c.atile(1024, BF16) for _ in range(2)]
    wos = [c.atile(2048, BF16) for _ in range(2)]
    for st in range(nst):
        e0 = HALO_M + st * 512
        lns = [ln2.sub(k * 512) for k in range(KT)]
        emit_ln(c, e0, e0 + 512, lns)
        _wslab_dma(c, wxs2[0], W["wxT"], 16, 1024)
        for mz in range(16):
            if mz + 1 < 16:
                _wslab_dma(c, wxs2[(mz + 1) % 2], W["wxT"], 16 + mz + 1, 1024)
            ps = c.pbtv(mz % 4, 0, 128, 0, 512)
            for k in range(KT):
                c.mm(ps, wxs2[mz % 2].tv(k * 128, (k + 1) * 128), lns[k].tv(0, 512),
                     start=(k == 0), stop=(k == KT - 1))
            c.sop("activation", sz.tv(), ps, func=AF.Silu)
            src = TV(W["y_dram"], W["y_dram"].handle.ap()[mz][:, st * 512:(st + 1) * 512],
                     0, 128, st * 1024, (st + 1) * 1024)
            c.p.dma("scalar", yb2.tv(), src)
            c.vop("tensor_mul", g.tv(mz * 512, (mz + 1) * 512), yb2.tv(), sz.tv())
        _wslab_dma(c, wos[0], W["outT"], 0, 2048)
        for mo in range(8):
            if mo + 1 < 8:
                _wslab_dma(c, wos[(mo + 1) % 2], W["outT"], mo + 1, 2048)
            ps = c.pbtv(4 + mo % 2, 0, 128, 0, 512)
            for k in range(16):
                c.mm(ps, wos[mo % 2].tv(k * 128, (k + 1) * 128),
                     g.tv(k * 512, (k + 1) * 512), start=(k == 0), stop=(k == 15))
            rs = c.resid[mo].tv(fs=(e0, e0 + 512))
            c.vop("scalar_tensor_tensor", rs, ps, 1.0, rs, op0=OP.mult, op1=OP.add,
                  extra_reads=[rs])
    c.areset()
    c.areset()


def prep_mamba_host(inputs, rank_sel):
    wx = inputs["L0_attn_b_in_proj_w"]
    xp = inputs["L0_attn_b_x_proj_w"]
    out = {
        "mam_wxT": tile_kxm(np.ascontiguousarray(wx), 8).astype(np.float32),
        "mam_xpdT": tile_kxm(np.ascontiguousarray(xp[:, 0:64]), 16).astype(np.float32),
        "mam_xpbT": tile_kxm(np.ascontiguousarray(xp[:, 64:80]), 16).astype(np.float32),
        "mam_xpcT": tile_kxm(np.ascontiguousarray(xp[:, 80:96]), 16).astype(np.float32),
        "mam_dtT": np.ascontiguousarray(
            inputs["L0_attn_b_dt_proj_w"].reshape(64, 16, 128).transpose(1, 0, 2)
        ).astype(np.float32),
        "mam_conv": np.ascontiguousarray(
            inputs["L0_attn_b_conv_w"][:, 0, :].reshape(16, 128, 4).transpose(1, 0, 2)
            .reshape(128, 64)).astype(np.float32),
        "mam_convb": inputs["L0_attn_b_conv_b"].reshape(16, 128).T.astype(np.float32),
        "mam_dtb": inputs["L0_attn_b_dt_proj_b"].reshape(16, 128).T.astype(np.float32),
        "mam_Alog": np.ascontiguousarray(
            inputs["L0_attn_b_A_log"].reshape(16, 128, 16).transpose(1, 0, 2)
            .reshape(128, 256)).astype(np.float32),
        "mam_D": inputs["L0_attn_b_D"].reshape(16, 128).T.astype(np.float32),
        "mam_outT": tile_kxm(np.ascontiguousarray(inputs["L0_attn_b_out_proj_w"]), 16
                             ).astype(np.float32),
    }
    sel16 = np.zeros((16, 16 * 128), np.float32)
    for n in range(16):
        sel16[n, n * 128:(n + 1) * 128] = 1.0
    out["mam_sel16"] = sel16
    selm = np.zeros((128, 8), np.float32)
    for pr in range(3):
        selm[:, pr] = 1.0 if rank_sel == pr + 1 else 0.0
    selm[:, 3] = 0.0 if rank_sel == 0 else 1.0
    out["mam_sel"] = selm
    return out


def declare_mamba(prog, T=NT, ranks=4):
    W = {
        "wxT": prog.dram("mam_wxT", [32, 128, 1024], F32, kind="ExternalInput"),
        "xpdT": prog.dram("mam_xpdT", [1, 128, 1024], F32, kind="ExternalInput"),
        "xpbT": prog.dram("mam_xpbT", [1, 128, 256], F32, kind="ExternalInput"),
        "xpcT": prog.dram("mam_xpcT", [1, 128, 256], F32, kind="ExternalInput"),
        "dtT": prog.dram("mam_dtT", [16, 64, 128], F32, kind="ExternalInput"),
        "conv_dram": prog.dram("mam_conv", [128, 64], F32, kind="ExternalInput"),
        "convb_dram": prog.dram("mam_convb", [128, 16], F32, kind="ExternalInput"),
        "dtb_dram": prog.dram("mam_dtb", [128, 16], F32, kind="ExternalInput"),
        "Alog_dram": prog.dram("mam_Alog", [128, 256], F32, kind="ExternalInput"),
        "D_dram": prog.dram("mam_D", [128, 16], F32, kind="ExternalInput"),
        "outT": prog.dram("mam_outT", [8, 128, 2048], F32, kind="ExternalInput"),
        "sel16_dram": prog.dram("mam_sel16", [16, 2048], F32, kind="ExternalInput"),
        "sel_dram": prog.dram("mam_sel", [128, 8], F32, kind="ExternalInput"),
        "xc_dram": prog.dram("mam_xc", [16, 128, T], BF16),
        "y_dram": prog.dram("mam_y", [16, 128, T], BF16),
        "cc_in": prog.dram("mam_cc_in", [128, 512], F32),
        "cc_out": prog.dram("mam_cc_out", [ranks * 128, 512], F32),
        "conv_sb": prog.sbuf("mam_conv_sb", [128, 64], F32),
        "convb_sb": prog.sbuf("mam_convb_sb", [128, 16], F32),
        "dtb_sb": prog.sbuf("mam_dtb_sb", [128, 16], F32),
        "A_dram_sb": prog.sbuf("mam_Alog_sb", [128, 256], F32),
        "A_sb": prog.sbuf("mam_A_sb", [128, 256], F32),
        "D_sb": prog.sbuf("mam_D_sb", [128, 16], F32),
        "sel_sb": prog.sbuf("mam_sel_sb", [128, 8], F32),
    }
    return W


def load_mamba_consts(core, W):
    p = core.p
    p.dma("sync", W["conv_sb"].tv(), W["conv_dram"].tv())
    p.dma("sync", W["convb_sb"].tv(), W["convb_dram"].tv())
    p.dma("sync", W["dtb_sb"].tv(), W["dtb_dram"].tv())
    p.dma("sync", W["A_dram_sb"].tv(), W["Alog_dram"].tv())
    p.dma("sync", W["D_sb"].tv(), W["D_dram"].tv())
    p.dma("sync", W["sel_sb"].tv(), W["sel_dram"].tv())



def emit_attn0a(core, W, nquarters=None):
    c = core
    NEXT = c.NEXT
    NQ = c.NT + 128
    ns = NQ // 32
    if nquarters is None:
        nquarters = ns // 4
    nsq = ns // nquarters
    bias = W["bias_sb"]

    def bcol(i, p1=128):
        return bias.tv(ps=(0, p1), fs=(i, i + 1))

    for k in range(KT):
        src = TV(W["x0e"], W["x0e"].handle.ap()[k * 128:(k + 1) * 128,
                                                RES_OFF:RES_OFF + c.NRES],
                 k * 128, (k + 1) * 128, RES_OFF * 4, (RES_OFF + c.NRES) * 4)
        c.p.dma("sync", c.resid[k].tv(fs=(0, c.NRES)), src)

    c.amark()
    posD = c.atile(3040, BF16)
    maskD = c.atile(3040, BF16)
    c.amark()
    relsb = c.atile(64, BF16)
    wpossb = c.atile(16, BF16)
    c.p.dma("gpsimd", relsb.tv(p0=0, p1=16), W["relT"].tv(ps=(0, 16)))
    c.p.dma("gpsimd", wpossb.tv(p0=0, p1=16), W["wpos"].tv(ps=(0, 16)))
    posf = c.atile(16, F32)
    pps = c.pbtv(0, 0, 64, 0, 16)
    c.mm(pps, relsb.tv(p0=0, p1=16), wpossb.tv(p0=0, p1=16), start=True, stop=True)
    c.vop("tensor_copy", posf.tv(p0=0, p1=64), pps)
    c.p.dma("sync", W["stageP"].tv(ps=(0, 64)), posf.tv(p0=0, p1=64))
    posfT = c.atile(64, F32)
    c.p.dma("sync", posfT.tv(p0=0, p1=16),
            TV(W["stageP"], W["stageP"].handle.ap().transpose([1, 0]), 0, 64, 0, 1 << 30),
            allow_slow_non_contiguous=True)
    posP = c.atile(128, F32)
    c.vop("memset", posP.tv(), 0.0)
    for h in range(8):
        c.p.dma("sync", posP.tv(32, 96, p0=h * 16, p1=(h + 1) * 16),
                posfT.tv(0, 64, p0=0, p1=16))
    pp = posP.tv(32, 96)
    c.vop("tensor_scalar", pp, pp, bcol(W["BFB"]), None, op0=OP.add,
          extra_reads=[pp])
    mext = c.atile(128, BF16)
    c.vop("memset", mext.tv(), -1.0e30)
    c.vop("memset", mext.tv(32, 96), 0.0)
    for dt in range(32):
        c.vop("tensor_copy", posD.tv(dt * 95, dt * 95 + 95),
              posP.tv(32 - dt, 127 - dt))
        c.vop("tensor_copy", maskD.tv(dt * 95, dt * 95 + 95),
              mext.tv(32 - dt, 127 - dt))
    c.areset()
    mb = maskD.tv(p0=0, p1=8)
    c.vop("tensor_scalar", mb, mb, bcol(17, 8), None, op0=OP.add,
          extra_reads=[mb])
    wfbd = c.atile(256, BF16)
    c.p.dma("gpsimd", wfbd.tv(0, 128), TV(W["wfbd"], W["wfbd"].handle.ap()[0], 0, 128, 0, 4))
    c.p.dma("gpsimd", wfbd.tv(128, 256), TV(W["wfbd"], W["wfbd"].handle.ap()[1], 0, 128, 4, 8))
    wbd = c.atile(16, BF16)
    c.p.dma("gpsimd", wbd.tv(0, 16), W["wbonddmg_dram"].tv())
    bvrow = c.atile(1024, BF16)
    c.p.dma("gpsimd", bvrow.tv(p0=0, p1=1), W["bvalrow"].tv(ps=(0, 1)))

    em = [c.atile(3040, BF16)] * 2
    lg = [c.atile(1520, F32) for _ in range(2)]
    wdt = [c.atile(1520, F32)] * 2
    wn = [c.atile(1520, BF16)] * 2
    wT = [c.atile(512, BF16)] * 2
    stag = [c.atile(1024, BF16)] * 2
    atok = [c.atile(1024, BF16)] * 2
    z1 = [c.atile(3040, BF16) for _ in range(2)]
    zbd = z1
    smst = [c.atile(32, F32)] * 2
    ctmp = [c.atile(128, F32)] * 2
    vsl8 = c.atile(8 * 1024, BF16)
    cs8 = c.atile(8 * 1024, BF16)
    dsl = c.atile(2 * 1024, BF16)
    for g in range(2):
        _wslab_dma(c, dsl.sub(g * 1024), W["dispT"], g, 1024)
    for k in range(KT):
        _wslab_dma(c, vsl8.sub(k * 1024), W["valT"], k, 1024)
        _wslab_dma(c, cs8.sub(k * 1024), W["cprojT"], k, 1024)

    WH = 191
    assert nsq == 4 and NEXT - 128 * (nquarters - 1) == WH
    lne_s = [c.atile(WH, BF16) for _ in range(KT)]
    disp_s = c.atile(WH, BF16)
    qt = []
    for par in range(2):
        qt.append(dict(
            lne=lne_s,
            Pb=[c.atile(WH, F32) for _ in range(2)],
            Pd=[c.atile(WH, F32) for _ in range(2)],
            disp=disp_s,
            vtok=c.atile(2 * 1024, BF16),
        ))
    afm_all = c.atile(KT * nsq * 32, BF16)

    def preamble(q):
        t = qt[q % 2]
        e_lo = 128 * q
        e_hi = e_lo + WH
        lne, Pb, Pd, disp, vtok = (t["lne"], t["Pb"], t["Pd"], t["disp"],
                                   t["vtok"])
        c.amark()
        x0t = [c.atile(WH, F32) for _ in range(KT)]
        for k in range(KT):
            src = TV(W["x0e"], W["x0e"].handle.ap()[k * 128:(k + 1) * 128,
                                                    e_lo:e_hi],
                     k * 128, (k + 1) * 128, e_lo * 4, e_hi * 4)
            c.p.dma("sync", x0t[k].tv(), src)
        emit_ln(c, 0, WH, lne, src_tiles=x0t, pb0=0, pb1=1)
        c.areset()
        for g in range(2):
            ps = c.pbtv(g, 0, 128, 0, WH)
            for k in range(KT):
                c.mm(ps, dsl.tv(g * 1024 + k * 128, g * 1024 + (k + 1) * 128),
                     lne[k].tv(0, WH), start=(k == 0), stop=(k == KT - 1))
            c.sop("activation", disp.tv(0, WH), ps, func=AF.Identity,
                  bias=bcol(W["BDISP"] + g))
            psb = c.pbtv(2 + g, 0, 128, 0, WH)
            c.mm(psb, wfbd.tv(0, 128), disp.tv(0, WH), start=True, stop=True)
            c.vop("tensor_copy", Pb[g].tv(0, WH), psb)
            psd = c.pbtv(4 + g, 0, 128, 0, WH)
            c.mm(psd, wfbd.tv(128, 256), disp.tv(0, WH), start=True, stop=True)
            c.vop("tensor_copy", Pd[g].tv(0, WH), psd)
        for tt in range(2):
            a0 = tt * 128
            a1 = min(WH, a0 + 128)
            wq = a1 - a0
            for nh in range(2):
                ps = c.pbtv(nh, 0, wq, 0, 512)
                for k in range(KT):
                    c.mm(ps, lne[k].tv(a0, a1),
                         vsl8.tv(k * 1024 + nh * 512, k * 1024 + (nh + 1) * 512),
                         start=(k == 0), stop=False)
                c.mm(ps, c.ones_row_bf.tv(ps=(0, 1), fs=(0, wq)),
                     bvrow.tv(nh * 512, (nh + 1) * 512, p0=0, p1=1),
                     start=False, stop=True)
                c.vop("tensor_copy",
                      vtok.tv(tt * 1024 + nh * 512, tt * 1024 + (nh + 1) * 512,
                              p0=0, p1=wq), ps)

    def sub_head(q, li):
        Pb, Pd = qt[q % 2]["Pb"], qt[q % 2]["Pd"]
        i0 = q * nsq
        e_lo = 128 * q
        gi = i0 + li
        pr = li % 2
        qe = 63 + 32 * gi - e_lo
        klo = 32 * gi - e_lo
        if gi < 6:
            c.p.dma("scalar", em[pr].tv(p0=0, p1=8),
                    TV(W["em"], W["em"].handle.ap()[gi], 0, 8, 0, 1 << 30))
        for g in range(2):
            for c2 in range(2):
                cp = (li * 4 + g * 2 + c2) % 2
                dt0 = c2 * 16
                pbs = Pb[g].tv(klo, klo + 95).with_ap(
                    Pb[g].tv(klo, klo + 95).ap.unsqueeze(1)
                    .broadcast_to((128, 16, 95)))
                pbt = Pb[g].tv(qe + dt0, qe + dt0 + 16).with_ap(
                    Pb[g].tv(qe + dt0, qe + dt0 + 16).ap.unsqueeze(2)
                    .broadcast_to((128, 16, 95)))
                z1v = z1[cp].tv(0, 1520).with_ap(
                    z1[cp].tv(0, 1520).ap.rearrange("p (t u) -> p t u", u=95))
                c.gop("tensor_sub", z1v, pbs, pbt)
                c.vop("tensor_add", z1[cp].tv(0, 1520), z1[cp].tv(0, 1520),
                      posD.tv(dt0 * 95, (dt0 + 16) * 95),
                      extra_reads=[z1[cp].tv(0, 1520)])
                pds = Pd[g].tv(klo, klo + 95).with_ap(
                    Pd[g].tv(klo, klo + 95).ap.unsqueeze(1)
                    .broadcast_to((128, 16, 95)))
                pdt = Pd[g].tv(qe + dt0, qe + dt0 + 16).with_ap(
                    Pd[g].tv(qe + dt0, qe + dt0 + 16).ap.unsqueeze(2)
                    .broadcast_to((128, 16, 95)))
                z2v = z1[cp].tv(1520, 3040).with_ap(
                    z1[cp].tv(1520, 3040).ap.rearrange("p (t u) -> p t u", u=95))
                c.gop("tensor_sub", z2v, pds, pdt)
                c.sop("activation", zbd[cp].tv(0, 1520), z1[cp].tv(0, 1520),
                      func=AF.Gelu)
                c.sop("activation", zbd[cp].tv(1520, 3040), z1[cp].tv(1520, 3040),
                      func=AF.Gelu, bias=bcol(W["BFD"]))
                pwb = c.pbw(0, 1520, 0, 8)
                pwd = c.pbw(3, 1520, 0, 8)
                for sb in range(3):
                    s0, s1 = sb * 512, min(1520, (sb + 1) * 512)
                    psl = c.pbtv(sb, 0, 8, 0, s1 - s0)
                    c.mm(psl, wbd.tv(0, 8), zbd[cp].tv(s0, s1), start=True, stop=False)
                    c.mm(psl, c.ident.tv(ps=(0, 8), fs=(0, 8)),
                        maskD.tv(dt0 * 95 + s0, dt0 * 95 + s1, p0=0, p1=8),
                        start=False, stop=(gi >= 6))
                    if gi < 6:
                        c.mm(psl, c.ident.tv(ps=(0, 8), fs=(0, 8)),
                            em[pr].tv(dt0 * 95 + s0, dt0 * 95 + s1, p0=0, p1=8),
                            start=False, stop=True)
                    psl2 = c.pbtv(3 + sb, 0, 8, 0, s1 - s0)
                    c.mm(psl2, wbd.tv(8, 16), zbd[cp].tv(1520 + s0, 1520 + s1), start=True, stop=True)
                c.sop("activation", lg[cp].tv(p0=0, p1=8), pwd, func=AF.Tanh,
                      bias=bcol(16, 8), scale=0.5)
                c.vop("scalar_tensor_tensor", lg[cp].tv(p0=0, p1=8), lg[cp].tv(p0=0, p1=8),
                      -5.0, pwb, op0=OP.mult, op1=OP.add,
                      extra_reads=[lg[cp].tv(p0=0, p1=8)])
                dst = TV(W["stage"], W["stage"].handle.ap()[pr][g * 8:(g + 1) * 8,
                                                           dt0 * 95:(dt0 + 16) * 95],
                         g * 8, (g + 1) * 8, pr * 50000 + dt0 * 380,
                         pr * 50000 + (dt0 + 16) * 380)
                c.p.dma("sync", dst, lg[cp].tv(p0=0, p1=8))

    def sub_tail(q, li):
        vtok = qt[q % 2]["vtok"]
        i0 = q * nsq
        e_lo = 128 * q
        tt0 = q
        gi = i0 + li
        pr = li % 2
        klo = 32 * gi - e_lo
        src = TV(W["stage"], W["stage"].handle.ap()[pr]
                 .rearrange("h (t u) -> h t u", u=95).transpose([1, 0, 2]),
                 0, 16, pr * 50000, pr * 50000 + 49000)
        c.p.dma("sync", wdt[pr].tv(p0=0, p1=32).with_ap(
            wdt[pr].tv(p0=0, p1=32).ap.rearrange("p (h u) -> p h u", u=95)), src)
        w3 = wdt[pr].tv(p0=0, p1=32).with_ap(
            wdt[pr].tv(p0=0, p1=32).ap.rearrange("p (h u) -> p h u", u=95))
        nm = smst[pr].tv(0, 16, p0=0, p1=32)
        c.vop("tensor_reduce", nm, w3, AX.X, OP.max, negate=True)
        nmb = nm.with_ap(nm.ap.unsqueeze(2).broadcast_to((32, 16, 95)))
        c.vop("tensor_add", w3, w3, nmb, extra_reads=[wdt[pr].tv(p0=0, p1=32)])
        c.sop("activation", wn[pr].tv(p0=0, p1=32), wdt[pr].tv(p0=0, p1=32),
              func=AF.Exp)
        wnv = wn[pr].tv(p0=0, p1=32).with_ap(
            wn[pr].tv(p0=0, p1=32).ap.rearrange("p (h u) -> p h u", u=95))
        sm = smst[pr].tv(16, 32, p0=0, p1=32)
        c.vop("tensor_reduce", sm, wnv, AX.X, OP.add)
        c.vop("reciprocal", sm, sm, extra_reads=[sm])
        for h in range(16):
            pt = c.pbt16(6, 0, 95, h * 32, (h + 1) * 32)
            c.transpose(pt, wn[pr].tv(h * 95, (h + 1) * 95, p0=0, p1=32))
        c.vop("tensor_copy", wT[pr].tv(p0=0, p1=95), c.pbt16(6, 0, 95, 0, 512))
        abs0 = e_lo + klo
        ta = abs0 // 128 - tt0
        r0 = abs0 % 128
        n1 = min(95, 128 - r0)
        c.p.dma("scalar", stag[pr].tv(p0=0, p1=n1),
                vtok.tv(ta * 1024, (ta + 1) * 1024, p0=r0, p1=r0 + n1))
        if n1 < 95:
            c.p.dma("scalar", stag[pr].tv(p0=n1, p1=95),
                    vtok.tv((ta + 1) * 1024, (ta + 2) * 1024, p0=0, p1=95 - n1))
        po = c.pbw(6, 1024, 0, 32)
        for h in range(16):
            pslice = c.pbtv(6 + h // 8, 0, 32, (h % 8) * 64, (h % 8) * 64 + 64)
            c.mm(pslice, wT[pr].tv(h * 32, (h + 1) * 32, p0=0, p1=95),
                 stag[pr].tv(h * 64, (h + 1) * 64, p0=0, p1=95),
                 start=True, stop=True)
        po3 = po.with_ap(po.ap.rearrange("p (h f) -> p h f", f=64))
        smb = sm.with_ap(sm.ap.unsqueeze(2).broadcast_to((32, 16, 64)))
        at3 = atok[pr].tv(p0=0, p1=32).with_ap(
            atok[pr].tv(p0=0, p1=32).ap.rearrange("p (h f) -> p h f", f=64))
        c.vop("tensor_mul", at3, po3, smb)
        if "dbg" in W and gi == W.get("dbg_gi", -1):
            c.p.dma("gpsimd", W["dbg"].tv(ps=(0, 32)), atok[pr].tv(p0=0, p1=32))
            c.p.dma("gpsimd", W["dbg2"].tv(ps=(0, 32)), wn[pr].tv(p0=0, p1=32))
            c.p.dma("gpsimd", W["dbg3"].tv(ps=(0, 95)), stag[pr].tv(p0=0, p1=95))
            c.p.dma("gpsimd", W["dbg4"].tv(), vtok.tv(0, 1024))
        for ft in range(KT):
            pt = c.pbt16(6, 0, 128, ft * 32, (ft + 1) * 32)
            c.transpose(pt, atok[pr].tv(ft * 128, (ft + 1) * 128, p0=0, p1=32))
        afull = afm_all.tv(0, KT * nsq * 32)
        dst3 = afull.with_ap(afull.ap.rearrange("p (f x) -> p f x", f=KT)
                             [:, :, li * 32:(li + 1) * 32])
        s2 = c.pbt16(6, 0, 128, 0, KT * 32)
        src3 = s2.with_ap(s2.ap.rearrange("p (f u) -> p f u", f=KT))
        c.vop("tensor_copy", dst3, src3)

    def cproj(q):
        i0, i1 = q * nsq, (q + 1) * nsq
        r0 = max(0, 63 + 32 * i0 - RES_OFF)
        r1 = min(c.NRES, 63 + 32 * i1 - RES_OFF)
        a0 = r0 + RES_OFF - 63 - 32 * i0
        for mo in range(KT):
            ps = c.pbtv(6 + mo % 2, 0, 128, 0, r1 - r0)
            for k in range(KT):
                c.mm(ps, cs8.tv(mo * 1024 + k * 128, mo * 1024 + (k + 1) * 128),
                     afm_all.tv(k * nsq * 32 + a0, k * nsq * 32 + a0 + r1 - r0),
                     start=(k == 0), stop=(k == KT - 1))
            tmp = ctmp[mo % 2].tv(0, r1 - r0)
            c.sop("activation", tmp, ps, func=AF.Identity,
                  bias=bcol(W["BCPROJ"] + mo))
            rs = c.resid[mo].tv(fs=(r0, r1))
            c.vop("tensor_add", rs, rs, tmp, extra_reads=[rs])

    preamble(0)
    for q in range(nquarters):
        sub_head(q, 0)
        sub_head(q, 1)
        sub_tail(q, 0)
        sub_head(q, 2)
        sub_tail(q, 1)
        sub_head(q, 3)
        if q + 1 < nquarters:
            preamble(q + 1)
        sub_tail(q, 2)
        sub_tail(q, 3)
        cproj(q)
    c.areset()


A0_BDISP, A0_BFD, A0_BFB, A0_BBOND, A0_BDMG, A0_BVAL, A0_BCPROJ = 0, 2, 3, 4, 5, 6, 8


def prep_attn0a_host(inputs, seq_start, T=NT):
    import ml_dtypes
    Wd = inputs["L0_attn_a_W_disp"]; Wv = inputs["L0_attn_a_W_val"]
    Wc = inputs["L0_attn_a_W_cproj"]; Wf = inputs["L0_attn_a_W_fused"]
    out = {
        "a0_dispT": tile_kxm(np.ascontiguousarray(Wd), 8).astype(np.float32),
        "a0_valT": np.ascontiguousarray(Wv.reshape(8, 128, 1024)).astype(np.float32),
        "a0_cprojT": tile_kxm(np.ascontiguousarray(Wc), 8).astype(np.float32),
        "a0_relT": np.ascontiguousarray(inputs["L0_attn_a_rel_pos_emb"].T).astype(np.float32),
        "a0_wpos": inputs["L0_attn_a_W_pos"].astype(np.float32),
        "a0_bvalrow": inputs["L0_attn_a_b_val"].reshape(1, 1024).astype(np.float32),
    }
    wfbd = np.zeros((2, 128, 128), np.float32)
    for s in range(2):
        for h in range(8):
            wfbd[s, h * 16:(h + 1) * 16, h * 16:(h + 1) * 16] = Wf[:, s * 16:(s + 1) * 16]
    out["a0_wfbd"] = wfbd
    wbd = np.zeros((128, 16), np.float32)
    for h in range(8):
        wbd[h * 16:(h + 1) * 16, h] = inputs["L0_attn_a_W_bond"][:, 0]
        wbd[h * 16:(h + 1) * 16, 8 + h] = inputs["L0_attn_a_W_dmg"][:, 0]
    out["a0_wbonddmg"] = wbd
    bias = np.zeros((128, 16), np.float32)
    bias[:, A0_BDISP] = inputs["L0_attn_a_b_disp"][0:128]
    bias[:, A0_BDISP + 1] = inputs["L0_attn_a_b_disp"][128:256]
    bias[:, A0_BFD] = np.tile(inputs["L0_attn_a_b_fused"][16:32], 8)
    bias[:, A0_BFB] = np.tile(inputs["L0_attn_a_b_fused"][0:16], 8)
    bias[:, A0_BBOND] = inputs["L0_attn_a_b_bond"][0]
    bias[:, A0_BDMG] = inputs["L0_attn_a_b_dmg"][0]
    for mo in range(8):
        bias[:, A0_BCPROJ + mo] = inputs["L0_attn_a_b_cproj"][mo * 128:(mo + 1) * 128]
    out["a0_bias"] = bias
    em = np.zeros((6, 8, 3040), np.float32)
    for gi in range(6):
        for dt in range(32):
            u = np.arange(95)
            kk = seq_start - 191 + 32 * gi + u
            em[gi, :, dt * 95:(dt + 1) * 95] = np.where(kk < 0, -1.0e30, 0.0)
    out["a0_em"] = em.astype(ml_dtypes.bfloat16)
    return out


def declare_attn0a(prog, T=NT):
    NEXTl = T + 191
    W = {
        "x0e": prog.dram("a0_x0e", [1024, NEXTl], F32, kind="ExternalInput"),
        "dispT": prog.dram("a0_dispT", [2, 128, 1024], F32, kind="ExternalInput"),
        "valT": prog.dram("a0_valT", [8, 128, 1024], F32, kind="ExternalInput"),
        "cprojT": prog.dram("a0_cprojT", [8, 128, 1024], F32, kind="ExternalInput"),
        "relT": prog.dram("a0_relT", [16, 64], F32, kind="ExternalInput"),
        "wpos": prog.dram("a0_wpos", [16, 16], F32, kind="ExternalInput"),
        "wfbd": prog.dram("a0_wfbd", [2, 128, 128], F32, kind="ExternalInput"),
        "wbonddmg_dram": prog.dram("a0_wbonddmg", [128, 16], F32, kind="ExternalInput"),
        "bvalrow": prog.dram("a0_bvalrow", [1, 1024], F32, kind="ExternalInput"),
        "bias_dram": prog.dram("a0_bias", [128, 16], F32, kind="ExternalInput"),
        "em": prog.dram("a0_em", [6, 8, 3040], BF16, kind="ExternalInput"),
        "stage": prog.dram("a0_stage", [2, 16, 3040], F32),
        "cprojT16": prog.dram("a0_cprojT16", [8, 128, 1024], BF16),
        "stageP": prog.dram("a0_stageP", [64, 16], F32),
        "bias_sb": prog.sbuf("a0_bias_sb", [128, 18], F32),
        "BDISP": A0_BDISP, "BFD": A0_BFD, "BFB": A0_BFB, "BBOND": A0_BBOND,
        "BDMG": A0_BDMG, "BCPROJ": A0_BCPROJ,
    }
    return W


def load_attn0a_consts(core, W):
    core.p.dma("sync", W["bias_sb"].tv(fs=(0, 16)), W["bias_dram"].tv())
    core.vop("tensor_scalar", W["bias_sb"].tv(fs=(16, 17)),
             W["bias_sb"].tv(fs=(A0_BDMG, A0_BDMG + 1)), 0.5, None, op0=OP.mult)
    core.vop("tensor_scalar", W["bias_sb"].tv(fs=(17, 18)),
             W["bias_sb"].tv(fs=(A0_BBOND, A0_BBOND + 1)), -5.0, None, op0=OP.add)




SUBLAYERS = ["a0a", "ffn0a", "mam", "ffn0b", "a1a", "ffn1a", "a1b", "ffn1b"]

GROUPS = [SUBLAYERS]


def _declare_for(prog, names, nt, ranks):
    D_ = {}
    if "a0a" in names:
        D_["a0a"] = declare_attn0a(prog, T=nt)
    if "ffn0a" in names or "ffn0b" in names:
        D_["ffn0"] = declare_ffn0(prog)
    if "mam" in names:
        D_["mam"] = declare_mamba(prog, T=nt, ranks=ranks)
    if "a1a" in names:
        D_["a1a"] = declare_attn1a(prog)
    if "ffn1a" in names or "ffn1b" in names:
        D_["ffn1"] = declare_ffn1(prog)
    if "a1b" in names:
        D_["a1b"] = declare_attn1b(prog, ranks=ranks, T=nt)
    return D_


def _load_consts_for(core, D_):
    if "a0a" in D_:
        load_attn0a_consts(core, D_["a0a"])
    if "ffn0" in D_:
        load_ffn0_consts(core, D_["ffn0"])
    if "mam" in D_:
        load_mamba_consts(core, D_["mam"])
    if "a1a" in D_:
        load_attn1a_consts(core, D_["a1a"])
    if "a1b" in D_:
        load_attn1b_consts(core, D_["a1b"])


def build_group(names, nt, rg, ranks, first, last):
    nc = bass.Bass("TRN2", target_bir_lowering=False, debug=False)
    prog = Prog(nc)
    nres = nt + HALO_M
    if not first:
        rin = prog.dram("rin", [1024, nres], F32, kind="ExternalInput")
    rout = prog.dram("rout", [1024, nt if last else nres], F32, kind="ExternalOutput")
    D_ = _declare_for(prog, names, nt, ranks)
    core = Core(nc, prog, nt=nt)
    core.init_consts()
    _load_consts_for(core, D_)
    if not first:
        for k in range(KT):
            src = TV(rin, rin.handle.ap()[k * 128:(k + 1) * 128, :],
                     k * 128, (k + 1) * 128, 0, 1 << 30)
            prog.dma("sync", core.resid[k].tv(fs=(0, nres)), src)
    for nm in names:
        prog.phase = nm
        if nm == "a0a":
            emit_attn0a(core, D_["a0a"])
        elif nm == "ffn0a":
            emit_ffn0(core, D_["ffn0"], 0, nres)
        elif nm == "ffn0b":
            emit_ffn0(core, D_["ffn0"], HALO_M, nres)
        elif nm == "mam":
            emit_mamba(core, D_["mam"], rg)
        elif nm == "a1a":
            emit_attn1a(core, D_["a1a"], HALO_M, nres, rg)
        elif nm == "ffn1a" or nm == "ffn1b":
            emit_ffn1(core, D_["ffn1"], HALO_M, nres)
        elif nm == "a1b":
            emit_attn1b(core, D_["a1b"], HALO_M, nres, rg, ranks)
    prog.phase = "out"
    c0 = HALO_M if last else 0
    for k in range(KT):
        dst = TV(rout, rout.handle.ap()[k * 128:(k + 1) * 128, :],
                 k * 128, (k + 1) * 128, 0, 1 << 30)
        prog.dma("sync", dst, core.resid[k].tv(fs=(c0, nres)))
    prog.emit_all()
    return nc, prog


def prep_host_all(inputs, nt=NT, ncores=8, per_batch=4):
    shared = {}
    shared.update(prep_ffn0_host(inputs))
    shared.update(prep_ffn1_host(inputs))
    shared.update(prep_attn1a_host(inputs))
    ab = prep_attn1b_host(inputs, ranks=per_batch, T=nt)
    masks = {r: ab.pop(f"a1b_mask_r{r}") for r in range(per_batch)}
    shared.update(ab)
    mam_shared = None
    x = inputs["L0_attn_a_x"]
    B = x.shape[0]
    percore = []
    for c in range(ncores):
        b, r = c // per_batch, c % per_batch
        s = r * nt
        pc = {}
        mh = prep_mamba_host(inputs, rank_sel=r)
        a0 = prep_attn0a_host(inputs, seq_start=s, T=nt)
        if mam_shared is None:
            mam_shared = {k: v for k, v in mh.items() if k != "mam_sel"}
            shared.update(mam_shared)
            shared.update({k: v for k, v in a0.items() if k != "a0_em"})
        pc["mam_sel"] = mh["mam_sel"]
        pc["a0_em"] = a0["a0_em"]
        pc["a1b_mask"] = masks[r]
        x0e = np.zeros((nt + 191, 1024), np.float32)
        lo = s - 191
        src_lo = max(0, lo)
        x0e[src_lo - lo:] = x[b, src_lo:s + nt]
        pc["a0_x0e"] = np.ascontiguousarray(x0e.T)
        percore.append(pc)
    return shared, percore


def kernel(**inputs):
    nt = NT
    ncores = 8
    per_batch = 4
    rg = [[0, 1, 2, 3], [4, 5, 6, 7]]
    shared, percore = prep_host_all(inputs, nt, ncores, per_batch)
    x = inputs["L0_attn_a_x"]
    B, S, Dm = x.shape
    state = [None] * ncores
    n_groups = len(GROUPS)
    for gi, names in enumerate(GROUPS):
        first = gi == 0
        last = gi == n_groups - 1
        nc, prog = build_group(names, nt, rg, per_batch, first, last)
        needed = {b.name for b in prog.bufs
                  if b.space == "dram" and b.handle is not None}
        req = set()
        for alloc in nc.m.functions[0].allocations:
            try:
                if alloc.kind == "ExternalInput":
                    req.add(alloc.memorylocations[0].name)
            except AttributeError:
                pass
        in_maps = []
        for c in range(ncores):
            m = {}
            for k2 in req:
                if k2 == "partition_id":
                    continue
                if k2 == "rin":
                    m[k2] = state[c]
                elif k2 in percore[c]:
                    m[k2] = percore[c][k2]
                elif k2 in shared:
                    m[k2] = shared[k2]
                else:
                    raise KeyError(f"missing input {k2}")
            in_maps.append(m)
        res = run_bass_kernel_spmd(nc, in_maps, core_ids=list(range(ncores)))
        for c in range(ncores):
            state[c] = res.results[c]["rout"]
    out = np.empty((B, S, Dm), np.float32)
    for c in range(ncores):
        b, r = c // per_batch, c % per_batch
        out[b, r * nt:(r + 1) * nt] = state[c].T
    return out
